# version 02
# baseline (speedup 1.0000x reference)
import numpy as np
import ml_dtypes
from contextlib import ExitStack

import concourse.bass as bass
import concourse.mybir as mybir
from concourse.bass_utils import run_bass_kernel_spmd

F32 = mybir.dt.float32
BF16 = mybir.dt.bfloat16
AF = mybir.ActivationFunctionType
ALU = mybir.AluOpType
AX = mybir.AxisListType
BFNP = ml_dtypes.bfloat16

T = 2048
D = 1024
DC = 8
HALF = 1024
NCORES = 8

ENGINES = ("sync", "scalar", "vector", "tensor", "gpsimd")



class Builder:

    def __init__(self, ns):
        self.ns = ns
        self.cur = None
        self.eng = None
        self.cnt = {}

    def begin(self, engine_name, engine_obj):
        self.cur = engine_name
        self.eng = engine_obj
        self.cnt = {k: 0 for k in self.ns.sems}

    def val(self, sem):
        return self.cnt[sem]

    def tok(self, sem):
        return (sem, self.cnt[sem])

    def wait(self, eng, sem, val=None):
        if val is None:
            val = self.cnt[sem]
        if self.cur == eng and val > 0:
            self.eng.wait_ge(self.ns.sems[sem], val)

    def wait_tok(self, eng, t):
        if t is not None:
            self.wait(eng, t[0], t[1])

    AUTODRAIN = ("vector", "scalar", "gpsimd")

    def op(self, eng, method, *args, inc=None, inc_n=1, **kw):
        if inc is not None:
            self.cnt[inc] += inc_n
        if self.cur != eng:
            return None
        if eng in self.AUTODRAIN and method not in ("dma_start", "nop"):
            if getattr(self, "_dirty_" + eng, False):
                self.eng.drain()
            setattr(self, "_dirty_" + eng, True)
        ins = getattr(self.eng, method)(*args, **kw)
        if inc is not None:
            ins.then_inc(self.ns.sems[inc], inc_n)
        return ins

    def dma(self, eng, out, in_, inc, **kw):
        return self.op(eng, "dma_start", inc=inc, inc_n=16, out=out, in_=in_, **kw)

    def drain(self, eng):
        if self.cur == eng:
            self.eng.drain()

    def barrier(self):
        self.cnt["bar"] += 5
        tgt = self.cnt["bar"]
        if self.cur is not None:
            if self.cur != "sync":
                self.eng.drain()
            self.eng.nop().then_inc(self.ns.sems["bar"], 1)
            self.eng.wait_ge(self.ns.sems["bar"], tgt)
            for e in self.AUTODRAIN:
                setattr(self, "_dirty_" + e, False)


class NS:
    pass



def _rope_tables():
    i = np.arange(D // 2, dtype=np.float32)
    theta = (np.float32(10000.0) ** (np.float32(-2.0) * (i - np.float32(1.0)) / np.float32(D))).astype(np.float32)
    ang = (np.arange(T, dtype=np.float32)[:, None] * theta[None, :]).astype(np.float32)
    c = np.cos(ang).astype(np.float32)
    s = np.sin(ang).astype(np.float32)
    return np.ascontiguousarray(c.T), np.ascontiguousarray(s.T)


def _causal_masks():
    qq = np.arange(128)[:, None]; kk = np.arange(512)[None, :]
    m1 = np.stack([(kk <= j * 128 + qq) for j in range(4)]).astype(np.float32)
    m1 = (1.0 - m1) * np.float32(-1e30)
    kk2 = np.arange(128)[:, None]; qq2 = np.arange(512)[None, :]
    m2 = np.stack([(kk2 <= qq2 - j * 128) for j in range(4)]).astype(np.float32)
    m2 = (1.0 - m2) * np.float32(-1e30)
    return np.ascontiguousarray(m1), np.ascontiguousarray(m2)


def _part_layout(v):
    E, L = v.shape
    n = L // 128
    return np.ascontiguousarray(v.reshape(E, n, 128).transpose(2, 0, 1).reshape(128, E * n))


def prep_host(inputs):
    g = {}
    f32 = np.float32
    asc = np.ascontiguousarray

    perm = np.concatenate([np.arange(0, D, 2), np.arange(1, D, 2)])
    g["wq_t"] = asc(np.asarray(inputs["L0_attn_wq"], f32).T[:, perm])
    g["wk_t"] = asc(np.asarray(inputs["L0_attn_wk"], f32).T[:, perm])
    g["wv_t"] = asc(np.asarray(inputs["L0_attn_wv"], f32).T[:, perm])
    ipw = np.asarray(inputs["L0_attn_in_proj_w"], f32)
    ipb = np.asarray(inputs["L0_attn_in_proj_b"], f32)
    g["wqi_t"] = asc(ipw[0:D].T[perm, :])
    g["wki_t"] = asc(ipw[D:2 * D].T[perm, :])
    g["wvi_t"] = asc(ipw[2 * D:3 * D].T[perm, :])
    g["bqi_p"] = _part_layout(ipb[0:D][None, :])
    g["bki_p"] = _part_layout(ipb[D:2 * D][None, :])
    g["bvi_p"] = _part_layout(ipb[2 * D:3 * D][None, :])
    g["outw_t"] = asc(np.asarray(inputs["L0_attn_out_w"], f32).T)
    g["bvi_row"] = asc(ipb[2 * D:3 * D][None, :])
    g["outb_row"] = asc(np.asarray(inputs["L0_attn_out_b"], f32)[None, :])
    g["rope_cos"], g["rope_sin"] = _rope_tables()
    g["mask1"], g["mask2"] = _causal_masks()

    g["l0_ns_p"] = _part_layout(np.asarray(inputs["L0_ffn_norm_scale"], f32)[None, :])
    g["l0_wg"] = asc(np.asarray(inputs["L0_ffn_wg"], f32))
    g["l0_bg_row"] = asc(np.asarray(inputs["L0_ffn_bg"], f32)[None, :])
    w1 = np.asarray(inputs["L0_ffn_w1"], f32)
    g["l0_w1a"] = asc(w1[:, :, 0::2])
    g["l0_w1b"] = asc(w1[:, :, 1::2])
    b1 = np.asarray(inputs["L0_ffn_b1"], f32)
    g["l0_b1a_p"] = _part_layout(asc(b1[:, 0::2]))
    g["l0_b1b_p"] = _part_layout(asc(b1[:, 1::2]))
    w2l0 = np.asarray(inputs["L0_ffn_w2"], f32)
    g["l0_w2p"] = asc(w2l0.reshape(16, 8, 128, 8, 128).transpose(0, 3, 2, 1, 4)
                      .reshape(16, 8, 128, 8 * 128))
    g["l0_b2"] = asc(np.asarray(inputs["L0_ffn_b2"], f32))

    wqkv = np.asarray(inputs["L1_attn_W_qkv"], f32)
    g["l1_wqT"] = asc(wqkv[0:D].T)
    g["l1_wkT"] = asc(wqkv[D:2 * D].T)
    g["l1_wvT"] = asc(wqkv[2 * D:3 * D].T)
    g["l1_wpT"] = asc(np.asarray(inputs["L1_attn_W_proj"], f32).T)
    g["l1_bp_row"] = asc(np.asarray(inputs["L1_attn_b_proj"], f32)[None, :])

    g["l1_gw"] = asc(np.asarray(inputs["L1_ffn_gate_W"], f32))
    g["l1_gb_row"] = asc(np.asarray(inputs["L1_ffn_gate_b"], f32)[None, :])
    g["l1_w1"] = asc(np.asarray(inputs["L1_ffn_W1"], f32))
    g["l1_b1_p"] = _part_layout(np.asarray(inputs["L1_ffn_b1"], f32))
    w2l1 = np.asarray(inputs["L1_ffn_W2"], f32)
    g["l1_w2p"] = asc(w2l1.reshape(8, 16, 128, 8, 128).transpose(0, 3, 2, 1, 4)
                      .reshape(8, 8, 128, 16 * 128))
    g["l1_b2"] = asc(np.asarray(inputs["L1_ffn_b2"], f32))
    g["l1_gam_p"] = _part_layout(np.asarray(inputs["L1_ffn_ln_gamma"], f32)[None, :])
    g["l1_bet_p"] = _part_layout(np.asarray(inputs["L1_ffn_ln_beta"], f32)[None, :])

    g["ident_f"] = np.eye(128, dtype=f32)

    x = np.asarray(inputs["L0_attn_x"], f32)
    xT = np.ascontiguousarray(x.transpose(0, 2, 1))
    return g, [np.ascontiguousarray(xT[b]) for b in range(8)]



def Xc(ns, dc, t0=0, tlen=T):
    return ns.X[:, dc * T + t0: dc * T + t0 + tlen]


def make_ns(nc, g, sublayers):
    ns = NS()
    ns.nc = nc
    ns.g = g
    d = {}

    def din(name, arr):
        d[name] = nc.dram_tensor(name, list(arr.shape), F32, kind="ExternalInput").ap()
        return d[name]

    ns.dram = d
    d["xT"] = nc.dram_tensor("xT", [D, T], F32, kind="ExternalInput").ap()
    d["yT"] = nc.dram_tensor("yT", [D, T], F32, kind="ExternalOutput").ap()
    for k, v in g.items():
        din(k, v)
    d["row_dram"] = nc.dram_tensor("row_dram", [128, 128], F32, kind="Internal").ap()
    d["qh_dram"] = nc.dram_tensor("qh_dram", [D, T], BF16, kind="Internal").ap()
    d["kh_dram"] = nc.dram_tensor("kh_dram", [D, T], BF16, kind="Internal").ap()
    d["q2_dram"] = nc.dram_tensor("q2_dram", [D, T], BF16, kind="Internal").ap()
    d["o2_dram"] = nc.dram_tensor("o2_dram", [D, T], BF16, kind="Internal").ap()
    return ns



def emit_load_x(b, ns):
    for dc in range(DC):
        b.dma("sync", out=Xc(ns, dc), in_=ns.dram["xT"][dc * 128:(dc + 1) * 128, :], inc="xl")
    ns.x_tok = b.tok("xl")


def emit_store_x(b, ns):
    b.barrier()
    for dc in range(DC):
        b.dma("scalar", out=ns.dram["yT"][dc * 128:(dc + 1) * 128, :], in_=Xc(ns, dc), inc="st")
    b.wait("scalar", "st")


def emit_load_consts(b, ns):
    b.op("vector", "memset", ns.ones_col[:], 1.0)
    b.op("vector", "memset", ns.ones_bf[:], 1.0, inc="dv")
    b.dma("sync", out=ns.ident_f[:], in_=ns.dram["ident_f"], inc="xl")
    ns.const_tok = b.tok("xl")
    ns.ones_tok = b.tok("dv")



def emit_colstats(b, ns, half, src, kinds_fn, out_rows, tag, nt=8):
    n_rows = len(out_rows)
    nw = nt
    prev_mm = None
    for dc in range(DC):
        b.wait_tok("scalar", prev_mm)
        b.op("scalar", "activation", out=ns.sq[:], in_=src(dc), func=AF.Square, inc="ac")
        b.wait("tensor", "ac", b.val("ac"))
        if dc == 0:
            b.wait("tensor", "dv", b.val("dv"))
        for w in range(nw):
            b.op("tensor", "matmul", ns.ps_misc[:, 2 * w:2 * w + 1],
                 src(dc)[:, 128 * w:128 * w + 128], ns.ones_col[:, 0:1],
                 start=(dc == 0 and w == 0), stop=False)
            b.op("tensor", "matmul", ns.ps_misc[:, 2 * w + 1:2 * w + 2],
                 ns.sq[:, 128 * w:128 * w + 128], ns.ones_col[:, 0:1],
                 start=False, stop=(dc == DC - 1 and w == nw - 1),
                 inc=("pe" if (w == nw - 1) else None))
        prev_mm = b.tok("pe")
    pe_done = b.tok("pe")
    b.wait_tok("vector", pe_done)
    b.wait_tok("scalar", pe_done)
    kinds_fn(b)
    b.wait("tensor", "dv", b.val("dv"))
    b.wait("tensor", "ac", b.val("ac"))
    b.op("tensor", "transpose", ns.ps_misc[0:n_rows * nw, 0:128],
         ns.scol[:, 0:n_rows * nw], ns.ident_f[:], inc="pe")
    b.wait("vector", "pe", b.val("pe"))
    b.wait("vector", "m1", b.val("m1"))
    b.op("vector", "tensor_copy", out=ns.rowstage[0:n_rows * nw, 0:128],
         in_=ns.ps_misc[0:n_rows * nw, 0:128], inc="dv")
    b.wait("sync", "dv", b.val("dv"))
    b.dma("sync", out=ns.dram["row_dram"][0:n_rows * nw, 0:128],
          in_=ns.rowstage[0:n_rows * nw, 0:128], inc="m1")
    b.wait("sync", "m1", b.val("m1"))
    for r in range(n_rows):
        b.dma("sync", out=out_rows[r][0:1, 0:nw * 128],
              in_=ns.dram["row_dram"][r * nw:(r + 1) * nw, 0:128]
              .rearrange("a c -> (a c)")[None, :], inc="m1")
    return b.tok("m1")


def emit_bcast(b, ns, row_ap, out_bc, n, wait_t=None, base=0, bf=False):
    nblk = (n + 511) // 512
    for j in range(nblk):
        w = min(512, n - j * 512)
        b.wait_tok("tensor", wait_t)
        b.wait("tensor", "dv", b.val("dv"))
        ones = ns.ones_bf if bf else ns.ones_col
        b.op("tensor", "matmul", ns.ps_misc[:, 0:w],
             ones[base:base + 1, 0:128], row_ap[0:1, j * 512:j * 512 + w],
             start=True, stop=True, inc="pe")
        b.wait("vector", "pe", b.val("pe"))
        b.op("vector", "tensor_copy", out=out_bc[:, j * 512:j * 512 + w],
             in_=ns.ps_misc[:, 0:w], inc="dv")
    return b.tok("dv")



def emit_l0ffn(b, ns):
    g = ns.g
    E = 16
    for half in range(2):
        t0 = half * HALF
        _l0ffn_half(b, ns, half, t0, E)
        b.barrier()


def _l0ffn_half(b, ns, half, t0, E):
    d = ns.dram

    def kinds(bb):
        v = "vector"; s = "scalar"
        nw = 8
        S = ns.scol
        P = ns.ps_misc
        bb.op(v, "tensor_scalar", out=S[:, 0:nw], in0=P[:, 0:16].rearrange("p (w k) -> p k w", k=2)[:, 0, :],
              scalar1=1.0 / D, scalar2=None, op0=ALU.mult)
        bb.op(v, "tensor_scalar", out=S[:, 24:24 + nw], in0=P[:, 0:16].rearrange("p (w k) -> p k w", k=2)[:, 1, :],
              scalar1=1.0 / D, scalar2=None, op0=ALU.mult)
        bb.drain(v)
        bb.op(v, "tensor_tensor", out=S[:, 32:32 + nw], in0=S[:, 0:nw], in1=S[:, 0:nw], op=ALU.mult)
        bb.drain(v)
        bb.op(v, "tensor_tensor", out=S[:, 40:40 + nw], in0=S[:, 24:24 + nw], in1=S[:, 32:32 + nw], op=ALU.subtract)
        bb.drain(v)
        bb.op(v, "tensor_scalar", out=S[:, 48:48 + nw], in0=S[:, 40:40 + nw],
              scalar1=1e-5, scalar2=None, op0=ALU.add, inc="dv")
        bb.wait(s, "dv", bb.val("dv"))
        bb.op(s, "activation", out=S[:, 56:56 + nw], in_=S[:, 48:48 + nw], func=AF.Ln)
        bb.drain(s)
        bb.op(s, "activation", out=S[:, 8:8 + nw], in_=S[:, 56:56 + nw], func=AF.Exp, scale=-0.5, inc="ac")
        bb.wait(v, "ac", bb.val("ac"))
        bb.op(v, "reciprocal", out=S[:, 32:32 + nw], in_=S[:, 48:48 + nw])
        bb.drain(v)
        bb.op(v, "tensor_tensor", out=S[:, 40:40 + nw], in0=S[:, 40:40 + nw], in1=S[:, 32:32 + nw], op=ALU.mult)
        bb.drain(v)
        bb.op(v, "tensor_scalar", out=S[:, 40:40 + nw], in0=S[:, 40:40 + nw],
              scalar1=1e-5, scalar2=None, op0=ALU.add, inc="dv")
        bb.wait(s, "dv", bb.val("dv"))
        bb.op(s, "activation", out=S[:, 48:48 + nw], in_=S[:, 40:40 + nw], func=AF.Ln)
        bb.drain(s)
        bb.op(s, "activation", out=S[:, 48:48 + nw], in_=S[:, 48:48 + nw], func=AF.Exp, scale=-0.5, inc="ac")
        bb.wait(v, "ac", bb.val("ac"))
        bb.op(v, "tensor_tensor", out=S[:, 16:16 + nw], in0=S[:, 8:8 + nw], in1=S[:, 48:48 + nw], op=ALU.mult)
        bb.drain(v)
        bb.op(v, "tensor_scalar", out=S[:, 8:8 + nw], in0=S[:, 8:8 + nw],
              scalar1=0.5, scalar2=None, op0=ALU.mult, inc="dv")

    rows = [ns.row_m, ns.row_a, ns.row_b]
    rows_tok = emit_colstats(b, ns, half, lambda dc: Xc(ns, dc, t0, HALF), kinds, rows, "l0ln")

    t_m = emit_bcast(b, ns, ns.row_m, ns.bc_m, HALF, rows_tok, base=0)
    t_a = emit_bcast(b, ns, ns.row_a, ns.bc_a, HALF, rows_tok, base=32)
    t_b = emit_bcast(b, ns, ns.row_b, ns.bc_b, HALF, rows_tok, base=64)

    b.wait_tok("vector", t_b)
    for dc in range(DC):
        X = Xc(ns, dc, t0, HALF)
        t1 = ns.tmp1[:, 0:HALF]
        b.wait("vector", "ac", b.val("ac"))
        b.op("vector", "tensor_tensor", out=t1, in0=X, in1=ns.bc_m[:, 0:HALF], op=ALU.subtract)
        b.drain("vector")
        b.op("vector", "tensor_tensor", out=ns.tmp2[:, 0:HALF], in0=t1, in1=ns.bc_a[:, 0:HALF], op=ALU.mult)
        b.drain("vector")
        b.op("vector", "tensor_tensor", out=X, in0=X, in1=ns.tmp2[:, 0:HALF], op=ALU.add)
        b.drain("vector")
        b.op("vector", "tensor_tensor", out=ns.tmp2[:, 0:HALF], in0=t1, in1=ns.bc_b[:, 0:HALF], op=ALU.mult,
             inc="dv")
        b.wait("scalar", "dv", b.val("dv"))
        b.op("scalar", "activation", out=ns.xn_bf[:, dc * HALF:(dc + 1) * HALF], in_=ns.tmp2[:, 0:HALF],
             func=AF.Identity, scale=ns.l0_ns[:, dc:dc + 1], inc="ac")
    xn_tok = b.tok("ac")

    b.wait_tok("tensor", xn_tok)
    b.wait("tensor", "dv", b.val("dv"))
    gtok = _l0_gate(b, ns, half, t0)

    _l0_experts(b, ns, half, t0, gtok)


def _l0_gate(b, ns, half, t0):
    d = ns.dram
    nw = 8
    b.wait_tok("tensor", ns.wg_tok)
    bgt = emit_bcast(b, ns, ns.bg_row, ns.bc_g16, 16)
    ps_free = bgt
    for w in range(nw):
        tw = t0 + 128 * w
        b.wait_tok("tensor", ps_free)
        for dc in range(DC):
            b.op("tensor", "matmul", ns.ps_misc[:, 0:16],
                 ns.xn_bf[:, dc * HALF + 128 * w: dc * HALF + 128 * w + 128],
                 ns.wg_bf[:, dc * 16:(dc + 1) * 16],
                 start=(dc == 0), stop=(dc == DC - 1), inc=("pe" if dc == DC - 1 else None))
        b.wait("vector", "pe", b.val("pe"))
        v = "vector"
        L = ns.gl
        CUR = ns.gcur
        b.op(v, "tensor_tensor", out=L[:], in0=ns.ps_misc[:, 0:16], in1=ns.bc_g16[:, 0:16], op=ALU.add,
             inc="dv")
        ps_free = b.tok("dv")
        b.drain(v)
        b.op(v, "tensor_copy", out=CUR[:], in_=L[:])
        b.drain(v)
        b.op(v, "reduce_max", out=ns.gm[:, 0:1], in_=CUR[:], axis=AX.X, negate=True)
        b.drain(v)
        for j in range(3):
            b.op(v, "tensor_scalar", out=ns.gmask[:], in0=CUR[:], scalar1=ns.gm[:, 0:1],
                 scalar2=0.0, op0=ALU.add, op1=ALU.is_ge)
            b.drain(v)
            b.op(v, "scalar_tensor_tensor", out=CUR[:], in0=ns.gmask[:], scalar=-1e30,
                 in1=CUR[:], op0=ALU.mult, op1=ALU.add)
            b.drain(v)
            b.op(v, "reduce_max", out=ns.gm[:, 0:1], in_=CUR[:], axis=AX.X, negate=True)
            b.drain(v)
        b.op(v, "tensor_scalar", out=ns.gmask[:], in0=L[:], scalar1=ns.gm[:, 0:1],
             scalar2=0.0, op0=ALU.add, op1=ALU.is_ge)
        b.op(v, "reduce_max", out=ns.gm[:, 1:2], in_=L[:], axis=AX.X, negate=True, inc="dv")
        b.wait("scalar", "dv", b.val("dv"))
        GX = ns.gex[:, 0:16]
        b.op("scalar", "activation", out=GX, in_=L[:], func=AF.Exp, bias=ns.gm[:, 1:2], inc="ac")
        b.wait(v, "ac", b.val("ac"))
        b.op(v, "tensor_tensor", out=GX, in0=GX, in1=ns.gmask[:], op=ALU.mult)
        b.drain(v)
        b.op(v, "reduce_sum", out=ns.gm[:, 2:3], in_=GX, axis=AX.X)
        b.drain(v)
        b.op(v, "reciprocal", out=ns.gm[:, 3:4], in_=ns.gm[:, 2:3])
        b.drain(v)
        b.op(v, "tensor_scalar", out=GX, in0=GX, scalar1=ns.gm[:, 3:4],
             scalar2=0.5, op0=ALU.mult, op1=ALU.mult, inc="dv")
        b.wait("tensor", "dv", b.val("dv"))
        b.op("tensor", "transpose", ns.ps_misc2[0:16, 0:128], GX, ns.ident_f[:], inc="pe")
        b.wait(v, "pe", b.val("pe"))
        b.op(v, "tensor_copy", out=ns.gp_T[0:16, tw:tw + 128], in_=ns.ps_misc2[0:16, 0:128])
        b.op(v, "tensor_copy", out=ns.gp_Tb[0:16, tw:tw + 128], in_=ns.ps_misc2[0:16, 0:128], inc="dv")
    return b.tok("dv")


def _l0_w2_load(b, ns, e, hc, slot, sem):
    d = ns.dram
    b.wait_tok("gpsimd", ns.w2_slot_tok[slot])
    b.dma("gpsimd", out=ns.w2_sb[slot], in_=d["l0_w2p"][e, hc], inc=sem)
    return b.tok(sem)


def _l0_experts(b, ns, half, t0, gtok):
    d = ns.dram
    E = 16
    NFH = 2
    for e in range(E):
        for fh in range(NFH):
            slot = ns.w1_slot_idx
            ns.w1_slot_idx = (ns.w1_slot_idx + 1) % 2
            free_tok = ns.w1_slot_tok[slot]
            b.wait_tok("gpsimd", free_tok)
            b.dma("gpsimd",
                  out=ns.w1a_sb[slot].rearrange("p (dc f) -> p dc f", dc=8),
                  in_=d["l0_w1a"][e, :, fh * 512:(fh + 1) * 512]
                  .rearrange("(dc p) f -> p dc f", p=128), inc="wl")
            b.dma("gpsimd",
                  out=ns.w1b_sb[slot].rearrange("p (dc f) -> p dc f", dc=8),
                  in_=d["l0_w1b"][e, :, fh * 512:(fh + 1) * 512]
                  .rearrange("(dc p) f -> p dc f", p=128), inc="wl")
        for hc0 in range(2):
            _l0_w2_load(b, ns, e, hc0, hc0, "wl")
        wtok = b.tok("wl")
        for fh in range(NFH):
            ns.w1_load_tok[(e, fh)] = wtok
        ns.w2_pre_tok_l0 = wtok
        ns.w2_cur_tok_l0 = [None, None]

        b.wait_tok("sync", gtok)
        b.wait("sync", "dv", ns.gbc_guard)
        b.dma("sync", out=ns.g_row[0:1, 0:HALF], in_=ns.gp_T[e:e + 1, t0:t0 + HALF], inc="m1")
        b.wait("tensor", "m1", b.val("m1"))
        gb_tok = emit_bcast(b, ns, ns.g_row, ns.g_bc, HALF)
        ns.gbc_guard = b.val("dv")

        for fh in range(NFH):
            w1slot = (e * NFH + fh) % 2
            b.wait_tok("tensor", ns.w1_load_tok[(e, fh)])
            if e == 0 and fh == 0:
                b.wait_tok("tensor", b.tok("ac"))
            for c in range(4):
                ic = fh * 4 + c
                for tau in range(2):
                    psA = ns.psA[:, tau * 512:(tau + 1) * 512]
                    psB = ns.psB[:, tau * 512:(tau + 1) * 512]
                    b.wait_tok("tensor", ns.psA_tok[tau])
                    b.wait_tok("tensor", ns.psB_tok[tau])
                    for dc in range(DC):
                        b.op("tensor", "matmul", psA,
                             ns.w1a_sb[w1slot][:, dc * 512 + c * 128: dc * 512 + c * 128 + 128],
                             ns.xn_bf[:, dc * HALF + tau * 512: dc * HALF + (tau + 1) * 512],
                             start=(dc == 0), stop=(dc == DC - 1),
                             inc=("pe" if dc == DC - 1 else None))
                    peA = b.tok("pe")
                    for dc in range(DC):
                        b.op("tensor", "matmul", psB,
                             ns.w1b_sb[w1slot][:, dc * 512 + c * 128: dc * 512 + c * 128 + 128],
                             ns.xn_bf[:, dc * HALF + tau * 512: dc * HALF + (tau + 1) * 512],
                             start=(dc == 0), stop=(dc == DC - 1),
                             inc=("pe" if dc == DC - 1 else None))
                    peB = b.tok("pe")
                    v = "vector"
                    b.wait_tok(v, peA)
                    b.op(v, "tensor_scalar", out=ns.gluA[:], in0=psA,
                         scalar1=ns.l0_b1a[:, e * 8 + ic:e * 8 + ic + 1], scalar2=7.0,
                         op0=ALU.add, op1=ALU.min, inc="dv")
                    ns.psA_tok[tau] = b.tok("dv")
                    b.wait("scalar", "dv", b.val("dv"))
                    b.op("scalar", "activation", out=ns.gluS[:], in_=ns.gluA[:],
                         func=AF.Sigmoid, scale=1.702, inc="ac")
                    b.wait_tok("scalar", peB)
                    b.op("scalar", "activation", out=ns.gluB[:], in_=psB,
                         func=AF.Identity, bias=ns.l0_b1b[:, e * 8 + ic:e * 8 + ic + 1], inc="ac")
                    ns.psB_tok[tau] = b.tok("ac")
                    b.wait(v, "ac", b.val("ac"))
                    b.op(v, "tensor_scalar", out=ns.gluB[:], in0=ns.gluB[:],
                         scalar1=7.0, scalar2=-7.0, op0=ALU.min, op1=ALU.max)
                    b.op(v, "tensor_tensor", out=ns.gluA[:], in0=ns.gluA[:], in1=ns.gluS[:], op=ALU.mult)
                    b.drain(v)
                    b.op(v, "scalar_tensor_tensor", out=ns.gluA[:], in0=ns.gluB[:], scalar=1.0,
                         in1=ns.gluA[:], op0=ALU.add, op1=ALU.mult)
                    b.drain(v)
                    b.wait_tok(v, gb_tok)
                    b.wait_tok(v, ns.sg_guard)
                    b.op(v, "tensor_tensor", out=ns.sg_bf[:, ic * HALF + tau * 512: ic * HALF + (tau + 1) * 512],
                         in0=ns.gluA[:], in1=ns.g_bc[:, tau * 512:(tau + 1) * 512], op=ALU.mult,
                         inc="dv")
            ns.w1_slot_tok[w1slot] = b.tok("pe")
        sg_tok = b.tok("dv")

        b.wait_tok("tensor", sg_tok)
        for hc in range(DC):
            w2slot = hc % 2
            w2tok = ns.w2_pre_tok_l0 if hc < 2 else ns.w2_cur_tok_l0[w2slot]
            b.wait_tok("tensor", w2tok)
            for tau in range(2):
                psY = ns.psY[:, tau * 512:(tau + 1) * 512]
                b.wait_tok("tensor", ns.psY_tok[tau])
                for ic in range(8):
                    b.op("tensor", "matmul", psY,
                         ns.w2_sb[w2slot][:, ic * 128:(ic + 1) * 128],
                         ns.sg_bf[:, ic * HALF + tau * 512: ic * HALF + (tau + 1) * 512],
                         start=(ic == 0), stop=(ic == 7 and e != 0),
                         inc=("pe" if (ic == 7 and e != 0) else None))
                if e == 0:
                    b.op("tensor", "matmul", psY,
                         ns.b2_bf[:, hc * 128:(hc + 1) * 128],
                         ns.gp_Tb[0:16, t0 + tau * 512: t0 + (tau + 1) * 512],
                         start=False, stop=True, inc="pe")
                peY = b.tok("pe")
                b.wait_tok("vector", peY)
                b.op("vector", "tensor_tensor",
                     out=Xc(ns, hc, t0 + tau * 512, 512),
                     in0=Xc(ns, hc, t0 + tau * 512, 512), in1=psY, op=ALU.add, inc="dv")
                ns.psY_tok[tau] = b.tok("dv")
            ns.w2_slot_tok[w2slot] = b.tok("pe")
            if hc + 2 < DC:
                ns.w2_cur_tok_l0[w2slot] = _l0_w2_load(b, ns, e, hc + 2, w2slot,
                                                       "w2a" if w2slot == 0 else "w2b")
        ns.sg_guard = b.tok("pe")





def carve_l0attn(ns):
    f32, bf, al, pos = _mk_carver(ns)
    ns.a_vh = bf(16 * 1024)
    u0 = al(0, align=64)
    f32p, bfp, alp, posp = _mk_carver(ns)
    posp[0] = u0
    ns.a_bcm = f32p(512)
    ns.a_bcr = f32p(512)
    ns.a_xln = bfp(8 * 512)
    ns.a_pbf = bfp(8 * 512)
    ns.a_cos = f32p(2 * 512)
    ns.a_sin = f32p(2 * 512)
    ns.a_bvi = f32p(1024)
    ns.sq = f32p(1024)
    ns.tmp1 = f32p(512)
    ns.a_w = bfp(8 * 1024)
    ns.a_wi = bfp(8 * 1024)
    ns.a_stage = bfp(8 * 512)
    ns.a_rs1 = f32p(512)
    ns.a_rs2 = f32p(512)
    proj_end = posp[0]
    f32a, bfa, ala, posa = _mk_carver(ns)
    posa[0] = u0
    pj = ala(8192)
    ns.a_junk = ns.AR[:, pj // 4: pj // 4 + 2048]
    ns.a_expT = ns.AR_B[:, pj // 2: pj // 2 + 4096]
    ns.a_outw = bfa(8 * 1024)
    ns.a_m1 = bfa(4 * 512)
    ns.a_m2 = bfa(4 * 512)
    ns.a_qh = [bfa(2048, rows=64), bfa(2048, rows=64)]
    ns.a_kh = [bfa(2048, rows=64), bfa(2048, rows=64)]
    ns.a_opair = bfa(T)
    ns.a_zcol = f32a(16)
    ns.a_zbc = bfa(512)
    ns.a_sc1 = f32a(512)
    ns.a_outb = f32a(1024)
    attn_end = posa[0]
    pos[0] = max(proj_end, attn_end)
    ns.a_bytes = pos[0]


def emit_l0attn(b, ns):
    d = ns.dram
    v = "vector"
    NT4 = 4
    H = 16
    HD = 64

    b.dma("sync", out=ns.a_bvi[0:1, 0:D], in_=d["bvi_row"], inc="xl")
    ns.bvi_tok = b.tok("xl")
    for sh in range(2):
        st0 = sh * HALF
        rows = [ns.row_m[0:1, st0:st0 + HALF], ns.row_a[0:1, st0:st0 + HALF]]
        rt = emit_colstats(b, ns, 0, lambda dc: Xc(ns, dc, st0, HALF),
                           lambda bb: _kinds_ln2(bb, ns, nw=8), rows, "aln", nt=8)

    for P in range(3):
        wsrc = ("wq_t", "wk_t", "wv_t")[P]
        wisrc = ("wqi_t", "wki_t", "wvi_t")[P]
        b.wait_tok("gpsimd", ns.aw_guard)
        b.dma("gpsimd", out=ns.a_w.rearrange("p (dc f) -> p dc f", dc=8),
              in_=d[wsrc].rearrange("(dc p) f -> p dc f", p=128), inc="wl")
        b.dma("gpsimd", out=ns.a_wi.rearrange("p (dc f) -> p dc f", dc=8),
              in_=d[wisrc].rearrange("(dc p) f -> p dc f", p=128), inc="wl")
        wtok = b.tok("wl")
        for tau in range(NT4):
            tw = tau * 512
            b.wait(v, "pe", ns.a_xln_guard)
            emit_bcast(b, ns, ns.row_m[0:1, tw:tw + 512], ns.a_bcm, 512, rt, base=0)
            t_r = emit_bcast(b, ns, ns.row_a[0:1, tw:tw + 512], ns.a_bcr, 512, rt, base=32)
            b.wait_tok(v, t_r)
            for dc in range(DC):
                b.op(v, "tensor_tensor", out=ns.tmp1[:, 0:512], in0=Xc(ns, dc, tw, 512),
                     in1=ns.a_bcm[:, 0:512], op=ALU.subtract)
                b.op(v, "tensor_tensor", out=ns.a_xln[:, dc * 512:(dc + 1) * 512],
                     in0=ns.tmp1[:, 0:512], in1=ns.a_bcr[:, 0:512], op=ALU.mult, inc="dv")
            xln_tok = b.tok("dv")
            def cs_load(i, csl):
                b.wait("sync", "dv", ns.a_cs_guard[csl])
                b.dma("sync", out=ns.a_cos[:, csl * 512:(csl + 1) * 512],
                      in_=d["rope_cos"][i * 128:(i + 1) * 128, tw:tw + 512], inc="m2")
                b.dma("sync", out=ns.a_sin[:, csl * 512:(csl + 1) * 512],
                      in_=d["rope_sin"][i * 128:(i + 1) * 128, tw:tw + 512], inc="m2")
                return b.tok("m2")

            cs_load(0, 0)
            cs_pre = cs_load(1, 1)
            cs_next = [cs_pre, cs_pre]
            b.wait_tok("tensor", wtok)
            b.wait_tok("tensor", xln_tok)
            for i in range(4):
                csl = i % 2
                cs_tok = cs_next[csl]
                for half_ in range(2):
                    fc = i + 4 * half_
                    ps = ns.ps4[:, half_ * 512:(half_ + 1) * 512]
                    b.wait_tok("tensor", ns.a_ps_tok[half_])
                    for dc in range(DC):
                        b.op("tensor", "matmul", ps,
                             ns.a_w[:, dc * 1024 + fc * 128: dc * 1024 + fc * 128 + 128],
                             ns.a_xln[:, dc * 512:(dc + 1) * 512],
                             start=(dc == 0), stop=(dc == DC - 1),
                             inc=("pe" if dc == DC - 1 else None))
                pe_pair = b.tok("pe")
                b.wait_tok(v, pe_pair)
                b.wait_tok(v, cs_tok)
                b.wait(v, "pe", ns.a_pbf_guard)
                ci = ns.a_cos[:, csl * 512:(csl + 1) * 512]
                si = ns.a_sin[:, csl * 512:(csl + 1) * 512]
                xe = ns.ps4[:, 0:512]
                xo = ns.ps4[:, 512:1024]
                b.op(v, "tensor_tensor", out=ns.a_rs1[:, 0:512], in0=xe, in1=ci, op=ALU.mult)
                b.op(v, "tensor_tensor", out=ns.a_rs2[:, 0:512], in0=xo, in1=si, op=ALU.mult)
                b.op(v, "tensor_tensor", out=ns.a_pbf[:, i * 512:(i + 1) * 512],
                     in0=ns.a_rs1[:, 0:512], in1=ns.a_rs2[:, 0:512], op=ALU.add)
                b.op(v, "tensor_tensor", out=ns.a_rs1[:, 0:512], in0=xo, in1=ci, op=ALU.mult)
                b.op(v, "tensor_tensor", out=ns.a_rs2[:, 0:512], in0=xe, in1=si, op=ALU.mult,
                     inc="dv")
                ns.a_ps_tok = [b.tok("dv"), b.tok("dv")]
                b.op(v, "tensor_tensor", out=ns.a_pbf[:, (i + 4) * 512:(i + 5) * 512],
                     in0=ns.a_rs1[:, 0:512], in1=ns.a_rs2[:, 0:512], op=ALU.subtract, inc="dv")
                ns.a_cs_guard[csl] = b.val("dv")
                if i == 1:
                    cs_load(2, 0)
                    tk = cs_load(3, 1)
                    cs_next = [tk, tk]
            pbf_tok = b.tok("dv")
            b.wait_tok("tensor", pbf_tok)
            if P < 2:
                bia = (ns.bqi_p, ns.bki_p)[P]
                for fc in range(DC):
                    ps = ns.psY[:, (fc % 2) * 512:(fc % 2) * 512 + 512]
                    b.wait_tok("tensor", ns.a_psY_tok[fc % 2])
                    for dc in range(DC):
                        b.op("tensor", "matmul", ps,
                             ns.a_wi[:, dc * 1024 + fc * 128: dc * 1024 + fc * 128 + 128],
                             ns.a_pbf[:, dc * 512:(dc + 1) * 512],
                             start=(dc == 0), stop=(dc == DC - 1),
                             inc=("pe" if dc == DC - 1 else None))
                    peq = b.tok("pe")
                    b.wait_tok("scalar", peq)
                    b.wait("scalar", "m1", ns.a_stage_guard)
                    b.op("scalar", "activation", out=ns.a_stage[:, fc * 512:(fc + 1) * 512],
                         in_=ps, func=AF.Identity, bias=bia[:, fc:fc + 1], inc="ac")
                    ns.a_psY_tok[fc % 2] = b.tok("ac")
                st_tok = b.tok("ac")
                b.wait_tok("sync", st_tok)
                dst = d["qh_dram"] if P == 0 else d["kh_dram"]
                b.dma("sync", out=dst.rearrange("(fc p) t -> p fc t", p=128)[:, :, tw:tw + 512],
                      in_=ns.a_stage.rearrange("p (fc t) -> p fc t", fc=8), inc="m1")
                ns.a_stage_guard = b.val("m1")
            else:
                b.wait_tok("tensor", ns.bvi_tok)
                for nw_ in range(4):
                    for fh in range(2):
                        ps = ns.psY[:, fh * 512:(fh + 1) * 512]
                        b.wait_tok("tensor", ns.a_psY_tok[fh])
                        for dc in range(DC):
                            b.op("tensor", "matmul", ps,
                                 ns.a_pbf[:, dc * 512 + nw_ * 128: dc * 512 + nw_ * 128 + 128],
                                 ns.a_wi[:, dc * 1024 + fh * 512: dc * 1024 + (fh + 1) * 512],
                                 start=(dc == 0), stop=False)
                        b.op("tensor", "matmul", ps,
                             ns.ones_col[0:1, 0:128],
                             ns.a_bvi[0:1, fh * 512:(fh + 1) * 512],
                             start=False, stop=True, inc="pe")
                        pev = b.tok("pe")
                        b.wait_tok(v, pev)
                        b.op(v, "tensor_copy",
                             out=ns.a_vh[:, (tau * 4 + nw_) * 1024 + fh * 512:
                                         (tau * 4 + nw_) * 1024 + (fh + 1) * 512],
                             in_=ps, inc="dv")
                        ns.a_psY_tok[fh] = b.tok("dv")
            ns.a_xln_guard = b.val("pe")
            ns.a_pbf_guard = b.val("pe")
        ns.aw_guard = b.tok("pe")

    b.wait("sync", "m1")
    b.barrier()
    b.dma("gpsimd", out=ns.a_m1.rearrange("p (j k) -> p j k", j=4),
          in_=d["mask1"].rearrange("j p k -> p j k"), inc="wl")
    b.dma("gpsimd", out=ns.a_m2.rearrange("p (j k) -> p j k", j=4),
          in_=d["mask2"].rearrange("j p k -> p j k"), inc="wl")
    b.dma("gpsimd", out=ns.a_outw.rearrange("p (pr f) -> p pr f", pr=8),
          in_=d["outw_t"].rearrange("(pr p) f -> p pr f", p=128), inc="wl")
    b.dma("sync", out=ns.a_outb[0:1, 0:D], in_=d["outb_row"], inc="m1")
    amw_tok = b.tok("wl")
    outb_tok = b.tok("m1")
    b.wait_tok("tensor", outb_tok)
    b.wait_tok("tensor", amw_tok)

    def qk_load(h, hs, sem):
        b.wait_tok("gpsimd", ns.a_qk_guard[hs])
        b.dma("gpsimd", out=ns.a_qh[hs][0:64, :], in_=d["qh_dram"][h * 64:(h + 1) * 64, :], inc=sem)
        b.dma("gpsimd", out=ns.a_kh[hs][0:64, :], in_=d["kh_dram"][h * 64:(h + 1) * 64, :], inc=sem)
        return b.tok(sem)

    qk_load(0, 0, "qk")
    qk_pre = qk_load(1, 1, "qk")
    qk_next = [qk_pre, qk_pre]
    for h in range(H):
        hs = h % 2
        qk_tok = qk_next[hs]
        b.wait_tok("tensor", qk_tok)
        for qs in range(16):
            ktn = qs // 4 + 1
            for kt in range(ktn):
                ps = ns.ps4[:, kt * 512:(kt + 1) * 512]
                b.wait_tok("tensor", ns.a_p1_tok[kt])
                b.op("tensor", "matmul", ps,
                     ns.a_qh[hs][0:64, qs * 128:(qs + 1) * 128],
                     ns.a_kh[hs][0:64, kt * 512:(kt + 1) * 512],
                     start=True, stop=True, inc="pe")
            p1_tok = b.tok("pe")
            b.wait_tok(v, p1_tok)
            jdiag = qs % 4
            b.op(v, "tensor_tensor", out=ns.ps4[:, (ktn - 1) * 512:ktn * 512],
                 in0=ns.ps4[:, (ktn - 1) * 512:ktn * 512],
                 in1=ns.a_m1[:, jdiag * 512:(jdiag + 1) * 512], op=ALU.add, inc="dv")
            b.op(v, "reduce_max", out=ns.gm[:, 0:1], in_=ns.ps4[:, 0:ktn * 512], axis=AX.X,
                 negate=True)
            b.op(v, "tensor_scalar", out=ns.gm[:, 1:2], in0=ns.gm[:, 0:1],
                 scalar1=0.125, scalar2=None, op0=ALU.mult, inc="dv")
            b.wait("scalar", "dv", b.val("dv"))
            b.op("scalar", "activation", out=ns.a_junk[:, 0:ktn * 512], in_=ns.ps4[:, 0:ktn * 512],
                 func=AF.Exp, scale=0.125, bias=ns.gm[:, 1:2],
                 accum_out=ns.gm[:, 2:3], inc="ac")
            ps1_free = b.tok("ac")
            ns.a_p1_tok = [ps1_free] * 4
            b.wait("scalar", "ac", b.val("ac"))
            b.op("scalar", "activation", out=ns.gm[:, 3:4], in_=ns.gm[:, 2:3], func=AF.Ln, inc="ac")
            b.wait(v, "ac", b.val("ac"))
            b.op(v, "tensor_tensor", out=ns.a_zcol[:, qs:qs + 1], in0=ns.gm[:, 1:2],
                 in1=ns.gm[:, 3:4], op=ALU.subtract, inc="dv")
        zc_tok = b.tok("dv")
        b.wait_tok("tensor", zc_tok)
        b.wait("tensor", "dv", b.val("dv"))
        b.op("tensor", "transpose", ns.ps_misc2[0:16, 0:128], ns.a_zcol[:, 0:16],
             ns.ident_f[:], inc="pe")
        b.wait(v, "pe", b.val("pe"))
        b.op(v, "tensor_copy", out=ns.rowstage[0:16, 0:128], in_=ns.ps_misc2[0:16, 0:128], inc="dv")
        b.wait("sync", "dv", b.val("dv"))
        b.dma("sync", out=d["row_dram"][0:16, 0:128], in_=ns.rowstage[0:16, 0:128], inc="m1")
        b.wait("sync", "m1", b.val("m1"))
        b.wait("gpsimd", "m1", b.val("m1"))
        b.dma("gpsimd", out=ns.g_row_bf[0:1, 0:T],
              in_=d["row_dram"][0:16, 0:128].rearrange("a c -> (a c)")[None, :], inc="gx")
        zrow_tok = b.tok("gx")

        for tq in range(NT4):
            b.wait_tok("tensor", zrow_tok)
            b.wait("tensor", "dv", b.val("dv"))
            b.op("tensor", "matmul", ns.ps_misc[:, 0:512],
                 ns.ones_bf[0:1, 0:128], ns.g_row_bf[0:1, tq * 512:(tq + 1) * 512],
                 start=True, stop=True, inc="pe")
            b.wait(v, "pe", b.val("pe"))
            b.op(v, "tensor_copy", out=ns.a_zbc[:, 0:512], in_=ns.ps_misc[:, 0:512], inc="dv")
            zbc_tok = b.tok("dv")
            ktn2 = tq * 4 + 4
            for kt in range(ktn2):
                slot = kt % 8
                ps = ns.psY[:, (kt % 2) * 512:(kt % 2) * 512 + 512]
                b.wait_tok("tensor", ns.a_p2_tok[kt % 2])
                b.op("tensor", "matmul", ps,
                     ns.a_kh[hs][0:64, kt * 128:(kt + 1) * 128],
                     ns.a_qh[hs][0:64, tq * 512:(tq + 1) * 512],
                     start=True, stop=True, inc="pe")
                p2_tok = b.tok("pe")
                b.wait_tok(v, p2_tok)
                b.wait_tok(v, zbc_tok)
                b.wait_tok(v, ns.a_expT_guard[slot])
                jd = kt - (tq * 4)
                if jd >= 0:
                    b.op(v, "tensor_tensor", out=ps, in0=ps,
                         in1=ns.a_m2[:, jd * 512:(jd + 1) * 512], op=ALU.add)
                b.op(v, "scalar_tensor_tensor", out=ns.a_sc1[:, 0:512], in0=ps, scalar=0.125,
                     in1=ns.a_zbc[:, 0:512], op0=ALU.mult, op1=ALU.add, inc="dv")
                ns.a_p2_tok[kt % 2] = b.tok("dv")
                b.wait("scalar", "dv", b.val("dv"))
                b.op("scalar", "activation", out=ns.a_expT[:, slot * 512:(slot + 1) * 512],
                     in_=ns.a_sc1[:, 0:512], func=AF.Exp, inc="ac")
                exp_tok = b.tok("ac")
                b.wait_tok("tensor", exp_tok)
                b.op("tensor", "matmul", ns.ps_misc2[0:64, 0:512],
                     ns.a_vh[:, kt * 1024 + h * 64: kt * 1024 + h * 64 + 64],
                     ns.a_expT[:, slot * 512:(slot + 1) * 512],
                     start=(kt == 0), stop=(kt == ktn2 - 1),
                     inc=("pe" if kt == ktn2 - 1 else None))
                ns.a_expT_guard[slot] = b.tok("pe")
            av_tok = b.tok("pe")
            b.wait_tok(v, av_tok)
            b.op(v, "tensor_copy", out=ns.a_opair[64 * hs:64 * hs + 64, tq * 512:(tq + 1) * 512],
                 in_=ns.ps_misc2[0:64, 0:512], inc="dv")

        if hs == 1:
            opair_tok = b.tok("dv")
            pair = h // 2
            b.wait_tok("tensor", opair_tok)
            for dc in range(DC):
                for tq in range(NT4):
                    ps = ns.ps_misc[:, 0:512]
                    b.wait("tensor", "dv", ns.a_op_guard)
                    if pair == 0:
                        b.op("tensor", "matmul", ps,
                             ns.a_outw[:, pair * 1024 + dc * 128: pair * 1024 + dc * 128 + 128],
                             ns.a_opair[:, tq * 512:(tq + 1) * 512],
                             start=True, stop=False)
                        b.op("tensor", "matmul", ps,
                             ns.a_outb[0:1, dc * 128:(dc + 1) * 128],
                             ns.ones_col[0:1, 0:512],
                             start=False, stop=True, inc="pe")
                    else:
                        b.op("tensor", "matmul", ps,
                             ns.a_outw[:, pair * 1024 + dc * 128: pair * 1024 + dc * 128 + 128],
                             ns.a_opair[:, tq * 512:(tq + 1) * 512],
                             start=True, stop=True, inc="pe")
                    peo = b.tok("pe")
                    b.wait_tok(v, peo)
                    b.op(v, "tensor_tensor", out=Xc(ns, dc, tq * 512, 512),
                         in0=Xc(ns, dc, tq * 512, 512), in1=ps, op=ALU.add, inc="dv")
                    ns.a_op_guard = b.val("dv")
        ns.a_qk_guard[hs] = b.tok("pe")
        if h + 2 < H:
            qk_next[hs] = qk_load(h + 2, hs, "qka" if hs == 0 else "qkb")



def carve_l1attn(ns):
    f32, bf, al, pos = _mk_carver(ns)
    ns.n_ztile = bf(8 * 512)
    ns.n_w = bf(3 * 8 * 1024)
    ns.n_ktile = bf(4 * 1024)
    ns.n_vtile = bf(4 * 1024)
    ns.n_stage = bf(8 * 512)
    ns.n_vk = f32(32 * 32)
    ns.n_vkT = bf(64)
    ns.n_qh = [bf(2048, rows=32), bf(2048, rows=32)]
    ns.n_rbc = f32(512)
    u0 = al(0, align=64)
    c1 = _mk_carver(ns); c1[3][0] = u0
    ns.sq = c1[0](1024)
    ns.tmp1 = c1[0](512)
    ns.bc_m = c1[0](512)
    ns.bc_a = c1[0](512)
    end1 = c1[3][0]
    c2 = _mk_carver(ns); c2[3][0] = u0
    ns.n_ohead = c2[0](2048)
    ns.n_ostage = c2[1](2048)
    end2 = c2[3][0]
    pos[0] = max(end1, end2)
    ns.n_ochunk = [ns.n_stage[:, 0:2048], ns.n_stage[:, 2048:4096]]
    ns.n_den = ns.g_row_full
    ns.n_bp = ns.row_b
    ns.l1a_bytes = pos[0]


def emit_l1attn(b, ns):
    d = ns.dram
    v = "vector"
    NH = 32
    vk_sb = ns.n_vk

    b.dma("sync", out=ns.n_bp[0:1, 0:D], in_=d["l1_bp_row"], inc="xl")
    bp_tok = b.tok("xl")
    for sh in range(2):
        st0 = sh * HALF
        rows = [ns.row_m[0:1, st0:st0 + HALF], ns.row_a[0:1, st0:st0 + HALF]]
        emit_colstats(b, ns, 0, lambda dc: Xc(ns, dc, st0, HALF),
                      lambda bb: _kinds_ln2(bb, ns, nw=8), rows, "nln", nt=8)
    rt = b.tok("m1")

    b.wait_tok("gpsimd", ns.aw_guard)
    for i, wn in enumerate(("l1_wqT", "l1_wkT", "l1_wvT")):
        b.dma("gpsimd", out=ns.n_w.rearrange("p (i dc f) -> p (i dc) f", i=3, dc=8)
              [:, i * 8:(i + 1) * 8, :],
              in_=d[wn].rearrange("(dc p) f -> p dc f", p=128), inc="wl")
    wtok = b.tok("wl")
    b.op(v, "memset", vk_sb[:, 0:1024], 0.0, inc="dv")
    vk_zero = b.tok("dv")

    for tau in range(4):
        tw = tau * 512
        b.wait(v, "pe", ns.a_xln_guard)
        emit_bcast(b, ns, ns.row_m[0:1, tw:tw + 512], ns.bc_m, 512, rt, base=0)
        t_r = emit_bcast(b, ns, ns.row_a[0:1, tw:tw + 512], ns.bc_a, 512, rt, base=32)
        b.wait_tok(v, t_r)
        for dc in range(DC):
            b.op(v, "tensor_tensor", out=ns.tmp1[:, 0:512], in0=Xc(ns, dc, tw, 512),
                 in1=ns.bc_m[:, 0:512], op=ALU.subtract)
            b.op(v, "tensor_tensor", out=ns.n_ztile[:, dc * 512:(dc + 1) * 512],
                 in0=ns.tmp1[:, 0:512], in1=ns.bc_a[:, 0:512], op=ALU.mult, inc="dv")
        z_tok = b.tok("dv")
        b.wait_tok("tensor", wtok)
        b.wait_tok("tensor", z_tok)
        for fc in range(DC):
            ps = ns.psY[:, (fc % 2) * 512:(fc % 2) * 512 + 512]
            b.wait_tok("tensor", ns.a_psY_tok[fc % 2])
            for dc in range(DC):
                b.op("tensor", "matmul", ps,
                     ns.n_w[:, dc * 1024 + fc * 128: dc * 1024 + fc * 128 + 128],
                     ns.n_ztile[:, dc * 512:(dc + 1) * 512],
                     start=(dc == 0), stop=(dc == DC - 1),
                     inc=("pe" if dc == DC - 1 else None))
            peq = b.tok("pe")
            b.wait_tok("scalar", peq)
            b.wait("scalar", "m1", ns.a_stage_guard)
            b.op("scalar", "activation", out=ns.n_stage[:, fc * 512:(fc + 1) * 512],
                 in_=ps, func=AF.Relu, inc="ac")
            ns.a_psY_tok[fc % 2] = b.tok("ac")
        st_tok = b.tok("ac")
        b.wait_tok("sync", st_tok)
        b.dma("sync", out=d["q2_dram"].rearrange("(fc p) t -> p fc t", p=128)[:, :, tw:tw + 512],
              in_=ns.n_stage.rearrange("p (fc t) -> p fc t", fc=8), inc="m1")
        ns.a_stage_guard = b.val("m1")
        for P in (1, 2):
            dst = ns.n_ktile if P == 1 else ns.n_vtile
            for nw_ in range(4):
                for fh in range(2):
                    ps = ns.psY[:, fh * 512:(fh + 1) * 512]
                    b.wait_tok("tensor", ns.a_psY_tok[fh])
                    for dc in range(DC):
                        b.op("tensor", "matmul", ps,
                             ns.n_ztile[:, dc * 512 + nw_ * 128: dc * 512 + nw_ * 128 + 128],
                             ns.n_w[:, P * 8192 + dc * 1024 + fh * 512: P * 8192 + dc * 1024 + (fh + 1) * 512],
                             start=(dc == 0), stop=(dc == DC - 1),
                             inc=("pe" if dc == DC - 1 else None))
                    pet = b.tok("pe")
                    b.wait_tok("scalar", pet)
                    b.wait_tok("scalar", ns.n_kv_guard)
                    b.op("scalar", "activation", out=dst[:, nw_ * 1024 + fh * 512: nw_ * 1024 + (fh + 1) * 512],
                         in_=ps, func=(AF.Relu if P == 1 else AF.Identity), inc="ac")
                    ns.a_psY_tok[fh] = b.tok("ac")
        kv_tok = b.tok("ac")
        b.wait_tok("tensor", kv_tok)
        if tau == 0:
            b.wait_tok("tensor", vk_zero)
        for h in range(NH):
            par = h % 2
            vkps = ns.ps4[0:32, par * 512:par * 512 + 32]
            denps = ns.ps4[0:1, 1024 + par * 512:1024 + par * 512 + 32]
            b.wait_tok("tensor", ns.n_vkps_tok[par])
            for ns_ in range(4):
                b.op("tensor", "matmul", vkps,
                     ns.n_vtile[:, ns_ * 1024 + h * 32: ns_ * 1024 + h * 32 + 32],
                     ns.n_ktile[:, ns_ * 1024 + h * 32: ns_ * 1024 + h * 32 + 32],
                     start=(ns_ == 0), stop=(ns_ == 3))
                b.op("tensor", "matmul", denps,
                     ns.ones_bf[:, 0:1],
                     ns.n_ktile[:, ns_ * 1024 + h * 32: ns_ * 1024 + h * 32 + 32],
                     start=(ns_ == 0), stop=(ns_ == 3),
                     inc=("pe" if ns_ == 3 else None))
            pevk = b.tok("pe")
            b.wait_tok(v, pevk)
            b.op(v, "tensor_tensor", out=vk_sb[0:32, h * 32:(h + 1) * 32],
                 in0=vk_sb[0:32, h * 32:(h + 1) * 32], in1=vkps,
                 op=ALU.add)
            b.op(v, "tensor_tensor", out=vk_sb[32:33, h * 32:(h + 1) * 32],
                 in0=vk_sb[32:33, h * 32:(h + 1) * 32], in1=denps,
                 op=ALU.add, inc="dv")
            ns.n_vkps_tok[par] = b.tok("dv")
        ns.n_vk_tok = b.tok("dv")
        ns.a_xln_guard = b.val("pe")
        ns.n_kv_guard = ("pe", b.val("pe"))

    b.wait("sync", "m1")
    b.barrier()

    def qh_load(h, hs, sem):
        b.wait_tok("gpsimd", ns.a_qk_guard[hs])
        b.dma("gpsimd", out=ns.n_qh[hs][0:32, :], in_=d["q2_dram"][h * 32:(h + 1) * 32, :], inc=sem)
        return b.tok(sem)

    qh_load(0, 0, "qk")
    qh_pre = qh_load(1, 1, "qk")
    qh_next = [qh_pre, qh_pre]
    for h in range(NH):
        hs = h % 2
        qh_tok = qh_next[hs]
        b.wait_tok("tensor", ns.n_vk_tok)
        b.wait("tensor", "dv", b.val("dv"))
        b.op("tensor", "transpose", ns.ps_misc2[0:32, 0:33], vk_sb[0:33, h * 32:(h + 1) * 32],
             ns.ident_f[0:33, 0:33], inc="pe")
        b.wait(v, "pe", b.val("pe"))
        b.op(v, "tensor_copy", out=ns.n_vkT[0:32, 0:33], in_=ns.ps_misc2[0:32, 0:33], inc="dv")
        vkt_tok = b.tok("dv")
        b.wait_tok("tensor", qh_tok)
        b.wait_tok("tensor", vkt_tok)
        for tau in range(4):
            ps33 = ns.psY[0:33, (tau % 2) * 512:(tau % 2) * 512 + 512]
            b.wait_tok("tensor", ns.a_psY_tok[tau % 2])
            b.op("tensor", "matmul", ps33,
                 ns.n_vkT[0:32, 0:33],
                 ns.n_qh[hs][0:32, tau * 512:(tau + 1) * 512],
                 start=True, stop=True, inc="pe")
            peo = b.tok("pe")
            b.wait_tok(v, peo)
            b.wait(v, "m1", ns.n_oh_guard)
            b.op(v, "tensor_copy", out=ns.n_ohead[0:33, tau * 512:(tau + 1) * 512],
                 in_=ps33, inc="dv")
            ns.a_psY_tok[tau % 2] = b.tok("dv")
        oh_tok = b.tok("dv")
        b.wait_tok(v, oh_tok)
        b.op(v, "tensor_scalar", out=ns.n_den[0:1, 0:T], in0=ns.n_ohead[32:33, 0:T],
             scalar1=1e-15, scalar2=None, op0=ALU.add, inc="dv")
        b.wait("scalar", "dv", b.val("dv"))
        b.op("scalar", "activation", out=ns.n_den[0:1, 0:T], in_=ns.n_den[0:1, 0:T],
             func=AF.Ln)
        b.op("scalar", "activation", out=ns.n_den[0:1, 0:T], in_=ns.n_den[0:1, 0:T],
             func=AF.Exp, scale=-1.0, inc="ac")
        r_tok = b.tok("ac")
        for tau in range(4):
            b.wait_tok("tensor", r_tok)
            b.wait("tensor", "dv", b.val("dv"))
            b.op("tensor", "matmul", ns.ps_misc[:, 0:512],
                 ns.ones_col[0:1, 0:128], ns.n_den[0:1, tau * 512:(tau + 1) * 512],
                 start=True, stop=True, inc="pe")
            b.wait(v, "pe", b.val("pe"))
            b.op(v, "tensor_copy", out=ns.n_rbc[:, 0:512], in_=ns.ps_misc[:, 0:512], inc="dv")
            b.drain(v)
            b.op(v, "tensor_tensor", out=ns.n_ostage[0:32, tau * 512:(tau + 1) * 512],
                 in0=ns.n_ohead[0:32, tau * 512:(tau + 1) * 512],
                 in1=ns.n_rbc[0:32, 0:512], op=ALU.mult, inc="dv")
        ost_tok = b.tok("dv")
        b.wait_tok("sync", ost_tok)
        b.dma("sync", out=d["o2_dram"][h * 32:(h + 1) * 32, :], in_=ns.n_ostage[0:32, 0:T], inc="m1")
        ns.n_oh_guard = b.val("m1")
        ns.a_qk_guard[hs] = b.tok("pe")
        if h + 2 < NH:
            qh_next[hs] = qh_load(h + 2, hs, "qka" if hs == 0 else "qkb")

    b.wait("sync", "m1")
    b.barrier()
    b.dma("gpsimd", out=ns.n_w.rearrange("p (i dc f) -> p (i dc) f", i=3, dc=8)[:, 0:8, :],
          in_=d["l1_wpT"].rearrange("(dc p) f -> p dc f", p=128), inc="wl")
    wp_tok = b.tok("wl")
    b.wait_tok("tensor", bp_tok)
    oc_next = [None, None]
    for oc in range(DC):
        slot = oc % 2
        b.wait_tok("gpsimd", ns.n_oc_guard[slot])
        b.dma("gpsimd", out=ns.n_ochunk[slot],
              in_=d["o2_dram"][oc * 128:(oc + 1) * 128, :],
              inc=("qka" if slot == 0 else "qkb"))
        octok = b.tok("qka" if slot == 0 else "qkb")
        b.wait_tok("tensor", wp_tok)
        b.wait_tok("tensor", octok)
        for dc in range(DC):
            for tau in range(4):
                ps = ns.ps_misc[:, 0:512]
                b.wait("tensor", "dv", ns.a_op_guard)
                wsl = ns.n_w[:, oc * 1024 + dc * 128: oc * 1024 + dc * 128 + 128]
                if oc == 0:
                    b.op("tensor", "matmul", ps, wsl,
                         ns.n_ochunk[slot][:, tau * 512:(tau + 1) * 512],
                         start=True, stop=False)
                    b.op("tensor", "matmul", ps,
                         ns.n_bp[0:1, dc * 128:(dc + 1) * 128],
                         ns.ones_col[64:65, 0:512],
                         start=False, stop=True, inc="pe")
                else:
                    b.op("tensor", "matmul", ps, wsl,
                         ns.n_ochunk[slot][:, tau * 512:(tau + 1) * 512],
                         start=True, stop=True, inc="pe")
                peo = b.tok("pe")
                b.wait_tok(v, peo)
                b.op(v, "tensor_tensor", out=Xc(ns, dc, tau * 512, 512),
                     in0=Xc(ns, dc, tau * 512, 512), in1=ps, op=ALU.add, inc="dv")
                ns.a_op_guard = b.val("dv")
        ns.n_oc_guard[slot] = b.tok("pe")



def _kinds_ln2(b, ns, nw=8):
    v = "vector"; s = "scalar"
    S = ns.scol
    P = ns.ps_misc
    b.op(v, "tensor_scalar", out=S[:, 0:nw], in0=P[:, 0:2 * nw].rearrange("p (w k) -> p k w", k=2)[:, 0, :],
         scalar1=1.0 / D, scalar2=None, op0=ALU.mult)
    b.op(v, "tensor_scalar", out=S[:, 3 * nw:4 * nw], in0=P[:, 0:2 * nw].rearrange("p (w k) -> p k w", k=2)[:, 1, :],
         scalar1=1.0 / D, scalar2=None, op0=ALU.mult)
    b.op(v, "tensor_tensor", out=S[:, 4 * nw:5 * nw], in0=S[:, 0:nw], in1=S[:, 0:nw], op=ALU.mult)
    b.op(v, "tensor_tensor", out=S[:, 5 * nw:6 * nw], in0=S[:, 3 * nw:4 * nw], in1=S[:, 4 * nw:5 * nw], op=ALU.subtract)
    b.op(v, "tensor_scalar", out=S[:, 6 * nw:7 * nw], in0=S[:, 5 * nw:6 * nw],
         scalar1=1e-5, scalar2=None, op0=ALU.add, inc="dv")
    b.wait(s, "dv", b.val("dv"))
    b.op(s, "activation", out=S[:, 7 * nw:8 * nw], in_=S[:, 6 * nw:7 * nw], func=AF.Ln)
    b.op(s, "activation", out=S[:, nw:2 * nw], in_=S[:, 7 * nw:8 * nw], func=AF.Exp, scale=-0.5, inc="ac")
    b.wait(v, "ac", b.val("ac"))
    b.op(v, "tensor_copy", out=S[:, nw:2 * nw], in_=S[:, nw:2 * nw], inc="dv")


def emit_l1ffn(b, ns):
    for half in range(2):
        _l1ffn_half(b, ns, half, half * HALF)
        b.barrier()


def _l1_stats_bcast(b, ns, src, tag):
    rows = [ns.row_m, ns.row_a]
    rt = emit_colstats(b, ns, 0, src, lambda bb: _kinds_ln2(bb, ns), rows, tag)
    t_m = emit_bcast(b, ns, ns.row_m, ns.bc_m, HALF, rt, base=0)
    t_a = emit_bcast(b, ns, ns.row_a, ns.bc_a, HALF, rt, base=32)
    return t_a


def _l1_gate(b, ns, half, t0):
    nw = 8
    b.wait_tok("tensor", ns.wg_tok)
    bgt = emit_bcast(b, ns, ns.l1_gb_row, ns.bc_g16, 8)
    ps_free = bgt
    v = "vector"
    for w in range(nw):
        tw = t0 + 128 * w
        b.wait_tok("tensor", ps_free)
        for dc in range(DC):
            b.op("tensor", "matmul", ns.ps_misc[:, 0:8],
                 ns.l1_in_bf[:, dc * HALF + 128 * w: dc * HALF + 128 * w + 128],
                 ns.l1_gw_bf[:, dc * 8:(dc + 1) * 8],
                 start=(dc == 0), stop=(dc == DC - 1), inc=("pe" if dc == DC - 1 else None))
        b.wait(v, "pe", b.val("pe"))
        L = ns.gl[:, 0:8]
        G = ns.gex
        b.op(v, "tensor_tensor", out=L, in0=ns.ps_misc[:, 0:8], in1=ns.bc_g16[:, 0:8], op=ALU.add,
             inc="dv")
        ps_free = b.tok("dv")
        b.op(v, "reduce_max", out=ns.gm[:, 0:1], in_=L, axis=AX.X, negate=True)
        b.wait("scalar", "dv", b.val("dv"))
        b.op(v, "tensor_copy", out=ns.gm[:, 1:2], in_=ns.gm[:, 0:1], inc="dv")
        b.wait("scalar", "dv", b.val("dv"))
        b.op("scalar", "activation", out=ns.gcur[:, 0:8], in_=L, func=AF.Exp, bias=ns.gm[:, 1:2], inc="ac")
        b.wait(v, "ac", b.val("ac"))
        b.op(v, "reduce_sum", out=ns.gm[:, 2:3], in_=ns.gcur[:, 0:8], axis=AX.X)
        b.op(v, "reciprocal", out=ns.gm[:, 3:4], in_=ns.gm[:, 2:3])
        b.op(v, "tensor_scalar", out=ns.gl[:, 8:16], in0=ns.gcur[:, 0:8],
             scalar1=ns.gm[:, 3:4], scalar2=None, op0=ALU.mult)
        W_ = ns.gl[:, 8:16]
        b.op(v, "reduce_max", out=ns.gm[:, 0:1], in_=W_, axis=AX.X, negate=True)
        b.op(v, "tensor_scalar", out=G[:, 0:8], in0=W_, scalar1=ns.gm[:, 0:1],
             scalar2=0.0, op0=ALU.add, op1=ALU.is_ge)
        b.op(v, "tensor_scalar", out=G[:, 8:9], in0=ns.gm[:, 0:1],
             scalar1=-1.0, scalar2=None, op0=ALU.mult)
        b.op(v, "scalar_tensor_tensor", out=ns.gcur[:, 0:8], in0=G[:, 0:8], scalar=-1e30,
             in1=W_, op0=ALU.mult, op1=ALU.add)
        b.op(v, "reduce_max", out=ns.gm[:, 1:2], in_=ns.gcur[:, 0:8], axis=AX.X, negate=True)
        b.op(v, "tensor_scalar", out=ns.gmask[:, 0:8], in0=ns.gcur[:, 0:8], scalar1=ns.gm[:, 1:2],
             scalar2=0.0, op0=ALU.add, op1=ALU.is_ge)
        b.op(v, "tensor_scalar", out=ns.gm[:, 2:3], in0=ns.gm[:, 1:2],
             scalar1=-1.0, scalar2=None, op0=ALU.mult)
        b.op(v, "tensor_scalar", out=G[:, 32:40], in0=ns.gmask[:, 0:8],
             scalar1=ns.gm[:, 2:3], scalar2=None, op0=ALU.mult, inc="dv")
        b.wait("tensor", "dv", b.val("dv"))
        b.op("tensor", "transpose", ns.ps_misc2[0:40, 0:128], G[:, 0:40], ns.ident_f[:], inc="pe")
        b.wait(v, "pe", b.val("pe"))
        lw = tw - t0
        b.op(v, "tensor_copy", out=ns.l1_oh[0:9, lw:lw + 128], in_=ns.ps_misc2[0:9, 0:128])
        b.op(v, "tensor_copy", out=ns.l1_oh[32:40, lw:lw + 128], in_=ns.ps_misc2[32:40, 0:128])
        b.op(v, "tensor_copy", out=ns.l1_ohb[0:8, lw:lw + 128], in_=ns.ps_misc2[0:8, 0:128])
        b.op(v, "tensor_copy", out=ns.l1_ohb[32:40, lw:lw + 128], in_=ns.ps_misc2[32:40, 0:128], inc="dv")
    return b.tok("dv")


def _l1_iter(b, ns, half, t0, it, gtok):
    d = ns.dram
    E = 8
    rbase = 0 if it == 0 else 32
    def w2_load(e, hc, slot, sem="wl"):
        b.wait_tok("gpsimd", ns.w2_slot_tok[slot])
        b.dma("gpsimd", out=ns.l1_w2[slot], in_=d["l1_w2p"][e, hc], inc=sem)
        return b.tok(sem)

    for e in range(E):
        b.wait_tok("gpsimd", ns.w1_slot_tok[0])
        w1v = ns.l1_w1full.rearrange("p (dc f) -> p dc f", dc=8)
        for q in range(4):
            b.dma("gpsimd",
                  out=w1v[:, :, q * 512:(q + 1) * 512],
                  in_=d["l1_w1"][e, :, q * 512:(q + 1) * 512]
                  .rearrange("(dc p) f -> p dc f", p=128), inc="wl")
        w2_load(e, 0, 0)
        w2_load(e, 1, 1)
        wtok = b.tok("wl")
        ns.w2_pre_tok = wtok

        b.wait_tok("gpsimd", gtok)
        b.wait("gpsimd", "dv", ns.gbc_guard)
        b.dma("gpsimd", out=ns.g_row_bf[0:1, 0:HALF],
              in_=ns.l1_oh[rbase + e:rbase + e + 1, 0:HALF], inc="gx")
        b.wait("tensor", "gx", b.val("gx"))
        mb_tok = emit_bcast(b, ns, ns.g_row_bf, ns.l1_ohbc, HALF, bf=True)
        ns.gbc_guard = b.val("dv")

        b.wait_tok("tensor", wtok)
        if e == 0:
            b.wait_tok("tensor", ns.l1_in_tok)
        hview = ns.l1_h.rearrange("p (fc t) -> p fc t", fc=16)
        for tau in range(2):
            for fc in range(16):
                psH = ns.psA[:, (fc % 2) * 512:(fc % 2) * 512 + 512]
                b.wait_tok("tensor", ns.psA_tok[fc % 2])
                for dc in range(DC):
                    b.op("tensor", "matmul", psH,
                         ns.l1_w1full[:, dc * 2048 + fc * 128: dc * 2048 + fc * 128 + 128],
                         ns.l1_in_bf[:, dc * HALF + tau * 512: dc * HALF + (tau + 1) * 512],
                         start=(dc == 0), stop=(dc == DC - 1),
                         inc=("pe" if dc == DC - 1 else None))
                peH = b.tok("pe")
                v = "vector"
                b.wait_tok(v, peH)
                b.wait_tok(v, ns.l1_h_guard)
                b.op(v, "tensor_scalar", out=ns.l1_hs[:, 0:512],
                     in0=psH, scalar1=ns.l1_b1[:, e * 16 + fc:e * 16 + fc + 1],
                     scalar2=0.0, op0=ALU.add, op1=ALU.max, inc="dv")
                ns.psA_tok[fc % 2] = b.tok("dv")
                b.wait_tok(v, mb_tok)
                b.op(v, "tensor_tensor",
                     out=hview[:, fc, 0:512],
                     in0=ns.l1_hs[:, 0:512], in1=ns.l1_ohbc[:, tau * 512:(tau + 1) * 512],
                     op=ALU.mult, inc="dv")
            h_tok = b.tok("dv")
            if tau == 1:
                ns.l1_w1_free = b.tok("pe")

            for hc in range(DC):
                gidx = tau * DC + hc
                w2slot = gidx % 2
                w2tok = ns.w2_pre_tok if gidx < 2 else ns.w2_cur_tok[w2slot]
                psY = ns.psY[:, (hc % 2) * 512:(hc % 2) * 512 + 512]
                b.wait_tok("tensor", ns.psY_tok[hc % 2])
                b.wait_tok("tensor", h_tok)
                b.wait_tok("tensor", w2tok)
                for ic in range(16):
                    last = (ic == 15 and e != 0)
                    b.op("tensor", "matmul", psY,
                         ns.l1_w2[w2slot][:, ic * 128: ic * 128 + 128],
                         hview[:, ic, 0:512],
                         start=(ic == 0), stop=last, inc=("pe" if last else None))
                if e == 0:
                    b.op("tensor", "matmul", psY,
                         ns.l1_b2t[rbase:rbase + 8, hc * 128:(hc + 1) * 128],
                         ns.l1_ohb[rbase:rbase + 8, tau * 512: (tau + 1) * 512],
                         start=False, stop=True, inc="pe")
                peY = b.tok("pe")
                ns.w2_slot_tok[w2slot] = peY
                if gidx + 2 < 16:
                    g2i = gidx + 2
                    ns.w2_cur_tok[w2slot] = w2_load(e, g2i % DC, w2slot,
                                                    sem=("w2a" if w2slot == 0 else "w2b"))
                b.wait_tok("vector", peY)
                if it == 0 and e == 0:
                    b.op("vector", "tensor_copy",
                         out=ns.l1_acc[:, hc * HALF + tau * 512: hc * HALF + (tau + 1) * 512],
                         in_=psY, inc="dv")
                else:
                    b.op("vector", "tensor_tensor",
                         out=ns.l1_acc[:, hc * HALF + tau * 512: hc * HALF + (tau + 1) * 512],
                         in0=ns.l1_acc[:, hc * HALF + tau * 512: hc * HALF + (tau + 1) * 512],
                         in1=psY, op=ALU.add, inc="dv")
                ns.psY_tok[hc % 2] = b.tok("dv")
            ns.l1_h_guard = b.tok("pe")
            if tau == 1:
                ns.w1_slot_tok = [ns.l1_w1_free, ns.l1_w1_free]


def _l1ffn_half(b, ns, half, t0):
    v = "vector"
    ns.l1_h_guard = None
    t_a = _l1_stats_bcast(b, ns, lambda dc: Xc(ns, dc, t0, HALF), "l1ln")
    b.wait_tok(v, t_a)
    for dc in range(DC):
        X = Xc(ns, dc, t0, HALF)
        b.op(v, "tensor_tensor", out=ns.tmp1[:, 0:HALF], in0=X, in1=ns.bc_m[:, 0:HALF], op=ALU.subtract)
        b.op(v, "tensor_tensor", out=ns.l1_in_bf[:, dc * HALF:(dc + 1) * HALF],
             in0=ns.tmp1[:, 0:HALF], in1=ns.bc_a[:, 0:HALF], op=ALU.mult, inc="dv")
    ns.l1_in_tok = b.tok("dv")

    b.wait_tok("tensor", ns.l1_in_tok)
    b.wait("tensor", "dv", b.val("dv"))
    gtok = _l1_gate(b, ns, half, t0)

    _l1_iter(b, ns, half, t0, 0, gtok)

    b.wait(v, "pe", b.val("pe"))
    t_m2 = emit_bcast(b, ns, ns.row_m, ns.bc_m, HALF, None, base=0)
    t_a2 = emit_bcast(b, ns, ns.row_a, ns.bc_a, HALF, None, base=32)
    b.wait("gpsimd", "dv", ns.gbc_guard)
    b.dma("gpsimd", out=ns.g_row_bf[0:1, 0:HALF], in_=ns.l1_oh[8:9, 0:HALF], inc="gx")
    b.wait("tensor", "gx", b.val("gx"))
    v1_tok = emit_bcast(b, ns, ns.g_row_bf, ns.l1_v1bc, HALF, bf=True)
    ns.gbc_guard = b.val("dv")
    b.wait_tok(v, t_a2)
    b.wait_tok(v, v1_tok)
    for dc in range(DC):
        X = Xc(ns, dc, t0, HALF)
        ACC = ns.l1_acc[:, dc * HALF:(dc + 1) * HALF]
        b.op(v, "tensor_tensor", out=ns.tmp1[:, 0:HALF], in0=X, in1=ns.bc_m[:, 0:HALF], op=ALU.subtract)
        b.op(v, "tensor_tensor", out=ns.tmp2[:, 0:HALF], in0=ns.tmp1[:, 0:HALF], in1=ns.bc_a[:, 0:HALF],
             op=ALU.mult)
        b.op(v, "scalar_tensor_tensor", out=ns.l1_in_bf[:, dc * HALF:(dc + 1) * HALF],
             in0=ACC, scalar=0.5, in1=ns.tmp2[:, 0:HALF], op0=ALU.mult, op1=ALU.add)
        b.op(v, "tensor_tensor", out=ACC, in0=ACC, in1=ns.l1_v1bc[:, 0:HALF], op=ALU.mult)
        b.op(v, "tensor_tensor", out=ACC, in0=ACC, in1=ns.tmp2[:, 0:HALF], op=ALU.add, inc="dv")
    ns.l1_in_tok = b.tok("dv")
    ns.l1_h_guard = None

    _l1_iter(b, ns, half, t0, 1, gtok)

    b.wait(v, "pe", b.val("pe"))
    b.wait("scalar", "pe", b.val("pe"))
    b.wait("scalar", "dv", b.val("dv"))
    b.wait("tensor", "dv", b.val("dv"))
    t_a3 = _l1_stats_bcast(b, ns, lambda dc: ns.l1_acc[:, dc * HALF:(dc + 1) * HALF], "l1ln2")
    b.wait_tok(v, t_a3)
    for dc in range(DC):
        X = Xc(ns, dc, t0, HALF)
        ACC = ns.l1_acc[:, dc * HALF:(dc + 1) * HALF]
        b.op(v, "tensor_tensor", out=ns.tmp1[:, 0:HALF], in0=ACC, in1=ns.bc_m[:, 0:HALF], op=ALU.subtract)
        b.op(v, "tensor_tensor", out=ns.tmp2[:, 0:HALF], in0=ns.tmp1[:, 0:HALF], in1=ns.bc_a[:, 0:HALF],
             op=ALU.mult, inc="dv")
        b.wait("scalar", "dv", b.val("dv"))
        b.op("scalar", "activation", out=ns.tmp1[:, 0:HALF], in_=ns.tmp2[:, 0:HALF], func=AF.Identity,
             scale=ns.l1_gam05[:, dc:dc + 1], bias=ns.l1_bet05[:, dc:dc + 1], inc="ac")
        b.wait(v, "ac", b.val("ac"))
        b.op(v, "tensor_tensor", out=X, in0=X, in1=ns.tmp1[:, 0:HALF], op=ALU.add, inc="dv")


ARENA_WORDS = 29952


def _mk_carver(ns):
    pos = [0]

    def al(nbytes, align=64):
        p = (pos[0] + align - 1) // align * align
        pos[0] = p + nbytes
        assert pos[0] <= ARENA_WORDS * 4, f"arena overflow {pos[0]}"
        return p

    def f32(n, rows=128):
        p = al(n * 4)
        return ns.AR[0:rows, p // 4: p // 4 + n]

    def bf(n, rows=128):
        p = al(n * 2)
        return ns.AR_B[0:rows, p // 2: p // 2 + n]

    return f32, bf, al, pos


def carve_l0ffn(ns):
    f32, bf, al, pos = _mk_carver(ns)
    ns.xn_bf = bf(DC * HALF)
    ns.sg_bf = bf(DC * HALF)
    ns.w1a_sb = [bf(8 * 512), bf(8 * 512)]
    ns.w1b_sb = [bf(8 * 512), bf(8 * 512)]
    ns.w2_sb = [bf(8 * 128), bf(8 * 128)]
    ns.gluA = f32(512)
    ns.gluS = f32(512)
    ns.gluB = f32(512)
    ns.g_bc = bf(HALF)
    ns.bc_b = f32(HALF)
    ns.gp_T = f32(T)
    ns.gp_Tb = bf(T)
    ns.bc_m = f32(HALF)
    ns.bc_a = f32(HALF)
    ns.tmp1 = f32(HALF)
    ns.tmp2 = f32(HALF)
    ns.sq = ns.tmp2
    ns.l0_bytes = pos[0]


def carve_l1ffn(ns):
    f32, bf, al, pos = _mk_carver(ns)
    ns.l1_in_bf = bf(DC * HALF)
    ns.l1_acc = f32(DC * HALF)
    ns.l1_w1full = bf(8 * 2048)
    ns.l1_w2 = [bf(16 * 128), bf(16 * 128)]
    ns.l1_hs = f32(512)
    ns.l1_oh = f32(HALF)
    ns.l1_ohb = bf(HALF)
    ns.l1_ohbc = bf(HALF)
    ns.l1_v1bc = bf(HALF)
    p = al(16384)
    ns.bc_m = ns.AR[:, p // 4: p // 4 + HALF]
    ns.bc_a = ns.AR[:, p // 4 + HALF: p // 4 + 2 * HALF]
    ns.tmp1 = ns.AR[:, p // 4 + 2 * HALF: p // 4 + 3 * HALF]
    ns.tmp2 = ns.AR[:, p // 4 + 3 * HALF: p // 4 + 4 * HALF]
    ns.sq = ns.tmp2
    ns.l1_h = ns.AR_B[:, p // 2: p // 2 + 16 * 512]
    ns.l1_bytes = pos[0]


def build_program(g, sublayers):
    nc = bass.Bass("TRN2", target_bir_lowering=False, debug=False)
    ns = make_ns(nc, g, sublayers)
    d = ns.dram

    with ExitStack() as ctx:
        sb = lambda name, shape, dt: ctx.enter_context(nc.sbuf_tensor("s_" + name, shape, dt))
        psum = lambda name, shape: ctx.enter_context(nc.psum_tensor(name, shape, F32))

        ns.X = sb("X", [128, DC * T], F32)
        ns.ident_f = sb("ident_f", [128, 128], F32)
        ns.ones_col = sb("ones_col", [128, 512], F32)
        ns.ones_bf = sb("ones_bf", [128, 128], BF16)
        arena = sb("arena", [128, ARENA_WORDS], F32)
        ns.AR = arena[:, :]
        ns.AR_B = arena.bitcast(BF16)[:, :]

        rows1 = sb("rows1", [65, T], F32)
        rows2 = sb("rows2", [1, T], F32)
        ns.row_m = rows1[0:1, :]
        ns.row_a = rows1[32:33, :]
        ns.row_b = rows1[64:65, :]
        ns.g_row = rows2[0:1, :]
        ns.g_row_full = rows2[0:1, :]
        ns.g_row_bf = rows2.bitcast(BF16)[0:1, :]
        ns.bc_g16 = sb("bc_g16", [128, 16], F32)
        ns.scol = sb("scol", [128, 128], F32)
        ns.rowstage = sb("rowstage", [128, 128], F32)

        ns.gl = sb("gl", [128, 16], F32)
        ns.gcur = sb("gcur", [128, 16], F32)
        ns.gmask = sb("gmask", [128, 16], F32)
        ns.gm = sb("gm", [128, 4], F32)
        ns.gex = sb("gex", [128, 48], F32)

        ns.wg_bf = sb("wg_bf", [128, DC * 16], BF16)
        ns.bg_row = sb("bg_row", [1, 16], F32)[0:1, :]
        ns.l0_ns = sb("l0_ns", [128, DC], F32)
        ns.l0_b1a = sb("l0_b1a", [128, 16 * 8], F32)
        ns.l0_b1b = sb("l0_b1b", [128, 16 * 8], F32)
        ns.b2_bf = sb("b2_bf", [16, 1024], BF16)

        ns.bqi_p = sb("bqi_p", [128, DC], F32)
        ns.bki_p = sb("bki_p", [128, DC], F32)

        ns.l1_gw_bf = sb("l1_gw_bf", [128, DC * 8], BF16)
        ns.l1_gb_row = sb("l1_gb_row", [1, 8], F32)[0:1, :]
        ns.l1_b1 = sb("l1_b1", [128, 8 * 16], F32)
        ns.l1_b2t = sb("l1_b2t", [40, 1024], BF16)
        ns.l1_gam = sb("l1_gam", [128, DC], F32)
        ns.l1_bet = sb("l1_bet", [128, DC], F32)
        ns.l1_gam05 = sb("l1_gam05", [128, DC], F32)
        ns.l1_bet05 = sb("l1_bet05", [128, DC], F32)

        ns.ps4 = psum("ps4", [128, 2048])
        ns.psA = ns.ps4[:, 0:1024]
        ns.psB = ns.ps4[:, 1024:2048]
        ns.psY = psum("psY", [128, 1024])
        ns.ps_misc = psum("ps_misc", [128, 512])
        ns.ps_misc2 = psum("ps_misc2", [128, 512])

        ns.sems = {}
        for s in ("xl", "wl", "w2a", "w2b", "qk", "qka", "qkb", "gx", "m1", "m2", "pe", "dv", "ac", "gp", "bar", "st"):
            ns.sems[s] = ctx.enter_context(nc.semaphore(s))

        block = ctx.enter_context(nc.Block())
        b = Builder(ns)

        def program(bb):
            ns.w1_slot_idx = 0
            ns.w2_slot_idx = 0
            ns.w1_slot_tok = [None, None]
            ns.w2_slot_tok = [None, None]
            ns.w1_load_tok = {}
            ns.w2_load_tok = {}
            ns.psA_tok = [None, None]
            ns.psB_tok = [None, None]
            ns.psY_tok = [None, None]
            ns.gbc_guard = 0
            ns.sg_guard = None
            ns.w2_cur_tok = [None, None]
            ns.w2_pre_tok = None
            ns.aw_guard = None
            ns.a_xln_guard = 0
            ns.a_pbf_guard = 0
            ns.a_cs_guard = [0, 0]
            ns.a_ps_tok = [None, None]
            ns.a_psY_tok = [None, None]
            ns.a_stage_guard = 0
            ns.a_p1_tok = [None] * 4
            ns.a_p2_tok = [None, None]
            ns.a_expT_guard = [None] * 8
            ns.a_qk_guard = [None, None]
            ns.a_op_guard = 0
            ns.bvi_tok = None
            ns.n_kv_guard = None
            ns.n_vk_tok = None
            ns.n_oh_guard = 0
            ns.n_oc_guard = [None, None]
            ns.n_vkps_tok = [None, None]
            ns.l1_w1_free = None
            for e in Builder.AUTODRAIN:
                setattr(bb, "_dirty_" + e, False)

            emit_load_consts(bb, ns)
            emit_load_x(bb, ns)
            need_l0 = "l0ffn" in sublayers
            need_l1 = "l1ffn" in sublayers
            if need_l0:
                bb.dma("gpsimd", out=ns.wg_bf.rearrange("p (dc e) -> p dc e", dc=DC),
                       in_=d["l0_wg"].rearrange("(dc p) e -> p dc e", p=128), inc="wl")
                bb.dma("sync", out=ns.bg_row, in_=d["l0_bg_row"], inc="xl")
                bb.dma("sync", out=ns.l0_ns[:], in_=d["l0_ns_p"], inc="xl")
                bb.dma("sync", out=ns.l0_b1a[:], in_=d["l0_b1a_p"], inc="xl")
                bb.dma("sync", out=ns.l0_b1b[:], in_=d["l0_b1b_p"], inc="xl")
                bb.dma("gpsimd", out=ns.b2_bf[:], in_=d["l0_b2"], inc="wl")
            if "l0attn" in sublayers:
                bb.dma("sync", out=ns.bqi_p[:], in_=d["bqi_p"], inc="xl")
                bb.dma("sync", out=ns.bki_p[:], in_=d["bki_p"], inc="xl")
            if need_l1:
                bb.dma("gpsimd", out=ns.l1_gw_bf.rearrange("p (dc e) -> p dc e", dc=DC),
                       in_=d["l1_gw"].rearrange("(dc p) e -> p dc e", p=128), inc="wl")
                bb.dma("sync", out=ns.l1_gb_row, in_=d["l1_gb_row"], inc="xl")
                bb.dma("sync", out=ns.l1_b1[:], in_=d["l1_b1_p"], inc="xl")
                bb.dma("gpsimd", out=ns.l1_b2t[0:8, :], in_=d["l1_b2"], inc="wl")
                bb.dma("gpsimd", out=ns.l1_b2t[32:40, :], in_=d["l1_b2"], inc="wl")
                bb.dma("sync", out=ns.l1_gam[:], in_=d["l1_gam_p"], inc="xl")
                bb.dma("sync", out=ns.l1_bet[:], in_=d["l1_bet_p"], inc="xl")
            ns.wg_tok = ("wl", bb.val("wl"))
            bb.wait("vector", "xl")
            bb.wait("vector", "wl")
            bb.wait("tensor", "xl")
            bb.wait_tok("tensor", ns.ones_tok)
            bb.wait("tensor", "wl")
            bb.wait("scalar", "xl")
            if need_l1:
                bb.op("vector", "tensor_scalar", out=ns.l1_gam05[:], in0=ns.l1_gam[:],
                      scalar1=0.5, scalar2=None, op0=ALU.mult)
                bb.op("vector", "tensor_scalar", out=ns.l1_bet05[:], in0=ns.l1_bet[:],
                      scalar1=0.5, scalar2=None, op0=ALU.mult, inc="dv")
            bb.barrier()

            for sl in sublayers:
                if sl == "l0ffn":
                    carve_l0ffn(ns)
                    emit_l0ffn(bb, ns)
                elif sl == "l1ffn":
                    carve_l1ffn(ns)
                    emit_l1ffn(bb, ns)
                elif sl == "l0attn":
                    carve_l0attn(ns)
                    emit_l0attn(bb, ns)
                elif sl == "l1attn":
                    carve_l1attn(ns)
                    emit_l1attn(bb, ns)
                else:
                    raise NotImplementedError(sl)
                bb.barrier()

            emit_store_x(bb, ns)

        for name in ENGINES:
            def body(engine, _name=name):
                b.begin(_name, engine)
                program(b)
            getattr(block, name)(body)

    return nc


def _run(nc, in_maps, **kw):
    return run_bass_kernel_spmd(nc, in_maps, core_ids=list(range(NCORES)), **kw)



def kernel(**inputs):
    g, xs = prep_host(inputs)
    seq = ["l0ffn", "l0attn", "l0ffn", "l1ffn", "l1attn", "l1ffn"]
    nc = build_program(g, seq)
    in_maps = [dict(g, xT=xs[b]) for b in range(NCORES)]
    res = run_bass_kernel_spmd(nc, in_maps, core_ids=list(range(NCORES)))
    out = np.stack([np.asarray(r["yT"]).T for r in res.results])
    return np.ascontiguousarray(out, dtype=np.float32)


# version 04
# speedup vs baseline: 1.0577x; 1.0577x over previous
import numpy as np
import ml_dtypes
from contextlib import ExitStack

import concourse.bass as bass
import concourse.mybir as mybir
from concourse.bass_utils import run_bass_kernel_spmd

F32 = mybir.dt.float32
BF16 = mybir.dt.bfloat16
AF = mybir.ActivationFunctionType
ALU = mybir.AluOpType
AX = mybir.AxisListType
BFNP = ml_dtypes.bfloat16

T = 2048
D = 1024
DC = 8
HALF = 1024
NCORES = 8

ENGINES = ("sync", "scalar", "vector", "tensor", "gpsimd")



class Builder:

    def __init__(self, ns):
        self.ns = ns
        self.cur = None
        self.eng = None
        self.cnt = {}

    def begin(self, engine_name, engine_obj):
        self.cur = engine_name
        self.eng = engine_obj
        self.cnt = {k: 0 for k in self.ns.sems}

    def val(self, sem):
        return self.cnt[sem]

    def tok(self, sem):
        return (sem, self.cnt[sem])

    def wait(self, eng, sem, val=None):
        if val is None:
            val = self.cnt[sem]
        if self.cur == eng and val > 0:
            self.eng.wait_ge(self.ns.sems[sem], val)

    def wait_tok(self, eng, t):
        if t is not None:
            self.wait(eng, t[0], t[1])

    AUTODRAIN = ("vector", "scalar", "gpsimd")

    def op(self, eng, method, *args, inc=None, inc_n=1, **kw):
        if inc is not None:
            self.cnt[inc] += inc_n
        if self.cur != eng:
            return None
        if eng in self.AUTODRAIN and method not in ("dma_start", "nop"):
            if getattr(self, "_dirty_" + eng, False):
                self.eng.drain()
            setattr(self, "_dirty_" + eng, True)
        ins = getattr(self.eng, method)(*args, **kw)
        if inc is not None:
            ins.then_inc(self.ns.sems[inc], inc_n)
        return ins

    def dma(self, eng, out, in_, inc, **kw):
        return self.op(eng, "dma_start", inc=inc, inc_n=16, out=out, in_=in_, **kw)

    def drain(self, eng):
        if self.cur == eng:
            self.eng.drain()

    def barrier(self):
        self.cnt["bar"] += 5
        tgt = self.cnt["bar"]
        if self.cur is not None:
            if self.cur != "sync":
                self.eng.drain()
            self.eng.nop().then_inc(self.ns.sems["bar"], 1)
            self.eng.wait_ge(self.ns.sems["bar"], tgt)
            for e in self.AUTODRAIN:
                setattr(self, "_dirty_" + e, False)


class NS:
    pass



def _rope_tables():
    i = np.arange(D // 2, dtype=np.float32)
    theta = (np.float32(10000.0) ** (np.float32(-2.0) * (i - np.float32(1.0)) / np.float32(D))).astype(np.float32)
    ang = (np.arange(T, dtype=np.float32)[:, None] * theta[None, :]).astype(np.float32)
    c = np.cos(ang).astype(np.float32)
    s = np.sin(ang).astype(np.float32)
    return np.ascontiguousarray(c.T), np.ascontiguousarray(s.T)


def _causal_masks():
    qq = np.arange(128)[:, None]; kk = np.arange(512)[None, :]
    m1 = np.stack([(kk <= j * 128 + qq) for j in range(4)]).astype(np.float32)
    m1 = (1.0 - m1) * np.float32(-1e30)
    kk2 = np.arange(128)[:, None]; qq2 = np.arange(512)[None, :]
    m2 = np.stack([(kk2 <= qq2 - j * 128) for j in range(4)]).astype(np.float32)
    m2 = (1.0 - m2) * np.float32(-1e30)
    return np.ascontiguousarray(m1), np.ascontiguousarray(m2)


def _part_layout(v):
    E, L = v.shape
    n = L // 128
    return np.ascontiguousarray(v.reshape(E, n, 128).transpose(2, 0, 1).reshape(128, E * n))


def prep_host(inputs):
    g = {}
    f32 = np.float32
    asc = np.ascontiguousarray

    perm = np.concatenate([np.arange(0, D, 2), np.arange(1, D, 2)])
    g["wq_t"] = asc(np.asarray(inputs["L0_attn_wq"], f32).T[:, perm])
    g["wk_t"] = asc(np.asarray(inputs["L0_attn_wk"], f32).T[:, perm])
    g["wv_t"] = asc(np.asarray(inputs["L0_attn_wv"], f32).T[:, perm])
    ipw = np.asarray(inputs["L0_attn_in_proj_w"], f32)
    ipb = np.asarray(inputs["L0_attn_in_proj_b"], f32)
    g["wqi_t"] = asc(ipw[0:D].T[perm, :])
    g["wki_t"] = asc(ipw[D:2 * D].T[perm, :])
    g["wvi_t"] = asc(ipw[2 * D:3 * D].T[perm, :])
    g["bqi_p"] = _part_layout(ipb[0:D][None, :])
    g["bki_p"] = _part_layout(ipb[D:2 * D][None, :])
    g["bvi_p"] = _part_layout(ipb[2 * D:3 * D][None, :])
    g["outw_t"] = asc(np.asarray(inputs["L0_attn_out_w"], f32).T)
    g["bvi_row"] = asc(ipb[2 * D:3 * D][None, :])
    g["outb_row"] = asc(np.asarray(inputs["L0_attn_out_b"], f32)[None, :])
    g["rope_cos"], g["rope_sin"] = _rope_tables()
    g["mask1"], g["mask2"] = _causal_masks()

    g["l0_ns_p"] = _part_layout(np.asarray(inputs["L0_ffn_norm_scale"], f32)[None, :])
    g["l0_wg"] = asc(np.asarray(inputs["L0_ffn_wg"], f32))
    g["l0_bg_row"] = asc(np.asarray(inputs["L0_ffn_bg"], f32)[None, :])
    w1 = np.asarray(inputs["L0_ffn_w1"], f32)
    g["l0_w1a"] = asc(w1[:, :, 0::2])
    g["l0_w1b"] = asc(w1[:, :, 1::2])
    b1 = np.asarray(inputs["L0_ffn_b1"], f32)
    g["l0_b1a_p"] = _part_layout(asc(b1[:, 0::2]))
    g["l0_b1b_p"] = _part_layout(asc(b1[:, 1::2]))
    w2l0 = np.asarray(inputs["L0_ffn_w2"], f32)
    g["l0_w2p"] = asc(w2l0.reshape(16, 8, 128, 8, 128).transpose(0, 3, 2, 1, 4)
                      .reshape(16, 8, 128, 8 * 128))
    g["l0_b2"] = asc(np.asarray(inputs["L0_ffn_b2"], f32))

    wqkv = np.asarray(inputs["L1_attn_W_qkv"], f32)
    g["l1_wqT"] = asc(wqkv[0:D].T)
    g["l1_wkT"] = asc(wqkv[D:2 * D].T)
    g["l1_wvT"] = asc(wqkv[2 * D:3 * D].T)
    g["l1_wpT"] = asc(np.asarray(inputs["L1_attn_W_proj"], f32).T)
    g["l1_bp_row"] = asc(np.asarray(inputs["L1_attn_b_proj"], f32)[None, :])

    g["l1_gw"] = asc(np.asarray(inputs["L1_ffn_gate_W"], f32))
    g["l1_gb_row"] = asc(np.asarray(inputs["L1_ffn_gate_b"], f32)[None, :])
    g["l1_w1"] = asc(np.asarray(inputs["L1_ffn_W1"], f32))
    g["l1_b1_p"] = _part_layout(np.asarray(inputs["L1_ffn_b1"], f32))
    w2l1 = np.asarray(inputs["L1_ffn_W2"], f32)
    g["l1_w2p"] = asc(w2l1.reshape(8, 16, 128, 8, 128).transpose(0, 3, 2, 1, 4)
                      .reshape(8, 8, 128, 16 * 128))
    g["l1_b2"] = asc(np.asarray(inputs["L1_ffn_b2"], f32))
    g["l1_gam_p"] = _part_layout(np.asarray(inputs["L1_ffn_ln_gamma"], f32)[None, :])
    g["l1_bet_p"] = _part_layout(np.asarray(inputs["L1_ffn_ln_beta"], f32)[None, :])

    g["ident_f"] = np.eye(128, dtype=f32)

    x = np.asarray(inputs["L0_attn_x"], f32)
    xT = np.ascontiguousarray(x.transpose(0, 2, 1))
    return g, [np.ascontiguousarray(xT[b]) for b in range(8)]



def Xc(ns, dc, t0=0, tlen=T):
    return ns.X[:, dc * T + t0: dc * T + t0 + tlen]


def make_ns(nc, g, sublayers):
    ns = NS()
    ns.nc = nc
    ns.g = g
    d = {}

    def din(name, arr):
        d[name] = nc.dram_tensor(name, list(arr.shape), F32, kind="ExternalInput").ap()
        return d[name]

    ns.dram = d
    d["xT"] = nc.dram_tensor("xT", [D, T], F32, kind="ExternalInput").ap()
    d["yT"] = nc.dram_tensor("yT", [D, T], F32, kind="ExternalOutput").ap()
    for k, v in g.items():
        din(k, v)
    d["row_dram"] = nc.dram_tensor("row_dram", [128, 128], F32, kind="Internal").ap()
    d["qh_dram"] = nc.dram_tensor("qh_dram", [D, T], BF16, kind="Internal").ap()
    d["kh_dram"] = nc.dram_tensor("kh_dram", [D, T], BF16, kind="Internal").ap()
    d["q2_dram"] = nc.dram_tensor("q2_dram", [D, T], BF16, kind="Internal").ap()
    d["o2_dram"] = nc.dram_tensor("o2_dram", [D, T], BF16, kind="Internal").ap()
    return ns



def emit_load_x(b, ns):
    for dc in range(DC):
        b.dma("sync", out=Xc(ns, dc), in_=ns.dram["xT"][dc * 128:(dc + 1) * 128, :], inc="xl")
    ns.x_tok = b.tok("xl")


def emit_store_x(b, ns):
    b.barrier()
    for dc in range(DC):
        b.dma("scalar", out=ns.dram["yT"][dc * 128:(dc + 1) * 128, :], in_=Xc(ns, dc), inc="st")
    b.wait("scalar", "st")


def emit_load_consts(b, ns):
    b.op("vector", "memset", ns.ones_col[:], 1.0)
    b.op("vector", "memset", ns.ones_bf[:], 1.0, inc="dv")
    b.dma("sync", out=ns.ident_f[:], in_=ns.dram["ident_f"], inc="xl")
    ns.const_tok = b.tok("xl")
    ns.ones_tok = b.tok("dv")



def emit_colstats(b, ns, half, src, kinds_fn, out_rows, tag, nt=8):
    n_rows = len(out_rows)
    nw = nt
    prev_mm = None
    for dc in range(DC):
        b.wait_tok("scalar", prev_mm)
        b.op("scalar", "activation", out=ns.sq[:], in_=src(dc), func=AF.Square, inc="ac")
        b.wait("tensor", "ac", b.val("ac"))
        if dc == 0:
            b.wait("tensor", "dv", b.val("dv"))
        for w in range(nw):
            b.op("tensor", "matmul", ns.ps_misc[:, 2 * w:2 * w + 1],
                 src(dc)[:, 128 * w:128 * w + 128], ns.ones_col[:, 0:1],
                 start=(dc == 0 and w == 0), stop=False)
            b.op("tensor", "matmul", ns.ps_misc[:, 2 * w + 1:2 * w + 2],
                 ns.sq[:, 128 * w:128 * w + 128], ns.ones_col[:, 0:1],
                 start=False, stop=(dc == DC - 1 and w == nw - 1),
                 inc=("pe" if (w == nw - 1) else None))
        prev_mm = b.tok("pe")
    pe_done = b.tok("pe")
    b.wait_tok("vector", pe_done)
    b.wait_tok("scalar", pe_done)
    kinds_fn(b)
    b.wait("tensor", "dv", b.val("dv"))
    b.wait("tensor", "ac", b.val("ac"))
    b.op("tensor", "transpose", ns.ps_misc[0:n_rows * nw, 0:128],
         ns.scol[:, 0:n_rows * nw], ns.ident_f[:], inc="pe")
    b.wait("vector", "pe", b.val("pe"))
    b.wait("vector", "m1", b.val("m1"))
    b.op("vector", "tensor_copy", out=ns.rowstage[0:n_rows * nw, 0:128],
         in_=ns.ps_misc[0:n_rows * nw, 0:128], inc="dv")
    b.wait("sync", "dv", b.val("dv"))
    b.dma("sync", out=ns.dram["row_dram"][0:n_rows * nw, 0:128],
          in_=ns.rowstage[0:n_rows * nw, 0:128], inc="m1")
    b.wait("sync", "m1", b.val("m1"))
    for r in range(n_rows):
        b.dma("sync", out=out_rows[r][0:1, 0:nw * 128],
              in_=ns.dram["row_dram"][r * nw:(r + 1) * nw, 0:128]
              .rearrange("a c -> (a c)")[None, :], inc="m1")
    return b.tok("m1")


def emit_bcast(b, ns, row_ap, out_bc, n, wait_t=None, base=0, bf=False):
    nblk = (n + 511) // 512
    for j in range(nblk):
        w = min(512, n - j * 512)
        b.wait_tok("tensor", wait_t)
        b.wait("tensor", "dv", b.val("dv"))
        ones = ns.ones_bf if bf else ns.ones_col
        b.op("tensor", "matmul", ns.ps_misc[:, 0:w],
             ones[base:base + 1, 0:128], row_ap[0:1, j * 512:j * 512 + w],
             start=True, stop=True, inc="pe")
        b.wait("vector", "pe", b.val("pe"))
        b.op("vector", "tensor_copy", out=out_bc[:, j * 512:j * 512 + w],
             in_=ns.ps_misc[:, 0:w], inc="dv")
    return b.tok("dv")



def emit_l0ffn(b, ns):
    g = ns.g
    E = 16
    for half in range(2):
        t0 = half * HALF
        _l0ffn_half(b, ns, half, t0, E)
        b.barrier()


def _l0ffn_half(b, ns, half, t0, E):
    d = ns.dram

    def kinds(bb):
        v = "vector"; s = "scalar"
        nw = 8
        S = ns.scol
        P = ns.ps_misc
        bb.op(v, "tensor_scalar", out=S[:, 0:nw], in0=P[:, 0:16].rearrange("p (w k) -> p k w", k=2)[:, 0, :],
              scalar1=1.0 / D, scalar2=None, op0=ALU.mult)
        bb.op(v, "tensor_scalar", out=S[:, 24:24 + nw], in0=P[:, 0:16].rearrange("p (w k) -> p k w", k=2)[:, 1, :],
              scalar1=1.0 / D, scalar2=None, op0=ALU.mult)
        bb.drain(v)
        bb.op(v, "tensor_tensor", out=S[:, 32:32 + nw], in0=S[:, 0:nw], in1=S[:, 0:nw], op=ALU.mult)
        bb.drain(v)
        bb.op(v, "tensor_tensor", out=S[:, 40:40 + nw], in0=S[:, 24:24 + nw], in1=S[:, 32:32 + nw], op=ALU.subtract)
        bb.drain(v)
        bb.op(v, "tensor_scalar", out=S[:, 48:48 + nw], in0=S[:, 40:40 + nw],
              scalar1=1e-5, scalar2=None, op0=ALU.add, inc="dv")
        bb.wait(s, "dv", bb.val("dv"))
        bb.op(s, "activation", out=S[:, 56:56 + nw], in_=S[:, 48:48 + nw], func=AF.Ln)
        bb.drain(s)
        bb.op(s, "activation", out=S[:, 8:8 + nw], in_=S[:, 56:56 + nw], func=AF.Exp, scale=-0.5, inc="ac")
        bb.wait(v, "ac", bb.val("ac"))
        bb.op(v, "reciprocal", out=S[:, 32:32 + nw], in_=S[:, 48:48 + nw])
        bb.drain(v)
        bb.op(v, "tensor_tensor", out=S[:, 40:40 + nw], in0=S[:, 40:40 + nw], in1=S[:, 32:32 + nw], op=ALU.mult)
        bb.drain(v)
        bb.op(v, "tensor_scalar", out=S[:, 40:40 + nw], in0=S[:, 40:40 + nw],
              scalar1=1e-5, scalar2=None, op0=ALU.add, inc="dv")
        bb.wait(s, "dv", bb.val("dv"))
        bb.op(s, "activation", out=S[:, 48:48 + nw], in_=S[:, 40:40 + nw], func=AF.Ln)
        bb.drain(s)
        bb.op(s, "activation", out=S[:, 48:48 + nw], in_=S[:, 48:48 + nw], func=AF.Exp, scale=-0.5, inc="ac")
        bb.wait(v, "ac", bb.val("ac"))
        bb.op(v, "tensor_tensor", out=S[:, 16:16 + nw], in0=S[:, 8:8 + nw], in1=S[:, 48:48 + nw], op=ALU.mult)
        bb.drain(v)
        bb.op(v, "tensor_scalar", out=S[:, 8:8 + nw], in0=S[:, 8:8 + nw],
              scalar1=0.5, scalar2=None, op0=ALU.mult, inc="dv")

    rows = [ns.row_m, ns.row_a, ns.row_b]
    rows_tok = emit_colstats(b, ns, half, lambda dc: Xc(ns, dc, t0, HALF), kinds, rows, "l0ln")

    t_m = emit_bcast(b, ns, ns.row_m, ns.bc_m, HALF, rows_tok, base=0)
    t_a = emit_bcast(b, ns, ns.row_a, ns.bc_a, HALF, rows_tok, base=32)
    t_b = emit_bcast(b, ns, ns.row_b, ns.bc_b, HALF, rows_tok, base=64)

    b.wait_tok("vector", t_b)
    for dc in range(DC):
        X = Xc(ns, dc, t0, HALF)
        t1 = ns.tmp1[:, 0:HALF]
        b.wait("vector", "ac", b.val("ac"))
        b.op("vector", "tensor_tensor", out=t1, in0=X, in1=ns.bc_m[:, 0:HALF], op=ALU.subtract)
        b.drain("vector")
        b.op("vector", "tensor_tensor", out=ns.tmp2[:, 0:HALF], in0=t1, in1=ns.bc_a[:, 0:HALF], op=ALU.mult)
        b.drain("vector")
        b.op("vector", "tensor_tensor", out=X, in0=X, in1=ns.tmp2[:, 0:HALF], op=ALU.add)
        b.drain("vector")
        b.op("vector", "tensor_tensor", out=ns.tmp2[:, 0:HALF], in0=t1, in1=ns.bc_b[:, 0:HALF], op=ALU.mult,
             inc="dv")
        b.wait("scalar", "dv", b.val("dv"))
        b.op("scalar", "activation", out=ns.xn_bf[:, dc * HALF:(dc + 1) * HALF], in_=ns.tmp2[:, 0:HALF],
             func=AF.Identity, scale=ns.l0_ns[:, dc:dc + 1], inc="ac")
    xn_tok = b.tok("ac")

    b.wait_tok("tensor", xn_tok)
    b.wait("tensor", "dv", b.val("dv"))
    gtok = _l0_gate(b, ns, half, t0)

    _l0_experts(b, ns, half, t0, gtok)


def _l0_gate(b, ns, half, t0):
    d = ns.dram
    nw = 8
    b.wait_tok("tensor", ns.wg_tok)
    bgt = emit_bcast(b, ns, ns.bg_row, ns.bc_g16, 16)
    ps_free = bgt
    for w in range(nw):
        tw = t0 + 128 * w
        b.wait_tok("tensor", ps_free)
        for dc in range(DC):
            b.op("tensor", "matmul", ns.ps_misc[:, 0:16],
                 ns.xn_bf[:, dc * HALF + 128 * w: dc * HALF + 128 * w + 128],
                 ns.wg_bf[:, dc * 16:(dc + 1) * 16],
                 start=(dc == 0), stop=(dc == DC - 1), inc=("pe" if dc == DC - 1 else None))
        b.wait("vector", "pe", b.val("pe"))
        v = "vector"
        L = ns.gl
        CUR = ns.gcur
        b.op(v, "tensor_tensor", out=L[:], in0=ns.ps_misc[:, 0:16], in1=ns.bc_g16[:, 0:16], op=ALU.add,
             inc="dv")
        ps_free = b.tok("dv")
        b.drain(v)
        b.op(v, "tensor_copy", out=CUR[:], in_=L[:])
        b.drain(v)
        b.op(v, "reduce_max", out=ns.gm[:, 0:1], in_=CUR[:], axis=AX.X, negate=True)
        b.drain(v)
        for j in range(3):
            b.op(v, "tensor_scalar", out=ns.gmask[:], in0=CUR[:], scalar1=ns.gm[:, 0:1],
                 scalar2=0.0, op0=ALU.add, op1=ALU.is_ge)
            b.drain(v)
            b.op(v, "scalar_tensor_tensor", out=CUR[:], in0=ns.gmask[:], scalar=-1e30,
                 in1=CUR[:], op0=ALU.mult, op1=ALU.add)
            b.drain(v)
            b.op(v, "reduce_max", out=ns.gm[:, 0:1], in_=CUR[:], axis=AX.X, negate=True)
            b.drain(v)
        b.op(v, "tensor_scalar", out=ns.gmask[:], in0=L[:], scalar1=ns.gm[:, 0:1],
             scalar2=0.0, op0=ALU.add, op1=ALU.is_ge)
        b.op(v, "reduce_max", out=ns.gm[:, 1:2], in_=L[:], axis=AX.X, negate=True, inc="dv")
        b.wait("scalar", "dv", b.val("dv"))
        GX = ns.gex[:, 0:16]
        b.op("scalar", "activation", out=GX, in_=L[:], func=AF.Exp, bias=ns.gm[:, 1:2], inc="ac")
        b.wait(v, "ac", b.val("ac"))
        b.op(v, "tensor_tensor", out=GX, in0=GX, in1=ns.gmask[:], op=ALU.mult)
        b.drain(v)
        b.op(v, "reduce_sum", out=ns.gm[:, 2:3], in_=GX, axis=AX.X)
        b.drain(v)
        b.op(v, "reciprocal", out=ns.gm[:, 3:4], in_=ns.gm[:, 2:3])
        b.drain(v)
        b.op(v, "tensor_scalar", out=GX, in0=GX, scalar1=ns.gm[:, 3:4],
             scalar2=0.5, op0=ALU.mult, op1=ALU.mult, inc="dv")
        b.wait("tensor", "dv", b.val("dv"))
        b.op("tensor", "transpose", ns.ps_misc2[0:16, 0:128], GX, ns.ident_f[:], inc="pe")
        b.wait(v, "pe", b.val("pe"))
        b.op(v, "tensor_copy", out=ns.gp_T[0:16, tw:tw + 128], in_=ns.ps_misc2[0:16, 0:128])
        b.op(v, "tensor_copy", out=ns.gp_Tb[0:16, tw:tw + 128], in_=ns.ps_misc2[0:16, 0:128], inc="dv")
    return b.tok("dv")


def _l0_w2_load(b, ns, e, hc, slot, sem):
    d = ns.dram
    b.wait_tok("gpsimd", ns.w2_slot_tok[slot])
    b.dma("gpsimd", out=ns.w2_sb[slot], in_=d["l0_w2p"][e, hc], inc=sem)
    return b.tok(sem)


def _l0_experts(b, ns, half, t0, gtok):
    d = ns.dram
    E = 16
    NFH = 2
    for e in range(E):
        for fh in range(NFH):
            slot = ns.w1_slot_idx
            ns.w1_slot_idx = (ns.w1_slot_idx + 1) % 2
            free_tok = ns.w1_slot_tok[slot]
            b.wait_tok("gpsimd", free_tok)
            b.dma("gpsimd",
                  out=ns.w1a_sb[slot].rearrange("p (dc f) -> p dc f", dc=8),
                  in_=d["l0_w1a"][e, :, fh * 512:(fh + 1) * 512]
                  .rearrange("(dc p) f -> p dc f", p=128), inc="wl")
            b.dma("gpsimd",
                  out=ns.w1b_sb[slot].rearrange("p (dc f) -> p dc f", dc=8),
                  in_=d["l0_w1b"][e, :, fh * 512:(fh + 1) * 512]
                  .rearrange("(dc p) f -> p dc f", p=128), inc="wl")
        for hc0 in range(2):
            _l0_w2_load(b, ns, e, hc0, hc0, "wl")
        wtok = b.tok("wl")
        for fh in range(NFH):
            ns.w1_load_tok[(e, fh)] = wtok
        ns.w2_pre_tok_l0 = wtok
        ns.w2_cur_tok_l0 = [None, None]

        b.wait_tok("sync", gtok)
        b.wait("sync", "dv", ns.gbc_guard)
        b.dma("sync", out=ns.g_row[0:1, 0:HALF], in_=ns.gp_T[e:e + 1, t0:t0 + HALF], inc="m1")
        b.wait("tensor", "m1", b.val("m1"))
        gb_tok = emit_bcast(b, ns, ns.g_row, ns.g_bc, HALF)
        ns.gbc_guard = b.val("dv")

        for fh in range(NFH):
            w1slot = (e * NFH + fh) % 2
            b.wait_tok("tensor", ns.w1_load_tok[(e, fh)])
            if e == 0 and fh == 0:
                b.wait_tok("tensor", b.tok("ac"))
            for c in range(4):
                ic = fh * 4 + c
                for tau in range(2):
                    psA = ns.psA[:, tau * 512:(tau + 1) * 512]
                    psB = ns.psB[:, tau * 512:(tau + 1) * 512]
                    b.wait_tok("tensor", ns.psA_tok[tau])
                    b.wait_tok("tensor", ns.psB_tok[tau])
                    for dc in range(DC):
                        b.op("tensor", "matmul", psA,
                             ns.w1a_sb[w1slot][:, dc * 512 + c * 128: dc * 512 + c * 128 + 128],
                             ns.xn_bf[:, dc * HALF + tau * 512: dc * HALF + (tau + 1) * 512],
                             start=(dc == 0), stop=(dc == DC - 1),
                             inc=("pe" if dc == DC - 1 else None))
                    peA = b.tok("pe")
                    for dc in range(DC):
                        b.op("tensor", "matmul", psB,
                             ns.w1b_sb[w1slot][:, dc * 512 + c * 128: dc * 512 + c * 128 + 128],
                             ns.xn_bf[:, dc * HALF + tau * 512: dc * HALF + (tau + 1) * 512],
                             start=(dc == 0), stop=(dc == DC - 1),
                             inc=("pe" if dc == DC - 1 else None))
                    peB = b.tok("pe")
                    v = "vector"
                    b.wait_tok(v, peA)
                    b.wait(v, "gp", b.val("gp"))
                    b.op(v, "tensor_scalar", out=ns.gluA[:], in0=psA,
                         scalar1=ns.l0_b1a[:, e * 8 + ic:e * 8 + ic + 1], scalar2=7.0,
                         op0=ALU.add, op1=ALU.min, inc="dv")
                    ns.psA_tok[tau] = b.tok("dv")
                    b.wait("scalar", "dv", b.val("dv"))
                    b.wait("scalar", "gp", b.val("gp"))
                    b.op("scalar", "activation", out=ns.gluS[:], in_=ns.gluA[:],
                         func=AF.Sigmoid, scale=1.702, inc="ac")
                    b.wait_tok("scalar", peB)
                    b.op("scalar", "activation", out=ns.gluB[:], in_=psB,
                         func=AF.Identity, bias=ns.l0_b1b[:, e * 8 + ic:e * 8 + ic + 1], inc="ac")
                    ns.psB_tok[tau] = b.tok("ac")
                    b.wait(v, "ac", b.val("ac"))
                    b.op(v, "tensor_scalar", out=ns.gluB[:], in0=ns.gluB[:],
                         scalar1=7.0, scalar2=-7.0, op0=ALU.min, op1=ALU.max, inc="dv")
                    gp = "gpsimd"
                    b.wait(gp, "dv", b.val("dv"))
                    b.wait(gp, "ac", b.val("ac"))
                    b.op(gp, "tensor_tensor", out=ns.gluA[:], in0=ns.gluA[:], in1=ns.gluS[:], op=ALU.mult,
                         inc="gp")
                    b.wait(v, "gp", b.val("gp"))
                    b.op(v, "scalar_tensor_tensor", out=ns.gluB[:], in0=ns.gluB[:], scalar=1.0,
                         in1=ns.gluA[:], op0=ALU.add, op1=ALU.mult, inc="dv")
                    b.wait(gp, "dv", b.val("dv"))
                    b.wait_tok(gp, gb_tok)
                    b.wait_tok(gp, ns.sg_guard)
                    b.op(gp, "tensor_tensor", out=ns.sg_bf[:, ic * HALF + tau * 512: ic * HALF + (tau + 1) * 512],
                         in0=ns.gluB[:], in1=ns.g_bc[:, tau * 512:(tau + 1) * 512], op=ALU.mult,
                         inc="gp")
            ns.w1_slot_tok[w1slot] = b.tok("pe")
        sg_tok = b.tok("gp")

        b.wait_tok("tensor", sg_tok)
        for hc in range(DC):
            w2slot = hc % 2
            w2tok = ns.w2_pre_tok_l0 if hc < 2 else ns.w2_cur_tok_l0[w2slot]
            b.wait_tok("tensor", w2tok)
            b.wait_tok("tensor", ns.psY_tok[0])
            for tau in range(2):
                psY = ns.psY[:, tau * 512:(tau + 1) * 512]
                for ic in range(8):
                    b.op("tensor", "matmul", psY,
                         ns.w2_sb[w2slot][:, ic * 128:(ic + 1) * 128],
                         ns.sg_bf[:, ic * HALF + tau * 512: ic * HALF + (tau + 1) * 512],
                         start=(ic == 0), stop=(ic == 7 and e != 0),
                         inc=("pe" if (ic == 7 and e != 0 and tau == 1) else None))
                if e == 0:
                    b.op("tensor", "matmul", psY,
                         ns.b2_bf[:, hc * 128:(hc + 1) * 128],
                         ns.gp_Tb[0:16, t0 + tau * 512: t0 + (tau + 1) * 512],
                         start=False, stop=True, inc=("pe" if tau == 1 else None))
            peY = b.tok("pe")
            b.wait_tok("vector", peY)
            b.op("vector", "tensor_tensor",
                 out=Xc(ns, hc, t0, HALF),
                 in0=Xc(ns, hc, t0, HALF), in1=ns.psY[:, 0:HALF], op=ALU.add, inc="dv")
            ns.psY_tok[0] = b.tok("dv")
            ns.w2_slot_tok[w2slot] = b.tok("pe")
            if hc + 2 < DC:
                ns.w2_cur_tok_l0[w2slot] = _l0_w2_load(b, ns, e, hc + 2, w2slot,
                                                       "w2a" if w2slot == 0 else "w2b")
        ns.sg_guard = b.tok("pe")





def carve_l0attn(ns):
    f32, bf, al, pos = _mk_carver(ns)
    ns.a_vh = bf(16 * 1024)
    u0 = al(0, align=64)
    f32p, bfp, alp, posp = _mk_carver(ns)
    posp[0] = u0
    ns.a_bcm = f32p(512)
    ns.a_bcr = f32p(512)
    ns.a_xln = bfp(8 * 512)
    ns.a_pbf = bfp(8 * 512)
    ns.a_cos = f32p(2 * 512)
    ns.a_sin = f32p(2 * 512)
    ns.a_bvi = f32p(1024)
    ns.sq = f32p(1024)
    ns.tmp1 = f32p(512)
    ns.a_w = bfp(8 * 1024)
    ns.a_wi = bfp(8 * 1024)
    ns.a_stage = bfp(8 * 512)
    ns.a_rs1 = f32p(512)
    ns.a_rs2 = f32p(512)
    proj_end = posp[0]
    f32a, bfa, ala, posa = _mk_carver(ns)
    posa[0] = u0
    pj = ala(8192)
    ns.a_junk = ns.AR[:, pj // 4: pj // 4 + 2048]
    ns.a_expT = ns.AR_B[:, pj // 2: pj // 2 + 4096]
    ns.a_outw = bfa(8 * 1024)
    ns.a_m1 = bfa(4 * 512)
    ns.a_m2 = bfa(4 * 512)
    ns.a_qh = [bfa(2048, rows=64), bfa(2048, rows=64)]
    ns.a_kh = [bfa(2048, rows=64), bfa(2048, rows=64)]
    ns.a_opair = bfa(T)
    ns.a_zcol = f32a(16)
    ns.a_zbc = bfa(512)
    ns.a_sc1 = f32a(512)
    ns.a_outb = f32a(1024)
    attn_end = posa[0]
    pos[0] = max(proj_end, attn_end)
    ns.a_bytes = pos[0]


def emit_l0attn(b, ns):
    d = ns.dram
    v = "vector"
    NT4 = 4
    H = 16
    HD = 64

    b.dma("sync", out=ns.a_bvi[0:1, 0:D], in_=d["bvi_row"], inc="xl")
    ns.bvi_tok = b.tok("xl")
    for sh in range(2):
        st0 = sh * HALF
        rows = [ns.row_m[0:1, st0:st0 + HALF], ns.row_a[0:1, st0:st0 + HALF]]
        rt = emit_colstats(b, ns, 0, lambda dc: Xc(ns, dc, st0, HALF),
                           lambda bb: _kinds_ln2(bb, ns, nw=8), rows, "aln", nt=8)

    for P in range(3):
        wsrc = ("wq_t", "wk_t", "wv_t")[P]
        wisrc = ("wqi_t", "wki_t", "wvi_t")[P]
        b.wait_tok("gpsimd", ns.aw_guard)
        b.dma("gpsimd", out=ns.a_w.rearrange("p (dc f) -> p dc f", dc=8),
              in_=d[wsrc].rearrange("(dc p) f -> p dc f", p=128), inc="wl")
        b.dma("gpsimd", out=ns.a_wi.rearrange("p (dc f) -> p dc f", dc=8),
              in_=d[wisrc].rearrange("(dc p) f -> p dc f", p=128), inc="wl")
        wtok = b.tok("wl")
        for tau in range(NT4):
            tw = tau * 512
            b.wait(v, "pe", ns.a_xln_guard)
            emit_bcast(b, ns, ns.row_m[0:1, tw:tw + 512], ns.a_bcm, 512, rt, base=0)
            t_r = emit_bcast(b, ns, ns.row_a[0:1, tw:tw + 512], ns.a_bcr, 512, rt, base=32)
            b.wait_tok(v, t_r)
            for dc in range(DC):
                b.op(v, "tensor_tensor", out=ns.tmp1[:, 0:512], in0=Xc(ns, dc, tw, 512),
                     in1=ns.a_bcm[:, 0:512], op=ALU.subtract)
                b.op(v, "tensor_tensor", out=ns.a_xln[:, dc * 512:(dc + 1) * 512],
                     in0=ns.tmp1[:, 0:512], in1=ns.a_bcr[:, 0:512], op=ALU.mult, inc="dv")
            xln_tok = b.tok("dv")
            def cs_load(i, csl):
                b.wait("sync", "dv", ns.a_cs_guard[csl])
                b.dma("sync", out=ns.a_cos[:, csl * 512:(csl + 1) * 512],
                      in_=d["rope_cos"][i * 128:(i + 1) * 128, tw:tw + 512], inc="m2")
                b.dma("sync", out=ns.a_sin[:, csl * 512:(csl + 1) * 512],
                      in_=d["rope_sin"][i * 128:(i + 1) * 128, tw:tw + 512], inc="m2")
                return b.tok("m2")

            cs_load(0, 0)
            cs_pre = cs_load(1, 1)
            cs_next = [cs_pre, cs_pre]
            b.wait_tok("tensor", wtok)
            b.wait_tok("tensor", xln_tok)
            for i in range(4):
                csl = i % 2
                cs_tok = cs_next[csl]
                for half_ in range(2):
                    fc = i + 4 * half_
                    ps = ns.ps4[:, half_ * 512:(half_ + 1) * 512]
                    b.wait_tok("tensor", ns.a_ps_tok[half_])
                    for dc in range(DC):
                        b.op("tensor", "matmul", ps,
                             ns.a_w[:, dc * 1024 + fc * 128: dc * 1024 + fc * 128 + 128],
                             ns.a_xln[:, dc * 512:(dc + 1) * 512],
                             start=(dc == 0), stop=(dc == DC - 1),
                             inc=("pe" if dc == DC - 1 else None))
                pe_pair = b.tok("pe")
                b.wait_tok(v, pe_pair)
                b.wait_tok(v, cs_tok)
                b.wait(v, "pe", ns.a_pbf_guard)
                ci = ns.a_cos[:, csl * 512:(csl + 1) * 512]
                si = ns.a_sin[:, csl * 512:(csl + 1) * 512]
                xe = ns.ps4[:, 0:512]
                xo = ns.ps4[:, 512:1024]
                b.op(v, "tensor_tensor", out=ns.a_rs1[:, 0:512], in0=xe, in1=ci, op=ALU.mult)
                b.op(v, "tensor_tensor", out=ns.a_rs2[:, 0:512], in0=xo, in1=si, op=ALU.mult)
                b.op(v, "tensor_tensor", out=ns.a_pbf[:, i * 512:(i + 1) * 512],
                     in0=ns.a_rs1[:, 0:512], in1=ns.a_rs2[:, 0:512], op=ALU.add)
                b.op(v, "tensor_tensor", out=ns.a_rs1[:, 0:512], in0=xo, in1=ci, op=ALU.mult)
                b.op(v, "tensor_tensor", out=ns.a_rs2[:, 0:512], in0=xe, in1=si, op=ALU.mult,
                     inc="dv")
                ns.a_ps_tok = [b.tok("dv"), b.tok("dv")]
                b.op(v, "tensor_tensor", out=ns.a_pbf[:, (i + 4) * 512:(i + 5) * 512],
                     in0=ns.a_rs1[:, 0:512], in1=ns.a_rs2[:, 0:512], op=ALU.subtract, inc="dv")
                ns.a_cs_guard[csl] = b.val("dv")
                if i == 1:
                    cs_load(2, 0)
                    tk = cs_load(3, 1)
                    cs_next = [tk, tk]
            pbf_tok = b.tok("dv")
            b.wait_tok("tensor", pbf_tok)
            if P < 2:
                bia = (ns.bqi_p, ns.bki_p)[P]
                for fc in range(DC):
                    ps = ns.psY[:, (fc % 2) * 512:(fc % 2) * 512 + 512]
                    b.wait_tok("tensor", ns.a_psY_tok[fc % 2])
                    for dc in range(DC):
                        b.op("tensor", "matmul", ps,
                             ns.a_wi[:, dc * 1024 + fc * 128: dc * 1024 + fc * 128 + 128],
                             ns.a_pbf[:, dc * 512:(dc + 1) * 512],
                             start=(dc == 0), stop=(dc == DC - 1),
                             inc=("pe" if dc == DC - 1 else None))
                    peq = b.tok("pe")
                    b.wait_tok("scalar", peq)
                    b.wait("scalar", "m1", ns.a_stage_guard)
                    b.op("scalar", "activation", out=ns.a_stage[:, fc * 512:(fc + 1) * 512],
                         in_=ps, func=AF.Identity, bias=bia[:, fc:fc + 1], inc="ac")
                    ns.a_psY_tok[fc % 2] = b.tok("ac")
                st_tok = b.tok("ac")
                b.wait_tok("sync", st_tok)
                dst = d["qh_dram"] if P == 0 else d["kh_dram"]
                b.dma("sync", out=dst.rearrange("(fc p) t -> p fc t", p=128)[:, :, tw:tw + 512],
                      in_=ns.a_stage.rearrange("p (fc t) -> p fc t", fc=8), inc="m1")
                ns.a_stage_guard = b.val("m1")
            else:
                b.wait_tok("tensor", ns.bvi_tok)
                for nw_ in range(4):
                    for fh in range(2):
                        ps = ns.psY[:, fh * 512:(fh + 1) * 512]
                        b.wait_tok("tensor", ns.a_psY_tok[fh])
                        for dc in range(DC):
                            b.op("tensor", "matmul", ps,
                                 ns.a_pbf[:, dc * 512 + nw_ * 128: dc * 512 + nw_ * 128 + 128],
                                 ns.a_wi[:, dc * 1024 + fh * 512: dc * 1024 + (fh + 1) * 512],
                                 start=(dc == 0), stop=False)
                        b.op("tensor", "matmul", ps,
                             ns.ones_col[0:1, 0:128],
                             ns.a_bvi[0:1, fh * 512:(fh + 1) * 512],
                             start=False, stop=True, inc="pe")
                        pev = b.tok("pe")
                        b.wait_tok(v, pev)
                        b.op(v, "tensor_copy",
                             out=ns.a_vh[:, (tau * 4 + nw_) * 1024 + fh * 512:
                                         (tau * 4 + nw_) * 1024 + (fh + 1) * 512],
                             in_=ps, inc="dv")
                        ns.a_psY_tok[fh] = b.tok("dv")
            ns.a_xln_guard = b.val("pe")
            ns.a_pbf_guard = b.val("pe")
        ns.aw_guard = b.tok("pe")

    b.wait("sync", "m1")
    b.barrier()
    b.dma("gpsimd", out=ns.a_m1.rearrange("p (j k) -> p j k", j=4),
          in_=d["mask1"].rearrange("j p k -> p j k"), inc="wl")
    b.dma("gpsimd", out=ns.a_m2.rearrange("p (j k) -> p j k", j=4),
          in_=d["mask2"].rearrange("j p k -> p j k"), inc="wl")
    b.dma("gpsimd", out=ns.a_outw.rearrange("p (pr f) -> p pr f", pr=8),
          in_=d["outw_t"].rearrange("(pr p) f -> p pr f", p=128), inc="wl")
    b.dma("sync", out=ns.a_outb[0:1, 0:D], in_=d["outb_row"], inc="m1")
    amw_tok = b.tok("wl")
    outb_tok = b.tok("m1")
    b.wait_tok("tensor", outb_tok)
    b.wait_tok("tensor", amw_tok)

    def qk_load(h, hs, sem):
        b.wait_tok("gpsimd", ns.a_qk_guard[hs])
        b.dma("gpsimd", out=ns.a_qh[hs][0:64, :], in_=d["qh_dram"][h * 64:(h + 1) * 64, :], inc=sem)
        b.dma("gpsimd", out=ns.a_kh[hs][0:64, :], in_=d["kh_dram"][h * 64:(h + 1) * 64, :], inc=sem)
        return b.tok(sem)

    qk_load(0, 0, "qk")
    qk_pre = qk_load(1, 1, "qk")
    qk_next = [qk_pre, qk_pre]
    for h in range(H):
        hs = h % 2
        qk_tok = qk_next[hs]
        b.wait_tok("tensor", qk_tok)
        for qs in range(16):
            ktn = qs // 4 + 1
            for kt in range(ktn):
                ps = ns.ps4[:, kt * 512:(kt + 1) * 512]
                b.wait_tok("tensor", ns.a_p1_tok[kt])
                b.op("tensor", "matmul", ps,
                     ns.a_qh[hs][0:64, qs * 128:(qs + 1) * 128],
                     ns.a_kh[hs][0:64, kt * 512:(kt + 1) * 512],
                     start=True, stop=True, inc="pe")
            p1_tok = b.tok("pe")
            b.wait_tok(v, p1_tok)
            jdiag = qs % 4
            b.op(v, "tensor_tensor", out=ns.ps4[:, (ktn - 1) * 512:ktn * 512],
                 in0=ns.ps4[:, (ktn - 1) * 512:ktn * 512],
                 in1=ns.a_m1[:, jdiag * 512:(jdiag + 1) * 512], op=ALU.add, inc="dv")
            b.op(v, "reduce_max", out=ns.gm[:, 0:1], in_=ns.ps4[:, 0:ktn * 512], axis=AX.X,
                 negate=True)
            b.op(v, "tensor_scalar", out=ns.gm[:, 1:2], in0=ns.gm[:, 0:1],
                 scalar1=0.125, scalar2=None, op0=ALU.mult, inc="dv")
            b.wait("scalar", "dv", b.val("dv"))
            b.op("scalar", "activation", out=ns.a_junk[:, 0:ktn * 512], in_=ns.ps4[:, 0:ktn * 512],
                 func=AF.Exp, scale=0.125, bias=ns.gm[:, 1:2],
                 accum_out=ns.gm[:, 2:3], inc="ac")
            ps1_free = b.tok("ac")
            ns.a_p1_tok = [ps1_free] * 4
            b.wait("scalar", "ac", b.val("ac"))
            b.op("scalar", "activation", out=ns.gm[:, 3:4], in_=ns.gm[:, 2:3], func=AF.Ln, inc="ac")
            b.wait(v, "ac", b.val("ac"))
            b.op(v, "tensor_tensor", out=ns.a_zcol[:, qs:qs + 1], in0=ns.gm[:, 1:2],
                 in1=ns.gm[:, 3:4], op=ALU.subtract, inc="dv")
        zc_tok = b.tok("dv")
        b.wait_tok("tensor", zc_tok)
        b.wait("tensor", "dv", b.val("dv"))
        b.op("tensor", "transpose", ns.ps_misc2[0:16, 0:128], ns.a_zcol[:, 0:16],
             ns.ident_f[:], inc="pe")
        b.wait(v, "pe", b.val("pe"))
        b.op(v, "tensor_copy", out=ns.rowstage[0:16, 0:128], in_=ns.ps_misc2[0:16, 0:128], inc="dv")
        b.wait("sync", "dv", b.val("dv"))
        b.dma("sync", out=d["row_dram"][0:16, 0:128], in_=ns.rowstage[0:16, 0:128], inc="m1")
        b.wait("sync", "m1", b.val("m1"))
        b.wait("gpsimd", "m1", b.val("m1"))
        b.dma("gpsimd", out=ns.g_row_bf[0:1, 0:T],
              in_=d["row_dram"][0:16, 0:128].rearrange("a c -> (a c)")[None, :], inc="gx")
        zrow_tok = b.tok("gx")

        for tq in range(NT4):
            b.wait_tok("tensor", zrow_tok)
            b.wait("tensor", "dv", b.val("dv"))
            b.op("tensor", "matmul", ns.ps_misc[:, 0:512],
                 ns.ones_bf[0:1, 0:128], ns.g_row_bf[0:1, tq * 512:(tq + 1) * 512],
                 start=True, stop=True, inc="pe")
            b.wait(v, "pe", b.val("pe"))
            b.op(v, "tensor_copy", out=ns.a_zbc[:, 0:512], in_=ns.ps_misc[:, 0:512], inc="dv")
            zbc_tok = b.tok("dv")
            ktn2 = tq * 4 + 4
            for kt in range(ktn2):
                slot = kt % 8
                ps = ns.psY[:, (kt % 2) * 512:(kt % 2) * 512 + 512]
                b.wait_tok("tensor", ns.a_p2_tok[kt % 2])
                b.op("tensor", "matmul", ps,
                     ns.a_kh[hs][0:64, kt * 128:(kt + 1) * 128],
                     ns.a_qh[hs][0:64, tq * 512:(tq + 1) * 512],
                     start=True, stop=True, inc="pe")
                p2_tok = b.tok("pe")
                b.wait_tok(v, p2_tok)
                b.wait_tok(v, zbc_tok)
                b.wait_tok(v, ns.a_expT_guard[slot])
                jd = kt - (tq * 4)
                if jd >= 0:
                    b.op(v, "tensor_tensor", out=ps, in0=ps,
                         in1=ns.a_m2[:, jd * 512:(jd + 1) * 512], op=ALU.add)
                b.op(v, "scalar_tensor_tensor", out=ns.a_sc1[:, 0:512], in0=ps, scalar=0.125,
                     in1=ns.a_zbc[:, 0:512], op0=ALU.mult, op1=ALU.add, inc="dv")
                ns.a_p2_tok[kt % 2] = b.tok("dv")
                b.wait("scalar", "dv", b.val("dv"))
                b.op("scalar", "activation", out=ns.a_expT[:, slot * 512:(slot + 1) * 512],
                     in_=ns.a_sc1[:, 0:512], func=AF.Exp, inc="ac")
                exp_tok = b.tok("ac")
                b.wait_tok("tensor", exp_tok)
                b.op("tensor", "matmul", ns.ps_misc2[0:64, 0:512],
                     ns.a_vh[:, kt * 1024 + h * 64: kt * 1024 + h * 64 + 64],
                     ns.a_expT[:, slot * 512:(slot + 1) * 512],
                     start=(kt == 0), stop=(kt == ktn2 - 1),
                     inc=("pe" if kt == ktn2 - 1 else None))
                ns.a_expT_guard[slot] = b.tok("pe")
            av_tok = b.tok("pe")
            b.wait_tok(v, av_tok)
            b.op(v, "tensor_copy", out=ns.a_opair[64 * hs:64 * hs + 64, tq * 512:(tq + 1) * 512],
                 in_=ns.ps_misc2[0:64, 0:512], inc="dv")

        if hs == 1:
            opair_tok = b.tok("dv")
            pair = h // 2
            b.wait_tok("tensor", opair_tok)
            for dc in range(DC):
                for tq in range(NT4):
                    ps = ns.ps_misc[:, 0:512]
                    b.wait("tensor", "dv", ns.a_op_guard)
                    if pair == 0:
                        b.op("tensor", "matmul", ps,
                             ns.a_outw[:, pair * 1024 + dc * 128: pair * 1024 + dc * 128 + 128],
                             ns.a_opair[:, tq * 512:(tq + 1) * 512],
                             start=True, stop=False)
                        b.op("tensor", "matmul", ps,
                             ns.a_outb[0:1, dc * 128:(dc + 1) * 128],
                             ns.ones_col[0:1, 0:512],
                             start=False, stop=True, inc="pe")
                    else:
                        b.op("tensor", "matmul", ps,
                             ns.a_outw[:, pair * 1024 + dc * 128: pair * 1024 + dc * 128 + 128],
                             ns.a_opair[:, tq * 512:(tq + 1) * 512],
                             start=True, stop=True, inc="pe")
                    peo = b.tok("pe")
                    b.wait_tok(v, peo)
                    b.op(v, "tensor_tensor", out=Xc(ns, dc, tq * 512, 512),
                         in0=Xc(ns, dc, tq * 512, 512), in1=ps, op=ALU.add, inc="dv")
                    ns.a_op_guard = b.val("dv")
        ns.a_qk_guard[hs] = b.tok("pe")
        if h + 2 < H:
            qk_next[hs] = qk_load(h + 2, hs, "qka" if hs == 0 else "qkb")



def carve_l1attn(ns):
    f32, bf, al, pos = _mk_carver(ns)
    ns.n_ztile = bf(8 * 512)
    ns.n_w = bf(3 * 8 * 1024)
    ns.n_ktile = bf(4 * 1024)
    ns.n_vtile = bf(4 * 1024)
    ns.n_stage = bf(8 * 512)
    ns.n_vk = f32(32 * 32)
    ns.n_vkT = bf(64)
    ns.n_qh = [bf(2048, rows=32), bf(2048, rows=32)]
    ns.n_rbc = f32(512)
    u0 = al(0, align=64)
    c1 = _mk_carver(ns); c1[3][0] = u0
    ns.sq = c1[0](1024)
    ns.tmp1 = c1[0](512)
    ns.bc_m = c1[0](512)
    ns.bc_a = c1[0](512)
    end1 = c1[3][0]
    c2 = _mk_carver(ns); c2[3][0] = u0
    ns.n_ohead = c2[0](2048)
    ns.n_ostage = c2[1](2048)
    end2 = c2[3][0]
    pos[0] = max(end1, end2)
    ns.n_ochunk = [ns.n_stage[:, 0:2048], ns.n_stage[:, 2048:4096]]
    ns.n_den = ns.g_row_full
    ns.n_bp = ns.row_b
    ns.l1a_bytes = pos[0]


def emit_l1attn(b, ns):
    d = ns.dram
    v = "vector"
    NH = 32
    vk_sb = ns.n_vk

    b.dma("sync", out=ns.n_bp[0:1, 0:D], in_=d["l1_bp_row"], inc="xl")
    bp_tok = b.tok("xl")
    for sh in range(2):
        st0 = sh * HALF
        rows = [ns.row_m[0:1, st0:st0 + HALF], ns.row_a[0:1, st0:st0 + HALF]]
        emit_colstats(b, ns, 0, lambda dc: Xc(ns, dc, st0, HALF),
                      lambda bb: _kinds_ln2(bb, ns, nw=8), rows, "nln", nt=8)
    rt = b.tok("m1")

    b.wait_tok("gpsimd", ns.aw_guard)
    for i, wn in enumerate(("l1_wqT", "l1_wkT", "l1_wvT")):
        b.dma("gpsimd", out=ns.n_w.rearrange("p (i dc f) -> p (i dc) f", i=3, dc=8)
              [:, i * 8:(i + 1) * 8, :],
              in_=d[wn].rearrange("(dc p) f -> p dc f", p=128), inc="wl")
    wtok = b.tok("wl")
    b.op(v, "memset", vk_sb[:, 0:1024], 0.0, inc="dv")
    vk_zero = b.tok("dv")

    for tau in range(4):
        tw = tau * 512
        b.wait(v, "pe", ns.a_xln_guard)
        emit_bcast(b, ns, ns.row_m[0:1, tw:tw + 512], ns.bc_m, 512, rt, base=0)
        t_r = emit_bcast(b, ns, ns.row_a[0:1, tw:tw + 512], ns.bc_a, 512, rt, base=32)
        b.wait_tok(v, t_r)
        for dc in range(DC):
            b.op(v, "tensor_tensor", out=ns.tmp1[:, 0:512], in0=Xc(ns, dc, tw, 512),
                 in1=ns.bc_m[:, 0:512], op=ALU.subtract)
            b.op(v, "tensor_tensor", out=ns.n_ztile[:, dc * 512:(dc + 1) * 512],
                 in0=ns.tmp1[:, 0:512], in1=ns.bc_a[:, 0:512], op=ALU.mult, inc="dv")
        z_tok = b.tok("dv")
        b.wait_tok("tensor", wtok)
        b.wait_tok("tensor", z_tok)
        for fc in range(DC):
            ps = ns.psY[:, (fc % 2) * 512:(fc % 2) * 512 + 512]
            b.wait_tok("tensor", ns.a_psY_tok[fc % 2])
            for dc in range(DC):
                b.op("tensor", "matmul", ps,
                     ns.n_w[:, dc * 1024 + fc * 128: dc * 1024 + fc * 128 + 128],
                     ns.n_ztile[:, dc * 512:(dc + 1) * 512],
                     start=(dc == 0), stop=(dc == DC - 1),
                     inc=("pe" if dc == DC - 1 else None))
            peq = b.tok("pe")
            b.wait_tok("scalar", peq)
            b.wait("scalar", "m1", ns.a_stage_guard)
            b.op("scalar", "activation", out=ns.n_stage[:, fc * 512:(fc + 1) * 512],
                 in_=ps, func=AF.Relu, inc="ac")
            ns.a_psY_tok[fc % 2] = b.tok("ac")
        st_tok = b.tok("ac")
        b.wait_tok("sync", st_tok)
        b.dma("sync", out=d["q2_dram"].rearrange("(fc p) t -> p fc t", p=128)[:, :, tw:tw + 512],
              in_=ns.n_stage.rearrange("p (fc t) -> p fc t", fc=8), inc="m1")
        ns.a_stage_guard = b.val("m1")
        for P in (1, 2):
            dst = ns.n_ktile if P == 1 else ns.n_vtile
            for nw_ in range(4):
                for fh in range(2):
                    ps = ns.psY[:, fh * 512:(fh + 1) * 512]
                    b.wait_tok("tensor", ns.a_psY_tok[fh])
                    for dc in range(DC):
                        b.op("tensor", "matmul", ps,
                             ns.n_ztile[:, dc * 512 + nw_ * 128: dc * 512 + nw_ * 128 + 128],
                             ns.n_w[:, P * 8192 + dc * 1024 + fh * 512: P * 8192 + dc * 1024 + (fh + 1) * 512],
                             start=(dc == 0), stop=(dc == DC - 1),
                             inc=("pe" if dc == DC - 1 else None))
                    pet = b.tok("pe")
                    b.wait_tok("scalar", pet)
                    b.wait_tok("scalar", ns.n_kv_guard)
                    b.op("scalar", "activation", out=dst[:, nw_ * 1024 + fh * 512: nw_ * 1024 + (fh + 1) * 512],
                         in_=ps, func=(AF.Relu if P == 1 else AF.Identity), inc="ac")
                    ns.a_psY_tok[fh] = b.tok("ac")
        kv_tok = b.tok("ac")
        b.wait_tok("tensor", kv_tok)
        if tau == 0:
            b.wait_tok("tensor", vk_zero)
        for h in range(NH):
            par = h % 2
            vkps = ns.ps4[0:32, par * 512:par * 512 + 32]
            denps = ns.ps4[0:1, 1024 + par * 512:1024 + par * 512 + 32]
            b.wait_tok("tensor", ns.n_vkps_tok[par])
            for ns_ in range(4):
                b.op("tensor", "matmul", vkps,
                     ns.n_vtile[:, ns_ * 1024 + h * 32: ns_ * 1024 + h * 32 + 32],
                     ns.n_ktile[:, ns_ * 1024 + h * 32: ns_ * 1024 + h * 32 + 32],
                     start=(ns_ == 0), stop=(ns_ == 3))
                b.op("tensor", "matmul", denps,
                     ns.ones_bf[:, 0:1],
                     ns.n_ktile[:, ns_ * 1024 + h * 32: ns_ * 1024 + h * 32 + 32],
                     start=(ns_ == 0), stop=(ns_ == 3),
                     inc=("pe" if ns_ == 3 else None))
            pevk = b.tok("pe")
            b.wait_tok(v, pevk)
            b.op(v, "tensor_tensor", out=vk_sb[0:32, h * 32:(h + 1) * 32],
                 in0=vk_sb[0:32, h * 32:(h + 1) * 32], in1=vkps,
                 op=ALU.add)
            b.op(v, "tensor_tensor", out=vk_sb[32:33, h * 32:(h + 1) * 32],
                 in0=vk_sb[32:33, h * 32:(h + 1) * 32], in1=denps,
                 op=ALU.add, inc="dv")
            ns.n_vkps_tok[par] = b.tok("dv")
        ns.n_vk_tok = b.tok("dv")
        ns.a_xln_guard = b.val("pe")
        ns.n_kv_guard = ("pe", b.val("pe"))

    b.wait("sync", "m1")
    b.barrier()

    def qh_load(h, hs, sem):
        b.wait_tok("gpsimd", ns.a_qk_guard[hs])
        b.dma("gpsimd", out=ns.n_qh[hs][0:32, :], in_=d["q2_dram"][h * 32:(h + 1) * 32, :], inc=sem)
        return b.tok(sem)

    qh_load(0, 0, "qk")
    qh_pre = qh_load(1, 1, "qk")
    qh_next = [qh_pre, qh_pre]
    for h in range(NH):
        hs = h % 2
        qh_tok = qh_next[hs]
        b.wait_tok("tensor", ns.n_vk_tok)
        b.wait("tensor", "dv", b.val("dv"))
        b.op("tensor", "transpose", ns.ps_misc2[0:32, 0:33], vk_sb[0:33, h * 32:(h + 1) * 32],
             ns.ident_f[0:33, 0:33], inc="pe")
        b.wait(v, "pe", b.val("pe"))
        b.op(v, "tensor_copy", out=ns.n_vkT[0:32, 0:33], in_=ns.ps_misc2[0:32, 0:33], inc="dv")
        vkt_tok = b.tok("dv")
        b.wait_tok("tensor", qh_tok)
        b.wait_tok("tensor", vkt_tok)
        for tau in range(4):
            ps33 = ns.psY[0:33, (tau % 2) * 512:(tau % 2) * 512 + 512]
            b.wait_tok("tensor", ns.a_psY_tok[tau % 2])
            b.op("tensor", "matmul", ps33,
                 ns.n_vkT[0:32, 0:33],
                 ns.n_qh[hs][0:32, tau * 512:(tau + 1) * 512],
                 start=True, stop=True, inc="pe")
            peo = b.tok("pe")
            b.wait_tok(v, peo)
            b.wait(v, "m1", ns.n_oh_guard)
            b.op(v, "tensor_copy", out=ns.n_ohead[0:33, tau * 512:(tau + 1) * 512],
                 in_=ps33, inc="dv")
            ns.a_psY_tok[tau % 2] = b.tok("dv")
        oh_tok = b.tok("dv")
        b.wait_tok(v, oh_tok)
        b.op(v, "tensor_scalar", out=ns.n_den[0:1, 0:T], in0=ns.n_ohead[32:33, 0:T],
             scalar1=1e-15, scalar2=None, op0=ALU.add, inc="dv")
        b.wait("scalar", "dv", b.val("dv"))
        b.op("scalar", "activation", out=ns.n_den[0:1, 0:T], in_=ns.n_den[0:1, 0:T],
             func=AF.Ln)
        b.op("scalar", "activation", out=ns.n_den[0:1, 0:T], in_=ns.n_den[0:1, 0:T],
             func=AF.Exp, scale=-1.0, inc="ac")
        r_tok = b.tok("ac")
        for tau in range(4):
            b.wait_tok("tensor", r_tok)
            b.wait("tensor", "dv", b.val("dv"))
            b.op("tensor", "matmul", ns.ps_misc[:, 0:512],
                 ns.ones_col[0:1, 0:128], ns.n_den[0:1, tau * 512:(tau + 1) * 512],
                 start=True, stop=True, inc="pe")
            b.wait(v, "pe", b.val("pe"))
            b.op(v, "tensor_copy", out=ns.n_rbc[:, 0:512], in_=ns.ps_misc[:, 0:512], inc="dv")
            b.drain(v)
            b.op(v, "tensor_tensor", out=ns.n_ostage[0:32, tau * 512:(tau + 1) * 512],
                 in0=ns.n_ohead[0:32, tau * 512:(tau + 1) * 512],
                 in1=ns.n_rbc[0:32, 0:512], op=ALU.mult, inc="dv")
        ost_tok = b.tok("dv")
        b.wait_tok("sync", ost_tok)
        b.dma("sync", out=d["o2_dram"][h * 32:(h + 1) * 32, :], in_=ns.n_ostage[0:32, 0:T], inc="m1")
        ns.n_oh_guard = b.val("m1")
        ns.a_qk_guard[hs] = b.tok("pe")
        if h + 2 < NH:
            qh_next[hs] = qh_load(h + 2, hs, "qka" if hs == 0 else "qkb")

    b.wait("sync", "m1")
    b.barrier()
    b.dma("gpsimd", out=ns.n_w.rearrange("p (i dc f) -> p (i dc) f", i=3, dc=8)[:, 0:8, :],
          in_=d["l1_wpT"].rearrange("(dc p) f -> p dc f", p=128), inc="wl")
    wp_tok = b.tok("wl")
    b.wait_tok("tensor", bp_tok)
    oc_next = [None, None]
    for oc in range(DC):
        slot = oc % 2
        b.wait_tok("gpsimd", ns.n_oc_guard[slot])
        b.dma("gpsimd", out=ns.n_ochunk[slot],
              in_=d["o2_dram"][oc * 128:(oc + 1) * 128, :],
              inc=("qka" if slot == 0 else "qkb"))
        octok = b.tok("qka" if slot == 0 else "qkb")
        b.wait_tok("tensor", wp_tok)
        b.wait_tok("tensor", octok)
        for dc in range(DC):
            for tau in range(4):
                ps = ns.ps_misc[:, 0:512]
                b.wait("tensor", "dv", ns.a_op_guard)
                wsl = ns.n_w[:, oc * 1024 + dc * 128: oc * 1024 + dc * 128 + 128]
                if oc == 0:
                    b.op("tensor", "matmul", ps, wsl,
                         ns.n_ochunk[slot][:, tau * 512:(tau + 1) * 512],
                         start=True, stop=False)
                    b.op("tensor", "matmul", ps,
                         ns.n_bp[0:1, dc * 128:(dc + 1) * 128],
                         ns.ones_col[64:65, 0:512],
                         start=False, stop=True, inc="pe")
                else:
                    b.op("tensor", "matmul", ps, wsl,
                         ns.n_ochunk[slot][:, tau * 512:(tau + 1) * 512],
                         start=True, stop=True, inc="pe")
                peo = b.tok("pe")
                b.wait_tok(v, peo)
                b.op(v, "tensor_tensor", out=Xc(ns, dc, tau * 512, 512),
                     in0=Xc(ns, dc, tau * 512, 512), in1=ps, op=ALU.add, inc="dv")
                ns.a_op_guard = b.val("dv")
        ns.n_oc_guard[slot] = b.tok("pe")



def _kinds_ln2(b, ns, nw=8):
    v = "vector"; s = "scalar"
    S = ns.scol
    P = ns.ps_misc
    b.op(v, "tensor_scalar", out=S[:, 0:nw], in0=P[:, 0:2 * nw].rearrange("p (w k) -> p k w", k=2)[:, 0, :],
         scalar1=1.0 / D, scalar2=None, op0=ALU.mult)
    b.op(v, "tensor_scalar", out=S[:, 3 * nw:4 * nw], in0=P[:, 0:2 * nw].rearrange("p (w k) -> p k w", k=2)[:, 1, :],
         scalar1=1.0 / D, scalar2=None, op0=ALU.mult)
    b.op(v, "tensor_tensor", out=S[:, 4 * nw:5 * nw], in0=S[:, 0:nw], in1=S[:, 0:nw], op=ALU.mult)
    b.op(v, "tensor_tensor", out=S[:, 5 * nw:6 * nw], in0=S[:, 3 * nw:4 * nw], in1=S[:, 4 * nw:5 * nw], op=ALU.subtract)
    b.op(v, "tensor_scalar", out=S[:, 6 * nw:7 * nw], in0=S[:, 5 * nw:6 * nw],
         scalar1=1e-5, scalar2=None, op0=ALU.add, inc="dv")
    b.wait(s, "dv", b.val("dv"))
    b.op(s, "activation", out=S[:, 7 * nw:8 * nw], in_=S[:, 6 * nw:7 * nw], func=AF.Ln)
    b.op(s, "activation", out=S[:, nw:2 * nw], in_=S[:, 7 * nw:8 * nw], func=AF.Exp, scale=-0.5, inc="ac")
    b.wait(v, "ac", b.val("ac"))
    b.op(v, "tensor_copy", out=S[:, nw:2 * nw], in_=S[:, nw:2 * nw], inc="dv")


def emit_l1ffn(b, ns):
    for half in range(2):
        _l1ffn_half(b, ns, half, half * HALF)
        b.barrier()


def _l1_stats_bcast(b, ns, src, tag):
    rows = [ns.row_m, ns.row_a]
    rt = emit_colstats(b, ns, 0, src, lambda bb: _kinds_ln2(bb, ns), rows, tag)
    t_m = emit_bcast(b, ns, ns.row_m, ns.bc_m, HALF, rt, base=0)
    t_a = emit_bcast(b, ns, ns.row_a, ns.bc_a, HALF, rt, base=32)
    return t_a


def _l1_gate(b, ns, half, t0):
    nw = 8
    b.wait_tok("tensor", ns.wg_tok)
    bgt = emit_bcast(b, ns, ns.l1_gb_row, ns.bc_g16, 8)
    ps_free = bgt
    v = "vector"
    for w in range(nw):
        tw = t0 + 128 * w
        b.wait_tok("tensor", ps_free)
        for dc in range(DC):
            b.op("tensor", "matmul", ns.ps_misc[:, 0:8],
                 ns.l1_in_bf[:, dc * HALF + 128 * w: dc * HALF + 128 * w + 128],
                 ns.l1_gw_bf[:, dc * 8:(dc + 1) * 8],
                 start=(dc == 0), stop=(dc == DC - 1), inc=("pe" if dc == DC - 1 else None))
        b.wait(v, "pe", b.val("pe"))
        L = ns.gl[:, 0:8]
        G = ns.gex
        b.op(v, "tensor_tensor", out=L, in0=ns.ps_misc[:, 0:8], in1=ns.bc_g16[:, 0:8], op=ALU.add,
             inc="dv")
        ps_free = b.tok("dv")
        b.op(v, "reduce_max", out=ns.gm[:, 0:1], in_=L, axis=AX.X, negate=True)
        b.wait("scalar", "dv", b.val("dv"))
        b.op(v, "tensor_copy", out=ns.gm[:, 1:2], in_=ns.gm[:, 0:1], inc="dv")
        b.wait("scalar", "dv", b.val("dv"))
        b.op("scalar", "activation", out=ns.gcur[:, 0:8], in_=L, func=AF.Exp, bias=ns.gm[:, 1:2], inc="ac")
        b.wait(v, "ac", b.val("ac"))
        b.op(v, "reduce_sum", out=ns.gm[:, 2:3], in_=ns.gcur[:, 0:8], axis=AX.X)
        b.op(v, "reciprocal", out=ns.gm[:, 3:4], in_=ns.gm[:, 2:3])
        b.op(v, "tensor_scalar", out=ns.gl[:, 8:16], in0=ns.gcur[:, 0:8],
             scalar1=ns.gm[:, 3:4], scalar2=None, op0=ALU.mult)
        W_ = ns.gl[:, 8:16]
        b.op(v, "reduce_max", out=ns.gm[:, 0:1], in_=W_, axis=AX.X, negate=True)
        b.op(v, "tensor_scalar", out=G[:, 0:8], in0=W_, scalar1=ns.gm[:, 0:1],
             scalar2=0.0, op0=ALU.add, op1=ALU.is_ge)
        b.op(v, "tensor_scalar", out=G[:, 8:9], in0=ns.gm[:, 0:1],
             scalar1=-1.0, scalar2=None, op0=ALU.mult)
        b.op(v, "scalar_tensor_tensor", out=ns.gcur[:, 0:8], in0=G[:, 0:8], scalar=-1e30,
             in1=W_, op0=ALU.mult, op1=ALU.add)
        b.op(v, "reduce_max", out=ns.gm[:, 1:2], in_=ns.gcur[:, 0:8], axis=AX.X, negate=True)
        b.op(v, "tensor_scalar", out=ns.gmask[:, 0:8], in0=ns.gcur[:, 0:8], scalar1=ns.gm[:, 1:2],
             scalar2=0.0, op0=ALU.add, op1=ALU.is_ge)
        b.op(v, "tensor_scalar", out=ns.gm[:, 2:3], in0=ns.gm[:, 1:2],
             scalar1=-1.0, scalar2=None, op0=ALU.mult)
        b.op(v, "tensor_scalar", out=G[:, 32:40], in0=ns.gmask[:, 0:8],
             scalar1=ns.gm[:, 2:3], scalar2=None, op0=ALU.mult, inc="dv")
        b.wait("tensor", "dv", b.val("dv"))
        b.op("tensor", "transpose", ns.ps_misc2[0:40, 0:128], G[:, 0:40], ns.ident_f[:], inc="pe")
        b.wait(v, "pe", b.val("pe"))
        lw = tw - t0
        b.op(v, "tensor_copy", out=ns.l1_oh[0:9, lw:lw + 128], in_=ns.ps_misc2[0:9, 0:128])
        b.op(v, "tensor_copy", out=ns.l1_oh[32:40, lw:lw + 128], in_=ns.ps_misc2[32:40, 0:128])
        b.op(v, "tensor_copy", out=ns.l1_ohb[0:8, lw:lw + 128], in_=ns.ps_misc2[0:8, 0:128])
        b.op(v, "tensor_copy", out=ns.l1_ohb[32:40, lw:lw + 128], in_=ns.ps_misc2[32:40, 0:128], inc="dv")
    return b.tok("dv")


def _l1_iter(b, ns, half, t0, it, gtok):
    d = ns.dram
    E = 8
    rbase = 0 if it == 0 else 32
    def w2_load(e, hc, slot, sem="wl"):
        b.wait_tok("gpsimd", ns.w2_slot_tok[slot])
        b.dma("gpsimd", out=ns.l1_w2[slot], in_=d["l1_w2p"][e, hc], inc=sem)
        return b.tok(sem)

    for e in range(E):
        b.wait_tok("gpsimd", ns.w1_slot_tok[0])
        w1v = ns.l1_w1full.rearrange("p (dc f) -> p dc f", dc=8)
        for q in range(4):
            b.dma("gpsimd",
                  out=w1v[:, :, q * 512:(q + 1) * 512],
                  in_=d["l1_w1"][e, :, q * 512:(q + 1) * 512]
                  .rearrange("(dc p) f -> p dc f", p=128), inc="wl")
        w2_load(e, 0, 0)
        w2_load(e, 1, 1)
        wtok = b.tok("wl")
        ns.w2_pre_tok = wtok

        b.wait_tok("gpsimd", gtok)
        b.wait("gpsimd", "dv", ns.gbc_guard)
        b.dma("gpsimd", out=ns.g_row_bf[0:1, 0:HALF],
              in_=ns.l1_oh[rbase + e:rbase + e + 1, 0:HALF], inc="gx")
        b.wait("tensor", "gx", b.val("gx"))
        mb_tok = emit_bcast(b, ns, ns.g_row_bf, ns.l1_ohbc, HALF, bf=True)
        ns.gbc_guard = b.val("dv")

        b.wait_tok("tensor", wtok)
        if e == 0:
            b.wait_tok("tensor", ns.l1_in_tok)
        hview = ns.l1_h.rearrange("p (fc t) -> p fc t", fc=16)
        for tau in range(2):
            for fc in range(16):
                psH = ns.psA[:, (fc % 2) * 512:(fc % 2) * 512 + 512]
                b.wait_tok("tensor", ns.psA_tok[fc % 2])
                for dc in range(DC):
                    b.op("tensor", "matmul", psH,
                         ns.l1_w1full[:, dc * 2048 + fc * 128: dc * 2048 + fc * 128 + 128],
                         ns.l1_in_bf[:, dc * HALF + tau * 512: dc * HALF + (tau + 1) * 512],
                         start=(dc == 0), stop=(dc == DC - 1),
                         inc=("pe" if dc == DC - 1 else None))
                peH = b.tok("pe")
                v = "vector"
                b.wait_tok(v, peH)
                b.wait_tok(v, ns.l1_h_guard)
                b.op(v, "tensor_scalar", out=ns.l1_hs[:, 0:512],
                     in0=psH, scalar1=ns.l1_b1[:, e * 16 + fc:e * 16 + fc + 1],
                     scalar2=0.0, op0=ALU.add, op1=ALU.max, inc="dv")
                ns.psA_tok[fc % 2] = b.tok("dv")
                b.wait_tok(v, mb_tok)
                b.op(v, "tensor_tensor",
                     out=hview[:, fc, 0:512],
                     in0=ns.l1_hs[:, 0:512], in1=ns.l1_ohbc[:, tau * 512:(tau + 1) * 512],
                     op=ALU.mult, inc="dv")
            h_tok = b.tok("dv")
            if tau == 1:
                ns.l1_w1_free = b.tok("pe")

            for hc in range(DC):
                gidx = tau * DC + hc
                w2slot = gidx % 2
                w2tok = ns.w2_pre_tok if gidx < 2 else ns.w2_cur_tok[w2slot]
                psY = ns.psY[:, (hc % 2) * 512:(hc % 2) * 512 + 512]
                b.wait_tok("tensor", ns.psY_tok[hc % 2])
                b.wait_tok("tensor", h_tok)
                b.wait_tok("tensor", w2tok)
                for ic in range(16):
                    last = (ic == 15 and e != 0)
                    b.op("tensor", "matmul", psY,
                         ns.l1_w2[w2slot][:, ic * 128: ic * 128 + 128],
                         hview[:, ic, 0:512],
                         start=(ic == 0), stop=last, inc=("pe" if last else None))
                if e == 0:
                    b.op("tensor", "matmul", psY,
                         ns.l1_b2t[rbase:rbase + 8, hc * 128:(hc + 1) * 128],
                         ns.l1_ohb[rbase:rbase + 8, tau * 512: (tau + 1) * 512],
                         start=False, stop=True, inc="pe")
                peY = b.tok("pe")
                ns.w2_slot_tok[w2slot] = peY
                if gidx + 2 < 16:
                    g2i = gidx + 2
                    ns.w2_cur_tok[w2slot] = w2_load(e, g2i % DC, w2slot,
                                                    sem=("w2a" if w2slot == 0 else "w2b"))
                b.wait_tok("vector", peY)
                if it == 0 and e == 0:
                    b.op("vector", "tensor_copy",
                         out=ns.l1_acc[:, hc * HALF + tau * 512: hc * HALF + (tau + 1) * 512],
                         in_=psY, inc="dv")
                else:
                    b.op("vector", "tensor_tensor",
                         out=ns.l1_acc[:, hc * HALF + tau * 512: hc * HALF + (tau + 1) * 512],
                         in0=ns.l1_acc[:, hc * HALF + tau * 512: hc * HALF + (tau + 1) * 512],
                         in1=psY, op=ALU.add, inc="dv")
                ns.psY_tok[hc % 2] = b.tok("dv")
            ns.l1_h_guard = b.tok("pe")
            if tau == 1:
                ns.w1_slot_tok = [ns.l1_w1_free, ns.l1_w1_free]


def _l1ffn_half(b, ns, half, t0):
    v = "vector"
    ns.l1_h_guard = None
    t_a = _l1_stats_bcast(b, ns, lambda dc: Xc(ns, dc, t0, HALF), "l1ln")
    b.wait_tok(v, t_a)
    for dc in range(DC):
        X = Xc(ns, dc, t0, HALF)
        b.op(v, "tensor_tensor", out=ns.tmp1[:, 0:HALF], in0=X, in1=ns.bc_m[:, 0:HALF], op=ALU.subtract)
        b.op(v, "tensor_tensor", out=ns.l1_in_bf[:, dc * HALF:(dc + 1) * HALF],
             in0=ns.tmp1[:, 0:HALF], in1=ns.bc_a[:, 0:HALF], op=ALU.mult, inc="dv")
    ns.l1_in_tok = b.tok("dv")

    b.wait_tok("tensor", ns.l1_in_tok)
    b.wait("tensor", "dv", b.val("dv"))
    gtok = _l1_gate(b, ns, half, t0)

    _l1_iter(b, ns, half, t0, 0, gtok)

    b.wait(v, "pe", b.val("pe"))
    t_m2 = emit_bcast(b, ns, ns.row_m, ns.bc_m, HALF, None, base=0)
    t_a2 = emit_bcast(b, ns, ns.row_a, ns.bc_a, HALF, None, base=32)
    b.wait("gpsimd", "dv", ns.gbc_guard)
    b.dma("gpsimd", out=ns.g_row_bf[0:1, 0:HALF], in_=ns.l1_oh[8:9, 0:HALF], inc="gx")
    b.wait("tensor", "gx", b.val("gx"))
    v1_tok = emit_bcast(b, ns, ns.g_row_bf, ns.l1_v1bc, HALF, bf=True)
    ns.gbc_guard = b.val("dv")
    b.wait_tok(v, t_a2)
    b.wait_tok(v, v1_tok)
    for dc in range(DC):
        X = Xc(ns, dc, t0, HALF)
        ACC = ns.l1_acc[:, dc * HALF:(dc + 1) * HALF]
        b.op(v, "tensor_tensor", out=ns.tmp1[:, 0:HALF], in0=X, in1=ns.bc_m[:, 0:HALF], op=ALU.subtract)
        b.op(v, "tensor_tensor", out=ns.tmp2[:, 0:HALF], in0=ns.tmp1[:, 0:HALF], in1=ns.bc_a[:, 0:HALF],
             op=ALU.mult)
        b.op(v, "scalar_tensor_tensor", out=ns.l1_in_bf[:, dc * HALF:(dc + 1) * HALF],
             in0=ACC, scalar=0.5, in1=ns.tmp2[:, 0:HALF], op0=ALU.mult, op1=ALU.add)
        b.op(v, "tensor_tensor", out=ACC, in0=ACC, in1=ns.l1_v1bc[:, 0:HALF], op=ALU.mult)
        b.op(v, "tensor_tensor", out=ACC, in0=ACC, in1=ns.tmp2[:, 0:HALF], op=ALU.add, inc="dv")
    ns.l1_in_tok = b.tok("dv")
    ns.l1_h_guard = None

    _l1_iter(b, ns, half, t0, 1, gtok)

    b.wait(v, "pe", b.val("pe"))
    b.wait("scalar", "pe", b.val("pe"))
    b.wait("scalar", "dv", b.val("dv"))
    b.wait("tensor", "dv", b.val("dv"))
    t_a3 = _l1_stats_bcast(b, ns, lambda dc: ns.l1_acc[:, dc * HALF:(dc + 1) * HALF], "l1ln2")
    b.wait_tok(v, t_a3)
    for dc in range(DC):
        X = Xc(ns, dc, t0, HALF)
        ACC = ns.l1_acc[:, dc * HALF:(dc + 1) * HALF]
        b.op(v, "tensor_tensor", out=ns.tmp1[:, 0:HALF], in0=ACC, in1=ns.bc_m[:, 0:HALF], op=ALU.subtract)
        b.op(v, "tensor_tensor", out=ns.tmp2[:, 0:HALF], in0=ns.tmp1[:, 0:HALF], in1=ns.bc_a[:, 0:HALF],
             op=ALU.mult, inc="dv")
        b.wait("scalar", "dv", b.val("dv"))
        b.op("scalar", "activation", out=ns.tmp1[:, 0:HALF], in_=ns.tmp2[:, 0:HALF], func=AF.Identity,
             scale=ns.l1_gam05[:, dc:dc + 1], bias=ns.l1_bet05[:, dc:dc + 1], inc="ac")
        b.wait(v, "ac", b.val("ac"))
        b.op(v, "tensor_tensor", out=X, in0=X, in1=ns.tmp1[:, 0:HALF], op=ALU.add, inc="dv")


ARENA_WORDS = 29952


def _mk_carver(ns):
    pos = [0]

    def al(nbytes, align=64):
        p = (pos[0] + align - 1) // align * align
        pos[0] = p + nbytes
        assert pos[0] <= ARENA_WORDS * 4, f"arena overflow {pos[0]}"
        return p

    def f32(n, rows=128):
        p = al(n * 4)
        return ns.AR[0:rows, p // 4: p // 4 + n]

    def bf(n, rows=128):
        p = al(n * 2)
        return ns.AR_B[0:rows, p // 2: p // 2 + n]

    return f32, bf, al, pos


def carve_l0ffn(ns):
    f32, bf, al, pos = _mk_carver(ns)
    ns.xn_bf = bf(DC * HALF)
    ns.sg_bf = bf(DC * HALF)
    ns.w1a_sb = [bf(8 * 512), bf(8 * 512)]
    ns.w1b_sb = [bf(8 * 512), bf(8 * 512)]
    ns.w2_sb = [bf(8 * 128), bf(8 * 128)]
    ns.gluA = f32(512)
    ns.gluS = f32(512)
    ns.gluB = f32(512)
    ns.g_bc = bf(HALF)
    ns.bc_b = f32(HALF)
    ns.gp_T = f32(T)
    ns.gp_Tb = bf(T)
    ns.bc_m = f32(HALF)
    ns.bc_a = f32(HALF)
    ns.tmp1 = f32(HALF)
    ns.tmp2 = f32(HALF)
    ns.sq = ns.tmp2
    ns.l0_bytes = pos[0]


def carve_l1ffn(ns):
    f32, bf, al, pos = _mk_carver(ns)
    ns.l1_in_bf = bf(DC * HALF)
    ns.l1_acc = f32(DC * HALF)
    ns.l1_w1full = bf(8 * 2048)
    ns.l1_w2 = [bf(16 * 128), bf(16 * 128)]
    ns.l1_hs = f32(512)
    ns.l1_oh = f32(HALF)
    ns.l1_ohb = bf(HALF)
    ns.l1_ohbc = bf(HALF)
    ns.l1_v1bc = bf(HALF)
    p = al(16384)
    ns.bc_m = ns.AR[:, p // 4: p // 4 + HALF]
    ns.bc_a = ns.AR[:, p // 4 + HALF: p // 4 + 2 * HALF]
    ns.tmp1 = ns.AR[:, p // 4 + 2 * HALF: p // 4 + 3 * HALF]
    ns.tmp2 = ns.AR[:, p // 4 + 3 * HALF: p // 4 + 4 * HALF]
    ns.sq = ns.tmp2
    ns.l1_h = ns.AR_B[:, p // 2: p // 2 + 16 * 512]
    ns.l1_bytes = pos[0]


def build_program(g, sublayers):
    nc = bass.Bass("TRN2", target_bir_lowering=False, debug=False)
    ns = make_ns(nc, g, sublayers)
    d = ns.dram

    with ExitStack() as ctx:
        sb = lambda name, shape, dt: ctx.enter_context(nc.sbuf_tensor("s_" + name, shape, dt))
        psum = lambda name, shape: ctx.enter_context(nc.psum_tensor(name, shape, F32))

        ns.X = sb("X", [128, DC * T], F32)
        ns.ident_f = sb("ident_f", [128, 128], F32)
        ns.ones_col = sb("ones_col", [128, 512], F32)
        ns.ones_bf = sb("ones_bf", [128, 128], BF16)
        arena = sb("arena", [128, ARENA_WORDS], F32)
        ns.AR = arena[:, :]
        ns.AR_B = arena.bitcast(BF16)[:, :]

        rows1 = sb("rows1", [65, T], F32)
        rows2 = sb("rows2", [1, T], F32)
        ns.row_m = rows1[0:1, :]
        ns.row_a = rows1[32:33, :]
        ns.row_b = rows1[64:65, :]
        ns.g_row = rows2[0:1, :]
        ns.g_row_full = rows2[0:1, :]
        ns.g_row_bf = rows2.bitcast(BF16)[0:1, :]
        ns.bc_g16 = sb("bc_g16", [128, 16], F32)
        ns.scol = sb("scol", [128, 128], F32)
        ns.rowstage = sb("rowstage", [128, 128], F32)

        ns.gl = sb("gl", [128, 16], F32)
        ns.gcur = sb("gcur", [128, 16], F32)
        ns.gmask = sb("gmask", [128, 16], F32)
        ns.gm = sb("gm", [128, 4], F32)
        ns.gex = sb("gex", [128, 48], F32)

        ns.wg_bf = sb("wg_bf", [128, DC * 16], BF16)
        ns.bg_row = sb("bg_row", [1, 16], F32)[0:1, :]
        ns.l0_ns = sb("l0_ns", [128, DC], F32)
        ns.l0_b1a = sb("l0_b1a", [128, 16 * 8], F32)
        ns.l0_b1b = sb("l0_b1b", [128, 16 * 8], F32)
        ns.b2_bf = sb("b2_bf", [16, 1024], BF16)

        ns.bqi_p = sb("bqi_p", [128, DC], F32)
        ns.bki_p = sb("bki_p", [128, DC], F32)

        ns.l1_gw_bf = sb("l1_gw_bf", [128, DC * 8], BF16)
        ns.l1_gb_row = sb("l1_gb_row", [1, 8], F32)[0:1, :]
        ns.l1_b1 = sb("l1_b1", [128, 8 * 16], F32)
        ns.l1_b2t = sb("l1_b2t", [40, 1024], BF16)
        ns.l1_gam = sb("l1_gam", [128, DC], F32)
        ns.l1_bet = sb("l1_bet", [128, DC], F32)
        ns.l1_gam05 = sb("l1_gam05", [128, DC], F32)
        ns.l1_bet05 = sb("l1_bet05", [128, DC], F32)

        ns.ps4 = psum("ps4", [128, 2048])
        ns.psA = ns.ps4[:, 0:1024]
        ns.psB = ns.ps4[:, 1024:2048]
        ns.psY = psum("psY", [128, 1024])
        ns.ps_misc = psum("ps_misc", [128, 512])
        ns.ps_misc2 = psum("ps_misc2", [128, 512])

        ns.sems = {}
        for s in ("xl", "wl", "w2a", "w2b", "qk", "qka", "qkb", "gx", "m1", "m2", "pe", "dv", "ac", "gp", "bar", "st"):
            ns.sems[s] = ctx.enter_context(nc.semaphore(s))

        block = ctx.enter_context(nc.Block())
        b = Builder(ns)

        def program(bb):
            ns.w1_slot_idx = 0
            ns.w2_slot_idx = 0
            ns.w1_slot_tok = [None, None]
            ns.w2_slot_tok = [None, None]
            ns.w1_load_tok = {}
            ns.w2_load_tok = {}
            ns.psA_tok = [None, None]
            ns.psB_tok = [None, None]
            ns.psY_tok = [None, None]
            ns.gbc_guard = 0
            ns.sg_guard = None
            ns.glu_gp_guard = 0
            ns.w2_cur_tok = [None, None]
            ns.w2_pre_tok = None
            ns.aw_guard = None
            ns.a_xln_guard = 0
            ns.a_pbf_guard = 0
            ns.a_cs_guard = [0, 0]
            ns.a_ps_tok = [None, None]
            ns.a_psY_tok = [None, None]
            ns.a_stage_guard = 0
            ns.a_p1_tok = [None] * 4
            ns.a_p2_tok = [None, None]
            ns.a_expT_guard = [None] * 8
            ns.a_qk_guard = [None, None]
            ns.a_op_guard = 0
            ns.bvi_tok = None
            ns.n_kv_guard = None
            ns.n_vk_tok = None
            ns.n_oh_guard = 0
            ns.n_oc_guard = [None, None]
            ns.n_vkps_tok = [None, None]
            ns.l1_w1_free = None
            for e in Builder.AUTODRAIN:
                setattr(bb, "_dirty_" + e, False)

            emit_load_consts(bb, ns)
            emit_load_x(bb, ns)
            need_l0 = "l0ffn" in sublayers
            need_l1 = "l1ffn" in sublayers
            if need_l0:
                bb.dma("gpsimd", out=ns.wg_bf.rearrange("p (dc e) -> p dc e", dc=DC),
                       in_=d["l0_wg"].rearrange("(dc p) e -> p dc e", p=128), inc="wl")
                bb.dma("sync", out=ns.bg_row, in_=d["l0_bg_row"], inc="xl")
                bb.dma("sync", out=ns.l0_ns[:], in_=d["l0_ns_p"], inc="xl")
                bb.dma("sync", out=ns.l0_b1a[:], in_=d["l0_b1a_p"], inc="xl")
                bb.dma("sync", out=ns.l0_b1b[:], in_=d["l0_b1b_p"], inc="xl")
                bb.dma("gpsimd", out=ns.b2_bf[:], in_=d["l0_b2"], inc="wl")
            if "l0attn" in sublayers:
                bb.dma("sync", out=ns.bqi_p[:], in_=d["bqi_p"], inc="xl")
                bb.dma("sync", out=ns.bki_p[:], in_=d["bki_p"], inc="xl")
            if need_l1:
                bb.dma("gpsimd", out=ns.l1_gw_bf.rearrange("p (dc e) -> p dc e", dc=DC),
                       in_=d["l1_gw"].rearrange("(dc p) e -> p dc e", p=128), inc="wl")
                bb.dma("sync", out=ns.l1_gb_row, in_=d["l1_gb_row"], inc="xl")
                bb.dma("sync", out=ns.l1_b1[:], in_=d["l1_b1_p"], inc="xl")
                bb.dma("gpsimd", out=ns.l1_b2t[0:8, :], in_=d["l1_b2"], inc="wl")
                bb.dma("gpsimd", out=ns.l1_b2t[32:40, :], in_=d["l1_b2"], inc="wl")
                bb.dma("sync", out=ns.l1_gam[:], in_=d["l1_gam_p"], inc="xl")
                bb.dma("sync", out=ns.l1_bet[:], in_=d["l1_bet_p"], inc="xl")
            ns.wg_tok = ("wl", bb.val("wl"))
            bb.wait("vector", "xl")
            bb.wait("vector", "wl")
            bb.wait("tensor", "xl")
            bb.wait_tok("tensor", ns.ones_tok)
            bb.wait("tensor", "wl")
            bb.wait("scalar", "xl")
            if need_l1:
                bb.op("vector", "tensor_scalar", out=ns.l1_gam05[:], in0=ns.l1_gam[:],
                      scalar1=0.5, scalar2=None, op0=ALU.mult)
                bb.op("vector", "tensor_scalar", out=ns.l1_bet05[:], in0=ns.l1_bet[:],
                      scalar1=0.5, scalar2=None, op0=ALU.mult, inc="dv")
            bb.barrier()

            for sl in sublayers:
                if sl == "l0ffn":
                    carve_l0ffn(ns)
                    emit_l0ffn(bb, ns)
                elif sl == "l1ffn":
                    carve_l1ffn(ns)
                    emit_l1ffn(bb, ns)
                elif sl == "l0attn":
                    carve_l0attn(ns)
                    emit_l0attn(bb, ns)
                elif sl == "l1attn":
                    carve_l1attn(ns)
                    emit_l1attn(bb, ns)
                else:
                    raise NotImplementedError(sl)
                bb.barrier()

            emit_store_x(bb, ns)

        for name in ENGINES:
            def body(engine, _name=name):
                b.begin(_name, engine)
                program(b)
            getattr(block, name)(body)

    return nc


def _run(nc, in_maps, **kw):
    return run_bass_kernel_spmd(nc, in_maps, core_ids=list(range(NCORES)), **kw)



def kernel(**inputs):
    g, xs = prep_host(inputs)
    seq = ["l0ffn", "l0attn", "l0ffn", "l1ffn", "l1attn", "l1ffn"]
    nc = build_program(g, seq)
    in_maps = [dict(g, xT=xs[b]) for b in range(NCORES)]
    res = run_bass_kernel_spmd(nc, in_maps, core_ids=list(range(NCORES)))
    out = np.stack([np.asarray(r["yT"]).T for r in res.results])
    return np.ascontiguousarray(out, dtype=np.float32)


# version 05
# speedup vs baseline: 1.1185x; 1.0575x over previous
import numpy as np
import ml_dtypes
from contextlib import ExitStack

import concourse.bass as bass
import concourse.mybir as mybir
from concourse.bass_utils import run_bass_kernel_spmd

F32 = mybir.dt.float32
BF16 = mybir.dt.bfloat16
AF = mybir.ActivationFunctionType
ALU = mybir.AluOpType
AX = mybir.AxisListType
BFNP = ml_dtypes.bfloat16

T = 2048
D = 1024
DC = 8
HALF = 1024
NCORES = 8

ENGINES = ("sync", "scalar", "vector", "tensor", "gpsimd")



class Builder:

    def __init__(self, ns):
        self.ns = ns
        self.cur = None
        self.eng = None
        self.cnt = {}

    def begin(self, engine_name, engine_obj):
        self.cur = engine_name
        self.eng = engine_obj
        self.cnt = {k: 0 for k in self.ns.sems}

    def val(self, sem):
        return self.cnt[sem]

    def tok(self, sem):
        return (sem, self.cnt[sem])

    def wait(self, eng, sem, val=None):
        if val is None:
            val = self.cnt[sem]
        if self.cur == eng and val > 0:
            self.eng.wait_ge(self.ns.sems[sem], val)

    def wait_tok(self, eng, t):
        if t is not None:
            self.wait(eng, t[0], t[1])

    AUTODRAIN = ("vector", "scalar", "gpsimd")

    def op(self, eng, method, *args, inc=None, inc_n=1, **kw):
        if inc is not None:
            self.cnt[inc] += inc_n
        if self.cur != eng:
            return None
        if eng in self.AUTODRAIN and method not in ("dma_start", "nop"):
            if getattr(self, "_dirty_" + eng, False):
                self.eng.drain()
            setattr(self, "_dirty_" + eng, True)
        ins = getattr(self.eng, method)(*args, **kw)
        if inc is not None:
            ins.then_inc(self.ns.sems[inc], inc_n)
        return ins

    def dma(self, eng, out, in_, inc, **kw):
        return self.op(eng, "dma_start", inc=inc, inc_n=16, out=out, in_=in_, **kw)

    def drain(self, eng):
        if self.cur == eng:
            self.eng.drain()

    def barrier(self):
        self.cnt["bar"] += 5
        tgt = self.cnt["bar"]
        if self.cur is not None:
            if self.cur != "sync":
                self.eng.drain()
            self.eng.nop().then_inc(self.ns.sems["bar"], 1)
            self.eng.wait_ge(self.ns.sems["bar"], tgt)
            for e in self.AUTODRAIN:
                setattr(self, "_dirty_" + e, False)


class NS:
    pass



def _rope_tables():
    i = np.arange(D // 2, dtype=np.float32)
    theta = (np.float32(10000.0) ** (np.float32(-2.0) * (i - np.float32(1.0)) / np.float32(D))).astype(np.float32)
    ang = (np.arange(T, dtype=np.float32)[:, None] * theta[None, :]).astype(np.float32)
    c = np.cos(ang).astype(np.float32)
    s = np.sin(ang).astype(np.float32)
    return np.ascontiguousarray(c.T), np.ascontiguousarray(s.T)


def _causal_masks():
    qq = np.arange(128)[:, None]; kk = np.arange(512)[None, :]
    m1 = np.stack([(kk <= j * 128 + qq) for j in range(4)]).astype(np.float32)
    m1 = (1.0 - m1) * np.float32(-1e30)
    kk2 = np.arange(128)[:, None]; qq2 = np.arange(512)[None, :]
    m2 = np.stack([(kk2 <= qq2 - j * 128) for j in range(4)]).astype(np.float32)
    m2 = (1.0 - m2) * np.float32(-1e30)
    return np.ascontiguousarray(m1), np.ascontiguousarray(m2)


def _part_layout(v):
    E, L = v.shape
    n = L // 128
    return np.ascontiguousarray(v.reshape(E, n, 128).transpose(2, 0, 1).reshape(128, E * n))


def prep_host(inputs):
    g = {}
    f32 = np.float32
    asc = np.ascontiguousarray

    perm = np.concatenate([np.arange(0, D, 2), np.arange(1, D, 2)])
    g["wq_t"] = asc(np.asarray(inputs["L0_attn_wq"], f32).T[:, perm])
    g["wk_t"] = asc(np.asarray(inputs["L0_attn_wk"], f32).T[:, perm])
    g["wv_t"] = asc(np.asarray(inputs["L0_attn_wv"], f32).T[:, perm])
    ipw = np.asarray(inputs["L0_attn_in_proj_w"], f32)
    ipb = np.asarray(inputs["L0_attn_in_proj_b"], f32)
    g["wqi_t"] = asc(ipw[0:D].T[perm, :])
    g["wki_t"] = asc(ipw[D:2 * D].T[perm, :])
    g["wvi_t"] = asc(ipw[2 * D:3 * D].T[perm, :])
    g["bqi_p"] = _part_layout(ipb[0:D][None, :])
    g["bki_p"] = _part_layout(ipb[D:2 * D][None, :])
    g["bvi_p"] = _part_layout(ipb[2 * D:3 * D][None, :])
    g["outw_t"] = asc(np.asarray(inputs["L0_attn_out_w"], f32).T)
    g["bvi_row"] = asc(ipb[2 * D:3 * D][None, :])
    g["outb_row"] = asc(np.asarray(inputs["L0_attn_out_b"], f32)[None, :])
    g["rope_cos"], g["rope_sin"] = _rope_tables()
    g["mask1"], g["mask2"] = _causal_masks()

    g["l0_ns_p"] = _part_layout(np.asarray(inputs["L0_ffn_norm_scale"], f32)[None, :])
    g["l0_wg"] = asc(np.asarray(inputs["L0_ffn_wg"], f32))
    g["l0_bg_row"] = asc(np.asarray(inputs["L0_ffn_bg"], f32)[None, :])
    w1 = np.asarray(inputs["L0_ffn_w1"], f32)
    g["l0_w1a"] = asc(w1[:, :, 0::2])
    g["l0_w1b"] = asc(w1[:, :, 1::2])
    b1 = np.asarray(inputs["L0_ffn_b1"], f32)
    g["l0_b1a_p"] = _part_layout(asc(b1[:, 0::2]))
    g["l0_b1b_p"] = _part_layout(asc(b1[:, 1::2]))
    w2l0 = np.asarray(inputs["L0_ffn_w2"], f32)
    g["l0_w2p"] = asc(w2l0.reshape(16, 8, 128, 8, 128).transpose(0, 3, 2, 1, 4)
                      .reshape(16, 8, 128, 8 * 128))
    g["l0_b2"] = asc(np.asarray(inputs["L0_ffn_b2"], f32))

    wqkv = np.asarray(inputs["L1_attn_W_qkv"], f32)
    g["l1_wqT"] = asc(wqkv[0:D].T)
    g["l1_wkT"] = asc(wqkv[D:2 * D].T)
    g["l1_wvT"] = asc(wqkv[2 * D:3 * D].T)
    g["l1_wpT"] = asc(np.asarray(inputs["L1_attn_W_proj"], f32).T)
    g["l1_bp_row"] = asc(np.asarray(inputs["L1_attn_b_proj"], f32)[None, :])

    g["l1_gw"] = asc(np.asarray(inputs["L1_ffn_gate_W"], f32))
    g["l1_gb_row"] = asc(np.asarray(inputs["L1_ffn_gate_b"], f32)[None, :])
    g["l1_w1"] = asc(np.asarray(inputs["L1_ffn_W1"], f32))
    g["l1_b1_p"] = _part_layout(np.asarray(inputs["L1_ffn_b1"], f32))
    w2l1 = np.asarray(inputs["L1_ffn_W2"], f32)
    g["l1_w2p"] = asc(w2l1.reshape(8, 16, 128, 8, 128).transpose(0, 3, 2, 1, 4)
                      .reshape(8, 8, 128, 16 * 128))
    g["l1_b2"] = asc(np.asarray(inputs["L1_ffn_b2"], f32))
    g["l1_gam_p"] = _part_layout(np.asarray(inputs["L1_ffn_ln_gamma"], f32)[None, :])
    g["l1_bet_p"] = _part_layout(np.asarray(inputs["L1_ffn_ln_beta"], f32)[None, :])

    g["ident_f"] = np.eye(128, dtype=f32)

    x = np.asarray(inputs["L0_attn_x"], f32)
    xT = np.ascontiguousarray(x.transpose(0, 2, 1))
    return g, [np.ascontiguousarray(xT[b]) for b in range(8)]



def Xc(ns, dc, t0=0, tlen=T):
    return ns.X[:, dc * T + t0: dc * T + t0 + tlen]


def make_ns(nc, g, sublayers):
    ns = NS()
    ns.nc = nc
    ns.g = g
    d = {}

    def din(name, arr):
        d[name] = nc.dram_tensor(name, list(arr.shape), F32, kind="ExternalInput").ap()
        return d[name]

    ns.dram = d
    d["xT"] = nc.dram_tensor("xT", [D, T], F32, kind="ExternalInput").ap()
    d["yT"] = nc.dram_tensor("yT", [D, T], F32, kind="ExternalOutput").ap()
    for k, v in g.items():
        din(k, v)
    d["row_dram"] = nc.dram_tensor("row_dram", [128, 128], F32, kind="Internal").ap()
    d["qh_dram"] = nc.dram_tensor("qh_dram", [D, T], BF16, kind="Internal").ap()
    d["kh_dram"] = nc.dram_tensor("kh_dram", [D, T], BF16, kind="Internal").ap()
    d["q2_dram"] = nc.dram_tensor("q2_dram", [D, T], BF16, kind="Internal").ap()
    d["o2_dram"] = nc.dram_tensor("o2_dram", [D, T], BF16, kind="Internal").ap()
    return ns



def emit_load_x(b, ns):
    for dc in range(DC):
        b.dma("sync", out=Xc(ns, dc), in_=ns.dram["xT"][dc * 128:(dc + 1) * 128, :], inc="xl")
    ns.x_tok = b.tok("xl")


def emit_store_x(b, ns):
    b.barrier()
    for dc in range(DC):
        b.dma("scalar", out=ns.dram["yT"][dc * 128:(dc + 1) * 128, :], in_=Xc(ns, dc), inc="st")
    b.wait("scalar", "st")


def emit_load_consts(b, ns):
    b.op("vector", "memset", ns.ones_col[:], 1.0)
    b.op("vector", "memset", ns.ones_bf[:], 1.0, inc="dv")
    b.dma("sync", out=ns.ident_f[:], in_=ns.dram["ident_f"], inc="xl")
    ns.const_tok = b.tok("xl")
    ns.ones_tok = b.tok("dv")



def emit_colstats(b, ns, half, src, kinds_fn, out_rows, tag, nt=8):
    n_rows = len(out_rows)
    nw = nt
    prev_mm = None
    for dc in range(DC):
        b.wait_tok("scalar", prev_mm)
        b.op("scalar", "activation", out=ns.sq[:], in_=src(dc), func=AF.Square, inc="ac")
        b.wait("tensor", "ac", b.val("ac"))
        if dc == 0:
            b.wait("tensor", "dv", b.val("dv"))
        for w in range(nw):
            b.op("tensor", "matmul", ns.ps_misc[:, 2 * w:2 * w + 1],
                 src(dc)[:, 128 * w:128 * w + 128], ns.ones_col[:, 0:1],
                 start=(dc == 0 and w == 0), stop=False)
            b.op("tensor", "matmul", ns.ps_misc[:, 2 * w + 1:2 * w + 2],
                 ns.sq[:, 128 * w:128 * w + 128], ns.ones_col[:, 0:1],
                 start=False, stop=(dc == DC - 1 and w == nw - 1),
                 inc=("pe" if (w == nw - 1) else None))
        prev_mm = b.tok("pe")
    pe_done = b.tok("pe")
    b.wait_tok("vector", pe_done)
    b.wait_tok("scalar", pe_done)
    kinds_fn(b)
    b.wait("tensor", "dv", b.val("dv"))
    b.wait("tensor", "ac", b.val("ac"))
    b.op("tensor", "transpose", ns.ps_misc[0:n_rows * nw, 0:128],
         ns.scol[:, 0:n_rows * nw], ns.ident_f[:], inc="pe")
    b.wait("vector", "pe", b.val("pe"))
    b.wait("vector", "m1", b.val("m1"))
    b.op("vector", "tensor_copy", out=ns.rowstage[0:n_rows * nw, 0:128],
         in_=ns.ps_misc[0:n_rows * nw, 0:128], inc="dv")
    b.wait("sync", "dv", b.val("dv"))
    b.dma("sync", out=ns.dram["row_dram"][0:n_rows * nw, 0:128],
          in_=ns.rowstage[0:n_rows * nw, 0:128], inc="m1")
    b.wait("sync", "m1", b.val("m1"))
    for r in range(n_rows):
        b.dma("sync", out=out_rows[r][0:1, 0:nw * 128],
              in_=ns.dram["row_dram"][r * nw:(r + 1) * nw, 0:128]
              .rearrange("a c -> (a c)")[None, :], inc="m1")
    return b.tok("m1")


def emit_bcast(b, ns, row_ap, out_bc, n, wait_t=None, base=0, bf=False):
    nblk = (n + 511) // 512
    for j in range(nblk):
        w = min(512, n - j * 512)
        b.wait_tok("tensor", wait_t)
        b.wait("tensor", "dv", b.val("dv"))
        ones = ns.ones_bf if bf else ns.ones_col
        b.op("tensor", "matmul", ns.ps_misc[:, 0:w],
             ones[base:base + 1, 0:128], row_ap[0:1, j * 512:j * 512 + w],
             start=True, stop=True, inc="pe")
        b.wait("vector", "pe", b.val("pe"))
        b.op("vector", "tensor_copy", out=out_bc[:, j * 512:j * 512 + w],
             in_=ns.ps_misc[:, 0:w], inc="dv")
    return b.tok("dv")



def emit_l0ffn(b, ns):
    g = ns.g
    E = 16
    for half in range(2):
        t0 = half * HALF
        _l0ffn_half(b, ns, half, t0, E)
        b.barrier()


def _l0ffn_half(b, ns, half, t0, E):
    d = ns.dram

    def kinds(bb):
        v = "vector"; s = "scalar"
        nw = 8
        S = ns.scol
        P = ns.ps_misc
        bb.op(v, "tensor_scalar", out=S[:, 0:nw], in0=P[:, 0:16].rearrange("p (w k) -> p k w", k=2)[:, 0, :],
              scalar1=1.0 / D, scalar2=None, op0=ALU.mult)
        bb.op(v, "tensor_scalar", out=S[:, 24:24 + nw], in0=P[:, 0:16].rearrange("p (w k) -> p k w", k=2)[:, 1, :],
              scalar1=1.0 / D, scalar2=None, op0=ALU.mult)
        bb.drain(v)
        bb.op(v, "tensor_tensor", out=S[:, 32:32 + nw], in0=S[:, 0:nw], in1=S[:, 0:nw], op=ALU.mult)
        bb.drain(v)
        bb.op(v, "tensor_tensor", out=S[:, 40:40 + nw], in0=S[:, 24:24 + nw], in1=S[:, 32:32 + nw], op=ALU.subtract)
        bb.drain(v)
        bb.op(v, "tensor_scalar", out=S[:, 48:48 + nw], in0=S[:, 40:40 + nw],
              scalar1=1e-5, scalar2=None, op0=ALU.add, inc="dv")
        bb.wait(s, "dv", bb.val("dv"))
        bb.op(s, "activation", out=S[:, 56:56 + nw], in_=S[:, 48:48 + nw], func=AF.Ln)
        bb.drain(s)
        bb.op(s, "activation", out=S[:, 8:8 + nw], in_=S[:, 56:56 + nw], func=AF.Exp, scale=-0.5, inc="ac")
        bb.wait(v, "ac", bb.val("ac"))
        bb.op(v, "reciprocal", out=S[:, 32:32 + nw], in_=S[:, 48:48 + nw])
        bb.drain(v)
        bb.op(v, "tensor_tensor", out=S[:, 40:40 + nw], in0=S[:, 40:40 + nw], in1=S[:, 32:32 + nw], op=ALU.mult)
        bb.drain(v)
        bb.op(v, "tensor_scalar", out=S[:, 40:40 + nw], in0=S[:, 40:40 + nw],
              scalar1=1e-5, scalar2=None, op0=ALU.add, inc="dv")
        bb.wait(s, "dv", bb.val("dv"))
        bb.op(s, "activation", out=S[:, 48:48 + nw], in_=S[:, 40:40 + nw], func=AF.Ln)
        bb.drain(s)
        bb.op(s, "activation", out=S[:, 48:48 + nw], in_=S[:, 48:48 + nw], func=AF.Exp, scale=-0.5, inc="ac")
        bb.wait(v, "ac", bb.val("ac"))
        bb.op(v, "tensor_tensor", out=S[:, 16:16 + nw], in0=S[:, 8:8 + nw], in1=S[:, 48:48 + nw], op=ALU.mult)
        bb.drain(v)
        bb.op(v, "tensor_scalar", out=S[:, 8:8 + nw], in0=S[:, 8:8 + nw],
              scalar1=0.5, scalar2=None, op0=ALU.mult, inc="dv")

    rows = [ns.row_m, ns.row_a, ns.row_b]
    rows_tok = emit_colstats(b, ns, half, lambda dc: Xc(ns, dc, t0, HALF), kinds, rows, "l0ln")

    t_m = emit_bcast(b, ns, ns.row_m, ns.bc_m, HALF, rows_tok, base=0)
    t_a = emit_bcast(b, ns, ns.row_a, ns.bc_a, HALF, rows_tok, base=32)
    t_b = emit_bcast(b, ns, ns.row_b, ns.bc_b, HALF, rows_tok, base=64)

    b.wait_tok("vector", t_b)
    for dc in range(DC):
        X = Xc(ns, dc, t0, HALF)
        t1 = ns.tmp1[:, 0:HALF]
        b.wait("vector", "ac", b.val("ac"))
        b.op("vector", "tensor_tensor", out=t1, in0=X, in1=ns.bc_m[:, 0:HALF], op=ALU.subtract)
        b.drain("vector")
        b.op("vector", "tensor_tensor", out=ns.tmp2[:, 0:HALF], in0=t1, in1=ns.bc_a[:, 0:HALF], op=ALU.mult)
        b.drain("vector")
        b.op("vector", "tensor_tensor", out=X, in0=X, in1=ns.tmp2[:, 0:HALF], op=ALU.add)
        b.drain("vector")
        b.op("vector", "tensor_tensor", out=ns.tmp2[:, 0:HALF], in0=t1, in1=ns.bc_b[:, 0:HALF], op=ALU.mult,
             inc="dv")
        b.wait("scalar", "dv", b.val("dv"))
        b.op("scalar", "activation", out=ns.xn_bf[:, dc * HALF:(dc + 1) * HALF], in_=ns.tmp2[:, 0:HALF],
             func=AF.Identity, scale=ns.l0_ns[:, dc:dc + 1], inc="ac")
    xn_tok = b.tok("ac")

    b.wait_tok("tensor", xn_tok)
    b.wait("tensor", "dv", b.val("dv"))
    gtok = _l0_gate(b, ns, half, t0)

    _l0_experts(b, ns, half, t0, gtok)


def _l0_gate(b, ns, half, t0):
    d = ns.dram
    nw = 8
    b.wait_tok("tensor", ns.wg_tok)
    bgt = emit_bcast(b, ns, ns.bg_row, ns.bc_g16, 16)
    ps_free = bgt
    for w in range(nw):
        tw = t0 + 128 * w
        b.wait_tok("tensor", ps_free)
        for dc in range(DC):
            b.op("tensor", "matmul", ns.ps_misc[:, 0:16],
                 ns.xn_bf[:, dc * HALF + 128 * w: dc * HALF + 128 * w + 128],
                 ns.wg_bf[:, dc * 16:(dc + 1) * 16],
                 start=(dc == 0), stop=(dc == DC - 1), inc=("pe" if dc == DC - 1 else None))
        b.wait("vector", "pe", b.val("pe"))
        v = "vector"
        L = ns.gl
        CUR = ns.gcur
        b.op(v, "tensor_tensor", out=L[:], in0=ns.ps_misc[:, 0:16], in1=ns.bc_g16[:, 0:16], op=ALU.add,
             inc="dv")
        ps_free = b.tok("dv")
        b.drain(v)
        b.op(v, "tensor_copy", out=CUR[:], in_=L[:])
        b.drain(v)
        b.op(v, "reduce_max", out=ns.gm[:, 0:1], in_=CUR[:], axis=AX.X, negate=True)
        b.drain(v)
        for j in range(3):
            b.op(v, "tensor_scalar", out=ns.gmask[:], in0=CUR[:], scalar1=ns.gm[:, 0:1],
                 scalar2=0.0, op0=ALU.add, op1=ALU.is_ge)
            b.drain(v)
            b.op(v, "scalar_tensor_tensor", out=CUR[:], in0=ns.gmask[:], scalar=-1e30,
                 in1=CUR[:], op0=ALU.mult, op1=ALU.add)
            b.drain(v)
            b.op(v, "reduce_max", out=ns.gm[:, 0:1], in_=CUR[:], axis=AX.X, negate=True)
            b.drain(v)
        b.op(v, "tensor_scalar", out=ns.gmask[:], in0=L[:], scalar1=ns.gm[:, 0:1],
             scalar2=0.0, op0=ALU.add, op1=ALU.is_ge)
        b.op(v, "reduce_max", out=ns.gm[:, 1:2], in_=L[:], axis=AX.X, negate=True, inc="dv")
        b.wait("scalar", "dv", b.val("dv"))
        GX = ns.gex[:, 0:16]
        b.op("scalar", "activation", out=GX, in_=L[:], func=AF.Exp, bias=ns.gm[:, 1:2], inc="ac")
        b.wait(v, "ac", b.val("ac"))
        b.op(v, "tensor_tensor", out=GX, in0=GX, in1=ns.gmask[:], op=ALU.mult)
        b.drain(v)
        b.op(v, "reduce_sum", out=ns.gm[:, 2:3], in_=GX, axis=AX.X)
        b.drain(v)
        b.op(v, "reciprocal", out=ns.gm[:, 3:4], in_=ns.gm[:, 2:3])
        b.drain(v)
        b.op(v, "tensor_scalar", out=GX, in0=GX, scalar1=ns.gm[:, 3:4],
             scalar2=0.5, op0=ALU.mult, op1=ALU.mult, inc="dv")
        b.wait("tensor", "dv", b.val("dv"))
        b.op("tensor", "transpose", ns.ps_misc2[0:16, 0:128], GX, ns.ident_f[:], inc="pe")
        b.wait(v, "pe", b.val("pe"))
        b.op(v, "tensor_copy", out=ns.gp_T[0:16, tw:tw + 128], in_=ns.ps_misc2[0:16, 0:128])
        b.op(v, "tensor_copy", out=ns.gp_Tb[0:16, tw:tw + 128], in_=ns.ps_misc2[0:16, 0:128], inc="dv")
    return b.tok("dv")


def _l0_w2_load(b, ns, e, hc, slot, sem):
    d = ns.dram
    b.wait_tok("gpsimd", ns.w2_slot_tok[slot])
    b.dma("gpsimd", out=ns.w2_sb[slot], in_=d["l0_w2p"][e, hc], inc=sem)
    return b.tok(sem)


def _l0_experts(b, ns, half, t0, gtok):
    d = ns.dram
    E = 16
    NFH = 2
    for e in range(E):
        for fh in range(NFH):
            slot = ns.w1_slot_idx
            ns.w1_slot_idx = (ns.w1_slot_idx + 1) % 2
            free_tok = ns.w1_slot_tok[slot]
            b.wait_tok("gpsimd", free_tok)
            b.dma("gpsimd",
                  out=ns.w1a_sb[slot].rearrange("p (dc f) -> p dc f", dc=8),
                  in_=d["l0_w1a"][e, :, fh * 512:(fh + 1) * 512]
                  .rearrange("(dc p) f -> p dc f", p=128), inc="wl")
            b.dma("gpsimd",
                  out=ns.w1b_sb[slot].rearrange("p (dc f) -> p dc f", dc=8),
                  in_=d["l0_w1b"][e, :, fh * 512:(fh + 1) * 512]
                  .rearrange("(dc p) f -> p dc f", p=128), inc="wl")
        for hc0 in range(2):
            _l0_w2_load(b, ns, e, hc0, hc0, "wl")
        wtok = b.tok("wl")
        for fh in range(NFH):
            ns.w1_load_tok[(e, fh)] = wtok
        ns.w2_pre_tok_l0 = wtok
        ns.w2_cur_tok_l0 = [None, None]

        b.wait_tok("sync", gtok)
        b.wait("sync", "dv", ns.gbc_guard)
        b.dma("sync", out=ns.g_row[0:1, 0:HALF], in_=ns.gp_T[e:e + 1, t0:t0 + HALF], inc="m1")
        b.wait("tensor", "m1", b.val("m1"))
        gb_tok = emit_bcast(b, ns, ns.g_row, ns.g_bc, HALF)
        ns.gbc_guard = b.val("dv")

        for fh in range(NFH):
            w1slot = (e * NFH + fh) % 2
            b.wait_tok("tensor", ns.w1_load_tok[(e, fh)])
            if e == 0 and fh == 0:
                b.wait_tok("tensor", b.tok("ac"))
            for c in range(4):
                ic = fh * 4 + c
                for tau in range(2):
                    psA = ns.psA[:, tau * 512:(tau + 1) * 512]
                    psB = ns.psB[:, tau * 512:(tau + 1) * 512]
                    b.wait_tok("tensor", ns.psA_tok[tau])
                    b.wait_tok("tensor", ns.psB_tok[tau])
                    for dc in range(DC):
                        b.op("tensor", "matmul", psA,
                             ns.w1a_sb[w1slot][:, dc * 512 + c * 128: dc * 512 + c * 128 + 128],
                             ns.xn_bf[:, dc * HALF + tau * 512: dc * HALF + (tau + 1) * 512],
                             start=(dc == 0), stop=(dc == DC - 1),
                             inc=("pe" if dc == DC - 1 else None))
                    peA = b.tok("pe")
                    for dc in range(DC):
                        b.op("tensor", "matmul", psB,
                             ns.w1b_sb[w1slot][:, dc * 512 + c * 128: dc * 512 + c * 128 + 128],
                             ns.xn_bf[:, dc * HALF + tau * 512: dc * HALF + (tau + 1) * 512],
                             start=(dc == 0), stop=(dc == DC - 1),
                             inc=("pe" if dc == DC - 1 else None))
                    peB = b.tok("pe")
                    v = "vector"
                    par = ns.glu_par
                    ns.glu_par ^= 1
                    gluA = ns.gluA2[par]; gluS = ns.gluS2[par]; gluB = ns.gluB2[par]
                    b.wait_tok(v, peA)
                    b.wait(v, "gp", ns.glu_free[par])
                    b.op(v, "tensor_scalar", out=gluA[:], in0=psA,
                         scalar1=ns.l0_b1a[:, e * 8 + ic:e * 8 + ic + 1], scalar2=7.0,
                         op0=ALU.add, op1=ALU.min, inc="dv")
                    ns.psA_tok[tau] = b.tok("dv")
                    b.wait("scalar", "dv", b.val("dv"))
                    b.op("scalar", "activation", out=gluS[:], in_=gluA[:],
                         func=AF.Sigmoid, scale=1.702, inc="ac")
                    b.wait_tok("scalar", peB)
                    b.op("scalar", "activation", out=gluB[:], in_=psB,
                         func=AF.Identity, bias=ns.l0_b1b[:, e * 8 + ic:e * 8 + ic + 1], inc="ac")
                    ns.psB_tok[tau] = b.tok("ac")
                    b.wait(v, "ac", b.val("ac"))
                    b.op(v, "tensor_scalar", out=gluB[:], in0=gluB[:],
                         scalar1=7.0, scalar2=-7.0, op0=ALU.min, op1=ALU.max, inc="dv")
                    gp = "gpsimd"
                    b.wait(gp, "dv", b.val("dv"))
                    b.wait(gp, "ac", b.val("ac"))
                    b.op(gp, "tensor_tensor", out=gluA[:], in0=gluA[:], in1=gluS[:], op=ALU.mult,
                         inc="gp")
                    b.wait(v, "gp", b.val("gp"))
                    b.op(v, "scalar_tensor_tensor", out=gluB[:], in0=gluB[:], scalar=1.0,
                         in1=gluA[:], op0=ALU.add, op1=ALU.mult, inc="dv")
                    b.wait(gp, "dv", b.val("dv"))
                    b.wait_tok(gp, gb_tok)
                    b.wait_tok(gp, ns.sg_guard)
                    b.op(gp, "tensor_tensor", out=ns.sg_bf[:, ic * HALF + tau * 512: ic * HALF + (tau + 1) * 512],
                         in0=gluB[:], in1=ns.g_bc[:, tau * 512:(tau + 1) * 512], op=ALU.mult,
                         inc="gp")
                    ns.glu_free[par] = b.val("gp")
            ns.w1_slot_tok[w1slot] = b.tok("pe")
        sg_tok = b.tok("gp")

        b.wait_tok("tensor", sg_tok)
        for hc in range(DC):
            w2slot = hc % 2
            w2tok = ns.w2_pre_tok_l0 if hc < 2 else ns.w2_cur_tok_l0[w2slot]
            b.wait_tok("tensor", w2tok)
            b.wait_tok("tensor", ns.psY_tok[0])
            for tau in range(2):
                psY = ns.psY[:, tau * 512:(tau + 1) * 512]
                for ic in range(8):
                    b.op("tensor", "matmul", psY,
                         ns.w2_sb[w2slot][:, ic * 128:(ic + 1) * 128],
                         ns.sg_bf[:, ic * HALF + tau * 512: ic * HALF + (tau + 1) * 512],
                         start=(ic == 0), stop=(ic == 7 and e != 0),
                         inc=("pe" if (ic == 7 and e != 0 and tau == 1) else None))
                if e == 0:
                    b.op("tensor", "matmul", psY,
                         ns.b2_bf[:, hc * 128:(hc + 1) * 128],
                         ns.gp_Tb[0:16, t0 + tau * 512: t0 + (tau + 1) * 512],
                         start=False, stop=True, inc=("pe" if tau == 1 else None))
            peY = b.tok("pe")
            b.wait_tok("vector", peY)
            b.op("vector", "tensor_tensor",
                 out=Xc(ns, hc, t0, HALF),
                 in0=Xc(ns, hc, t0, HALF), in1=ns.psY[:, 0:HALF], op=ALU.add, inc="dv")
            ns.psY_tok[0] = b.tok("dv")
            ns.w2_slot_tok[w2slot] = b.tok("pe")
            if hc + 2 < DC:
                ns.w2_cur_tok_l0[w2slot] = _l0_w2_load(b, ns, e, hc + 2, w2slot,
                                                       "w2a" if w2slot == 0 else "w2b")
        ns.sg_guard = b.tok("pe")





def carve_l0attn(ns):
    f32, bf, al, pos = _mk_carver(ns)
    ns.a_vh = bf(16 * 1024)
    u0 = al(0, align=64)
    f32p, bfp, alp, posp = _mk_carver(ns)
    posp[0] = u0
    ns.a_bcm = f32p(512)
    ns.a_bcr = f32p(512)
    ns.a_xln = bfp(8 * 512)
    ns.a_pbf = bfp(8 * 512)
    ns.a_cos = f32p(2 * 512)
    ns.a_sin = f32p(2 * 512)
    ns.a_bvi = f32p(1024)
    ns.sq = f32p(1024)
    ns.tmp1 = f32p(512)
    ns.a_w = bfp(8 * 1024)
    ns.a_wi = bfp(8 * 1024)
    ns.a_stage = bfp(8 * 512)
    ns.a_rs1 = f32p(512)
    ns.a_rs2 = f32p(512)
    proj_end = posp[0]
    f32a, bfa, ala, posa = _mk_carver(ns)
    posa[0] = u0
    pj = ala(8192)
    ns.a_junk = ns.AR[:, pj // 4: pj // 4 + 2048]
    ns.a_expT = ns.AR_B[:, pj // 2: pj // 2 + 4096]
    ns.a_outw = bfa(8 * 1024)
    ns.a_m1 = bfa(4 * 512)
    ns.a_m2 = bfa(4 * 512)
    ns.a_qh = [bfa(2048, rows=64), bfa(2048, rows=64)]
    ns.a_kh = [bfa(2048, rows=64), bfa(2048, rows=64)]
    ns.a_opair = bfa(T)
    ns.a_zcol = f32a(16)
    ns.a_zbc = bfa(512)
    ns.a_sc1 = f32a(512)
    ns.a_outb = f32a(1024)
    attn_end = posa[0]
    pos[0] = max(proj_end, attn_end)
    ns.a_bytes = pos[0]


def emit_l0attn(b, ns):
    d = ns.dram
    v = "vector"
    NT4 = 4
    H = 16
    HD = 64

    b.dma("sync", out=ns.a_bvi[0:1, 0:D], in_=d["bvi_row"], inc="xl")
    ns.bvi_tok = b.tok("xl")
    for sh in range(2):
        st0 = sh * HALF
        rows = [ns.row_m[0:1, st0:st0 + HALF], ns.row_a[0:1, st0:st0 + HALF]]
        rt = emit_colstats(b, ns, 0, lambda dc: Xc(ns, dc, st0, HALF),
                           lambda bb: _kinds_ln2(bb, ns, nw=8), rows, "aln", nt=8)

    for P in range(3):
        wsrc = ("wq_t", "wk_t", "wv_t")[P]
        wisrc = ("wqi_t", "wki_t", "wvi_t")[P]
        b.wait_tok("gpsimd", ns.aw_guard)
        b.dma("gpsimd", out=ns.a_w.rearrange("p (dc f) -> p dc f", dc=8),
              in_=d[wsrc].rearrange("(dc p) f -> p dc f", p=128), inc="wl")
        b.dma("gpsimd", out=ns.a_wi.rearrange("p (dc f) -> p dc f", dc=8),
              in_=d[wisrc].rearrange("(dc p) f -> p dc f", p=128), inc="wl")
        wtok = b.tok("wl")
        for tau in range(NT4):
            tw = tau * 512
            b.wait(v, "pe", ns.a_xln_guard)
            emit_bcast(b, ns, ns.row_m[0:1, tw:tw + 512], ns.a_bcm, 512, rt, base=0)
            t_r = emit_bcast(b, ns, ns.row_a[0:1, tw:tw + 512], ns.a_bcr, 512, rt, base=32)
            b.wait_tok(v, t_r)
            for dc in range(DC):
                b.op(v, "tensor_tensor", out=ns.tmp1[:, 0:512], in0=Xc(ns, dc, tw, 512),
                     in1=ns.a_bcm[:, 0:512], op=ALU.subtract)
                b.op(v, "tensor_tensor", out=ns.a_xln[:, dc * 512:(dc + 1) * 512],
                     in0=ns.tmp1[:, 0:512], in1=ns.a_bcr[:, 0:512], op=ALU.mult, inc="dv")
            xln_tok = b.tok("dv")
            def cs_load(i, csl):
                b.wait("sync", "dv", ns.a_cs_guard[csl])
                b.dma("sync", out=ns.a_cos[:, csl * 512:(csl + 1) * 512],
                      in_=d["rope_cos"][i * 128:(i + 1) * 128, tw:tw + 512], inc="m2")
                b.dma("sync", out=ns.a_sin[:, csl * 512:(csl + 1) * 512],
                      in_=d["rope_sin"][i * 128:(i + 1) * 128, tw:tw + 512], inc="m2")
                return b.tok("m2")

            cs_load(0, 0)
            cs_pre = cs_load(1, 1)
            cs_next = [cs_pre, cs_pre]
            b.wait_tok("tensor", wtok)
            b.wait_tok("tensor", xln_tok)
            for i in range(4):
                csl = i % 2
                cs_tok = cs_next[csl]
                for half_ in range(2):
                    fc = i + 4 * half_
                    ps = ns.ps4[:, half_ * 512:(half_ + 1) * 512]
                    b.wait_tok("tensor", ns.a_ps_tok[half_])
                    for dc in range(DC):
                        b.op("tensor", "matmul", ps,
                             ns.a_w[:, dc * 1024 + fc * 128: dc * 1024 + fc * 128 + 128],
                             ns.a_xln[:, dc * 512:(dc + 1) * 512],
                             start=(dc == 0), stop=(dc == DC - 1),
                             inc=("pe" if dc == DC - 1 else None))
                pe_pair = b.tok("pe")
                b.wait_tok(v, pe_pair)
                b.wait_tok(v, cs_tok)
                b.wait(v, "pe", ns.a_pbf_guard)
                ci = ns.a_cos[:, csl * 512:(csl + 1) * 512]
                si = ns.a_sin[:, csl * 512:(csl + 1) * 512]
                xe = ns.ps4[:, 0:512]
                xo = ns.ps4[:, 512:1024]
                b.op(v, "tensor_tensor", out=ns.a_rs1[:, 0:512], in0=xe, in1=ci, op=ALU.mult)
                b.op(v, "tensor_tensor", out=ns.a_rs2[:, 0:512], in0=xo, in1=si, op=ALU.mult)
                b.op(v, "tensor_tensor", out=ns.a_pbf[:, i * 512:(i + 1) * 512],
                     in0=ns.a_rs1[:, 0:512], in1=ns.a_rs2[:, 0:512], op=ALU.add)
                b.op(v, "tensor_tensor", out=ns.a_rs1[:, 0:512], in0=xo, in1=ci, op=ALU.mult)
                b.op(v, "tensor_tensor", out=ns.a_rs2[:, 0:512], in0=xe, in1=si, op=ALU.mult,
                     inc="dv")
                ns.a_ps_tok = [b.tok("dv"), b.tok("dv")]
                b.op(v, "tensor_tensor", out=ns.a_pbf[:, (i + 4) * 512:(i + 5) * 512],
                     in0=ns.a_rs1[:, 0:512], in1=ns.a_rs2[:, 0:512], op=ALU.subtract, inc="dv")
                ns.a_cs_guard[csl] = b.val("dv")
                if i == 1:
                    cs_load(2, 0)
                    tk = cs_load(3, 1)
                    cs_next = [tk, tk]
            pbf_tok = b.tok("dv")
            b.wait_tok("tensor", pbf_tok)
            if P < 2:
                bia = (ns.bqi_p, ns.bki_p)[P]
                for fc in range(DC):
                    ps = ns.psY[:, (fc % 2) * 512:(fc % 2) * 512 + 512]
                    b.wait_tok("tensor", ns.a_psY_tok[fc % 2])
                    for dc in range(DC):
                        b.op("tensor", "matmul", ps,
                             ns.a_wi[:, dc * 1024 + fc * 128: dc * 1024 + fc * 128 + 128],
                             ns.a_pbf[:, dc * 512:(dc + 1) * 512],
                             start=(dc == 0), stop=(dc == DC - 1),
                             inc=("pe" if dc == DC - 1 else None))
                    peq = b.tok("pe")
                    b.wait_tok("scalar", peq)
                    b.wait("scalar", "m1", ns.a_stage_guard)
                    b.op("scalar", "activation", out=ns.a_stage[:, fc * 512:(fc + 1) * 512],
                         in_=ps, func=AF.Identity, bias=bia[:, fc:fc + 1], inc="ac")
                    ns.a_psY_tok[fc % 2] = b.tok("ac")
                st_tok = b.tok("ac")
                b.wait_tok("sync", st_tok)
                dst = d["qh_dram"] if P == 0 else d["kh_dram"]
                b.dma("sync", out=dst.rearrange("(fc p) t -> p fc t", p=128)[:, :, tw:tw + 512],
                      in_=ns.a_stage.rearrange("p (fc t) -> p fc t", fc=8), inc="m1")
                ns.a_stage_guard = b.val("m1")
            else:
                b.wait_tok("tensor", ns.bvi_tok)
                for nw_ in range(4):
                    for fh in range(2):
                        ps = ns.psY[:, fh * 512:(fh + 1) * 512]
                        b.wait_tok("tensor", ns.a_psY_tok[fh])
                        for dc in range(DC):
                            b.op("tensor", "matmul", ps,
                                 ns.a_pbf[:, dc * 512 + nw_ * 128: dc * 512 + nw_ * 128 + 128],
                                 ns.a_wi[:, dc * 1024 + fh * 512: dc * 1024 + (fh + 1) * 512],
                                 start=(dc == 0), stop=False)
                        b.op("tensor", "matmul", ps,
                             ns.ones_col[0:1, 0:128],
                             ns.a_bvi[0:1, fh * 512:(fh + 1) * 512],
                             start=False, stop=True, inc="pe")
                        pev = b.tok("pe")
                        b.wait_tok(v, pev)
                        b.op(v, "tensor_copy",
                             out=ns.a_vh[:, (tau * 4 + nw_) * 1024 + fh * 512:
                                         (tau * 4 + nw_) * 1024 + (fh + 1) * 512],
                             in_=ps, inc="dv")
                        ns.a_psY_tok[fh] = b.tok("dv")
            ns.a_xln_guard = b.val("pe")
            ns.a_pbf_guard = b.val("pe")
        ns.aw_guard = b.tok("pe")

    b.wait("sync", "m1")
    b.barrier()
    b.dma("gpsimd", out=ns.a_m1.rearrange("p (j k) -> p j k", j=4),
          in_=d["mask1"].rearrange("j p k -> p j k"), inc="wl")
    b.dma("gpsimd", out=ns.a_m2.rearrange("p (j k) -> p j k", j=4),
          in_=d["mask2"].rearrange("j p k -> p j k"), inc="wl")
    b.dma("gpsimd", out=ns.a_outw.rearrange("p (pr f) -> p pr f", pr=8),
          in_=d["outw_t"].rearrange("(pr p) f -> p pr f", p=128), inc="wl")
    b.dma("sync", out=ns.a_outb[0:1, 0:D], in_=d["outb_row"], inc="m1")
    amw_tok = b.tok("wl")
    outb_tok = b.tok("m1")
    b.wait_tok("tensor", outb_tok)
    b.wait_tok("tensor", amw_tok)

    def qk_load(h, hs, sem):
        b.wait_tok("gpsimd", ns.a_qk_guard[hs])
        b.dma("gpsimd", out=ns.a_qh[hs][0:64, :], in_=d["qh_dram"][h * 64:(h + 1) * 64, :], inc=sem)
        b.dma("gpsimd", out=ns.a_kh[hs][0:64, :], in_=d["kh_dram"][h * 64:(h + 1) * 64, :], inc=sem)
        return b.tok(sem)

    qk_load(0, 0, "qk")
    qk_pre = qk_load(1, 1, "qk")
    qk_next = [qk_pre, qk_pre]
    for h in range(H):
        hs = h % 2
        qk_tok = qk_next[hs]
        b.wait_tok("tensor", qk_tok)
        for qs in range(16):
            ktn = qs // 4 + 1
            for kt in range(ktn):
                ps = ns.ps4[:, kt * 512:(kt + 1) * 512]
                b.wait_tok("tensor", ns.a_p1_tok[kt])
                b.op("tensor", "matmul", ps,
                     ns.a_qh[hs][0:64, qs * 128:(qs + 1) * 128],
                     ns.a_kh[hs][0:64, kt * 512:(kt + 1) * 512],
                     start=True, stop=True, inc="pe")
            p1_tok = b.tok("pe")
            b.wait_tok(v, p1_tok)
            jdiag = qs % 4
            b.op(v, "tensor_tensor", out=ns.ps4[:, (ktn - 1) * 512:ktn * 512],
                 in0=ns.ps4[:, (ktn - 1) * 512:ktn * 512],
                 in1=ns.a_m1[:, jdiag * 512:(jdiag + 1) * 512], op=ALU.add, inc="dv")
            b.op(v, "reduce_max", out=ns.gm[:, 0:1], in_=ns.ps4[:, 0:ktn * 512], axis=AX.X,
                 negate=True)
            b.op(v, "tensor_scalar", out=ns.gm[:, 1:2], in0=ns.gm[:, 0:1],
                 scalar1=0.125, scalar2=None, op0=ALU.mult, inc="dv")
            b.wait("scalar", "dv", b.val("dv"))
            b.op("scalar", "activation", out=ns.a_junk[:, 0:ktn * 512], in_=ns.ps4[:, 0:ktn * 512],
                 func=AF.Exp, scale=0.125, bias=ns.gm[:, 1:2],
                 accum_out=ns.gm[:, 2:3], inc="ac")
            ps1_free = b.tok("ac")
            ns.a_p1_tok = [ps1_free] * 4
            b.wait("scalar", "ac", b.val("ac"))
            b.op("scalar", "activation", out=ns.gm[:, 3:4], in_=ns.gm[:, 2:3], func=AF.Ln, inc="ac")
            b.wait(v, "ac", b.val("ac"))
            b.op(v, "tensor_tensor", out=ns.a_zcol[:, qs:qs + 1], in0=ns.gm[:, 1:2],
                 in1=ns.gm[:, 3:4], op=ALU.subtract, inc="dv")
        zc_tok = b.tok("dv")
        b.wait_tok("tensor", zc_tok)
        b.wait("tensor", "dv", b.val("dv"))
        b.op("tensor", "transpose", ns.ps_misc2[0:16, 0:128], ns.a_zcol[:, 0:16],
             ns.ident_f[:], inc="pe")
        b.wait(v, "pe", b.val("pe"))
        b.op(v, "tensor_copy", out=ns.rowstage[0:16, 0:128], in_=ns.ps_misc2[0:16, 0:128], inc="dv")
        b.wait("sync", "dv", b.val("dv"))
        b.dma("sync", out=d["row_dram"][0:16, 0:128], in_=ns.rowstage[0:16, 0:128], inc="m1")
        b.wait("sync", "m1", b.val("m1"))
        b.wait("gpsimd", "m1", b.val("m1"))
        b.dma("gpsimd", out=ns.g_row_bf[0:1, 0:T],
              in_=d["row_dram"][0:16, 0:128].rearrange("a c -> (a c)")[None, :], inc="gx")
        zrow_tok = b.tok("gx")

        for tq in range(NT4):
            b.wait_tok("tensor", zrow_tok)
            b.wait("tensor", "dv", b.val("dv"))
            b.op("tensor", "matmul", ns.ps_misc[:, 0:512],
                 ns.ones_bf[0:1, 0:128], ns.g_row_bf[0:1, tq * 512:(tq + 1) * 512],
                 start=True, stop=True, inc="pe")
            b.wait(v, "pe", b.val("pe"))
            b.op(v, "tensor_copy", out=ns.a_zbc[:, 0:512], in_=ns.ps_misc[:, 0:512], inc="dv")
            zbc_tok = b.tok("dv")
            ktn2 = tq * 4 + 4
            for kt in range(ktn2):
                slot = kt % 8
                ps = ns.psY[:, (kt % 2) * 512:(kt % 2) * 512 + 512]
                b.wait_tok("tensor", ns.a_p2_tok[kt % 2])
                b.op("tensor", "matmul", ps,
                     ns.a_kh[hs][0:64, kt * 128:(kt + 1) * 128],
                     ns.a_qh[hs][0:64, tq * 512:(tq + 1) * 512],
                     start=True, stop=True, inc="pe")
                p2_tok = b.tok("pe")
                b.wait_tok(v, p2_tok)
                b.wait_tok(v, zbc_tok)
                b.wait_tok(v, ns.a_expT_guard[slot])
                jd = kt - (tq * 4)
                if jd >= 0:
                    b.op(v, "tensor_tensor", out=ps, in0=ps,
                         in1=ns.a_m2[:, jd * 512:(jd + 1) * 512], op=ALU.add)
                b.op(v, "scalar_tensor_tensor", out=ns.a_sc1[:, 0:512], in0=ps, scalar=0.125,
                     in1=ns.a_zbc[:, 0:512], op0=ALU.mult, op1=ALU.add, inc="dv")
                ns.a_p2_tok[kt % 2] = b.tok("dv")
                b.wait("scalar", "dv", b.val("dv"))
                b.op("scalar", "activation", out=ns.a_expT[:, slot * 512:(slot + 1) * 512],
                     in_=ns.a_sc1[:, 0:512], func=AF.Exp, inc="ac")
                exp_tok = b.tok("ac")
                b.wait_tok("tensor", exp_tok)
                b.op("tensor", "matmul", ns.ps_misc2[0:64, 0:512],
                     ns.a_vh[:, kt * 1024 + h * 64: kt * 1024 + h * 64 + 64],
                     ns.a_expT[:, slot * 512:(slot + 1) * 512],
                     start=(kt == 0), stop=(kt == ktn2 - 1),
                     inc=("pe" if kt == ktn2 - 1 else None))
                ns.a_expT_guard[slot] = b.tok("pe")
            av_tok = b.tok("pe")
            b.wait_tok(v, av_tok)
            b.op(v, "tensor_copy", out=ns.a_opair[64 * hs:64 * hs + 64, tq * 512:(tq + 1) * 512],
                 in_=ns.ps_misc2[0:64, 0:512], inc="dv")

        if hs == 1:
            opair_tok = b.tok("dv")
            pair = h // 2
            b.wait_tok("tensor", opair_tok)
            for dc in range(DC):
                for tq in range(NT4):
                    ps = ns.ps_misc[:, 0:512]
                    b.wait("tensor", "dv", ns.a_op_guard)
                    if pair == 0:
                        b.op("tensor", "matmul", ps,
                             ns.a_outw[:, pair * 1024 + dc * 128: pair * 1024 + dc * 128 + 128],
                             ns.a_opair[:, tq * 512:(tq + 1) * 512],
                             start=True, stop=False)
                        b.op("tensor", "matmul", ps,
                             ns.a_outb[0:1, dc * 128:(dc + 1) * 128],
                             ns.ones_col[0:1, 0:512],
                             start=False, stop=True, inc="pe")
                    else:
                        b.op("tensor", "matmul", ps,
                             ns.a_outw[:, pair * 1024 + dc * 128: pair * 1024 + dc * 128 + 128],
                             ns.a_opair[:, tq * 512:(tq + 1) * 512],
                             start=True, stop=True, inc="pe")
                    peo = b.tok("pe")
                    b.wait_tok(v, peo)
                    b.op(v, "tensor_tensor", out=Xc(ns, dc, tq * 512, 512),
                         in0=Xc(ns, dc, tq * 512, 512), in1=ps, op=ALU.add, inc="dv")
                    ns.a_op_guard = b.val("dv")
        ns.a_qk_guard[hs] = b.tok("pe")
        if h + 2 < H:
            qk_next[hs] = qk_load(h + 2, hs, "qka" if hs == 0 else "qkb")



def carve_l1attn(ns):
    f32, bf, al, pos = _mk_carver(ns)
    ns.n_ztile = bf(8 * 512)
    ns.n_w = bf(3 * 8 * 1024)
    ns.n_ktile = bf(4 * 1024)
    ns.n_vtile = bf(4 * 1024)
    ns.n_stage = bf(8 * 512)
    ns.n_vk = f32(32 * 32)
    ns.n_vkT = bf(64)
    ns.n_qh = [bf(2048, rows=32), bf(2048, rows=32)]
    ns.n_rbc = f32(512)
    u0 = al(0, align=64)
    c1 = _mk_carver(ns); c1[3][0] = u0
    ns.sq = c1[0](1024)
    ns.tmp1 = c1[0](512)
    ns.bc_m = c1[0](512)
    ns.bc_a = c1[0](512)
    end1 = c1[3][0]
    c2 = _mk_carver(ns); c2[3][0] = u0
    ns.n_ohead = c2[0](2048)
    ns.n_ostage = c2[1](2048)
    end2 = c2[3][0]
    pos[0] = max(end1, end2)
    ns.n_ochunk = [ns.n_stage[:, 0:2048], ns.n_stage[:, 2048:4096]]
    ns.n_den = ns.g_row_full
    ns.n_bp = ns.row_b
    ns.l1a_bytes = pos[0]


def emit_l1attn(b, ns):
    d = ns.dram
    v = "vector"
    NH = 32
    vk_sb = ns.n_vk

    b.dma("sync", out=ns.n_bp[0:1, 0:D], in_=d["l1_bp_row"], inc="xl")
    bp_tok = b.tok("xl")
    for sh in range(2):
        st0 = sh * HALF
        rows = [ns.row_m[0:1, st0:st0 + HALF], ns.row_a[0:1, st0:st0 + HALF]]
        emit_colstats(b, ns, 0, lambda dc: Xc(ns, dc, st0, HALF),
                      lambda bb: _kinds_ln2(bb, ns, nw=8), rows, "nln", nt=8)
    rt = b.tok("m1")

    b.wait_tok("gpsimd", ns.aw_guard)
    for i, wn in enumerate(("l1_wqT", "l1_wkT", "l1_wvT")):
        b.dma("gpsimd", out=ns.n_w.rearrange("p (i dc f) -> p (i dc) f", i=3, dc=8)
              [:, i * 8:(i + 1) * 8, :],
              in_=d[wn].rearrange("(dc p) f -> p dc f", p=128), inc="wl")
    wtok = b.tok("wl")
    b.op(v, "memset", vk_sb[:, 0:1024], 0.0, inc="dv")
    vk_zero = b.tok("dv")

    for tau in range(4):
        tw = tau * 512
        b.wait(v, "pe", ns.a_xln_guard)
        emit_bcast(b, ns, ns.row_m[0:1, tw:tw + 512], ns.bc_m, 512, rt, base=0)
        t_r = emit_bcast(b, ns, ns.row_a[0:1, tw:tw + 512], ns.bc_a, 512, rt, base=32)
        b.wait_tok(v, t_r)
        for dc in range(DC):
            b.op(v, "tensor_tensor", out=ns.tmp1[:, 0:512], in0=Xc(ns, dc, tw, 512),
                 in1=ns.bc_m[:, 0:512], op=ALU.subtract)
            b.op(v, "tensor_tensor", out=ns.n_ztile[:, dc * 512:(dc + 1) * 512],
                 in0=ns.tmp1[:, 0:512], in1=ns.bc_a[:, 0:512], op=ALU.mult, inc="dv")
        z_tok = b.tok("dv")
        b.wait_tok("tensor", wtok)
        b.wait_tok("tensor", z_tok)
        for fc in range(DC):
            ps = ns.psY[:, (fc % 2) * 512:(fc % 2) * 512 + 512]
            b.wait_tok("tensor", ns.a_psY_tok[fc % 2])
            for dc in range(DC):
                b.op("tensor", "matmul", ps,
                     ns.n_w[:, dc * 1024 + fc * 128: dc * 1024 + fc * 128 + 128],
                     ns.n_ztile[:, dc * 512:(dc + 1) * 512],
                     start=(dc == 0), stop=(dc == DC - 1),
                     inc=("pe" if dc == DC - 1 else None))
            peq = b.tok("pe")
            b.wait_tok("scalar", peq)
            b.wait("scalar", "m1", ns.a_stage_guard)
            b.op("scalar", "activation", out=ns.n_stage[:, fc * 512:(fc + 1) * 512],
                 in_=ps, func=AF.Relu, inc="ac")
            ns.a_psY_tok[fc % 2] = b.tok("ac")
        st_tok = b.tok("ac")
        b.wait_tok("sync", st_tok)
        b.dma("sync", out=d["q2_dram"].rearrange("(fc p) t -> p fc t", p=128)[:, :, tw:tw + 512],
              in_=ns.n_stage.rearrange("p (fc t) -> p fc t", fc=8), inc="m1")
        ns.a_stage_guard = b.val("m1")
        for P in (1, 2):
            dst = ns.n_ktile if P == 1 else ns.n_vtile
            for nw_ in range(4):
                for fh in range(2):
                    ps = ns.psY[:, fh * 512:(fh + 1) * 512]
                    b.wait_tok("tensor", ns.a_psY_tok[fh])
                    for dc in range(DC):
                        b.op("tensor", "matmul", ps,
                             ns.n_ztile[:, dc * 512 + nw_ * 128: dc * 512 + nw_ * 128 + 128],
                             ns.n_w[:, P * 8192 + dc * 1024 + fh * 512: P * 8192 + dc * 1024 + (fh + 1) * 512],
                             start=(dc == 0), stop=(dc == DC - 1),
                             inc=("pe" if dc == DC - 1 else None))
                    pet = b.tok("pe")
                    b.wait_tok("scalar", pet)
                    b.wait_tok("scalar", ns.n_kv_guard)
                    b.op("scalar", "activation", out=dst[:, nw_ * 1024 + fh * 512: nw_ * 1024 + (fh + 1) * 512],
                         in_=ps, func=(AF.Relu if P == 1 else AF.Identity), inc="ac")
                    ns.a_psY_tok[fh] = b.tok("ac")
        kv_tok = b.tok("ac")
        b.wait_tok("tensor", kv_tok)
        if tau == 0:
            b.wait_tok("tensor", vk_zero)
        for h in range(NH):
            par = h % 2
            vkps = ns.ps4[0:32, par * 512:par * 512 + 32]
            denps = ns.ps4[0:1, 1024 + par * 512:1024 + par * 512 + 32]
            b.wait_tok("tensor", ns.n_vkps_tok[par])
            for ns_ in range(4):
                b.op("tensor", "matmul", vkps,
                     ns.n_vtile[:, ns_ * 1024 + h * 32: ns_ * 1024 + h * 32 + 32],
                     ns.n_ktile[:, ns_ * 1024 + h * 32: ns_ * 1024 + h * 32 + 32],
                     start=(ns_ == 0), stop=(ns_ == 3))
                b.op("tensor", "matmul", denps,
                     ns.ones_bf[:, 0:1],
                     ns.n_ktile[:, ns_ * 1024 + h * 32: ns_ * 1024 + h * 32 + 32],
                     start=(ns_ == 0), stop=(ns_ == 3),
                     inc=("pe" if ns_ == 3 else None))
            pevk = b.tok("pe")
            b.wait_tok(v, pevk)
            b.op(v, "tensor_tensor", out=vk_sb[0:32, h * 32:(h + 1) * 32],
                 in0=vk_sb[0:32, h * 32:(h + 1) * 32], in1=vkps,
                 op=ALU.add)
            b.op(v, "tensor_tensor", out=vk_sb[32:33, h * 32:(h + 1) * 32],
                 in0=vk_sb[32:33, h * 32:(h + 1) * 32], in1=denps,
                 op=ALU.add, inc="dv")
            ns.n_vkps_tok[par] = b.tok("dv")
        ns.n_vk_tok = b.tok("dv")
        ns.a_xln_guard = b.val("pe")
        ns.n_kv_guard = ("pe", b.val("pe"))

    b.wait("sync", "m1")
    b.barrier()

    def qh_load(h, hs, sem):
        b.wait_tok("gpsimd", ns.a_qk_guard[hs])
        b.dma("gpsimd", out=ns.n_qh[hs][0:32, :], in_=d["q2_dram"][h * 32:(h + 1) * 32, :], inc=sem)
        return b.tok(sem)

    qh_load(0, 0, "qk")
    qh_pre = qh_load(1, 1, "qk")
    qh_next = [qh_pre, qh_pre]
    for h in range(NH):
        hs = h % 2
        qh_tok = qh_next[hs]
        b.wait_tok("tensor", ns.n_vk_tok)
        b.wait("tensor", "dv", b.val("dv"))
        b.op("tensor", "transpose", ns.ps_misc2[0:32, 0:33], vk_sb[0:33, h * 32:(h + 1) * 32],
             ns.ident_f[0:33, 0:33], inc="pe")
        b.wait(v, "pe", b.val("pe"))
        b.op(v, "tensor_copy", out=ns.n_vkT[0:32, 0:33], in_=ns.ps_misc2[0:32, 0:33], inc="dv")
        vkt_tok = b.tok("dv")
        b.wait_tok("tensor", qh_tok)
        b.wait_tok("tensor", vkt_tok)
        for tau in range(4):
            ps33 = ns.psY[0:33, (tau % 2) * 512:(tau % 2) * 512 + 512]
            b.wait_tok("tensor", ns.a_psY_tok[tau % 2])
            b.op("tensor", "matmul", ps33,
                 ns.n_vkT[0:32, 0:33],
                 ns.n_qh[hs][0:32, tau * 512:(tau + 1) * 512],
                 start=True, stop=True, inc="pe")
            peo = b.tok("pe")
            b.wait_tok(v, peo)
            b.wait(v, "m1", ns.n_oh_guard)
            b.op(v, "tensor_copy", out=ns.n_ohead[0:33, tau * 512:(tau + 1) * 512],
                 in_=ps33, inc="dv")
            ns.a_psY_tok[tau % 2] = b.tok("dv")
        oh_tok = b.tok("dv")
        b.wait_tok(v, oh_tok)
        b.op(v, "tensor_scalar", out=ns.n_den[0:1, 0:T], in0=ns.n_ohead[32:33, 0:T],
             scalar1=1e-15, scalar2=None, op0=ALU.add, inc="dv")
        b.wait("scalar", "dv", b.val("dv"))
        b.op("scalar", "activation", out=ns.n_den[0:1, 0:T], in_=ns.n_den[0:1, 0:T],
             func=AF.Ln)
        b.op("scalar", "activation", out=ns.n_den[0:1, 0:T], in_=ns.n_den[0:1, 0:T],
             func=AF.Exp, scale=-1.0, inc="ac")
        r_tok = b.tok("ac")
        for tau in range(4):
            b.wait_tok("tensor", r_tok)
            b.wait("tensor", "dv", b.val("dv"))
            b.op("tensor", "matmul", ns.ps_misc[:, 0:512],
                 ns.ones_col[0:1, 0:128], ns.n_den[0:1, tau * 512:(tau + 1) * 512],
                 start=True, stop=True, inc="pe")
            b.wait(v, "pe", b.val("pe"))
            b.op(v, "tensor_copy", out=ns.n_rbc[:, 0:512], in_=ns.ps_misc[:, 0:512], inc="dv")
            b.drain(v)
            b.op(v, "tensor_tensor", out=ns.n_ostage[0:32, tau * 512:(tau + 1) * 512],
                 in0=ns.n_ohead[0:32, tau * 512:(tau + 1) * 512],
                 in1=ns.n_rbc[0:32, 0:512], op=ALU.mult, inc="dv")
        ost_tok = b.tok("dv")
        b.wait_tok("sync", ost_tok)
        b.dma("sync", out=d["o2_dram"][h * 32:(h + 1) * 32, :], in_=ns.n_ostage[0:32, 0:T], inc="m1")
        ns.n_oh_guard = b.val("m1")
        ns.a_qk_guard[hs] = b.tok("pe")
        if h + 2 < NH:
            qh_next[hs] = qh_load(h + 2, hs, "qka" if hs == 0 else "qkb")

    b.wait("sync", "m1")
    b.barrier()
    b.dma("gpsimd", out=ns.n_w.rearrange("p (i dc f) -> p (i dc) f", i=3, dc=8)[:, 0:8, :],
          in_=d["l1_wpT"].rearrange("(dc p) f -> p dc f", p=128), inc="wl")
    wp_tok = b.tok("wl")
    b.wait_tok("tensor", bp_tok)
    oc_next = [None, None]
    for oc in range(DC):
        slot = oc % 2
        b.wait_tok("gpsimd", ns.n_oc_guard[slot])
        b.dma("gpsimd", out=ns.n_ochunk[slot],
              in_=d["o2_dram"][oc * 128:(oc + 1) * 128, :],
              inc=("qka" if slot == 0 else "qkb"))
        octok = b.tok("qka" if slot == 0 else "qkb")
        b.wait_tok("tensor", wp_tok)
        b.wait_tok("tensor", octok)
        for dc in range(DC):
            for tau in range(4):
                ps = ns.ps_misc[:, 0:512]
                b.wait("tensor", "dv", ns.a_op_guard)
                wsl = ns.n_w[:, oc * 1024 + dc * 128: oc * 1024 + dc * 128 + 128]
                if oc == 0:
                    b.op("tensor", "matmul", ps, wsl,
                         ns.n_ochunk[slot][:, tau * 512:(tau + 1) * 512],
                         start=True, stop=False)
                    b.op("tensor", "matmul", ps,
                         ns.n_bp[0:1, dc * 128:(dc + 1) * 128],
                         ns.ones_col[64:65, 0:512],
                         start=False, stop=True, inc="pe")
                else:
                    b.op("tensor", "matmul", ps, wsl,
                         ns.n_ochunk[slot][:, tau * 512:(tau + 1) * 512],
                         start=True, stop=True, inc="pe")
                peo = b.tok("pe")
                b.wait_tok(v, peo)
                b.op(v, "tensor_tensor", out=Xc(ns, dc, tau * 512, 512),
                     in0=Xc(ns, dc, tau * 512, 512), in1=ps, op=ALU.add, inc="dv")
                ns.a_op_guard = b.val("dv")
        ns.n_oc_guard[slot] = b.tok("pe")



def _kinds_ln2(b, ns, nw=8):
    v = "vector"; s = "scalar"
    S = ns.scol
    P = ns.ps_misc
    b.op(v, "tensor_scalar", out=S[:, 0:nw], in0=P[:, 0:2 * nw].rearrange("p (w k) -> p k w", k=2)[:, 0, :],
         scalar1=1.0 / D, scalar2=None, op0=ALU.mult)
    b.op(v, "tensor_scalar", out=S[:, 3 * nw:4 * nw], in0=P[:, 0:2 * nw].rearrange("p (w k) -> p k w", k=2)[:, 1, :],
         scalar1=1.0 / D, scalar2=None, op0=ALU.mult)
    b.op(v, "tensor_tensor", out=S[:, 4 * nw:5 * nw], in0=S[:, 0:nw], in1=S[:, 0:nw], op=ALU.mult)
    b.op(v, "tensor_tensor", out=S[:, 5 * nw:6 * nw], in0=S[:, 3 * nw:4 * nw], in1=S[:, 4 * nw:5 * nw], op=ALU.subtract)
    b.op(v, "tensor_scalar", out=S[:, 6 * nw:7 * nw], in0=S[:, 5 * nw:6 * nw],
         scalar1=1e-5, scalar2=None, op0=ALU.add, inc="dv")
    b.wait(s, "dv", b.val("dv"))
    b.op(s, "activation", out=S[:, 7 * nw:8 * nw], in_=S[:, 6 * nw:7 * nw], func=AF.Ln)
    b.op(s, "activation", out=S[:, nw:2 * nw], in_=S[:, 7 * nw:8 * nw], func=AF.Exp, scale=-0.5, inc="ac")
    b.wait(v, "ac", b.val("ac"))
    b.op(v, "tensor_copy", out=S[:, nw:2 * nw], in_=S[:, nw:2 * nw], inc="dv")


def emit_l1ffn(b, ns):
    for half in range(2):
        _l1ffn_half(b, ns, half, half * HALF)
        b.barrier()


def _l1_stats_bcast(b, ns, src, tag):
    rows = [ns.row_m, ns.row_a]
    rt = emit_colstats(b, ns, 0, src, lambda bb: _kinds_ln2(bb, ns), rows, tag)
    t_m = emit_bcast(b, ns, ns.row_m, ns.bc_m, HALF, rt, base=0)
    t_a = emit_bcast(b, ns, ns.row_a, ns.bc_a, HALF, rt, base=32)
    return t_a


def _l1_gate(b, ns, half, t0):
    nw = 8
    b.wait_tok("tensor", ns.wg_tok)
    bgt = emit_bcast(b, ns, ns.l1_gb_row, ns.bc_g16, 8)
    ps_free = bgt
    v = "vector"
    for w in range(nw):
        tw = t0 + 128 * w
        b.wait_tok("tensor", ps_free)
        for dc in range(DC):
            b.op("tensor", "matmul", ns.ps_misc[:, 0:8],
                 ns.l1_in_bf[:, dc * HALF + 128 * w: dc * HALF + 128 * w + 128],
                 ns.l1_gw_bf[:, dc * 8:(dc + 1) * 8],
                 start=(dc == 0), stop=(dc == DC - 1), inc=("pe" if dc == DC - 1 else None))
        b.wait(v, "pe", b.val("pe"))
        L = ns.gl[:, 0:8]
        G = ns.gex
        b.op(v, "tensor_tensor", out=L, in0=ns.ps_misc[:, 0:8], in1=ns.bc_g16[:, 0:8], op=ALU.add,
             inc="dv")
        ps_free = b.tok("dv")
        b.op(v, "reduce_max", out=ns.gm[:, 0:1], in_=L, axis=AX.X, negate=True)
        b.wait("scalar", "dv", b.val("dv"))
        b.op(v, "tensor_copy", out=ns.gm[:, 1:2], in_=ns.gm[:, 0:1], inc="dv")
        b.wait("scalar", "dv", b.val("dv"))
        b.op("scalar", "activation", out=ns.gcur[:, 0:8], in_=L, func=AF.Exp, bias=ns.gm[:, 1:2], inc="ac")
        b.wait(v, "ac", b.val("ac"))
        b.op(v, "reduce_sum", out=ns.gm[:, 2:3], in_=ns.gcur[:, 0:8], axis=AX.X)
        b.op(v, "reciprocal", out=ns.gm[:, 3:4], in_=ns.gm[:, 2:3])
        b.op(v, "tensor_scalar", out=ns.gl[:, 8:16], in0=ns.gcur[:, 0:8],
             scalar1=ns.gm[:, 3:4], scalar2=None, op0=ALU.mult)
        W_ = ns.gl[:, 8:16]
        b.op(v, "reduce_max", out=ns.gm[:, 0:1], in_=W_, axis=AX.X, negate=True)
        b.op(v, "tensor_scalar", out=G[:, 0:8], in0=W_, scalar1=ns.gm[:, 0:1],
             scalar2=0.0, op0=ALU.add, op1=ALU.is_ge)
        b.op(v, "tensor_scalar", out=G[:, 8:9], in0=ns.gm[:, 0:1],
             scalar1=-1.0, scalar2=None, op0=ALU.mult)
        b.op(v, "scalar_tensor_tensor", out=ns.gcur[:, 0:8], in0=G[:, 0:8], scalar=-1e30,
             in1=W_, op0=ALU.mult, op1=ALU.add)
        b.op(v, "reduce_max", out=ns.gm[:, 1:2], in_=ns.gcur[:, 0:8], axis=AX.X, negate=True)
        b.op(v, "tensor_scalar", out=ns.gmask[:, 0:8], in0=ns.gcur[:, 0:8], scalar1=ns.gm[:, 1:2],
             scalar2=0.0, op0=ALU.add, op1=ALU.is_ge)
        b.op(v, "tensor_scalar", out=ns.gm[:, 2:3], in0=ns.gm[:, 1:2],
             scalar1=-1.0, scalar2=None, op0=ALU.mult)
        b.op(v, "tensor_scalar", out=G[:, 32:40], in0=ns.gmask[:, 0:8],
             scalar1=ns.gm[:, 2:3], scalar2=None, op0=ALU.mult, inc="dv")
        b.wait("tensor", "dv", b.val("dv"))
        b.op("tensor", "transpose", ns.ps_misc2[0:40, 0:128], G[:, 0:40], ns.ident_f[:], inc="pe")
        b.wait(v, "pe", b.val("pe"))
        lw = tw - t0
        b.op(v, "tensor_copy", out=ns.l1_oh[0:9, lw:lw + 128], in_=ns.ps_misc2[0:9, 0:128])
        b.op(v, "tensor_copy", out=ns.l1_oh[32:40, lw:lw + 128], in_=ns.ps_misc2[32:40, 0:128])
        b.op(v, "tensor_copy", out=ns.l1_ohb[0:8, lw:lw + 128], in_=ns.ps_misc2[0:8, 0:128])
        b.op(v, "tensor_copy", out=ns.l1_ohb[32:40, lw:lw + 128], in_=ns.ps_misc2[32:40, 0:128], inc="dv")
    return b.tok("dv")


def _l1_iter(b, ns, half, t0, it, gtok):
    d = ns.dram
    E = 8
    rbase = 0 if it == 0 else 32
    def w2_load(e, hc, slot, sem="wl"):
        b.wait_tok("gpsimd", ns.w2_slot_tok[slot])
        b.dma("gpsimd", out=ns.l1_w2[slot], in_=d["l1_w2p"][e, hc], inc=sem)
        return b.tok(sem)

    for e in range(E):
        b.wait_tok("gpsimd", ns.w1_slot_tok[0])
        w1v = ns.l1_w1full.rearrange("p (dc f) -> p dc f", dc=8)
        for q in range(4):
            b.dma("gpsimd",
                  out=w1v[:, :, q * 512:(q + 1) * 512],
                  in_=d["l1_w1"][e, :, q * 512:(q + 1) * 512]
                  .rearrange("(dc p) f -> p dc f", p=128), inc="wl")
        w2_load(e, 0, 0)
        w2_load(e, 1, 1)
        wtok = b.tok("wl")
        ns.w2_pre_tok = wtok

        b.wait_tok("gpsimd", gtok)
        b.wait("gpsimd", "dv", ns.gbc_guard)
        b.dma("gpsimd", out=ns.g_row_bf[0:1, 0:HALF],
              in_=ns.l1_oh[rbase + e:rbase + e + 1, 0:HALF], inc="gx")
        b.wait("tensor", "gx", b.val("gx"))
        mb_tok = emit_bcast(b, ns, ns.g_row_bf, ns.l1_ohbc, HALF, bf=True)
        ns.gbc_guard = b.val("dv")

        b.wait_tok("tensor", wtok)
        if e == 0:
            b.wait_tok("tensor", ns.l1_in_tok)
        hview = ns.l1_h.rearrange("p (fc t) -> p fc t", fc=16)
        for tau in range(2):
            for fc in range(16):
                psH = ns.psA[:, (fc % 2) * 512:(fc % 2) * 512 + 512]
                b.wait_tok("tensor", ns.psA_tok[fc % 2])
                for dc in range(DC):
                    b.op("tensor", "matmul", psH,
                         ns.l1_w1full[:, dc * 2048 + fc * 128: dc * 2048 + fc * 128 + 128],
                         ns.l1_in_bf[:, dc * HALF + tau * 512: dc * HALF + (tau + 1) * 512],
                         start=(dc == 0), stop=(dc == DC - 1),
                         inc=("pe" if dc == DC - 1 else None))
                peH = b.tok("pe")
                v = "vector"
                b.wait_tok(v, peH)
                b.wait_tok(v, ns.l1_h_guard)
                b.op(v, "tensor_scalar", out=ns.l1_hs[:, 0:512],
                     in0=psH, scalar1=ns.l1_b1[:, e * 16 + fc:e * 16 + fc + 1],
                     scalar2=0.0, op0=ALU.add, op1=ALU.max, inc="dv")
                ns.psA_tok[fc % 2] = b.tok("dv")
                b.wait_tok(v, mb_tok)
                b.op(v, "tensor_tensor",
                     out=hview[:, fc, 0:512],
                     in0=ns.l1_hs[:, 0:512], in1=ns.l1_ohbc[:, tau * 512:(tau + 1) * 512],
                     op=ALU.mult, inc="dv")
            h_tok = b.tok("dv")
            if tau == 1:
                ns.l1_w1_free = b.tok("pe")

            for hc in range(DC):
                gidx = tau * DC + hc
                w2slot = gidx % 2
                w2tok = ns.w2_pre_tok if gidx < 2 else ns.w2_cur_tok[w2slot]
                psY = ns.psY[:, (hc % 2) * 512:(hc % 2) * 512 + 512]
                b.wait_tok("tensor", ns.psY_tok[hc % 2])
                b.wait_tok("tensor", h_tok)
                b.wait_tok("tensor", w2tok)
                for ic in range(16):
                    last = (ic == 15 and e != 0)
                    b.op("tensor", "matmul", psY,
                         ns.l1_w2[w2slot][:, ic * 128: ic * 128 + 128],
                         hview[:, ic, 0:512],
                         start=(ic == 0), stop=last, inc=("pe" if last else None))
                if e == 0:
                    b.op("tensor", "matmul", psY,
                         ns.l1_b2t[rbase:rbase + 8, hc * 128:(hc + 1) * 128],
                         ns.l1_ohb[rbase:rbase + 8, tau * 512: (tau + 1) * 512],
                         start=False, stop=True, inc="pe")
                peY = b.tok("pe")
                ns.w2_slot_tok[w2slot] = peY
                if gidx + 2 < 16:
                    g2i = gidx + 2
                    ns.w2_cur_tok[w2slot] = w2_load(e, g2i % DC, w2slot,
                                                    sem=("w2a" if w2slot == 0 else "w2b"))
                b.wait_tok("vector", peY)
                if it == 0 and e == 0:
                    b.op("vector", "tensor_copy",
                         out=ns.l1_acc[:, hc * HALF + tau * 512: hc * HALF + (tau + 1) * 512],
                         in_=psY, inc="dv")
                else:
                    b.op("vector", "tensor_tensor",
                         out=ns.l1_acc[:, hc * HALF + tau * 512: hc * HALF + (tau + 1) * 512],
                         in0=ns.l1_acc[:, hc * HALF + tau * 512: hc * HALF + (tau + 1) * 512],
                         in1=psY, op=ALU.add, inc="dv")
                ns.psY_tok[hc % 2] = b.tok("dv")
            ns.l1_h_guard = b.tok("pe")
            if tau == 1:
                ns.w1_slot_tok = [ns.l1_w1_free, ns.l1_w1_free]


def _l1ffn_half(b, ns, half, t0):
    v = "vector"
    ns.l1_h_guard = None
    t_a = _l1_stats_bcast(b, ns, lambda dc: Xc(ns, dc, t0, HALF), "l1ln")
    b.wait_tok(v, t_a)
    for dc in range(DC):
        X = Xc(ns, dc, t0, HALF)
        b.op(v, "tensor_tensor", out=ns.tmp1[:, 0:HALF], in0=X, in1=ns.bc_m[:, 0:HALF], op=ALU.subtract)
        b.op(v, "tensor_tensor", out=ns.l1_in_bf[:, dc * HALF:(dc + 1) * HALF],
             in0=ns.tmp1[:, 0:HALF], in1=ns.bc_a[:, 0:HALF], op=ALU.mult, inc="dv")
    ns.l1_in_tok = b.tok("dv")

    b.wait_tok("tensor", ns.l1_in_tok)
    b.wait("tensor", "dv", b.val("dv"))
    gtok = _l1_gate(b, ns, half, t0)

    _l1_iter(b, ns, half, t0, 0, gtok)

    b.wait(v, "pe", b.val("pe"))
    t_m2 = emit_bcast(b, ns, ns.row_m, ns.bc_m, HALF, None, base=0)
    t_a2 = emit_bcast(b, ns, ns.row_a, ns.bc_a, HALF, None, base=32)
    b.wait("gpsimd", "dv", ns.gbc_guard)
    b.dma("gpsimd", out=ns.g_row_bf[0:1, 0:HALF], in_=ns.l1_oh[8:9, 0:HALF], inc="gx")
    b.wait("tensor", "gx", b.val("gx"))
    v1_tok = emit_bcast(b, ns, ns.g_row_bf, ns.l1_v1bc, HALF, bf=True)
    ns.gbc_guard = b.val("dv")
    b.wait_tok(v, t_a2)
    b.wait_tok(v, v1_tok)
    for dc in range(DC):
        X = Xc(ns, dc, t0, HALF)
        ACC = ns.l1_acc[:, dc * HALF:(dc + 1) * HALF]
        b.op(v, "tensor_tensor", out=ns.tmp1[:, 0:HALF], in0=X, in1=ns.bc_m[:, 0:HALF], op=ALU.subtract)
        b.op(v, "tensor_tensor", out=ns.tmp2[:, 0:HALF], in0=ns.tmp1[:, 0:HALF], in1=ns.bc_a[:, 0:HALF],
             op=ALU.mult)
        b.op(v, "scalar_tensor_tensor", out=ns.l1_in_bf[:, dc * HALF:(dc + 1) * HALF],
             in0=ACC, scalar=0.5, in1=ns.tmp2[:, 0:HALF], op0=ALU.mult, op1=ALU.add)
        b.op(v, "tensor_tensor", out=ACC, in0=ACC, in1=ns.l1_v1bc[:, 0:HALF], op=ALU.mult)
        b.op(v, "tensor_tensor", out=ACC, in0=ACC, in1=ns.tmp2[:, 0:HALF], op=ALU.add, inc="dv")
    ns.l1_in_tok = b.tok("dv")
    ns.l1_h_guard = None

    _l1_iter(b, ns, half, t0, 1, gtok)

    b.wait(v, "pe", b.val("pe"))
    b.wait("scalar", "pe", b.val("pe"))
    b.wait("scalar", "dv", b.val("dv"))
    b.wait("tensor", "dv", b.val("dv"))
    t_a3 = _l1_stats_bcast(b, ns, lambda dc: ns.l1_acc[:, dc * HALF:(dc + 1) * HALF], "l1ln2")
    b.wait_tok(v, t_a3)
    for dc in range(DC):
        X = Xc(ns, dc, t0, HALF)
        ACC = ns.l1_acc[:, dc * HALF:(dc + 1) * HALF]
        b.op(v, "tensor_tensor", out=ns.tmp1[:, 0:HALF], in0=ACC, in1=ns.bc_m[:, 0:HALF], op=ALU.subtract)
        b.op(v, "tensor_tensor", out=ns.tmp2[:, 0:HALF], in0=ns.tmp1[:, 0:HALF], in1=ns.bc_a[:, 0:HALF],
             op=ALU.mult, inc="dv")
        b.wait("scalar", "dv", b.val("dv"))
        b.op("scalar", "activation", out=ns.tmp1[:, 0:HALF], in_=ns.tmp2[:, 0:HALF], func=AF.Identity,
             scale=ns.l1_gam05[:, dc:dc + 1], bias=ns.l1_bet05[:, dc:dc + 1], inc="ac")
        b.wait(v, "ac", b.val("ac"))
        b.op(v, "tensor_tensor", out=X, in0=X, in1=ns.tmp1[:, 0:HALF], op=ALU.add, inc="dv")


ARENA_WORDS = 29952


def _mk_carver(ns):
    pos = [0]

    def al(nbytes, align=64):
        p = (pos[0] + align - 1) // align * align
        pos[0] = p + nbytes
        assert pos[0] <= ARENA_WORDS * 4, f"arena overflow {pos[0]}"
        return p

    def f32(n, rows=128):
        p = al(n * 4)
        return ns.AR[0:rows, p // 4: p // 4 + n]

    def bf(n, rows=128):
        p = al(n * 2)
        return ns.AR_B[0:rows, p // 2: p // 2 + n]

    return f32, bf, al, pos


def carve_l0ffn(ns):
    f32, bf, al, pos = _mk_carver(ns)
    ns.xn_bf = bf(DC * HALF)
    ns.sg_bf = bf(DC * HALF)
    ns.w1a_sb = [bf(8 * 512), bf(8 * 512)]
    ns.w1b_sb = [bf(8 * 512), bf(8 * 512)]
    ns.w2_sb = [bf(8 * 128), bf(8 * 128)]
    ns.gluA2 = [f32(512), f32(512)]
    ns.gluS2 = [f32(512), f32(512)]
    ns.gluB2 = [f32(512), f32(512)]
    ns.g_bc = bf(HALF)
    ns.bc_b = f32(HALF)
    ns.gp_T = f32(T)
    ns.gp_Tb = bf(T)
    ns.bc_m = f32(HALF)
    ns.bc_a = f32(HALF)
    ns.tmp1 = f32(HALF)
    ns.tmp2 = f32(HALF)
    ns.sq = ns.tmp2
    ns.l0_bytes = pos[0]


def carve_l1ffn(ns):
    f32, bf, al, pos = _mk_carver(ns)
    ns.l1_in_bf = bf(DC * HALF)
    ns.l1_acc = f32(DC * HALF)
    ns.l1_w1full = bf(8 * 2048)
    ns.l1_w2 = [bf(16 * 128), bf(16 * 128)]
    ns.l1_hs = f32(512)
    ns.l1_oh = f32(HALF)
    ns.l1_ohb = bf(HALF)
    ns.l1_ohbc = bf(HALF)
    ns.l1_v1bc = bf(HALF)
    p = al(16384)
    ns.bc_m = ns.AR[:, p // 4: p // 4 + HALF]
    ns.bc_a = ns.AR[:, p // 4 + HALF: p // 4 + 2 * HALF]
    ns.tmp1 = ns.AR[:, p // 4 + 2 * HALF: p // 4 + 3 * HALF]
    ns.tmp2 = ns.AR[:, p // 4 + 3 * HALF: p // 4 + 4 * HALF]
    ns.sq = ns.tmp2
    ns.l1_h = ns.AR_B[:, p // 2: p // 2 + 16 * 512]
    ns.l1_bytes = pos[0]


def build_program(g, sublayers):
    nc = bass.Bass("TRN2", target_bir_lowering=False, debug=False)
    ns = make_ns(nc, g, sublayers)
    d = ns.dram

    with ExitStack() as ctx:
        sb = lambda name, shape, dt: ctx.enter_context(nc.sbuf_tensor("s_" + name, shape, dt))
        psum = lambda name, shape: ctx.enter_context(nc.psum_tensor(name, shape, F32))

        ns.X = sb("X", [128, DC * T], F32)
        ns.ident_f = sb("ident_f", [128, 128], F32)
        ns.ones_col = sb("ones_col", [128, 512], F32)
        ns.ones_bf = sb("ones_bf", [128, 128], BF16)
        arena = sb("arena", [128, ARENA_WORDS], F32)
        ns.AR = arena[:, :]
        ns.AR_B = arena.bitcast(BF16)[:, :]

        rows1 = sb("rows1", [65, T], F32)
        rows2 = sb("rows2", [1, T], F32)
        ns.row_m = rows1[0:1, :]
        ns.row_a = rows1[32:33, :]
        ns.row_b = rows1[64:65, :]
        ns.g_row = rows2[0:1, :]
        ns.g_row_full = rows2[0:1, :]
        ns.g_row_bf = rows2.bitcast(BF16)[0:1, :]
        ns.bc_g16 = sb("bc_g16", [128, 16], F32)
        ns.scol = sb("scol", [128, 128], F32)
        ns.rowstage = sb("rowstage", [128, 128], F32)

        ns.gl = sb("gl", [128, 16], F32)
        ns.gcur = sb("gcur", [128, 16], F32)
        ns.gmask = sb("gmask", [128, 16], F32)
        ns.gm = sb("gm", [128, 4], F32)
        ns.gex = sb("gex", [128, 48], F32)

        ns.wg_bf = sb("wg_bf", [128, DC * 16], BF16)
        ns.bg_row = sb("bg_row", [1, 16], F32)[0:1, :]
        ns.l0_ns = sb("l0_ns", [128, DC], F32)
        ns.l0_b1a = sb("l0_b1a", [128, 16 * 8], F32)
        ns.l0_b1b = sb("l0_b1b", [128, 16 * 8], F32)
        ns.b2_bf = sb("b2_bf", [16, 1024], BF16)

        ns.bqi_p = sb("bqi_p", [128, DC], F32)
        ns.bki_p = sb("bki_p", [128, DC], F32)

        ns.l1_gw_bf = sb("l1_gw_bf", [128, DC * 8], BF16)
        ns.l1_gb_row = sb("l1_gb_row", [1, 8], F32)[0:1, :]
        ns.l1_b1 = sb("l1_b1", [128, 8 * 16], F32)
        ns.l1_b2t = sb("l1_b2t", [40, 1024], BF16)
        ns.l1_gam = sb("l1_gam", [128, DC], F32)
        ns.l1_bet = sb("l1_bet", [128, DC], F32)
        ns.l1_gam05 = sb("l1_gam05", [128, DC], F32)
        ns.l1_bet05 = sb("l1_bet05", [128, DC], F32)

        ns.ps4 = psum("ps4", [128, 2048])
        ns.psA = ns.ps4[:, 0:1024]
        ns.psB = ns.ps4[:, 1024:2048]
        ns.psY = psum("psY", [128, 1024])
        ns.ps_misc = psum("ps_misc", [128, 512])
        ns.ps_misc2 = psum("ps_misc2", [128, 512])

        ns.sems = {}
        for s in ("xl", "wl", "w2a", "w2b", "qk", "qka", "qkb", "gx", "m1", "m2", "pe", "dv", "ac", "gp", "bar", "st"):
            ns.sems[s] = ctx.enter_context(nc.semaphore(s))

        block = ctx.enter_context(nc.Block())
        b = Builder(ns)

        def program(bb):
            ns.w1_slot_idx = 0
            ns.w2_slot_idx = 0
            ns.w1_slot_tok = [None, None]
            ns.w2_slot_tok = [None, None]
            ns.w1_load_tok = {}
            ns.w2_load_tok = {}
            ns.psA_tok = [None, None]
            ns.psB_tok = [None, None]
            ns.psY_tok = [None, None]
            ns.gbc_guard = 0
            ns.sg_guard = None
            ns.glu_par = 0
            ns.glu_free = [0, 0]
            ns.w2_cur_tok = [None, None]
            ns.w2_pre_tok = None
            ns.aw_guard = None
            ns.a_xln_guard = 0
            ns.a_pbf_guard = 0
            ns.a_cs_guard = [0, 0]
            ns.a_ps_tok = [None, None]
            ns.a_psY_tok = [None, None]
            ns.a_stage_guard = 0
            ns.a_p1_tok = [None] * 4
            ns.a_p2_tok = [None, None]
            ns.a_expT_guard = [None] * 8
            ns.a_qk_guard = [None, None]
            ns.a_op_guard = 0
            ns.bvi_tok = None
            ns.n_kv_guard = None
            ns.n_vk_tok = None
            ns.n_oh_guard = 0
            ns.n_oc_guard = [None, None]
            ns.n_vkps_tok = [None, None]
            ns.l1_w1_free = None
            for e in Builder.AUTODRAIN:
                setattr(bb, "_dirty_" + e, False)

            emit_load_consts(bb, ns)
            emit_load_x(bb, ns)
            need_l0 = "l0ffn" in sublayers
            need_l1 = "l1ffn" in sublayers
            if need_l0:
                bb.dma("gpsimd", out=ns.wg_bf.rearrange("p (dc e) -> p dc e", dc=DC),
                       in_=d["l0_wg"].rearrange("(dc p) e -> p dc e", p=128), inc="wl")
                bb.dma("sync", out=ns.bg_row, in_=d["l0_bg_row"], inc="xl")
                bb.dma("sync", out=ns.l0_ns[:], in_=d["l0_ns_p"], inc="xl")
                bb.dma("sync", out=ns.l0_b1a[:], in_=d["l0_b1a_p"], inc="xl")
                bb.dma("sync", out=ns.l0_b1b[:], in_=d["l0_b1b_p"], inc="xl")
                bb.dma("gpsimd", out=ns.b2_bf[:], in_=d["l0_b2"], inc="wl")
            if "l0attn" in sublayers:
                bb.dma("sync", out=ns.bqi_p[:], in_=d["bqi_p"], inc="xl")
                bb.dma("sync", out=ns.bki_p[:], in_=d["bki_p"], inc="xl")
            if need_l1:
                bb.dma("gpsimd", out=ns.l1_gw_bf.rearrange("p (dc e) -> p dc e", dc=DC),
                       in_=d["l1_gw"].rearrange("(dc p) e -> p dc e", p=128), inc="wl")
                bb.dma("sync", out=ns.l1_gb_row, in_=d["l1_gb_row"], inc="xl")
                bb.dma("sync", out=ns.l1_b1[:], in_=d["l1_b1_p"], inc="xl")
                bb.dma("gpsimd", out=ns.l1_b2t[0:8, :], in_=d["l1_b2"], inc="wl")
                bb.dma("gpsimd", out=ns.l1_b2t[32:40, :], in_=d["l1_b2"], inc="wl")
                bb.dma("sync", out=ns.l1_gam[:], in_=d["l1_gam_p"], inc="xl")
                bb.dma("sync", out=ns.l1_bet[:], in_=d["l1_bet_p"], inc="xl")
            ns.wg_tok = ("wl", bb.val("wl"))
            bb.wait("vector", "xl")
            bb.wait("vector", "wl")
            bb.wait("tensor", "xl")
            bb.wait_tok("tensor", ns.ones_tok)
            bb.wait("tensor", "wl")
            bb.wait("scalar", "xl")
            if need_l1:
                bb.op("vector", "tensor_scalar", out=ns.l1_gam05[:], in0=ns.l1_gam[:],
                      scalar1=0.5, scalar2=None, op0=ALU.mult)
                bb.op("vector", "tensor_scalar", out=ns.l1_bet05[:], in0=ns.l1_bet[:],
                      scalar1=0.5, scalar2=None, op0=ALU.mult, inc="dv")
            bb.barrier()

            for sl in sublayers:
                if sl == "l0ffn":
                    carve_l0ffn(ns)
                    emit_l0ffn(bb, ns)
                elif sl == "l1ffn":
                    carve_l1ffn(ns)
                    emit_l1ffn(bb, ns)
                elif sl == "l0attn":
                    carve_l0attn(ns)
                    emit_l0attn(bb, ns)
                elif sl == "l1attn":
                    carve_l1attn(ns)
                    emit_l1attn(bb, ns)
                else:
                    raise NotImplementedError(sl)
                bb.barrier()

            emit_store_x(bb, ns)

        for name in ENGINES:
            def body(engine, _name=name):
                b.begin(_name, engine)
                program(b)
            getattr(block, name)(body)

    return nc


def _run(nc, in_maps, **kw):
    return run_bass_kernel_spmd(nc, in_maps, core_ids=list(range(NCORES)), **kw)



def kernel(**inputs):
    g, xs = prep_host(inputs)
    seq = ["l0ffn", "l0attn", "l0ffn", "l1ffn", "l1attn", "l1ffn"]
    nc = build_program(g, seq)
    in_maps = [dict(g, xT=xs[b]) for b in range(NCORES)]
    res = run_bass_kernel_spmd(nc, in_maps, core_ids=list(range(NCORES)))
    out = np.stack([np.asarray(r["yT"]).T for r in res.results])
    return np.ascontiguousarray(out, dtype=np.float32)


# version 06
# speedup vs baseline: 1.1356x; 1.0152x over previous
import numpy as np
import ml_dtypes
from contextlib import ExitStack

import concourse.bass as bass
import concourse.mybir as mybir
from concourse.bass_utils import run_bass_kernel_spmd

F32 = mybir.dt.float32
BF16 = mybir.dt.bfloat16
AF = mybir.ActivationFunctionType
ALU = mybir.AluOpType
AX = mybir.AxisListType
BFNP = ml_dtypes.bfloat16

T = 2048
D = 1024
DC = 8
HALF = 1024
NCORES = 8

ENGINES = ("sync", "scalar", "vector", "tensor", "gpsimd")



class Builder:

    def __init__(self, ns):
        self.ns = ns
        self.cur = None
        self.eng = None
        self.cnt = {}

    def begin(self, engine_name, engine_obj):
        self.cur = engine_name
        self.eng = engine_obj
        self.cnt = {k: 0 for k in self.ns.sems}

    def val(self, sem):
        return self.cnt[sem]

    def tok(self, sem):
        return (sem, self.cnt[sem])

    def wait(self, eng, sem, val=None):
        if val is None:
            val = self.cnt[sem]
        if self.cur == eng and val > 0:
            self.eng.wait_ge(self.ns.sems[sem], val)

    def wait_tok(self, eng, t):
        if t is not None:
            self.wait(eng, t[0], t[1])

    AUTODRAIN = ("vector", "scalar", "gpsimd")

    def op(self, eng, method, *args, inc=None, inc_n=1, **kw):
        if inc is not None:
            self.cnt[inc] += inc_n
        if self.cur != eng:
            return None
        if eng in self.AUTODRAIN and method not in ("dma_start", "nop"):
            if getattr(self, "_dirty_" + eng, False):
                self.eng.drain()
            setattr(self, "_dirty_" + eng, True)
        ins = getattr(self.eng, method)(*args, **kw)
        if inc is not None:
            ins.then_inc(self.ns.sems[inc], inc_n)
        return ins

    def dma(self, eng, out, in_, inc, **kw):
        return self.op(eng, "dma_start", inc=inc, inc_n=16, out=out, in_=in_, **kw)

    def drain(self, eng):
        if self.cur == eng:
            self.eng.drain()

    def mark(self, label):
        if self.cur is not None:
            self.ns.marks.setdefault(self.cur, []).append((label, self.ns.nc.next_id()))

    def barrier(self):
        self.cnt["bar"] += 5
        tgt = self.cnt["bar"]
        if self.cur is not None:
            if self.cur != "sync":
                self.eng.drain()
            self.eng.nop().then_inc(self.ns.sems["bar"], 1)
            self.eng.wait_ge(self.ns.sems["bar"], tgt)
            for e in self.AUTODRAIN:
                setattr(self, "_dirty_" + e, False)


class NS:
    pass



def _rope_tables():
    i = np.arange(D // 2, dtype=np.float32)
    theta = (np.float32(10000.0) ** (np.float32(-2.0) * (i - np.float32(1.0)) / np.float32(D))).astype(np.float32)
    ang = (np.arange(T, dtype=np.float32)[:, None] * theta[None, :]).astype(np.float32)
    c = np.cos(ang).astype(np.float32)
    s = np.sin(ang).astype(np.float32)
    return np.ascontiguousarray(c.T), np.ascontiguousarray(s.T)


def _causal_masks():
    qq = np.arange(128)[:, None]; kk = np.arange(512)[None, :]
    m1 = np.stack([(kk <= j * 128 + qq) for j in range(4)]).astype(np.float32)
    m1 = (1.0 - m1) * np.float32(-1e30)
    kk2 = np.arange(128)[:, None]; qq2 = np.arange(512)[None, :]
    m2 = np.stack([(kk2 <= qq2 - j * 128) for j in range(4)]).astype(np.float32)
    m2 = (1.0 - m2) * np.float32(-1e30)
    return np.ascontiguousarray(m1), np.ascontiguousarray(m2)


def _part_layout(v):
    E, L = v.shape
    n = L // 128
    return np.ascontiguousarray(v.reshape(E, n, 128).transpose(2, 0, 1).reshape(128, E * n))


def prep_host(inputs):
    g = {}
    f32 = np.float32
    asc = np.ascontiguousarray

    perm = np.concatenate([np.arange(0, D, 2), np.arange(1, D, 2)])
    g["wq_t"] = asc(np.asarray(inputs["L0_attn_wq"], f32).T[:, perm])
    g["wk_t"] = asc(np.asarray(inputs["L0_attn_wk"], f32).T[:, perm])
    g["wv_t"] = asc(np.asarray(inputs["L0_attn_wv"], f32).T[:, perm])
    ipw = np.asarray(inputs["L0_attn_in_proj_w"], f32)
    ipb = np.asarray(inputs["L0_attn_in_proj_b"], f32)
    g["wqi_t"] = asc(ipw[0:D].T[perm, :])
    g["wki_t"] = asc(ipw[D:2 * D].T[perm, :])
    g["wvi_t"] = asc(ipw[2 * D:3 * D].T[perm, :])
    g["bqi_p"] = _part_layout(ipb[0:D][None, :])
    g["bki_p"] = _part_layout(ipb[D:2 * D][None, :])
    g["bvi_p"] = _part_layout(ipb[2 * D:3 * D][None, :])
    g["outw_t"] = asc(np.asarray(inputs["L0_attn_out_w"], f32).T)
    g["bvi_row"] = asc(ipb[2 * D:3 * D][None, :])
    g["outb_row"] = asc(np.asarray(inputs["L0_attn_out_b"], f32)[None, :])
    g["rope_cos"], g["rope_sin"] = _rope_tables()
    g["mask1"], g["mask2"] = _causal_masks()

    g["l0_ns_p"] = _part_layout(np.asarray(inputs["L0_ffn_norm_scale"], f32)[None, :])
    g["l0_wg"] = asc(np.asarray(inputs["L0_ffn_wg"], f32))
    g["l0_bg_row"] = asc(np.asarray(inputs["L0_ffn_bg"], f32)[None, :])
    w1 = np.asarray(inputs["L0_ffn_w1"], f32)
    g["l0_w1a"] = asc(w1[:, :, 0::2])
    g["l0_w1b"] = asc(w1[:, :, 1::2])
    b1 = np.asarray(inputs["L0_ffn_b1"], f32)
    g["l0_b1a_p"] = _part_layout(asc(b1[:, 0::2]))
    g["l0_b1b_p"] = _part_layout(asc(b1[:, 1::2]))
    w2l0 = np.asarray(inputs["L0_ffn_w2"], f32)
    g["l0_w2p"] = asc(w2l0.reshape(16, 8, 128, 8, 128).transpose(0, 3, 2, 1, 4)
                      .reshape(16, 8, 128, 8 * 128))
    g["l0_b2"] = asc(np.asarray(inputs["L0_ffn_b2"], f32))

    wqkv = np.asarray(inputs["L1_attn_W_qkv"], f32)
    g["l1_wqT"] = asc(wqkv[0:D].T)
    g["l1_wkT"] = asc(wqkv[D:2 * D].T)
    g["l1_wvT"] = asc(wqkv[2 * D:3 * D].T)
    g["l1_wpT"] = asc(np.asarray(inputs["L1_attn_W_proj"], f32).T)
    g["l1_bp_row"] = asc(np.asarray(inputs["L1_attn_b_proj"], f32)[None, :])

    g["l1_gw"] = asc(np.asarray(inputs["L1_ffn_gate_W"], f32))
    g["l1_gb_row"] = asc(np.asarray(inputs["L1_ffn_gate_b"], f32)[None, :])
    g["l1_w1"] = asc(np.asarray(inputs["L1_ffn_W1"], f32))
    g["l1_b1_p"] = _part_layout(np.asarray(inputs["L1_ffn_b1"], f32))
    w2l1 = np.asarray(inputs["L1_ffn_W2"], f32)
    g["l1_w2p"] = asc(w2l1.reshape(8, 16, 128, 8, 128).transpose(0, 3, 2, 1, 4)
                      .reshape(8, 8, 128, 16 * 128))
    g["l1_b2"] = asc(np.asarray(inputs["L1_ffn_b2"], f32))
    g["l1_gam_p"] = _part_layout(np.asarray(inputs["L1_ffn_ln_gamma"], f32)[None, :])
    g["l1_bet_p"] = _part_layout(np.asarray(inputs["L1_ffn_ln_beta"], f32)[None, :])

    g["ident_f"] = np.eye(128, dtype=f32)

    x = np.asarray(inputs["L0_attn_x"], f32)
    xT = np.ascontiguousarray(x.transpose(0, 2, 1))
    return g, [np.ascontiguousarray(xT[b]) for b in range(8)]



def Xc(ns, dc, t0=0, tlen=T):
    return ns.X[:, dc * T + t0: dc * T + t0 + tlen]


def make_ns(nc, g, sublayers):
    ns = NS()
    ns.nc = nc
    ns.marks = {}
    ns.g = g
    d = {}

    def din(name, arr):
        d[name] = nc.dram_tensor(name, list(arr.shape), F32, kind="ExternalInput").ap()
        return d[name]

    ns.dram = d
    d["xT"] = nc.dram_tensor("xT", [D, T], F32, kind="ExternalInput").ap()
    d["yT"] = nc.dram_tensor("yT", [D, T], F32, kind="ExternalOutput").ap()
    for k, v in g.items():
        din(k, v)
    d["row_dram"] = nc.dram_tensor("row_dram", [128, 128], F32, kind="Internal").ap()
    d["qh_dram"] = nc.dram_tensor("qh_dram", [D, T], BF16, kind="Internal").ap()
    d["kh_dram"] = nc.dram_tensor("kh_dram", [D, T], BF16, kind="Internal").ap()
    d["q2_dram"] = nc.dram_tensor("q2_dram", [D, T], BF16, kind="Internal").ap()
    d["o2_dram"] = nc.dram_tensor("o2_dram", [D, T], BF16, kind="Internal").ap()
    return ns



def emit_load_x(b, ns):
    for dc in range(DC):
        b.dma("sync", out=Xc(ns, dc), in_=ns.dram["xT"][dc * 128:(dc + 1) * 128, :], inc="xl")
    ns.x_tok = b.tok("xl")


def emit_store_x(b, ns):
    b.barrier()
    for dc in range(DC):
        b.dma("scalar", out=ns.dram["yT"][dc * 128:(dc + 1) * 128, :], in_=Xc(ns, dc), inc="st")
    b.wait("scalar", "st")


def emit_load_consts(b, ns):
    b.op("vector", "memset", ns.ones_col[:], 1.0)
    b.op("vector", "memset", ns.ones_bf[:], 1.0, inc="dv")
    b.dma("sync", out=ns.ident_f[:], in_=ns.dram["ident_f"], inc="xl")
    ns.const_tok = b.tok("xl")
    ns.ones_tok = b.tok("dv")



def emit_colstats(b, ns, half, src, kinds_fn, out_rows, tag, nt=8):
    n_rows = len(out_rows)
    nw = nt
    prev_mm = None
    for dc in range(DC):
        b.wait_tok("scalar", prev_mm)
        b.op("scalar", "activation", out=ns.sq[:], in_=src(dc), func=AF.Square, inc="ac")
        b.wait("tensor", "ac", b.val("ac"))
        if dc == 0:
            b.wait("tensor", "dv", b.val("dv"))
        for w in range(nw):
            b.op("tensor", "matmul", ns.ps_misc[:, 2 * w:2 * w + 1],
                 src(dc)[:, 128 * w:128 * w + 128], ns.ones_col[:, 0:1],
                 start=(dc == 0 and w == 0), stop=False)
            b.op("tensor", "matmul", ns.ps_misc[:, 2 * w + 1:2 * w + 2],
                 ns.sq[:, 128 * w:128 * w + 128], ns.ones_col[:, 0:1],
                 start=False, stop=(dc == DC - 1 and w == nw - 1),
                 inc=("pe" if (w == nw - 1) else None))
        prev_mm = b.tok("pe")
    pe_done = b.tok("pe")
    b.wait_tok("vector", pe_done)
    b.wait_tok("scalar", pe_done)
    kinds_fn(b)
    b.wait("tensor", "dv", b.val("dv"))
    b.wait("tensor", "ac", b.val("ac"))
    b.op("tensor", "transpose", ns.ps_misc[0:n_rows * nw, 0:128],
         ns.scol[:, 0:n_rows * nw], ns.ident_f[:], inc="pe")
    b.wait("vector", "pe", b.val("pe"))
    b.wait("vector", "m1", b.val("m1"))
    b.op("vector", "tensor_copy", out=ns.rowstage[0:n_rows * nw, 0:128],
         in_=ns.ps_misc[0:n_rows * nw, 0:128], inc="dv")
    b.wait("sync", "dv", b.val("dv"))
    b.dma("sync", out=ns.dram["row_dram"][0:n_rows * nw, 0:128],
          in_=ns.rowstage[0:n_rows * nw, 0:128], inc="m1")
    b.wait("sync", "m1", b.val("m1"))
    for r in range(n_rows):
        b.dma("sync", out=out_rows[r][0:1, 0:nw * 128],
              in_=ns.dram["row_dram"][r * nw:(r + 1) * nw, 0:128]
              .rearrange("a c -> (a c)")[None, :], inc="m1")
    return b.tok("m1")


def emit_bcast(b, ns, row_ap, out_bc, n, wait_t=None, base=0, bf=False):
    nblk = (n + 511) // 512
    for j in range(nblk):
        w = min(512, n - j * 512)
        b.wait_tok("tensor", wait_t)
        b.wait("tensor", "dv", b.val("dv"))
        ones = ns.ones_bf if bf else ns.ones_col
        b.op("tensor", "matmul", ns.ps_misc[:, 0:w],
             ones[base:base + 1, 0:128], row_ap[0:1, j * 512:j * 512 + w],
             start=True, stop=True, inc="pe")
        b.wait("vector", "pe", b.val("pe"))
        b.op("vector", "tensor_copy", out=out_bc[:, j * 512:j * 512 + w],
             in_=ns.ps_misc[:, 0:w], inc="dv")
    return b.tok("dv")



def emit_l0ffn(b, ns):
    g = ns.g
    E = 16
    for half in range(2):
        t0 = half * HALF
        _l0ffn_half(b, ns, half, t0, E)
        b.barrier()


def _l0ffn_half(b, ns, half, t0, E):
    b.mark(f"l0ffn.stats")
    d = ns.dram

    def kinds(bb):
        v = "vector"; s = "scalar"
        nw = 8
        S = ns.scol
        P = ns.ps_misc
        bb.op(v, "tensor_scalar", out=S[:, 0:nw], in0=P[:, 0:16].rearrange("p (w k) -> p k w", k=2)[:, 0, :],
              scalar1=1.0 / D, scalar2=None, op0=ALU.mult)
        bb.op(v, "tensor_scalar", out=S[:, 24:24 + nw], in0=P[:, 0:16].rearrange("p (w k) -> p k w", k=2)[:, 1, :],
              scalar1=1.0 / D, scalar2=None, op0=ALU.mult)
        bb.drain(v)
        bb.op(v, "tensor_tensor", out=S[:, 32:32 + nw], in0=S[:, 0:nw], in1=S[:, 0:nw], op=ALU.mult)
        bb.drain(v)
        bb.op(v, "tensor_tensor", out=S[:, 40:40 + nw], in0=S[:, 24:24 + nw], in1=S[:, 32:32 + nw], op=ALU.subtract)
        bb.drain(v)
        bb.op(v, "tensor_scalar", out=S[:, 48:48 + nw], in0=S[:, 40:40 + nw],
              scalar1=1e-5, scalar2=None, op0=ALU.add, inc="dv")
        bb.wait(s, "dv", bb.val("dv"))
        bb.op(s, "activation", out=S[:, 56:56 + nw], in_=S[:, 48:48 + nw], func=AF.Ln)
        bb.drain(s)
        bb.op(s, "activation", out=S[:, 8:8 + nw], in_=S[:, 56:56 + nw], func=AF.Exp, scale=-0.5, inc="ac")
        bb.wait(v, "ac", bb.val("ac"))
        bb.op(v, "reciprocal", out=S[:, 32:32 + nw], in_=S[:, 48:48 + nw])
        bb.drain(v)
        bb.op(v, "tensor_tensor", out=S[:, 40:40 + nw], in0=S[:, 40:40 + nw], in1=S[:, 32:32 + nw], op=ALU.mult)
        bb.drain(v)
        bb.op(v, "tensor_scalar", out=S[:, 40:40 + nw], in0=S[:, 40:40 + nw],
              scalar1=1e-5, scalar2=None, op0=ALU.add, inc="dv")
        bb.wait(s, "dv", bb.val("dv"))
        bb.op(s, "activation", out=S[:, 48:48 + nw], in_=S[:, 40:40 + nw], func=AF.Ln)
        bb.drain(s)
        bb.op(s, "activation", out=S[:, 48:48 + nw], in_=S[:, 48:48 + nw], func=AF.Exp, scale=-0.5, inc="ac")
        bb.wait(v, "ac", bb.val("ac"))
        bb.op(v, "tensor_tensor", out=S[:, 16:16 + nw], in0=S[:, 8:8 + nw], in1=S[:, 48:48 + nw], op=ALU.mult)
        bb.drain(v)
        bb.op(v, "tensor_scalar", out=S[:, 8:8 + nw], in0=S[:, 8:8 + nw],
              scalar1=0.5, scalar2=None, op0=ALU.mult, inc="dv")

    rows = [ns.row_m, ns.row_a, ns.row_b]
    rows_tok = emit_colstats(b, ns, half, lambda dc: Xc(ns, dc, t0, HALF), kinds, rows, "l0ln")

    t_m = emit_bcast(b, ns, ns.row_m, ns.bc_m, HALF, rows_tok, base=0)
    t_a = emit_bcast(b, ns, ns.row_a, ns.bc_a, HALF, rows_tok, base=32)
    t_b = emit_bcast(b, ns, ns.row_b, ns.bc_b, HALF, rows_tok, base=64)

    b.wait_tok("vector", t_b)
    for dc in range(DC):
        X = Xc(ns, dc, t0, HALF)
        t1 = ns.tmp1[:, 0:HALF]
        b.wait("vector", "ac", b.val("ac"))
        b.op("vector", "tensor_tensor", out=t1, in0=X, in1=ns.bc_m[:, 0:HALF], op=ALU.subtract)
        b.drain("vector")
        b.op("vector", "tensor_tensor", out=ns.tmp2[:, 0:HALF], in0=t1, in1=ns.bc_a[:, 0:HALF], op=ALU.mult)
        b.drain("vector")
        b.op("vector", "tensor_tensor", out=X, in0=X, in1=ns.tmp2[:, 0:HALF], op=ALU.add)
        b.drain("vector")
        b.op("vector", "tensor_tensor", out=ns.tmp2[:, 0:HALF], in0=t1, in1=ns.bc_b[:, 0:HALF], op=ALU.mult,
             inc="dv")
        b.wait("scalar", "dv", b.val("dv"))
        b.op("scalar", "activation", out=ns.xn_bf[:, dc * HALF:(dc + 1) * HALF], in_=ns.tmp2[:, 0:HALF],
             func=AF.Identity, scale=ns.l0_ns[:, dc:dc + 1], inc="ac")
    xn_tok = b.tok("ac")

    b.mark("l0ffn.gate")
    b.wait_tok("tensor", xn_tok)
    b.wait("tensor", "dv", b.val("dv"))
    gtok = _l0_gate(b, ns, half, t0)

    b.mark("l0ffn.experts")
    _l0_experts(b, ns, half, t0, gtok)


def _l0_gate(b, ns, half, t0):
    d = ns.dram
    nw = 8
    b.wait_tok("tensor", ns.wg_tok)
    bgt = emit_bcast(b, ns, ns.bg_row, ns.bc_g16, 16)
    ps_free = bgt
    for w in range(nw):
        tw = t0 + 128 * w
        b.wait_tok("tensor", ps_free)
        for dc in range(DC):
            b.op("tensor", "matmul", ns.ps_misc[:, 0:16],
                 ns.xn_bf[:, dc * HALF + 128 * w: dc * HALF + 128 * w + 128],
                 ns.wg_bf[:, dc * 16:(dc + 1) * 16],
                 start=(dc == 0), stop=(dc == DC - 1), inc=("pe" if dc == DC - 1 else None))
        b.wait("vector", "pe", b.val("pe"))
        v = "vector"
        L = ns.gl
        CUR = ns.gcur
        b.op(v, "tensor_tensor", out=L[:], in0=ns.ps_misc[:, 0:16], in1=ns.bc_g16[:, 0:16], op=ALU.add,
             inc="dv")
        ps_free = b.tok("dv")
        b.drain(v)
        b.op(v, "tensor_copy", out=CUR[:], in_=L[:])
        b.drain(v)
        b.op(v, "reduce_max", out=ns.gm[:, 0:1], in_=CUR[:], axis=AX.X, negate=True)
        b.drain(v)
        for j in range(3):
            b.op(v, "tensor_scalar", out=ns.gmask[:], in0=CUR[:], scalar1=ns.gm[:, 0:1],
                 scalar2=0.0, op0=ALU.add, op1=ALU.is_ge)
            b.drain(v)
            b.op(v, "scalar_tensor_tensor", out=CUR[:], in0=ns.gmask[:], scalar=-1e30,
                 in1=CUR[:], op0=ALU.mult, op1=ALU.add)
            b.drain(v)
            b.op(v, "reduce_max", out=ns.gm[:, 0:1], in_=CUR[:], axis=AX.X, negate=True)
            b.drain(v)
        b.op(v, "tensor_scalar", out=ns.gmask[:], in0=L[:], scalar1=ns.gm[:, 0:1],
             scalar2=0.0, op0=ALU.add, op1=ALU.is_ge)
        b.op(v, "reduce_max", out=ns.gm[:, 1:2], in_=L[:], axis=AX.X, negate=True, inc="dv")
        b.wait("scalar", "dv", b.val("dv"))
        GX = ns.gex[:, 0:16]
        b.op("scalar", "activation", out=GX, in_=L[:], func=AF.Exp, bias=ns.gm[:, 1:2], inc="ac")
        b.wait(v, "ac", b.val("ac"))
        b.op(v, "tensor_tensor", out=GX, in0=GX, in1=ns.gmask[:], op=ALU.mult)
        b.drain(v)
        b.op(v, "reduce_sum", out=ns.gm[:, 2:3], in_=GX, axis=AX.X)
        b.drain(v)
        b.op(v, "reciprocal", out=ns.gm[:, 3:4], in_=ns.gm[:, 2:3])
        b.drain(v)
        b.op(v, "tensor_scalar", out=GX, in0=GX, scalar1=ns.gm[:, 3:4],
             scalar2=0.5, op0=ALU.mult, op1=ALU.mult, inc="dv")
        b.wait("tensor", "dv", b.val("dv"))
        b.op("tensor", "transpose", ns.ps_misc2[0:16, 0:128], GX, ns.ident_f[:], inc="pe")
        b.wait(v, "pe", b.val("pe"))
        b.op(v, "tensor_copy", out=ns.gp_T[0:16, tw:tw + 128], in_=ns.ps_misc2[0:16, 0:128])
        b.op(v, "tensor_copy", out=ns.gp_Tb[0:16, tw:tw + 128], in_=ns.ps_misc2[0:16, 0:128], inc="dv")
    return b.tok("dv")


def _l0_w2_load(b, ns, e, hc, slot, sem):
    d = ns.dram
    b.wait_tok("gpsimd", ns.w2_slot_tok[slot])
    b.dma("gpsimd", out=ns.w2_sb[slot], in_=d["l0_w2p"][e, hc], inc=sem)
    return b.tok(sem)


def _l0_experts(b, ns, half, t0, gtok):
    d = ns.dram
    E = 16
    NFH = 2
    for e in range(E):
        for fh in range(NFH):
            slot = ns.w1_slot_idx
            ns.w1_slot_idx = (ns.w1_slot_idx + 1) % 2
            free_tok = ns.w1_slot_tok[slot]
            b.wait_tok("gpsimd", free_tok)
            b.dma("gpsimd",
                  out=ns.w1a_sb[slot].rearrange("p (dc f) -> p dc f", dc=8),
                  in_=d["l0_w1a"][e, :, fh * 512:(fh + 1) * 512]
                  .rearrange("(dc p) f -> p dc f", p=128), inc="wl")
            b.dma("gpsimd",
                  out=ns.w1b_sb[slot].rearrange("p (dc f) -> p dc f", dc=8),
                  in_=d["l0_w1b"][e, :, fh * 512:(fh + 1) * 512]
                  .rearrange("(dc p) f -> p dc f", p=128), inc="wl")
        for hc0 in range(2):
            _l0_w2_load(b, ns, e, hc0, hc0, "wl")
        wtok = b.tok("wl")
        for fh in range(NFH):
            ns.w1_load_tok[(e, fh)] = wtok
        ns.w2_pre_tok_l0 = wtok
        ns.w2_cur_tok_l0 = [None, None]

        b.wait_tok("sync", gtok)
        b.wait("sync", "dv", ns.gbc_guard)
        b.dma("sync", out=ns.g_row[0:1, 0:HALF], in_=ns.gp_T[e:e + 1, t0:t0 + HALF], inc="m1")
        b.wait("tensor", "m1", b.val("m1"))
        gb_tok = emit_bcast(b, ns, ns.g_row, ns.g_bc, HALF)
        ns.gbc_guard = b.val("dv")

        for fh in range(NFH):
            w1slot = (e * NFH + fh) % 2
            b.wait_tok("tensor", ns.w1_load_tok[(e, fh)])
            if e == 0 and fh == 0:
                b.wait_tok("tensor", b.tok("ac"))
            for c in range(4):
                ic = fh * 4 + c
                for tau in range(2):
                    psA = ns.psA[:, tau * 512:(tau + 1) * 512]
                    psB = ns.psB[:, tau * 512:(tau + 1) * 512]
                    b.wait_tok("tensor", ns.psA_tok[tau])
                    b.wait_tok("tensor", ns.psB_tok[tau])
                    for dc in range(DC):
                        b.op("tensor", "matmul", psA,
                             ns.w1a_sb[w1slot][:, dc * 512 + c * 128: dc * 512 + c * 128 + 128],
                             ns.xn_bf[:, dc * HALF + tau * 512: dc * HALF + (tau + 1) * 512],
                             start=(dc == 0), stop=(dc == DC - 1),
                             inc=("pe" if dc == DC - 1 else None))
                    peA = b.tok("pe")
                    for dc in range(DC):
                        b.op("tensor", "matmul", psB,
                             ns.w1b_sb[w1slot][:, dc * 512 + c * 128: dc * 512 + c * 128 + 128],
                             ns.xn_bf[:, dc * HALF + tau * 512: dc * HALF + (tau + 1) * 512],
                             start=(dc == 0), stop=(dc == DC - 1),
                             inc=("pe" if dc == DC - 1 else None))
                    peB = b.tok("pe")
                    v = "vector"
                    par = ns.glu_par
                    ns.glu_par ^= 1
                    gluA = ns.gluA2[par]; gluS = ns.gluS2[par]; gluB = ns.gluB2[par]
                    b.wait_tok(v, peA)
                    b.wait(v, "gp", ns.glu_free[par])
                    b.op(v, "tensor_scalar", out=gluA[:], in0=psA,
                         scalar1=ns.l0_b1a[:, e * 8 + ic:e * 8 + ic + 1], scalar2=7.0,
                         op0=ALU.add, op1=ALU.min, inc="dv")
                    ns.psA_tok[tau] = b.tok("dv")
                    b.wait("scalar", "dv", b.val("dv"))
                    b.op("scalar", "activation", out=gluS[:], in_=gluA[:],
                         func=AF.Sigmoid, scale=1.702, inc="ac")
                    b.wait_tok("scalar", peB)
                    b.op("scalar", "activation", out=gluB[:], in_=psB,
                         func=AF.Identity, bias=ns.l0_b1b[:, e * 8 + ic:e * 8 + ic + 1], inc="ac")
                    ns.psB_tok[tau] = b.tok("ac")
                    b.wait(v, "ac", b.val("ac"))
                    b.op(v, "tensor_scalar", out=gluB[:], in0=gluB[:],
                         scalar1=7.0, scalar2=-7.0, op0=ALU.min, op1=ALU.max, inc="dv")
                    gp = "gpsimd"
                    b.wait(gp, "dv", b.val("dv"))
                    b.wait(gp, "ac", b.val("ac"))
                    b.op(gp, "tensor_tensor", out=gluA[:], in0=gluA[:], in1=gluS[:], op=ALU.mult,
                         inc="gp")
                    b.wait(v, "gp", b.val("gp"))
                    b.op(v, "scalar_tensor_tensor", out=gluB[:], in0=gluB[:], scalar=1.0,
                         in1=gluA[:], op0=ALU.add, op1=ALU.mult, inc="dv")
                    b.wait(gp, "dv", b.val("dv"))
                    b.wait_tok(gp, gb_tok)
                    b.wait_tok(gp, ns.sg_guard)
                    b.op(gp, "tensor_tensor", out=ns.sg_bf[:, ic * HALF + tau * 512: ic * HALF + (tau + 1) * 512],
                         in0=gluB[:], in1=ns.g_bc[:, tau * 512:(tau + 1) * 512], op=ALU.mult,
                         inc="gp")
                    ns.glu_free[par] = b.val("gp")
            ns.w1_slot_tok[w1slot] = b.tok("pe")
        sg_tok = b.tok("gp")

        b.wait_tok("tensor", sg_tok)
        for hc in range(DC):
            w2slot = hc % 2
            w2tok = ns.w2_pre_tok_l0 if hc < 2 else ns.w2_cur_tok_l0[w2slot]
            b.wait_tok("tensor", w2tok)
            b.wait_tok("tensor", ns.psY_tok[0])
            for tau in range(2):
                psY = ns.psY[:, tau * 512:(tau + 1) * 512]
                for ic in range(8):
                    b.op("tensor", "matmul", psY,
                         ns.w2_sb[w2slot][:, ic * 128:(ic + 1) * 128],
                         ns.sg_bf[:, ic * HALF + tau * 512: ic * HALF + (tau + 1) * 512],
                         start=(ic == 0), stop=(ic == 7 and e != 0),
                         inc=("pe" if (ic == 7 and e != 0 and tau == 1) else None))
                if e == 0:
                    b.op("tensor", "matmul", psY,
                         ns.b2_bf[:, hc * 128:(hc + 1) * 128],
                         ns.gp_Tb[0:16, t0 + tau * 512: t0 + (tau + 1) * 512],
                         start=False, stop=True, inc=("pe" if tau == 1 else None))
            peY = b.tok("pe")
            b.wait_tok("vector", peY)
            b.op("vector", "tensor_tensor",
                 out=Xc(ns, hc, t0, HALF),
                 in0=Xc(ns, hc, t0, HALF), in1=ns.psY[:, 0:HALF], op=ALU.add, inc="dv")
            ns.psY_tok[0] = b.tok("dv")
            ns.w2_slot_tok[w2slot] = b.tok("pe")
            if hc + 2 < DC:
                ns.w2_cur_tok_l0[w2slot] = _l0_w2_load(b, ns, e, hc + 2, w2slot,
                                                       "w2a" if w2slot == 0 else "w2b")
        ns.sg_guard = b.tok("pe")





def carve_l0attn(ns):
    f32, bf, al, pos = _mk_carver(ns)
    ns.a_vh = bf(16 * 1024)
    u0 = al(0, align=64)
    f32p, bfp, alp, posp = _mk_carver(ns)
    posp[0] = u0
    ns.a_bcm = f32p(512)
    ns.a_bcr = f32p(512)
    ns.a_xln = bfp(8 * 512)
    ns.a_pbf = bfp(8 * 512)
    ns.a_cos = f32p(2 * 512)
    ns.a_sin = f32p(2 * 512)
    ns.a_bvi = f32p(1024)
    ns.sq = f32p(1024)
    ns.tmp1 = f32p(512)
    ns.a_w = bfp(8 * 1024)
    ns.a_wi = bfp(8 * 1024)
    ns.a_stage = bfp(8 * 512)
    ns.a_rs1 = f32p(512)
    ns.a_rs2 = f32p(512)
    proj_end = posp[0]
    f32a, bfa, ala, posa = _mk_carver(ns)
    posa[0] = u0
    pj = ala(8192)
    ns.a_junk = ns.AR[:, pj // 4: pj // 4 + 2048]
    ns.a_expT = ns.AR_B[:, pj // 2: pj // 2 + 4096]
    ns.a_outw = bfa(8 * 1024)
    ns.a_m1 = bfa(4 * 512)
    ns.a_m2 = bfa(4 * 512)
    ns.a_qh = [bfa(2048, rows=64), bfa(2048, rows=64)]
    ns.a_kh = [bfa(2048, rows=64), bfa(2048, rows=64)]
    ns.a_opair = bfa(T)
    ns.a_zcol = f32a(16)
    ns.a_zbc = bfa(512)
    ns.a_sc1 = f32a(512)
    ns.a_outb = f32a(1024)
    attn_end = posa[0]
    pos[0] = max(proj_end, attn_end)
    ns.a_bytes = pos[0]


def emit_l0attn(b, ns):
    b.mark("l0attn.proj")
    d = ns.dram
    v = "vector"
    NT4 = 4
    H = 16
    HD = 64

    b.dma("sync", out=ns.a_bvi[0:1, 0:D], in_=d["bvi_row"], inc="xl")
    ns.bvi_tok = b.tok("xl")
    for sh in range(2):
        st0 = sh * HALF
        rows = [ns.row_m[0:1, st0:st0 + HALF], ns.row_a[0:1, st0:st0 + HALF]]
        rt = emit_colstats(b, ns, 0, lambda dc: Xc(ns, dc, st0, HALF),
                           lambda bb: _kinds_ln2(bb, ns, nw=8), rows, "aln", nt=8)

    for P in range(3):
        wsrc = ("wq_t", "wk_t", "wv_t")[P]
        wisrc = ("wqi_t", "wki_t", "wvi_t")[P]
        b.wait_tok("gpsimd", ns.aw_guard)
        b.dma("gpsimd", out=ns.a_w.rearrange("p (dc f) -> p dc f", dc=8),
              in_=d[wsrc].rearrange("(dc p) f -> p dc f", p=128), inc="wl")
        b.dma("gpsimd", out=ns.a_wi.rearrange("p (dc f) -> p dc f", dc=8),
              in_=d[wisrc].rearrange("(dc p) f -> p dc f", p=128), inc="wl")
        wtok = b.tok("wl")
        for tau in range(NT4):
            tw = tau * 512
            b.wait(v, "pe", ns.a_xln_guard)
            emit_bcast(b, ns, ns.row_m[0:1, tw:tw + 512], ns.a_bcm, 512, rt, base=0)
            t_r = emit_bcast(b, ns, ns.row_a[0:1, tw:tw + 512], ns.a_bcr, 512, rt, base=32)
            b.wait_tok(v, t_r)
            for dc in range(DC):
                b.op(v, "tensor_tensor", out=ns.tmp1[:, 0:512], in0=Xc(ns, dc, tw, 512),
                     in1=ns.a_bcm[:, 0:512], op=ALU.subtract)
                b.op(v, "tensor_tensor", out=ns.a_xln[:, dc * 512:(dc + 1) * 512],
                     in0=ns.tmp1[:, 0:512], in1=ns.a_bcr[:, 0:512], op=ALU.mult, inc="dv")
            xln_tok = b.tok("dv")
            def cs_load(i, csl):
                b.wait("sync", "dv", ns.a_cs_guard[csl])
                b.dma("sync", out=ns.a_cos[:, csl * 512:(csl + 1) * 512],
                      in_=d["rope_cos"][i * 128:(i + 1) * 128, tw:tw + 512], inc="m2")
                b.dma("sync", out=ns.a_sin[:, csl * 512:(csl + 1) * 512],
                      in_=d["rope_sin"][i * 128:(i + 1) * 128, tw:tw + 512], inc="m2")
                return b.tok("m2")

            cs_load(0, 0)
            cs_pre = cs_load(1, 1)
            cs_next = [cs_pre, cs_pre]
            b.wait_tok("tensor", wtok)
            b.wait_tok("tensor", xln_tok)
            for i in range(4):
                csl = i % 2
                cs_tok = cs_next[csl]
                for half_ in range(2):
                    fc = i + 4 * half_
                    ps = ns.ps4[:, half_ * 512:(half_ + 1) * 512]
                    b.wait_tok("tensor", ns.a_ps_tok[half_])
                    for dc in range(DC):
                        b.op("tensor", "matmul", ps,
                             ns.a_w[:, dc * 1024 + fc * 128: dc * 1024 + fc * 128 + 128],
                             ns.a_xln[:, dc * 512:(dc + 1) * 512],
                             start=(dc == 0), stop=(dc == DC - 1),
                             inc=("pe" if dc == DC - 1 else None))
                pe_pair = b.tok("pe")
                b.wait_tok(v, pe_pair)
                b.wait_tok(v, cs_tok)
                b.wait(v, "pe", ns.a_pbf_guard)
                ci = ns.a_cos[:, csl * 512:(csl + 1) * 512]
                si = ns.a_sin[:, csl * 512:(csl + 1) * 512]
                xe = ns.ps4[:, 0:512]
                xo = ns.ps4[:, 512:1024]
                b.op(v, "tensor_tensor", out=ns.a_rs1[:, 0:512], in0=xe, in1=ci, op=ALU.mult)
                b.op(v, "tensor_tensor", out=ns.a_rs2[:, 0:512], in0=xo, in1=si, op=ALU.mult)
                b.op(v, "tensor_tensor", out=ns.a_pbf[:, i * 512:(i + 1) * 512],
                     in0=ns.a_rs1[:, 0:512], in1=ns.a_rs2[:, 0:512], op=ALU.add)
                b.op(v, "tensor_tensor", out=ns.a_rs1[:, 0:512], in0=xo, in1=ci, op=ALU.mult)
                b.op(v, "tensor_tensor", out=ns.a_rs2[:, 0:512], in0=xe, in1=si, op=ALU.mult,
                     inc="dv")
                ns.a_ps_tok = [b.tok("dv"), b.tok("dv")]
                b.op(v, "tensor_tensor", out=ns.a_pbf[:, (i + 4) * 512:(i + 5) * 512],
                     in0=ns.a_rs1[:, 0:512], in1=ns.a_rs2[:, 0:512], op=ALU.subtract, inc="dv")
                ns.a_cs_guard[csl] = b.val("dv")
                if i == 1:
                    cs_load(2, 0)
                    tk = cs_load(3, 1)
                    cs_next = [tk, tk]
            pbf_tok = b.tok("dv")
            b.wait_tok("tensor", pbf_tok)
            if P < 2:
                bia = (ns.bqi_p, ns.bki_p)[P]
                for fc in range(DC):
                    ps = ns.psY[:, (fc % 2) * 512:(fc % 2) * 512 + 512]
                    b.wait_tok("tensor", ns.a_psY_tok[fc % 2])
                    for dc in range(DC):
                        b.op("tensor", "matmul", ps,
                             ns.a_wi[:, dc * 1024 + fc * 128: dc * 1024 + fc * 128 + 128],
                             ns.a_pbf[:, dc * 512:(dc + 1) * 512],
                             start=(dc == 0), stop=(dc == DC - 1),
                             inc=("pe" if dc == DC - 1 else None))
                    peq = b.tok("pe")
                    b.wait_tok("scalar", peq)
                    b.wait("scalar", "m1", ns.a_stage_guard)
                    b.op("scalar", "activation", out=ns.a_stage[:, fc * 512:(fc + 1) * 512],
                         in_=ps, func=AF.Identity, bias=bia[:, fc:fc + 1], inc="ac")
                    ns.a_psY_tok[fc % 2] = b.tok("ac")
                st_tok = b.tok("ac")
                b.wait_tok("sync", st_tok)
                dst = d["qh_dram"] if P == 0 else d["kh_dram"]
                b.dma("sync", out=dst.rearrange("(fc p) t -> p fc t", p=128)[:, :, tw:tw + 512],
                      in_=ns.a_stage.rearrange("p (fc t) -> p fc t", fc=8), inc="m1")
                ns.a_stage_guard = b.val("m1")
            else:
                b.wait_tok("tensor", ns.bvi_tok)
                for nw_ in range(4):
                    for fh in range(2):
                        ps = ns.psY[:, fh * 512:(fh + 1) * 512]
                        b.wait_tok("tensor", ns.a_psY_tok[fh])
                        for dc in range(DC):
                            b.op("tensor", "matmul", ps,
                                 ns.a_pbf[:, dc * 512 + nw_ * 128: dc * 512 + nw_ * 128 + 128],
                                 ns.a_wi[:, dc * 1024 + fh * 512: dc * 1024 + (fh + 1) * 512],
                                 start=(dc == 0), stop=False)
                        b.op("tensor", "matmul", ps,
                             ns.ones_col[0:1, 0:128],
                             ns.a_bvi[0:1, fh * 512:(fh + 1) * 512],
                             start=False, stop=True, inc="pe")
                        pev = b.tok("pe")
                        b.wait_tok(v, pev)
                        b.op(v, "tensor_copy",
                             out=ns.a_vh[:, (tau * 4 + nw_) * 1024 + fh * 512:
                                         (tau * 4 + nw_) * 1024 + (fh + 1) * 512],
                             in_=ps, inc="dv")
                        ns.a_psY_tok[fh] = b.tok("dv")
            ns.a_xln_guard = b.val("pe")
            ns.a_pbf_guard = b.val("pe")
        ns.aw_guard = b.tok("pe")

    b.mark("l0attn.attn")
    b.wait("sync", "m1")
    b.barrier()
    b.dma("gpsimd", out=ns.a_m1.rearrange("p (j k) -> p j k", j=4),
          in_=d["mask1"].rearrange("j p k -> p j k"), inc="wl")
    b.dma("gpsimd", out=ns.a_m2.rearrange("p (j k) -> p j k", j=4),
          in_=d["mask2"].rearrange("j p k -> p j k"), inc="wl")
    b.dma("gpsimd", out=ns.a_outw.rearrange("p (pr f) -> p pr f", pr=8),
          in_=d["outw_t"].rearrange("(pr p) f -> p pr f", p=128), inc="wl")
    b.dma("sync", out=ns.a_outb[0:1, 0:D], in_=d["outb_row"], inc="m1")
    amw_tok = b.tok("wl")
    outb_tok = b.tok("m1")
    b.wait_tok("tensor", outb_tok)
    b.wait_tok("tensor", amw_tok)

    def qk_load(h, hs, sem):
        b.wait_tok("gpsimd", ns.a_qk_guard[hs])
        b.dma("gpsimd", out=ns.a_qh[hs][0:64, :], in_=d["qh_dram"][h * 64:(h + 1) * 64, :], inc=sem)
        b.dma("gpsimd", out=ns.a_kh[hs][0:64, :], in_=d["kh_dram"][h * 64:(h + 1) * 64, :], inc=sem)
        return b.tok(sem)

    qk_load(0, 0, "qk")
    qk_pre = qk_load(1, 1, "qk")
    qk_next = [qk_pre, qk_pre]
    for h in range(H):
        hs = h % 2
        qk_tok = qk_next[hs]
        b.wait_tok("tensor", qk_tok)
        for qs in range(16):
            ktn = qs // 4 + 1
            for kt in range(ktn):
                ps = ns.ps4[:, kt * 512:(kt + 1) * 512]
                b.wait_tok("tensor", ns.a_p1_tok[kt])
                b.op("tensor", "matmul", ps,
                     ns.a_qh[hs][0:64, qs * 128:(qs + 1) * 128],
                     ns.a_kh[hs][0:64, kt * 512:(kt + 1) * 512],
                     start=True, stop=True, inc="pe")
            p1_tok = b.tok("pe")
            b.wait_tok(v, p1_tok)
            jdiag = qs % 4
            b.op(v, "tensor_tensor", out=ns.ps4[:, (ktn - 1) * 512:ktn * 512],
                 in0=ns.ps4[:, (ktn - 1) * 512:ktn * 512],
                 in1=ns.a_m1[:, jdiag * 512:(jdiag + 1) * 512], op=ALU.add, inc="dv")
            b.op(v, "reduce_max", out=ns.gm[:, 0:1], in_=ns.ps4[:, 0:ktn * 512], axis=AX.X,
                 negate=True)
            b.op(v, "tensor_scalar", out=ns.gm[:, 1:2], in0=ns.gm[:, 0:1],
                 scalar1=0.125, scalar2=None, op0=ALU.mult, inc="dv")
            b.wait("scalar", "dv", b.val("dv"))
            b.op("scalar", "activation", out=ns.a_junk[:, 0:ktn * 512], in_=ns.ps4[:, 0:ktn * 512],
                 func=AF.Exp, scale=0.125, bias=ns.gm[:, 1:2],
                 accum_out=ns.gm[:, 2:3], inc="ac")
            ps1_free = b.tok("ac")
            ns.a_p1_tok = [ps1_free] * 4
            b.wait("scalar", "ac", b.val("ac"))
            b.op("scalar", "activation", out=ns.gm[:, 3:4], in_=ns.gm[:, 2:3], func=AF.Ln, inc="ac")
            b.wait(v, "ac", b.val("ac"))
            b.op(v, "tensor_tensor", out=ns.a_zcol[:, qs:qs + 1], in0=ns.gm[:, 1:2],
                 in1=ns.gm[:, 3:4], op=ALU.subtract, inc="dv")
        zc_tok = b.tok("dv")
        b.wait_tok("tensor", zc_tok)
        b.wait("tensor", "dv", b.val("dv"))
        b.op("tensor", "transpose", ns.ps_misc2[0:16, 0:128], ns.a_zcol[:, 0:16],
             ns.ident_f[:], inc="pe")
        b.wait(v, "pe", b.val("pe"))
        b.op(v, "tensor_copy", out=ns.rowstage[0:16, 0:128], in_=ns.ps_misc2[0:16, 0:128], inc="dv")
        b.wait("sync", "dv", b.val("dv"))
        b.dma("sync", out=d["row_dram"][0:16, 0:128], in_=ns.rowstage[0:16, 0:128], inc="m1")
        b.wait("sync", "m1", b.val("m1"))
        b.wait("gpsimd", "m1", b.val("m1"))
        b.dma("gpsimd", out=ns.g_row_bf[0:1, 0:T],
              in_=d["row_dram"][0:16, 0:128].rearrange("a c -> (a c)")[None, :], inc="gx")
        zrow_tok = b.tok("gx")

        for tq in range(NT4):
            b.wait_tok("tensor", zrow_tok)
            b.wait("tensor", "dv", b.val("dv"))
            b.op("tensor", "matmul", ns.ps_misc[:, 0:512],
                 ns.ones_bf[0:1, 0:128], ns.g_row_bf[0:1, tq * 512:(tq + 1) * 512],
                 start=True, stop=True, inc="pe")
            b.wait(v, "pe", b.val("pe"))
            b.op(v, "tensor_copy", out=ns.a_zbc[:, 0:512], in_=ns.ps_misc[:, 0:512], inc="dv")
            zbc_tok = b.tok("dv")
            ktn2 = tq * 4 + 4
            for kt in range(ktn2):
                slot = kt % 8
                ps = ns.psY[:, (kt % 2) * 512:(kt % 2) * 512 + 512]
                b.wait_tok("tensor", ns.a_p2_tok[kt % 2])
                b.op("tensor", "matmul", ps,
                     ns.a_kh[hs][0:64, kt * 128:(kt + 1) * 128],
                     ns.a_qh[hs][0:64, tq * 512:(tq + 1) * 512],
                     start=True, stop=True, inc="pe")
                p2_tok = b.tok("pe")
                b.wait_tok(v, p2_tok)
                b.wait_tok(v, zbc_tok)
                b.wait_tok(v, ns.a_expT_guard[slot])
                jd = kt - (tq * 4)
                if jd >= 0:
                    b.op(v, "tensor_tensor", out=ps, in0=ps,
                         in1=ns.a_m2[:, jd * 512:(jd + 1) * 512], op=ALU.add)
                b.op(v, "scalar_tensor_tensor", out=ns.a_sc1[:, 0:512], in0=ps, scalar=0.125,
                     in1=ns.a_zbc[:, 0:512], op0=ALU.mult, op1=ALU.add, inc="dv")
                ns.a_p2_tok[kt % 2] = b.tok("dv")
                b.wait("scalar", "dv", b.val("dv"))
                b.op("scalar", "activation", out=ns.a_expT[:, slot * 512:(slot + 1) * 512],
                     in_=ns.a_sc1[:, 0:512], func=AF.Exp, inc="ac")
                exp_tok = b.tok("ac")
                b.wait_tok("tensor", exp_tok)
                b.op("tensor", "matmul", ns.ps_misc2[0:64, 0:512],
                     ns.a_vh[:, kt * 1024 + h * 64: kt * 1024 + h * 64 + 64],
                     ns.a_expT[:, slot * 512:(slot + 1) * 512],
                     start=(kt == 0), stop=(kt == ktn2 - 1),
                     inc=("pe" if kt == ktn2 - 1 else None))
                ns.a_expT_guard[slot] = b.tok("pe")
            av_tok = b.tok("pe")
            b.wait_tok(v, av_tok)
            b.op(v, "tensor_copy", out=ns.a_opair[64 * hs:64 * hs + 64, tq * 512:(tq + 1) * 512],
                 in_=ns.ps_misc2[0:64, 0:512], inc="dv")

        if hs == 1:
            opair_tok = b.tok("dv")
            pair = h // 2
            b.wait_tok("tensor", opair_tok)
            for dc in range(DC):
                for tq in range(NT4):
                    ps = ns.ps_misc[:, 0:512]
                    b.wait("tensor", "dv", ns.a_op_guard)
                    if pair == 0:
                        b.op("tensor", "matmul", ps,
                             ns.a_outw[:, pair * 1024 + dc * 128: pair * 1024 + dc * 128 + 128],
                             ns.a_opair[:, tq * 512:(tq + 1) * 512],
                             start=True, stop=False)
                        b.op("tensor", "matmul", ps,
                             ns.a_outb[0:1, dc * 128:(dc + 1) * 128],
                             ns.ones_col[0:1, 0:512],
                             start=False, stop=True, inc="pe")
                    else:
                        b.op("tensor", "matmul", ps,
                             ns.a_outw[:, pair * 1024 + dc * 128: pair * 1024 + dc * 128 + 128],
                             ns.a_opair[:, tq * 512:(tq + 1) * 512],
                             start=True, stop=True, inc="pe")
                    peo = b.tok("pe")
                    b.wait_tok(v, peo)
                    b.op(v, "tensor_tensor", out=Xc(ns, dc, tq * 512, 512),
                         in0=Xc(ns, dc, tq * 512, 512), in1=ps, op=ALU.add, inc="dv")
                    ns.a_op_guard = b.val("dv")
        ns.a_qk_guard[hs] = b.tok("pe")
        if h + 2 < H:
            qk_next[hs] = qk_load(h + 2, hs, "qka" if hs == 0 else "qkb")



def carve_l1attn(ns):
    f32, bf, al, pos = _mk_carver(ns)
    ns.n_ztile = bf(8 * 512)
    ns.n_w = bf(3 * 8 * 1024)
    ns.n_ktile = bf(4 * 1024)
    ns.n_vtile = bf(4 * 1024)
    ns.n_stage = bf(8 * 512)
    ns.n_vk = f32(32 * 32)
    ns.n_vkT = bf(64)
    ns.n_qh = [bf(2048, rows=32), bf(2048, rows=32)]
    ns.n_rbc = f32(512)
    u0 = al(0, align=64)
    c1 = _mk_carver(ns); c1[3][0] = u0
    ns.sq = c1[0](1024)
    ns.tmp1 = c1[0](512)
    ns.bc_m = c1[0](512)
    ns.bc_a = c1[0](512)
    end1 = c1[3][0]
    c2 = _mk_carver(ns); c2[3][0] = u0
    ns.n_ohead = c2[0](2048)
    ns.n_ostage = c2[1](2048)
    end2 = c2[3][0]
    pos[0] = max(end1, end2)
    ns.n_ochunk = [ns.n_stage[:, 0:2048], ns.n_stage[:, 2048:4096]]
    ns.n_den = ns.g_row_full
    ns.n_bp = ns.row_b
    ns.l1a_bytes = pos[0]


def emit_l1attn(b, ns):
    b.mark("l1attn.proj")
    d = ns.dram
    v = "vector"
    NH = 32
    vk_sb = ns.n_vk

    b.dma("sync", out=ns.n_bp[0:1, 0:D], in_=d["l1_bp_row"], inc="xl")
    bp_tok = b.tok("xl")
    for sh in range(2):
        st0 = sh * HALF
        rows = [ns.row_m[0:1, st0:st0 + HALF], ns.row_a[0:1, st0:st0 + HALF]]
        emit_colstats(b, ns, 0, lambda dc: Xc(ns, dc, st0, HALF),
                      lambda bb: _kinds_ln2(bb, ns, nw=8), rows, "nln", nt=8)
    rt = b.tok("m1")

    b.wait_tok("gpsimd", ns.aw_guard)
    for i, wn in enumerate(("l1_wqT", "l1_wkT", "l1_wvT")):
        b.dma("gpsimd", out=ns.n_w.rearrange("p (i dc f) -> p (i dc) f", i=3, dc=8)
              [:, i * 8:(i + 1) * 8, :],
              in_=d[wn].rearrange("(dc p) f -> p dc f", p=128), inc="wl")
    wtok = b.tok("wl")
    b.op(v, "memset", vk_sb[:, 0:1024], 0.0, inc="dv")
    vk_zero = b.tok("dv")

    for tau in range(4):
        tw = tau * 512
        b.wait(v, "pe", ns.a_xln_guard)
        emit_bcast(b, ns, ns.row_m[0:1, tw:tw + 512], ns.bc_m, 512, rt, base=0)
        t_r = emit_bcast(b, ns, ns.row_a[0:1, tw:tw + 512], ns.bc_a, 512, rt, base=32)
        b.wait_tok(v, t_r)
        for dc in range(DC):
            b.op(v, "tensor_tensor", out=ns.tmp1[:, 0:512], in0=Xc(ns, dc, tw, 512),
                 in1=ns.bc_m[:, 0:512], op=ALU.subtract)
            b.op(v, "tensor_tensor", out=ns.n_ztile[:, dc * 512:(dc + 1) * 512],
                 in0=ns.tmp1[:, 0:512], in1=ns.bc_a[:, 0:512], op=ALU.mult, inc="dv")
        z_tok = b.tok("dv")
        b.wait_tok("tensor", wtok)
        b.wait_tok("tensor", z_tok)
        for fc in range(DC):
            ps = ns.psY[:, (fc % 2) * 512:(fc % 2) * 512 + 512]
            b.wait_tok("tensor", ns.a_psY_tok[fc % 2])
            for dc in range(DC):
                b.op("tensor", "matmul", ps,
                     ns.n_w[:, dc * 1024 + fc * 128: dc * 1024 + fc * 128 + 128],
                     ns.n_ztile[:, dc * 512:(dc + 1) * 512],
                     start=(dc == 0), stop=(dc == DC - 1),
                     inc=("pe" if dc == DC - 1 else None))
            peq = b.tok("pe")
            b.wait_tok("scalar", peq)
            b.wait("scalar", "m1", ns.a_stage_guard)
            b.op("scalar", "activation", out=ns.n_stage[:, fc * 512:(fc + 1) * 512],
                 in_=ps, func=AF.Relu, inc="ac")
            ns.a_psY_tok[fc % 2] = b.tok("ac")
        st_tok = b.tok("ac")
        b.wait_tok("sync", st_tok)
        b.dma("sync", out=d["q2_dram"].rearrange("(fc p) t -> p fc t", p=128)[:, :, tw:tw + 512],
              in_=ns.n_stage.rearrange("p (fc t) -> p fc t", fc=8), inc="m1")
        ns.a_stage_guard = b.val("m1")
        for P in (1, 2):
            dst = ns.n_ktile if P == 1 else ns.n_vtile
            for nw_ in range(4):
                for fh in range(2):
                    ps = ns.psY[:, fh * 512:(fh + 1) * 512]
                    b.wait_tok("tensor", ns.a_psY_tok[fh])
                    for dc in range(DC):
                        b.op("tensor", "matmul", ps,
                             ns.n_ztile[:, dc * 512 + nw_ * 128: dc * 512 + nw_ * 128 + 128],
                             ns.n_w[:, P * 8192 + dc * 1024 + fh * 512: P * 8192 + dc * 1024 + (fh + 1) * 512],
                             start=(dc == 0), stop=(dc == DC - 1),
                             inc=("pe" if dc == DC - 1 else None))
                    pet = b.tok("pe")
                    b.wait_tok("scalar", pet)
                    b.wait_tok("scalar", ns.n_kv_guard)
                    b.op("scalar", "activation", out=dst[:, nw_ * 1024 + fh * 512: nw_ * 1024 + (fh + 1) * 512],
                         in_=ps, func=(AF.Relu if P == 1 else AF.Identity), inc="ac")
                    ns.a_psY_tok[fh] = b.tok("ac")
        kv_tok = b.tok("ac")
        b.wait_tok("tensor", kv_tok)
        if tau == 0:
            b.wait_tok("tensor", vk_zero)
        for h in range(NH):
            par = h % 2
            vkps = ns.ps4[0:32, par * 512:par * 512 + 32]
            denps = ns.ps4[0:1, 1024 + par * 512:1024 + par * 512 + 32]
            b.wait_tok("tensor", ns.n_vkps_tok[par])
            for ns_ in range(4):
                b.op("tensor", "matmul", vkps,
                     ns.n_vtile[:, ns_ * 1024 + h * 32: ns_ * 1024 + h * 32 + 32],
                     ns.n_ktile[:, ns_ * 1024 + h * 32: ns_ * 1024 + h * 32 + 32],
                     start=(ns_ == 0), stop=(ns_ == 3))
                b.op("tensor", "matmul", denps,
                     ns.ones_bf[:, 0:1],
                     ns.n_ktile[:, ns_ * 1024 + h * 32: ns_ * 1024 + h * 32 + 32],
                     start=(ns_ == 0), stop=(ns_ == 3),
                     inc=("pe" if ns_ == 3 else None))
            pevk = b.tok("pe")
            b.wait_tok(v, pevk)
            b.op(v, "tensor_tensor", out=vk_sb[0:32, h * 32:(h + 1) * 32],
                 in0=vk_sb[0:32, h * 32:(h + 1) * 32], in1=vkps,
                 op=ALU.add)
            b.op(v, "tensor_tensor", out=vk_sb[32:33, h * 32:(h + 1) * 32],
                 in0=vk_sb[32:33, h * 32:(h + 1) * 32], in1=denps,
                 op=ALU.add, inc="dv")
            ns.n_vkps_tok[par] = b.tok("dv")
        ns.n_vk_tok = b.tok("dv")
        ns.a_xln_guard = b.val("pe")
        ns.n_kv_guard = ("pe", b.val("pe"))

    b.mark("l1attn.heads")
    b.wait("sync", "m1")
    b.barrier()

    def qh_load(h, hs, sem):
        b.wait_tok("gpsimd", ns.a_qk_guard[hs])
        b.dma("gpsimd", out=ns.n_qh[hs][0:32, :], in_=d["q2_dram"][h * 32:(h + 1) * 32, :], inc=sem)
        return b.tok(sem)

    qh_load(0, 0, "qk")
    qh_pre = qh_load(1, 1, "qk")
    qh_next = [qh_pre, qh_pre]
    for h in range(NH):
        hs = h % 2
        qh_tok = qh_next[hs]
        b.wait_tok("tensor", ns.n_vk_tok)
        b.wait("tensor", "dv", b.val("dv"))
        b.op("tensor", "transpose", ns.ps_misc2[0:32, 0:33], vk_sb[0:33, h * 32:(h + 1) * 32],
             ns.ident_f[0:33, 0:33], inc="pe")
        b.wait(v, "pe", b.val("pe"))
        b.op(v, "tensor_copy", out=ns.n_vkT[0:32, 0:33], in_=ns.ps_misc2[0:32, 0:33], inc="dv")
        vkt_tok = b.tok("dv")
        b.wait_tok("tensor", qh_tok)
        b.wait_tok("tensor", vkt_tok)
        for tau in range(4):
            ps33 = ns.psY[0:33, (tau % 2) * 512:(tau % 2) * 512 + 512]
            b.wait_tok("tensor", ns.a_psY_tok[tau % 2])
            b.op("tensor", "matmul", ps33,
                 ns.n_vkT[0:32, 0:33],
                 ns.n_qh[hs][0:32, tau * 512:(tau + 1) * 512],
                 start=True, stop=True, inc="pe")
            peo = b.tok("pe")
            b.wait_tok(v, peo)
            b.wait(v, "m1", ns.n_oh_guard)
            b.op(v, "tensor_copy", out=ns.n_ohead[0:33, tau * 512:(tau + 1) * 512],
                 in_=ps33, inc="dv")
            ns.a_psY_tok[tau % 2] = b.tok("dv")
        oh_tok = b.tok("dv")
        b.wait_tok(v, oh_tok)
        b.op(v, "tensor_scalar", out=ns.n_den[0:1, 0:T], in0=ns.n_ohead[32:33, 0:T],
             scalar1=1e-15, scalar2=None, op0=ALU.add, inc="dv")
        b.wait("scalar", "dv", b.val("dv"))
        b.op("scalar", "activation", out=ns.n_den[0:1, 0:T], in_=ns.n_den[0:1, 0:T],
             func=AF.Ln)
        b.op("scalar", "activation", out=ns.n_den[0:1, 0:T], in_=ns.n_den[0:1, 0:T],
             func=AF.Exp, scale=-1.0, inc="ac")
        r_tok = b.tok("ac")
        for tau in range(4):
            b.wait_tok("tensor", r_tok)
            b.wait("tensor", "dv", b.val("dv"))
            b.op("tensor", "matmul", ns.ps_misc[:, 0:512],
                 ns.ones_col[0:1, 0:128], ns.n_den[0:1, tau * 512:(tau + 1) * 512],
                 start=True, stop=True, inc="pe")
            b.wait(v, "pe", b.val("pe"))
            b.op(v, "tensor_copy", out=ns.n_rbc[:, 0:512], in_=ns.ps_misc[:, 0:512], inc="dv")
            b.drain(v)
            b.op(v, "tensor_tensor", out=ns.n_ostage[0:32, tau * 512:(tau + 1) * 512],
                 in0=ns.n_ohead[0:32, tau * 512:(tau + 1) * 512],
                 in1=ns.n_rbc[0:32, 0:512], op=ALU.mult, inc="dv")
        ost_tok = b.tok("dv")
        b.wait_tok("sync", ost_tok)
        b.dma("sync", out=d["o2_dram"][h * 32:(h + 1) * 32, :], in_=ns.n_ostage[0:32, 0:T], inc="m1")
        ns.n_oh_guard = b.val("m1")
        ns.a_qk_guard[hs] = b.tok("pe")
        if h + 2 < NH:
            qh_next[hs] = qh_load(h + 2, hs, "qka" if hs == 0 else "qkb")

    b.mark("l1attn.oproj")
    b.wait("sync", "m1")
    b.barrier()
    b.dma("gpsimd", out=ns.n_w.rearrange("p (i dc f) -> p (i dc) f", i=3, dc=8)[:, 0:8, :],
          in_=d["l1_wpT"].rearrange("(dc p) f -> p dc f", p=128), inc="wl")
    wp_tok = b.tok("wl")
    b.wait_tok("tensor", bp_tok)
    oc_next = [None, None]
    for oc in range(DC):
        slot = oc % 2
        b.wait_tok("gpsimd", ns.n_oc_guard[slot])
        b.dma("gpsimd", out=ns.n_ochunk[slot],
              in_=d["o2_dram"][oc * 128:(oc + 1) * 128, :],
              inc=("qka" if slot == 0 else "qkb"))
        octok = b.tok("qka" if slot == 0 else "qkb")
        b.wait_tok("tensor", wp_tok)
        b.wait_tok("tensor", octok)
        for dc in range(DC):
            for tau in range(4):
                ps = ns.ps_misc[:, 0:512]
                b.wait("tensor", "dv", ns.a_op_guard)
                wsl = ns.n_w[:, oc * 1024 + dc * 128: oc * 1024 + dc * 128 + 128]
                if oc == 0:
                    b.op("tensor", "matmul", ps, wsl,
                         ns.n_ochunk[slot][:, tau * 512:(tau + 1) * 512],
                         start=True, stop=False)
                    b.op("tensor", "matmul", ps,
                         ns.n_bp[0:1, dc * 128:(dc + 1) * 128],
                         ns.ones_col[64:65, 0:512],
                         start=False, stop=True, inc="pe")
                else:
                    b.op("tensor", "matmul", ps, wsl,
                         ns.n_ochunk[slot][:, tau * 512:(tau + 1) * 512],
                         start=True, stop=True, inc="pe")
                peo = b.tok("pe")
                b.wait_tok(v, peo)
                b.op(v, "tensor_tensor", out=Xc(ns, dc, tau * 512, 512),
                     in0=Xc(ns, dc, tau * 512, 512), in1=ps, op=ALU.add, inc="dv")
                ns.a_op_guard = b.val("dv")
        ns.n_oc_guard[slot] = b.tok("pe")



def _kinds_ln2(b, ns, nw=8):
    v = "vector"; s = "scalar"
    S = ns.scol
    P = ns.ps_misc
    b.op(v, "tensor_scalar", out=S[:, 0:nw], in0=P[:, 0:2 * nw].rearrange("p (w k) -> p k w", k=2)[:, 0, :],
         scalar1=1.0 / D, scalar2=None, op0=ALU.mult)
    b.op(v, "tensor_scalar", out=S[:, 3 * nw:4 * nw], in0=P[:, 0:2 * nw].rearrange("p (w k) -> p k w", k=2)[:, 1, :],
         scalar1=1.0 / D, scalar2=None, op0=ALU.mult)
    b.op(v, "tensor_tensor", out=S[:, 4 * nw:5 * nw], in0=S[:, 0:nw], in1=S[:, 0:nw], op=ALU.mult)
    b.op(v, "tensor_tensor", out=S[:, 5 * nw:6 * nw], in0=S[:, 3 * nw:4 * nw], in1=S[:, 4 * nw:5 * nw], op=ALU.subtract)
    b.op(v, "tensor_scalar", out=S[:, 6 * nw:7 * nw], in0=S[:, 5 * nw:6 * nw],
         scalar1=1e-5, scalar2=None, op0=ALU.add, inc="dv")
    b.wait(s, "dv", b.val("dv"))
    b.op(s, "activation", out=S[:, 7 * nw:8 * nw], in_=S[:, 6 * nw:7 * nw], func=AF.Ln)
    b.op(s, "activation", out=S[:, nw:2 * nw], in_=S[:, 7 * nw:8 * nw], func=AF.Exp, scale=-0.5, inc="ac")
    b.wait(v, "ac", b.val("ac"))
    b.op(v, "tensor_copy", out=S[:, nw:2 * nw], in_=S[:, nw:2 * nw], inc="dv")


def emit_l1ffn(b, ns):
    for half in range(2):
        _l1ffn_half(b, ns, half, half * HALF)
        b.barrier()


def _l1_stats_bcast(b, ns, src, tag):
    rows = [ns.row_m, ns.row_a]
    rt = emit_colstats(b, ns, 0, src, lambda bb: _kinds_ln2(bb, ns), rows, tag)
    t_m = emit_bcast(b, ns, ns.row_m, ns.bc_m, HALF, rt, base=0)
    t_a = emit_bcast(b, ns, ns.row_a, ns.bc_a, HALF, rt, base=32)
    return t_a


def _l1_gate(b, ns, half, t0):
    nw = 8
    b.wait_tok("tensor", ns.wg_tok)
    bgt = emit_bcast(b, ns, ns.l1_gb_row, ns.bc_g16, 8)
    ps_free = bgt
    v = "vector"
    for w in range(nw):
        tw = t0 + 128 * w
        b.wait_tok("tensor", ps_free)
        for dc in range(DC):
            b.op("tensor", "matmul", ns.ps_misc[:, 0:8],
                 ns.l1_in_bf[:, dc * HALF + 128 * w: dc * HALF + 128 * w + 128],
                 ns.l1_gw_bf[:, dc * 8:(dc + 1) * 8],
                 start=(dc == 0), stop=(dc == DC - 1), inc=("pe" if dc == DC - 1 else None))
        b.wait(v, "pe", b.val("pe"))
        L = ns.gl[:, 0:8]
        G = ns.gex
        b.op(v, "tensor_tensor", out=L, in0=ns.ps_misc[:, 0:8], in1=ns.bc_g16[:, 0:8], op=ALU.add,
             inc="dv")
        ps_free = b.tok("dv")
        b.op(v, "reduce_max", out=ns.gm[:, 0:1], in_=L, axis=AX.X, negate=True)
        b.wait("scalar", "dv", b.val("dv"))
        b.op(v, "tensor_copy", out=ns.gm[:, 1:2], in_=ns.gm[:, 0:1], inc="dv")
        b.wait("scalar", "dv", b.val("dv"))
        b.op("scalar", "activation", out=ns.gcur[:, 0:8], in_=L, func=AF.Exp, bias=ns.gm[:, 1:2], inc="ac")
        b.wait(v, "ac", b.val("ac"))
        b.op(v, "reduce_sum", out=ns.gm[:, 2:3], in_=ns.gcur[:, 0:8], axis=AX.X)
        b.op(v, "reciprocal", out=ns.gm[:, 3:4], in_=ns.gm[:, 2:3])
        b.op(v, "tensor_scalar", out=ns.gl[:, 8:16], in0=ns.gcur[:, 0:8],
             scalar1=ns.gm[:, 3:4], scalar2=None, op0=ALU.mult)
        W_ = ns.gl[:, 8:16]
        b.op(v, "reduce_max", out=ns.gm[:, 0:1], in_=W_, axis=AX.X, negate=True)
        b.op(v, "tensor_scalar", out=G[:, 0:8], in0=W_, scalar1=ns.gm[:, 0:1],
             scalar2=0.0, op0=ALU.add, op1=ALU.is_ge)
        b.op(v, "tensor_scalar", out=G[:, 8:9], in0=ns.gm[:, 0:1],
             scalar1=-1.0, scalar2=None, op0=ALU.mult)
        b.op(v, "scalar_tensor_tensor", out=ns.gcur[:, 0:8], in0=G[:, 0:8], scalar=-1e30,
             in1=W_, op0=ALU.mult, op1=ALU.add)
        b.op(v, "reduce_max", out=ns.gm[:, 1:2], in_=ns.gcur[:, 0:8], axis=AX.X, negate=True)
        b.op(v, "tensor_scalar", out=ns.gmask[:, 0:8], in0=ns.gcur[:, 0:8], scalar1=ns.gm[:, 1:2],
             scalar2=0.0, op0=ALU.add, op1=ALU.is_ge)
        b.op(v, "tensor_scalar", out=ns.gm[:, 2:3], in0=ns.gm[:, 1:2],
             scalar1=-1.0, scalar2=None, op0=ALU.mult)
        b.op(v, "tensor_scalar", out=G[:, 32:40], in0=ns.gmask[:, 0:8],
             scalar1=ns.gm[:, 2:3], scalar2=None, op0=ALU.mult, inc="dv")
        b.wait("tensor", "dv", b.val("dv"))
        b.op("tensor", "transpose", ns.ps_misc2[0:40, 0:128], G[:, 0:40], ns.ident_f[:], inc="pe")
        b.wait(v, "pe", b.val("pe"))
        lw = tw - t0
        b.op(v, "tensor_copy", out=ns.l1_oh[0:9, lw:lw + 128], in_=ns.ps_misc2[0:9, 0:128])
        b.op(v, "tensor_copy", out=ns.l1_oh[32:40, lw:lw + 128], in_=ns.ps_misc2[32:40, 0:128])
        b.op(v, "tensor_copy", out=ns.l1_ohb[0:8, lw:lw + 128], in_=ns.ps_misc2[0:8, 0:128])
        b.op(v, "tensor_copy", out=ns.l1_ohb[32:40, lw:lw + 128], in_=ns.ps_misc2[32:40, 0:128], inc="dv")
    return b.tok("dv")


def _l1_iter(b, ns, half, t0, it, gtok):
    d = ns.dram
    E = 8
    rbase = 0 if it == 0 else 32
    def w2_load(e, hc, slot, sem="wl"):
        b.wait_tok("gpsimd", ns.w2_slot_tok[slot])
        b.dma("gpsimd", out=ns.l1_w2[slot], in_=d["l1_w2p"][e, hc], inc=sem)
        return b.tok(sem)

    for e in range(E):
        b.wait_tok("gpsimd", ns.w1_slot_tok[0])
        w1v = ns.l1_w1full.rearrange("p (dc f) -> p dc f", dc=8)
        for q in range(4):
            b.dma("gpsimd",
                  out=w1v[:, :, q * 512:(q + 1) * 512],
                  in_=d["l1_w1"][e, :, q * 512:(q + 1) * 512]
                  .rearrange("(dc p) f -> p dc f", p=128), inc="wl")
        w2_load(e, 0, 0)
        w2_load(e, 1, 1)
        wtok = b.tok("wl")
        ns.w2_pre_tok = wtok

        b.wait_tok("gpsimd", gtok)
        b.wait("gpsimd", "dv", ns.gbc_guard)
        b.dma("gpsimd", out=ns.g_row_bf[0:1, 0:HALF],
              in_=ns.l1_oh[rbase + e:rbase + e + 1, 0:HALF], inc="gx")
        b.wait("tensor", "gx", b.val("gx"))
        mb_tok = emit_bcast(b, ns, ns.g_row_bf, ns.l1_ohbc, HALF, bf=True)
        ns.gbc_guard = b.val("dv")

        b.wait_tok("tensor", wtok)
        if e == 0:
            b.wait_tok("tensor", ns.l1_in_tok)
        hview = ns.l1_h.rearrange("p (fc t) -> p fc t", fc=16)
        for tau in range(2):
            for fc in range(16):
                psH = ns.psA[:, (fc % 2) * 512:(fc % 2) * 512 + 512]
                b.wait_tok("tensor", ns.psA_tok[fc % 2])
                for dc in range(DC):
                    b.op("tensor", "matmul", psH,
                         ns.l1_w1full[:, dc * 2048 + fc * 128: dc * 2048 + fc * 128 + 128],
                         ns.l1_in_bf[:, dc * HALF + tau * 512: dc * HALF + (tau + 1) * 512],
                         start=(dc == 0), stop=(dc == DC - 1),
                         inc=("pe" if dc == DC - 1 else None))
                peH = b.tok("pe")
                v = "vector"
                b.wait_tok(v, peH)
                b.wait_tok(v, ns.l1_h_guard)
                b.op(v, "tensor_scalar", out=ns.l1_hs[:, 0:512],
                     in0=psH, scalar1=ns.l1_b1[:, e * 16 + fc:e * 16 + fc + 1],
                     scalar2=0.0, op0=ALU.add, op1=ALU.max, inc="dv")
                ns.psA_tok[fc % 2] = b.tok("dv")
                b.wait_tok(v, mb_tok)
                b.op(v, "tensor_tensor",
                     out=hview[:, fc, 0:512],
                     in0=ns.l1_hs[:, 0:512], in1=ns.l1_ohbc[:, tau * 512:(tau + 1) * 512],
                     op=ALU.mult, inc="dv")
            h_tok = b.tok("dv")
            if tau == 1:
                ns.l1_w1_free = b.tok("pe")

            for hc in range(DC):
                gidx = tau * DC + hc
                w2slot = gidx % 2
                w2tok = ns.w2_pre_tok if gidx < 2 else ns.w2_cur_tok[w2slot]
                psY = ns.psY[:, (hc % 2) * 512:(hc % 2) * 512 + 512]
                b.wait_tok("tensor", ns.psY_tok[hc % 2])
                b.wait_tok("tensor", h_tok)
                b.wait_tok("tensor", w2tok)
                for ic in range(16):
                    last = (ic == 15 and e != 0)
                    b.op("tensor", "matmul", psY,
                         ns.l1_w2[w2slot][:, ic * 128: ic * 128 + 128],
                         hview[:, ic, 0:512],
                         start=(ic == 0), stop=last, inc=("pe" if last else None))
                if e == 0:
                    b.op("tensor", "matmul", psY,
                         ns.l1_b2t[rbase:rbase + 8, hc * 128:(hc + 1) * 128],
                         ns.l1_ohb[rbase:rbase + 8, tau * 512: (tau + 1) * 512],
                         start=False, stop=True, inc="pe")
                peY = b.tok("pe")
                ns.w2_slot_tok[w2slot] = peY
                if gidx + 2 < 16:
                    g2i = gidx + 2
                    ns.w2_cur_tok[w2slot] = w2_load(e, g2i % DC, w2slot,
                                                    sem=("w2a" if w2slot == 0 else "w2b"))
                b.wait_tok("vector", peY)
                if it == 0 and e == 0:
                    b.op("vector", "tensor_copy",
                         out=ns.l1_acc[:, hc * HALF + tau * 512: hc * HALF + (tau + 1) * 512],
                         in_=psY, inc="dv")
                else:
                    b.op("vector", "tensor_tensor",
                         out=ns.l1_acc[:, hc * HALF + tau * 512: hc * HALF + (tau + 1) * 512],
                         in0=ns.l1_acc[:, hc * HALF + tau * 512: hc * HALF + (tau + 1) * 512],
                         in1=psY, op=ALU.add, inc="dv")
                ns.psY_tok[hc % 2] = b.tok("dv")
            ns.l1_h_guard = b.tok("pe")
            if tau == 1:
                ns.w1_slot_tok = [ns.l1_w1_free, ns.l1_w1_free]


def _l1ffn_half(b, ns, half, t0):
    b.mark("l1ffn.stats")
    v = "vector"
    ns.l1_h_guard = None
    t_a = _l1_stats_bcast(b, ns, lambda dc: Xc(ns, dc, t0, HALF), "l1ln")
    b.wait_tok(v, t_a)
    for dc in range(DC):
        X = Xc(ns, dc, t0, HALF)
        b.op(v, "tensor_tensor", out=ns.tmp1[:, 0:HALF], in0=X, in1=ns.bc_m[:, 0:HALF], op=ALU.subtract)
        b.op(v, "tensor_tensor", out=ns.l1_in_bf[:, dc * HALF:(dc + 1) * HALF],
             in0=ns.tmp1[:, 0:HALF], in1=ns.bc_a[:, 0:HALF], op=ALU.mult, inc="dv")
    ns.l1_in_tok = b.tok("dv")

    b.wait_tok("tensor", ns.l1_in_tok)
    b.wait("tensor", "dv", b.val("dv"))
    b.mark("l1ffn.gate")
    gtok = _l1_gate(b, ns, half, t0)
    b.mark("l1ffn.iter1")

    _l1_iter(b, ns, half, t0, 0, gtok)

    b.mark("l1ffn.boundary")
    b.wait(v, "pe", b.val("pe"))
    t_m2 = emit_bcast(b, ns, ns.row_m, ns.bc_m, HALF, None, base=0)
    t_a2 = emit_bcast(b, ns, ns.row_a, ns.bc_a, HALF, None, base=32)
    b.wait("gpsimd", "dv", ns.gbc_guard)
    b.dma("gpsimd", out=ns.g_row_bf[0:1, 0:HALF], in_=ns.l1_oh[8:9, 0:HALF], inc="gx")
    b.wait("tensor", "gx", b.val("gx"))
    v1_tok = emit_bcast(b, ns, ns.g_row_bf, ns.l1_v1bc, HALF, bf=True)
    ns.gbc_guard = b.val("dv")
    b.wait_tok(v, t_a2)
    b.wait_tok(v, v1_tok)
    for dc in range(DC):
        X = Xc(ns, dc, t0, HALF)
        ACC = ns.l1_acc[:, dc * HALF:(dc + 1) * HALF]
        b.op(v, "tensor_tensor", out=ns.tmp1[:, 0:HALF], in0=X, in1=ns.bc_m[:, 0:HALF], op=ALU.subtract)
        b.op(v, "tensor_tensor", out=ns.tmp2[:, 0:HALF], in0=ns.tmp1[:, 0:HALF], in1=ns.bc_a[:, 0:HALF],
             op=ALU.mult)
        b.op(v, "scalar_tensor_tensor", out=ns.l1_in_bf[:, dc * HALF:(dc + 1) * HALF],
             in0=ACC, scalar=0.5, in1=ns.tmp2[:, 0:HALF], op0=ALU.mult, op1=ALU.add)
        b.op(v, "tensor_tensor", out=ACC, in0=ACC, in1=ns.l1_v1bc[:, 0:HALF], op=ALU.mult)
        b.op(v, "tensor_tensor", out=ACC, in0=ACC, in1=ns.tmp2[:, 0:HALF], op=ALU.add, inc="dv")
    ns.l1_in_tok = b.tok("dv")
    ns.l1_h_guard = None

    b.mark("l1ffn.iter2")
    _l1_iter(b, ns, half, t0, 1, gtok)
    b.mark("l1ffn.final")

    b.wait(v, "pe", b.val("pe"))
    b.wait("scalar", "pe", b.val("pe"))
    b.wait("scalar", "dv", b.val("dv"))
    b.wait("tensor", "dv", b.val("dv"))
    t_a3 = _l1_stats_bcast(b, ns, lambda dc: ns.l1_acc[:, dc * HALF:(dc + 1) * HALF], "l1ln2")
    b.wait_tok(v, t_a3)
    for dc in range(DC):
        X = Xc(ns, dc, t0, HALF)
        ACC = ns.l1_acc[:, dc * HALF:(dc + 1) * HALF]
        b.op(v, "tensor_tensor", out=ns.tmp1[:, 0:HALF], in0=ACC, in1=ns.bc_m[:, 0:HALF], op=ALU.subtract)
        b.op(v, "tensor_tensor", out=ns.tmp2[:, 0:HALF], in0=ns.tmp1[:, 0:HALF], in1=ns.bc_a[:, 0:HALF],
             op=ALU.mult, inc="dv")
        b.wait("scalar", "dv", b.val("dv"))
        b.op("scalar", "activation", out=ns.tmp1[:, 0:HALF], in_=ns.tmp2[:, 0:HALF], func=AF.Identity,
             scale=ns.l1_gam05[:, dc:dc + 1], bias=ns.l1_bet05[:, dc:dc + 1], inc="ac")
        b.wait(v, "ac", b.val("ac"))
        b.op(v, "tensor_tensor", out=X, in0=X, in1=ns.tmp1[:, 0:HALF], op=ALU.add, inc="dv")


ARENA_WORDS = 29952


def _mk_carver(ns):
    pos = [0]

    def al(nbytes, align=64):
        p = (pos[0] + align - 1) // align * align
        pos[0] = p + nbytes
        assert pos[0] <= ARENA_WORDS * 4, f"arena overflow {pos[0]}"
        return p

    def f32(n, rows=128):
        p = al(n * 4)
        return ns.AR[0:rows, p // 4: p // 4 + n]

    def bf(n, rows=128):
        p = al(n * 2)
        return ns.AR_B[0:rows, p // 2: p // 2 + n]

    return f32, bf, al, pos


def carve_l0ffn(ns):
    f32, bf, al, pos = _mk_carver(ns)
    ns.xn_bf = bf(DC * HALF)
    ns.sg_bf = bf(DC * HALF)
    ns.w1a_sb = [bf(8 * 512), bf(8 * 512)]
    ns.w1b_sb = [bf(8 * 512), bf(8 * 512)]
    ns.w2_sb = [bf(8 * 128), bf(8 * 128)]
    ns.gluA2 = [f32(512), f32(512)]
    ns.gluS2 = [f32(512), f32(512)]
    ns.gluB2 = [f32(512), f32(512)]
    ns.g_bc = bf(HALF)
    ns.bc_b = f32(HALF)
    ns.gp_T = f32(T)
    ns.gp_Tb = bf(T)
    ns.bc_m = f32(HALF)
    ns.bc_a = f32(HALF)
    ns.tmp1 = f32(HALF)
    ns.tmp2 = f32(HALF)
    ns.sq = ns.tmp2
    ns.l0_bytes = pos[0]


def carve_l1ffn(ns):
    f32, bf, al, pos = _mk_carver(ns)
    ns.l1_in_bf = bf(DC * HALF)
    ns.l1_acc = f32(DC * HALF)
    ns.l1_w1full = bf(8 * 2048)
    ns.l1_w2 = [bf(16 * 128), bf(16 * 128)]
    ns.l1_hs = f32(512)
    ns.l1_oh = f32(HALF)
    ns.l1_ohb = bf(HALF)
    ns.l1_ohbc = bf(HALF)
    ns.l1_v1bc = bf(HALF)
    p = al(16384)
    ns.bc_m = ns.AR[:, p // 4: p // 4 + HALF]
    ns.bc_a = ns.AR[:, p // 4 + HALF: p // 4 + 2 * HALF]
    ns.tmp1 = ns.AR[:, p // 4 + 2 * HALF: p // 4 + 3 * HALF]
    ns.tmp2 = ns.AR[:, p // 4 + 3 * HALF: p // 4 + 4 * HALF]
    ns.sq = ns.tmp2
    ns.l1_h = ns.AR_B[:, p // 2: p // 2 + 16 * 512]
    ns.l1_bytes = pos[0]


def build_program(g, sublayers):
    nc = bass.Bass("TRN2", target_bir_lowering=False, debug=False)
    ns = make_ns(nc, g, sublayers)
    d = ns.dram

    with ExitStack() as ctx:
        sb = lambda name, shape, dt: ctx.enter_context(nc.sbuf_tensor("s_" + name, shape, dt))
        psum = lambda name, shape: ctx.enter_context(nc.psum_tensor(name, shape, F32))

        ns.X = sb("X", [128, DC * T], F32)
        ns.ident_f = sb("ident_f", [128, 128], F32)
        ns.ones_col = sb("ones_col", [128, 512], F32)
        ns.ones_bf = sb("ones_bf", [128, 128], BF16)
        arena = sb("arena", [128, ARENA_WORDS], F32)
        ns.AR = arena[:, :]
        ns.AR_B = arena.bitcast(BF16)[:, :]

        rows1 = sb("rows1", [65, T], F32)
        rows2 = sb("rows2", [1, T], F32)
        ns.row_m = rows1[0:1, :]
        ns.row_a = rows1[32:33, :]
        ns.row_b = rows1[64:65, :]
        ns.g_row = rows2[0:1, :]
        ns.g_row_full = rows2[0:1, :]
        ns.g_row_bf = rows2.bitcast(BF16)[0:1, :]
        ns.bc_g16 = sb("bc_g16", [128, 16], F32)
        ns.scol = sb("scol", [128, 128], F32)
        ns.rowstage = sb("rowstage", [128, 128], F32)

        ns.gl = sb("gl", [128, 16], F32)
        ns.gcur = sb("gcur", [128, 16], F32)
        ns.gmask = sb("gmask", [128, 16], F32)
        ns.gm = sb("gm", [128, 4], F32)
        ns.gex = sb("gex", [128, 48], F32)

        ns.wg_bf = sb("wg_bf", [128, DC * 16], BF16)
        ns.bg_row = sb("bg_row", [1, 16], F32)[0:1, :]
        ns.l0_ns = sb("l0_ns", [128, DC], F32)
        ns.l0_b1a = sb("l0_b1a", [128, 16 * 8], F32)
        ns.l0_b1b = sb("l0_b1b", [128, 16 * 8], F32)
        ns.b2_bf = sb("b2_bf", [16, 1024], BF16)

        ns.bqi_p = sb("bqi_p", [128, DC], F32)
        ns.bki_p = sb("bki_p", [128, DC], F32)

        ns.l1_gw_bf = sb("l1_gw_bf", [128, DC * 8], BF16)
        ns.l1_gb_row = sb("l1_gb_row", [1, 8], F32)[0:1, :]
        ns.l1_b1 = sb("l1_b1", [128, 8 * 16], F32)
        ns.l1_b2t = sb("l1_b2t", [40, 1024], BF16)
        ns.l1_gam = sb("l1_gam", [128, DC], F32)
        ns.l1_bet = sb("l1_bet", [128, DC], F32)
        ns.l1_gam05 = sb("l1_gam05", [128, DC], F32)
        ns.l1_bet05 = sb("l1_bet05", [128, DC], F32)

        ns.ps4 = psum("ps4", [128, 2048])
        ns.psA = ns.ps4[:, 0:1024]
        ns.psB = ns.ps4[:, 1024:2048]
        ns.psY = psum("psY", [128, 1024])
        ns.ps_misc = psum("ps_misc", [128, 512])
        ns.ps_misc2 = psum("ps_misc2", [128, 512])

        ns.sems = {}
        for s in ("xl", "wl", "w2a", "w2b", "qk", "qka", "qkb", "gx", "m1", "m2", "pe", "dv", "ac", "gp", "bar", "st"):
            ns.sems[s] = ctx.enter_context(nc.semaphore(s))

        block = ctx.enter_context(nc.Block())
        b = Builder(ns)

        def program(bb):
            ns.w1_slot_idx = 0
            ns.w2_slot_idx = 0
            ns.w1_slot_tok = [None, None]
            ns.w2_slot_tok = [None, None]
            ns.w1_load_tok = {}
            ns.w2_load_tok = {}
            ns.psA_tok = [None, None]
            ns.psB_tok = [None, None]
            ns.psY_tok = [None, None]
            ns.gbc_guard = 0
            ns.sg_guard = None
            ns.glu_par = 0
            ns.glu_free = [0, 0]
            ns.w2_cur_tok = [None, None]
            ns.w2_pre_tok = None
            ns.aw_guard = None
            ns.a_xln_guard = 0
            ns.a_pbf_guard = 0
            ns.a_cs_guard = [0, 0]
            ns.a_ps_tok = [None, None]
            ns.a_psY_tok = [None, None]
            ns.a_stage_guard = 0
            ns.a_p1_tok = [None] * 4
            ns.a_p2_tok = [None, None]
            ns.a_expT_guard = [None] * 8
            ns.a_qk_guard = [None, None]
            ns.a_op_guard = 0
            ns.bvi_tok = None
            ns.n_kv_guard = None
            ns.n_vk_tok = None
            ns.n_oh_guard = 0
            ns.n_oc_guard = [None, None]
            ns.n_vkps_tok = [None, None]
            ns.l1_w1_free = None
            for e in Builder.AUTODRAIN:
                setattr(bb, "_dirty_" + e, False)

            emit_load_consts(bb, ns)
            emit_load_x(bb, ns)
            need_l0 = "l0ffn" in sublayers
            need_l1 = "l1ffn" in sublayers
            if need_l0:
                bb.dma("gpsimd", out=ns.wg_bf.rearrange("p (dc e) -> p dc e", dc=DC),
                       in_=d["l0_wg"].rearrange("(dc p) e -> p dc e", p=128), inc="wl")
                bb.dma("sync", out=ns.bg_row, in_=d["l0_bg_row"], inc="xl")
                bb.dma("sync", out=ns.l0_ns[:], in_=d["l0_ns_p"], inc="xl")
                bb.dma("sync", out=ns.l0_b1a[:], in_=d["l0_b1a_p"], inc="xl")
                bb.dma("sync", out=ns.l0_b1b[:], in_=d["l0_b1b_p"], inc="xl")
                bb.dma("gpsimd", out=ns.b2_bf[:], in_=d["l0_b2"], inc="wl")
            if "l0attn" in sublayers:
                bb.dma("sync", out=ns.bqi_p[:], in_=d["bqi_p"], inc="xl")
                bb.dma("sync", out=ns.bki_p[:], in_=d["bki_p"], inc="xl")
            if need_l1:
                bb.dma("gpsimd", out=ns.l1_gw_bf.rearrange("p (dc e) -> p dc e", dc=DC),
                       in_=d["l1_gw"].rearrange("(dc p) e -> p dc e", p=128), inc="wl")
                bb.dma("sync", out=ns.l1_gb_row, in_=d["l1_gb_row"], inc="xl")
                bb.dma("sync", out=ns.l1_b1[:], in_=d["l1_b1_p"], inc="xl")
                bb.dma("gpsimd", out=ns.l1_b2t[0:8, :], in_=d["l1_b2"], inc="wl")
                bb.dma("gpsimd", out=ns.l1_b2t[32:40, :], in_=d["l1_b2"], inc="wl")
                bb.dma("sync", out=ns.l1_gam[:], in_=d["l1_gam_p"], inc="xl")
                bb.dma("sync", out=ns.l1_bet[:], in_=d["l1_bet_p"], inc="xl")
            ns.wg_tok = ("wl", bb.val("wl"))
            bb.wait("vector", "xl")
            bb.wait("vector", "wl")
            bb.wait("tensor", "xl")
            bb.wait_tok("tensor", ns.ones_tok)
            bb.wait("tensor", "wl")
            bb.wait("scalar", "xl")
            if need_l1:
                bb.op("vector", "tensor_scalar", out=ns.l1_gam05[:], in0=ns.l1_gam[:],
                      scalar1=0.5, scalar2=None, op0=ALU.mult)
                bb.op("vector", "tensor_scalar", out=ns.l1_bet05[:], in0=ns.l1_bet[:],
                      scalar1=0.5, scalar2=None, op0=ALU.mult, inc="dv")
            bb.barrier()

            for sl in sublayers:
                if sl == "l0ffn":
                    carve_l0ffn(ns)
                    emit_l0ffn(bb, ns)
                elif sl == "l1ffn":
                    carve_l1ffn(ns)
                    emit_l1ffn(bb, ns)
                elif sl == "l0attn":
                    carve_l0attn(ns)
                    emit_l0attn(bb, ns)
                elif sl == "l1attn":
                    carve_l1attn(ns)
                    emit_l1attn(bb, ns)
                else:
                    raise NotImplementedError(sl)
                bb.barrier()

            bb.mark("store")
            emit_store_x(bb, ns)

        for name in ENGINES:
            def body(engine, _name=name):
                b.begin(_name, engine)
                program(b)
            getattr(block, name)(body)

    return nc


def _run(nc, in_maps, **kw):
    return run_bass_kernel_spmd(nc, in_maps, core_ids=list(range(NCORES)), **kw)



def kernel(**inputs):
    g, xs = prep_host(inputs)
    seq = ["l0ffn", "l0attn", "l0ffn", "l1ffn", "l1attn", "l1ffn"]
    nc = build_program(g, seq)
    in_maps = [dict(g, xT=xs[b]) for b in range(NCORES)]
    res = run_bass_kernel_spmd(nc, in_maps, core_ids=list(range(NCORES)))
    out = np.stack([np.asarray(r["yT"]).T for r in res.results])
    return np.ascontiguousarray(out, dtype=np.float32)


# version 07
# speedup vs baseline: 1.1837x; 1.0424x over previous
import numpy as np
import ml_dtypes
from contextlib import ExitStack

import concourse.bass as bass
import concourse.mybir as mybir
from concourse.bass_utils import run_bass_kernel_spmd

F32 = mybir.dt.float32
BF16 = mybir.dt.bfloat16
AF = mybir.ActivationFunctionType
ALU = mybir.AluOpType
AX = mybir.AxisListType
BFNP = ml_dtypes.bfloat16

T = 2048
D = 1024
DC = 8
HALF = 1024
NCORES = 8

ENGINES = ("sync", "scalar", "vector", "tensor", "gpsimd")



class Builder:

    def __init__(self, ns):
        self.ns = ns
        self.cur = None
        self.eng = None
        self.cnt = {}

    def begin(self, engine_name, engine_obj):
        self.cur = engine_name
        self.eng = engine_obj
        self.cnt = {k: 0 for k in self.ns.sems}

    def val(self, sem):
        return self.cnt[sem]

    def tok(self, sem):
        return (sem, self.cnt[sem])

    def wait(self, eng, sem, val=None):
        if val is None:
            val = self.cnt[sem]
        if self.cur == eng and val > 0:
            self.eng.wait_ge(self.ns.sems[sem], val)

    def wait_tok(self, eng, t):
        if t is None:
            return
        if isinstance(t, list):
            for x in t:
                self.wait_tok(eng, x)
            return
        self.wait(eng, t[0], t[1])

    AUTODRAIN = ("vector", "scalar", "gpsimd")

    def op(self, eng, method, *args, inc=None, inc_n=1, **kw):
        if inc is not None:
            self.cnt[inc] += inc_n
        if self.cur != eng:
            return None
        if eng in self.AUTODRAIN and method not in ("dma_start", "nop"):
            if getattr(self, "_dirty_" + eng, False):
                self.eng.drain()
            setattr(self, "_dirty_" + eng, True)
        ins = getattr(self.eng, method)(*args, **kw)
        if inc is not None:
            ins.then_inc(self.ns.sems[inc], inc_n)
        return ins

    def dma(self, eng, out, in_, inc, **kw):
        return self.op(eng, "dma_start", inc=inc, inc_n=16, out=out, in_=in_, **kw)

    def drain(self, eng):
        if self.cur == eng:
            self.eng.drain()

    def mark(self, label):
        if self.cur is not None:
            self.ns.marks.setdefault(self.cur, []).append((label, self.ns.nc.next_id()))

    def barrier(self):
        self.cnt["bar"] += 5
        tgt = self.cnt["bar"]
        if self.cur is not None:
            if self.cur != "sync":
                self.eng.drain()
            self.eng.nop().then_inc(self.ns.sems["bar"], 1)
            self.eng.wait_ge(self.ns.sems["bar"], tgt)
            for e in self.AUTODRAIN:
                setattr(self, "_dirty_" + e, False)


class NS:
    pass



def _rope_tables():
    i = np.arange(D // 2, dtype=np.float32)
    theta = (np.float32(10000.0) ** (np.float32(-2.0) * (i - np.float32(1.0)) / np.float32(D))).astype(np.float32)
    ang = (np.arange(T, dtype=np.float32)[:, None] * theta[None, :]).astype(np.float32)
    c = np.cos(ang).astype(np.float32)
    s = np.sin(ang).astype(np.float32)
    return np.ascontiguousarray(c.T), np.ascontiguousarray(s.T)


def _causal_masks():
    qq = np.arange(128)[:, None]; kk = np.arange(512)[None, :]
    m1 = np.stack([(kk <= j * 128 + qq) for j in range(4)]).astype(np.float32)
    m1 = (1.0 - m1) * np.float32(-1e30)
    kk2 = np.arange(128)[:, None]; qq2 = np.arange(512)[None, :]
    m2 = np.stack([(kk2 <= qq2 - j * 128) for j in range(4)]).astype(np.float32)
    m2 = (1.0 - m2) * np.float32(-1e30)
    return np.ascontiguousarray(m1), np.ascontiguousarray(m2)


def _part_layout(v):
    E, L = v.shape
    n = L // 128
    return np.ascontiguousarray(v.reshape(E, n, 128).transpose(2, 0, 1).reshape(128, E * n))


def prep_host(inputs):
    g = {}
    f32 = np.float32
    asc = np.ascontiguousarray

    perm = np.concatenate([np.arange(0, D, 2), np.arange(1, D, 2)])
    g["wq_t"] = asc(np.asarray(inputs["L0_attn_wq"], f32).T[:, perm])
    g["wk_t"] = asc(np.asarray(inputs["L0_attn_wk"], f32).T[:, perm])
    g["wv_t"] = asc(np.asarray(inputs["L0_attn_wv"], f32).T[:, perm])
    ipw = np.asarray(inputs["L0_attn_in_proj_w"], f32)
    ipb = np.asarray(inputs["L0_attn_in_proj_b"], f32)
    g["wqi_t"] = asc(ipw[0:D].T[perm, :])
    g["wki_t"] = asc(ipw[D:2 * D].T[perm, :])
    g["wvi_t"] = asc(ipw[2 * D:3 * D].T[perm, :])
    g["bqi_p"] = _part_layout(ipb[0:D][None, :])
    g["bki_p"] = _part_layout(ipb[D:2 * D][None, :])
    g["bvi_p"] = _part_layout(ipb[2 * D:3 * D][None, :])
    g["outw_t"] = asc(np.asarray(inputs["L0_attn_out_w"], f32).T)
    g["bvi_row"] = asc(ipb[2 * D:3 * D][None, :])
    g["outb_row"] = asc(np.asarray(inputs["L0_attn_out_b"], f32)[None, :])
    g["rope_cos"], g["rope_sin"] = _rope_tables()
    g["mask1"], g["mask2"] = _causal_masks()

    g["l0_ns_p"] = _part_layout(np.asarray(inputs["L0_ffn_norm_scale"], f32)[None, :])
    g["l0_wg"] = asc(np.asarray(inputs["L0_ffn_wg"], f32))
    g["l0_bg_row"] = asc(np.asarray(inputs["L0_ffn_bg"], f32)[None, :])
    w1 = np.asarray(inputs["L0_ffn_w1"], f32)
    g["l0_w1a"] = asc(w1[:, :, 0::2])
    g["l0_w1b"] = asc(w1[:, :, 1::2])
    b1 = np.asarray(inputs["L0_ffn_b1"], f32)
    g["l0_b1a_p"] = _part_layout(asc(b1[:, 0::2]))
    g["l0_b1b_p"] = _part_layout(asc(b1[:, 1::2]))
    w2l0 = np.asarray(inputs["L0_ffn_w2"], f32)
    g["l0_w2p"] = asc(w2l0.reshape(16, 8, 128, 8, 128).transpose(0, 3, 2, 1, 4)
                      .reshape(16, 8, 128, 8 * 128))
    g["l0_b2"] = asc(np.asarray(inputs["L0_ffn_b2"], f32))

    wqkv = np.asarray(inputs["L1_attn_W_qkv"], f32)
    g["l1_wqT"] = asc(wqkv[0:D].T)
    g["l1_wkT"] = asc(wqkv[D:2 * D].T)
    g["l1_wvT"] = asc(wqkv[2 * D:3 * D].T)
    g["l1_wpT"] = asc(np.asarray(inputs["L1_attn_W_proj"], f32).T)
    g["l1_bp_row"] = asc(np.asarray(inputs["L1_attn_b_proj"], f32)[None, :])

    g["l1_gw"] = asc(np.asarray(inputs["L1_ffn_gate_W"], f32))
    g["l1_gb_row"] = asc(np.asarray(inputs["L1_ffn_gate_b"], f32)[None, :])
    g["l1_w1"] = asc(np.asarray(inputs["L1_ffn_W1"], f32))
    g["l1_b1_p"] = _part_layout(np.asarray(inputs["L1_ffn_b1"], f32))
    w2l1 = np.asarray(inputs["L1_ffn_W2"], f32)
    g["l1_w2p"] = asc(w2l1.reshape(8, 16, 128, 8, 128).transpose(0, 3, 2, 1, 4)
                      .reshape(8, 8, 128, 16 * 128))
    g["l1_b2"] = asc(np.asarray(inputs["L1_ffn_b2"], f32))
    g["l1_gam_p"] = _part_layout(np.asarray(inputs["L1_ffn_ln_gamma"], f32)[None, :])
    g["l1_bet_p"] = _part_layout(np.asarray(inputs["L1_ffn_ln_beta"], f32)[None, :])

    g["ident_f"] = np.eye(128, dtype=f32)

    x = np.asarray(inputs["L0_attn_x"], f32)
    xT = np.ascontiguousarray(x.transpose(0, 2, 1))
    return g, [np.ascontiguousarray(xT[b]) for b in range(8)]



def Xc(ns, dc, t0=0, tlen=T):
    return ns.X[:, dc * T + t0: dc * T + t0 + tlen]


def make_ns(nc, g, sublayers):
    ns = NS()
    ns.nc = nc
    ns.marks = {}
    ns.g = g
    d = {}

    def din(name, arr):
        d[name] = nc.dram_tensor(name, list(arr.shape), F32, kind="ExternalInput").ap()
        return d[name]

    ns.dram = d
    d["xT"] = nc.dram_tensor("xT", [D, T], F32, kind="ExternalInput").ap()
    d["yT"] = nc.dram_tensor("yT", [D, T], F32, kind="ExternalOutput").ap()
    for k, v in g.items():
        din(k, v)
    d["row_dram"] = nc.dram_tensor("row_dram", [128, 128], F32, kind="Internal").ap()
    d["qh_dram"] = nc.dram_tensor("qh_dram", [D, T], BF16, kind="Internal").ap()
    d["kh_dram"] = nc.dram_tensor("kh_dram", [D, T], BF16, kind="Internal").ap()
    d["q2_dram"] = nc.dram_tensor("q2_dram", [D, T], BF16, kind="Internal").ap()
    d["o2_dram"] = nc.dram_tensor("o2_dram", [D, T], BF16, kind="Internal").ap()
    d["c_l0w1a"] = nc.dram_tensor("c_l0w1a", [16, 2, 128, 8 * 512], BF16, kind="Internal").ap()
    d["c_l0w1b"] = nc.dram_tensor("c_l0w1b", [16, 2, 128, 8 * 512], BF16, kind="Internal").ap()
    d["c_l0w2"] = nc.dram_tensor("c_l0w2", [16, 8, 128, 8 * 128], BF16, kind="Internal").ap()
    d["c_l1w1"] = nc.dram_tensor("c_l1w1", [8, 4, 128, 8 * 512], BF16, kind="Internal").ap()
    d["c_l1w2"] = nc.dram_tensor("c_l1w2", [8, 8, 128, 16 * 128], BF16, kind="Internal").ap()
    return ns



def emit_load_x(b, ns):
    for dc in range(DC):
        b.dma("sync", out=Xc(ns, dc), in_=ns.dram["xT"][dc * 128:(dc + 1) * 128, :], inc="xl")
    ns.x_tok = b.tok("xl")


def emit_store_x(b, ns):
    b.barrier()
    for dc in range(DC):
        b.dma("scalar", out=ns.dram["yT"][dc * 128:(dc + 1) * 128, :], in_=Xc(ns, dc), inc="st")
    b.wait("scalar", "st")


def emit_load_consts(b, ns):
    b.op("vector", "memset", ns.ones_col[:], 1.0)
    b.op("vector", "memset", ns.ones_bf[:], 1.0, inc="dv")
    b.dma("sync", out=ns.ident_f[:], in_=ns.dram["ident_f"], inc="xl")
    ns.const_tok = b.tok("xl")
    ns.ones_tok = b.tok("dv")



def emit_colstats(b, ns, half, src, kinds_fn, out_rows, tag, nt=8):
    n_rows = len(out_rows)
    nw = nt
    prev_mm = None
    for dc in range(DC):
        b.wait_tok("scalar", prev_mm)
        b.op("scalar", "activation", out=ns.sq[:], in_=src(dc), func=AF.Square, inc="ac")
        b.wait("tensor", "ac", b.val("ac"))
        if dc == 0:
            b.wait("tensor", "dv", b.val("dv"))
        for w in range(nw):
            b.op("tensor", "matmul", ns.ps_misc[:, 2 * w:2 * w + 1],
                 src(dc)[:, 128 * w:128 * w + 128], ns.ones_col[:, 0:1],
                 start=(dc == 0 and w == 0), stop=False)
            b.op("tensor", "matmul", ns.ps_misc[:, 2 * w + 1:2 * w + 2],
                 ns.sq[:, 128 * w:128 * w + 128], ns.ones_col[:, 0:1],
                 start=False, stop=(dc == DC - 1 and w == nw - 1),
                 inc=("pe" if (w == nw - 1) else None))
        prev_mm = b.tok("pe")
    pe_done = b.tok("pe")
    b.wait_tok("vector", pe_done)
    b.wait_tok("scalar", pe_done)
    kinds_fn(b)
    b.wait("tensor", "dv", b.val("dv"))
    b.wait("tensor", "ac", b.val("ac"))
    b.op("tensor", "transpose", ns.ps_misc[0:n_rows * nw, 0:128],
         ns.scol[:, 0:n_rows * nw], ns.ident_f[:], inc="pe")
    b.wait("vector", "pe", b.val("pe"))
    b.wait("vector", "m1", b.val("m1"))
    b.op("vector", "tensor_copy", out=ns.rowstage[0:n_rows * nw, 0:128],
         in_=ns.ps_misc[0:n_rows * nw, 0:128], inc="dv")
    b.wait("sync", "dv", b.val("dv"))
    b.dma("sync", out=ns.dram["row_dram"][0:n_rows * nw, 0:128],
          in_=ns.rowstage[0:n_rows * nw, 0:128], inc="m1")
    b.wait("sync", "m1", b.val("m1"))
    for r in range(n_rows):
        b.dma("sync", out=out_rows[r][0:1, 0:nw * 128],
              in_=ns.dram["row_dram"][r * nw:(r + 1) * nw, 0:128]
              .rearrange("a c -> (a c)")[None, :], inc="m1")
    return b.tok("m1")


def emit_bcast(b, ns, row_ap, out_bc, n, wait_t=None, base=0, bf=False):
    nblk = (n + 511) // 512
    for j in range(nblk):
        w = min(512, n - j * 512)
        b.wait_tok("tensor", wait_t)
        b.wait("tensor", "dv", b.val("dv"))
        ones = ns.ones_bf if bf else ns.ones_col
        b.op("tensor", "matmul", ns.ps_misc[:, 0:w],
             ones[base:base + 1, 0:128], row_ap[0:1, j * 512:j * 512 + w],
             start=True, stop=True, inc="pe")
        b.wait("vector", "pe", b.val("pe"))
        b.op("vector", "tensor_copy", out=out_bc[:, j * 512:j * 512 + w],
             in_=ns.ps_misc[:, 0:w], inc="dv")
    return b.tok("dv")



def emit_l0ffn(b, ns):
    g = ns.g
    E = 16
    for half in range(2):
        t0 = half * HALF
        _l0ffn_half(b, ns, half, t0, E)
        b.wait("scalar", "cw")
        b.barrier()


def _l0ffn_half(b, ns, half, t0, E):
    b.mark(f"l0ffn.stats")
    d = ns.dram

    def kinds(bb):
        v = "vector"; s = "scalar"
        nw = 8
        S = ns.scol
        P = ns.ps_misc
        bb.op(v, "tensor_scalar", out=S[:, 0:nw], in0=P[:, 0:16].rearrange("p (w k) -> p k w", k=2)[:, 0, :],
              scalar1=1.0 / D, scalar2=None, op0=ALU.mult)
        bb.op(v, "tensor_scalar", out=S[:, 24:24 + nw], in0=P[:, 0:16].rearrange("p (w k) -> p k w", k=2)[:, 1, :],
              scalar1=1.0 / D, scalar2=None, op0=ALU.mult)
        bb.drain(v)
        bb.op(v, "tensor_tensor", out=S[:, 32:32 + nw], in0=S[:, 0:nw], in1=S[:, 0:nw], op=ALU.mult)
        bb.drain(v)
        bb.op(v, "tensor_tensor", out=S[:, 40:40 + nw], in0=S[:, 24:24 + nw], in1=S[:, 32:32 + nw], op=ALU.subtract)
        bb.drain(v)
        bb.op(v, "tensor_scalar", out=S[:, 48:48 + nw], in0=S[:, 40:40 + nw],
              scalar1=1e-5, scalar2=None, op0=ALU.add, inc="dv")
        bb.wait(s, "dv", bb.val("dv"))
        bb.op(s, "activation", out=S[:, 56:56 + nw], in_=S[:, 48:48 + nw], func=AF.Ln)
        bb.drain(s)
        bb.op(s, "activation", out=S[:, 8:8 + nw], in_=S[:, 56:56 + nw], func=AF.Exp, scale=-0.5, inc="ac")
        bb.wait(v, "ac", bb.val("ac"))
        bb.op(v, "reciprocal", out=S[:, 32:32 + nw], in_=S[:, 48:48 + nw])
        bb.drain(v)
        bb.op(v, "tensor_tensor", out=S[:, 40:40 + nw], in0=S[:, 40:40 + nw], in1=S[:, 32:32 + nw], op=ALU.mult)
        bb.drain(v)
        bb.op(v, "tensor_scalar", out=S[:, 40:40 + nw], in0=S[:, 40:40 + nw],
              scalar1=1e-5, scalar2=None, op0=ALU.add, inc="dv")
        bb.wait(s, "dv", bb.val("dv"))
        bb.op(s, "activation", out=S[:, 48:48 + nw], in_=S[:, 40:40 + nw], func=AF.Ln)
        bb.drain(s)
        bb.op(s, "activation", out=S[:, 48:48 + nw], in_=S[:, 48:48 + nw], func=AF.Exp, scale=-0.5, inc="ac")
        bb.wait(v, "ac", bb.val("ac"))
        bb.op(v, "tensor_tensor", out=S[:, 16:16 + nw], in0=S[:, 8:8 + nw], in1=S[:, 48:48 + nw], op=ALU.mult)
        bb.drain(v)
        bb.op(v, "tensor_scalar", out=S[:, 8:8 + nw], in0=S[:, 8:8 + nw],
              scalar1=0.5, scalar2=None, op0=ALU.mult, inc="dv")

    rows = [ns.row_m, ns.row_a, ns.row_b]
    rows_tok = emit_colstats(b, ns, half, lambda dc: Xc(ns, dc, t0, HALF), kinds, rows, "l0ln")

    t_m = emit_bcast(b, ns, ns.row_m, ns.bc_m, HALF, rows_tok, base=0)
    t_a = emit_bcast(b, ns, ns.row_a, ns.bc_a, HALF, rows_tok, base=32)
    t_b = emit_bcast(b, ns, ns.row_b, ns.bc_b, HALF, rows_tok, base=64)

    b.wait_tok("vector", t_b)
    for dc in range(DC):
        X = Xc(ns, dc, t0, HALF)
        t1 = ns.tmp1[:, 0:HALF]
        b.wait("vector", "ac", b.val("ac"))
        b.op("vector", "tensor_tensor", out=t1, in0=X, in1=ns.bc_m[:, 0:HALF], op=ALU.subtract)
        b.drain("vector")
        b.op("vector", "tensor_tensor", out=ns.tmp2[:, 0:HALF], in0=t1, in1=ns.bc_a[:, 0:HALF], op=ALU.mult)
        b.drain("vector")
        b.op("vector", "tensor_tensor", out=X, in0=X, in1=ns.tmp2[:, 0:HALF], op=ALU.add)
        b.drain("vector")
        b.op("vector", "tensor_tensor", out=ns.tmp2[:, 0:HALF], in0=t1, in1=ns.bc_b[:, 0:HALF], op=ALU.mult,
             inc="dv")
        b.wait("scalar", "dv", b.val("dv"))
        b.op("scalar", "activation", out=ns.xn_bf[:, dc * HALF:(dc + 1) * HALF], in_=ns.tmp2[:, 0:HALF],
             func=AF.Identity, scale=ns.l0_ns[:, dc:dc + 1], inc="ac")
    xn_tok = b.tok("ac")

    b.mark("l0ffn.gate")
    b.wait_tok("tensor", xn_tok)
    b.wait("tensor", "dv", b.val("dv"))
    gtok = _l0_gate(b, ns, half, t0)

    b.mark("l0ffn.experts")
    _l0_experts(b, ns, half, t0, gtok)


def _l0_gate(b, ns, half, t0):
    d = ns.dram
    nw = 8
    b.wait_tok("tensor", ns.wg_tok)
    bgt = emit_bcast(b, ns, ns.bg_row, ns.bc_g16, 16)
    ps_free = bgt
    for w in range(nw):
        tw = t0 + 128 * w
        b.wait_tok("tensor", ps_free)
        for dc in range(DC):
            b.op("tensor", "matmul", ns.ps_misc[:, 0:16],
                 ns.xn_bf[:, dc * HALF + 128 * w: dc * HALF + 128 * w + 128],
                 ns.wg_bf[:, dc * 16:(dc + 1) * 16],
                 start=(dc == 0), stop=(dc == DC - 1), inc=("pe" if dc == DC - 1 else None))
        b.wait("vector", "pe", b.val("pe"))
        v = "vector"
        L = ns.gl
        CUR = ns.gcur
        b.op(v, "tensor_tensor", out=L[:], in0=ns.ps_misc[:, 0:16], in1=ns.bc_g16[:, 0:16], op=ALU.add,
             inc="dv")
        ps_free = b.tok("dv")
        b.drain(v)
        b.op(v, "tensor_copy", out=CUR[:], in_=L[:])
        b.drain(v)
        b.op(v, "reduce_max", out=ns.gm[:, 0:1], in_=CUR[:], axis=AX.X, negate=True)
        b.drain(v)
        for j in range(3):
            b.op(v, "tensor_scalar", out=ns.gmask[:], in0=CUR[:], scalar1=ns.gm[:, 0:1],
                 scalar2=0.0, op0=ALU.add, op1=ALU.is_ge)
            b.drain(v)
            b.op(v, "scalar_tensor_tensor", out=CUR[:], in0=ns.gmask[:], scalar=-1e30,
                 in1=CUR[:], op0=ALU.mult, op1=ALU.add)
            b.drain(v)
            b.op(v, "reduce_max", out=ns.gm[:, 0:1], in_=CUR[:], axis=AX.X, negate=True)
            b.drain(v)
        b.op(v, "tensor_scalar", out=ns.gmask[:], in0=L[:], scalar1=ns.gm[:, 0:1],
             scalar2=0.0, op0=ALU.add, op1=ALU.is_ge)
        b.op(v, "reduce_max", out=ns.gm[:, 1:2], in_=L[:], axis=AX.X, negate=True, inc="dv")
        b.wait("scalar", "dv", b.val("dv"))
        GX = ns.gex[:, 0:16]
        b.op("scalar", "activation", out=GX, in_=L[:], func=AF.Exp, bias=ns.gm[:, 1:2], inc="ac")
        b.wait(v, "ac", b.val("ac"))
        b.op(v, "tensor_tensor", out=GX, in0=GX, in1=ns.gmask[:], op=ALU.mult)
        b.drain(v)
        b.op(v, "reduce_sum", out=ns.gm[:, 2:3], in_=GX, axis=AX.X)
        b.drain(v)
        b.op(v, "reciprocal", out=ns.gm[:, 3:4], in_=ns.gm[:, 2:3])
        b.drain(v)
        b.op(v, "tensor_scalar", out=GX, in0=GX, scalar1=ns.gm[:, 3:4],
             scalar2=0.5, op0=ALU.mult, op1=ALU.mult, inc="dv")
        b.wait("tensor", "dv", b.val("dv"))
        b.op("tensor", "transpose", ns.ps_misc2[0:16, 0:128], GX, ns.ident_f[:], inc="pe")
        b.wait(v, "pe", b.val("pe"))
        b.op(v, "tensor_copy", out=ns.gp_T[0:16, tw:tw + 128], in_=ns.ps_misc2[0:16, 0:128])
        b.op(v, "tensor_copy", out=ns.gp_Tb[0:16, tw:tw + 128], in_=ns.ps_misc2[0:16, 0:128], inc="dv")
    return b.tok("dv")


def cached_load(b, ns, key, slot_ap, fp32_src, cache_ap, sem, slot_guard):
    if key not in ns.cache_done:
        ns.cache_done[key] = True
        b.wait_tok("gpsimd", slot_guard)
        b.wait("gpsimd", "cw", b.val("cw"))
        b.dma("gpsimd", out=slot_ap, in_=fp32_src, inc=sem)
        ns.pending_cw.append((slot_ap, cache_ap))
        return b.tok(sem)
    else:
        b.wait_tok("scalar", slot_guard)
        b.wait("scalar", "cw", b.val("cw"))
        b.dma("scalar", out=slot_ap, in_=cache_ap, inc=sem + "c")
        return b.tok(sem + "c")


def flush_cw(b, ns, batch_tok):
    if not ns.pending_cw:
        return None
    for slot_ap, cache_ap in ns.pending_cw:
        b.wait_tok("scalar", batch_tok)
        b.wait("scalar", "cw", b.val("cw"))
        b.dma("scalar", out=cache_ap, in_=slot_ap, inc="cw")
    ns.pending_cw = []
    return b.tok("cw")


def _l0_w2_load(b, ns, e, hc, slot, sem, flush=True):
    d = ns.dram
    tok = cached_load(b, ns, ("l0w2", e, hc), ns.w2_sb[slot],
                      d["l0_w2p"][e, hc], d["c_l0w2"][e, hc], sem,
                      ns.w2_slot_tok[slot])
    if flush:
        flush_cw(b, ns, tok)
    return tok


def _l0_experts(b, ns, half, t0, gtok):
    d = ns.dram
    E = 16
    NFH = 2
    for e in range(E):
        for fh in range(NFH):
            slot = ns.w1_slot_idx
            ns.w1_slot_idx = (ns.w1_slot_idx + 1) % 2
            cached_load(b, ns, ("l0w1a", e, fh),
                        ns.w1a_sb[slot].rearrange("p (dc f) -> p dc f", dc=8),
                        d["l0_w1a"][e, :, fh * 512:(fh + 1) * 512]
                        .rearrange("(dc p) f -> p dc f", p=128),
                        d["c_l0w1a"][e, fh].rearrange("p (dc f) -> p dc f", dc=8),
                        "wl", ns.w1_slot_tok[slot])
            cached_load(b, ns, ("l0w1b", e, fh),
                        ns.w1b_sb[slot].rearrange("p (dc f) -> p dc f", dc=8),
                        d["l0_w1b"][e, :, fh * 512:(fh + 1) * 512]
                        .rearrange("(dc p) f -> p dc f", p=128),
                        d["c_l0w1b"][e, fh].rearrange("p (dc f) -> p dc f", dc=8),
                        "wl", ns.w1_slot_tok[slot])
        for hc0 in range(2):
            _l0_w2_load(b, ns, e, hc0, hc0, "wl", flush=False)
        wtok = [b.tok("wl"), b.tok("wlc")]
        flush_cw(b, ns, wtok)
        for fh in range(NFH):
            ns.w1_load_tok[(e, fh)] = wtok
        ns.w2_pre_tok_l0 = wtok
        ns.w2_cur_tok_l0 = [None, None]

        b.wait_tok("sync", gtok)
        b.wait("sync", "dv", ns.gbc_guard)
        b.dma("sync", out=ns.g_row[0:1, 0:HALF], in_=ns.gp_T[e:e + 1, t0:t0 + HALF], inc="m1")
        b.wait("tensor", "m1", b.val("m1"))
        gb_tok = emit_bcast(b, ns, ns.g_row, ns.g_bc, HALF)
        ns.gbc_guard = b.val("dv")

        for fh in range(NFH):
            w1slot = (e * NFH + fh) % 2
            b.wait_tok("tensor", ns.w1_load_tok[(e, fh)])
            if e == 0 and fh == 0:
                b.wait_tok("tensor", b.tok("ac"))
            for c in range(4):
                ic = fh * 4 + c
                for tau in range(2):
                    psA = ns.psA[:, tau * 512:(tau + 1) * 512]
                    psB = ns.psB[:, tau * 512:(tau + 1) * 512]
                    b.wait_tok("tensor", ns.psA_tok[tau])
                    b.wait_tok("tensor", ns.psB_tok[tau])
                    for dc in range(DC):
                        b.op("tensor", "matmul", psA,
                             ns.w1a_sb[w1slot][:, dc * 512 + c * 128: dc * 512 + c * 128 + 128],
                             ns.xn_bf[:, dc * HALF + tau * 512: dc * HALF + (tau + 1) * 512],
                             start=(dc == 0), stop=(dc == DC - 1),
                             inc=("pe" if dc == DC - 1 else None))
                    peA = b.tok("pe")
                    for dc in range(DC):
                        b.op("tensor", "matmul", psB,
                             ns.w1b_sb[w1slot][:, dc * 512 + c * 128: dc * 512 + c * 128 + 128],
                             ns.xn_bf[:, dc * HALF + tau * 512: dc * HALF + (tau + 1) * 512],
                             start=(dc == 0), stop=(dc == DC - 1),
                             inc=("pe" if dc == DC - 1 else None))
                    peB = b.tok("pe")
                    v = "vector"
                    par = ns.glu_par
                    ns.glu_par ^= 1
                    gluA = ns.gluA2[par]; gluS = ns.gluS2[par]; gluB = ns.gluB2[par]
                    b.wait_tok(v, peA)
                    b.wait(v, "gp", ns.glu_free[par])
                    b.op(v, "tensor_scalar", out=gluA[:], in0=psA,
                         scalar1=ns.l0_b1a[:, e * 8 + ic:e * 8 + ic + 1], scalar2=7.0,
                         op0=ALU.add, op1=ALU.min, inc="dv")
                    ns.psA_tok[tau] = b.tok("dv")
                    b.wait("scalar", "dv", b.val("dv"))
                    b.op("scalar", "activation", out=gluS[:], in_=gluA[:],
                         func=AF.Sigmoid, scale=1.702, inc="ac")
                    b.wait_tok("scalar", peB)
                    b.op("scalar", "activation", out=gluB[:], in_=psB,
                         func=AF.Identity, bias=ns.l0_b1b[:, e * 8 + ic:e * 8 + ic + 1], inc="ac")
                    ns.psB_tok[tau] = b.tok("ac")
                    b.wait(v, "ac", b.val("ac"))
                    b.op(v, "tensor_scalar", out=gluB[:], in0=gluB[:],
                         scalar1=7.0, scalar2=-7.0, op0=ALU.min, op1=ALU.max, inc="dv")
                    gp = "gpsimd"
                    b.wait(gp, "dv", b.val("dv"))
                    b.wait(gp, "ac", b.val("ac"))
                    b.op(gp, "tensor_tensor", out=gluA[:], in0=gluA[:], in1=gluS[:], op=ALU.mult,
                         inc="gp")
                    b.wait(v, "gp", b.val("gp"))
                    b.op(v, "scalar_tensor_tensor", out=gluB[:], in0=gluB[:], scalar=1.0,
                         in1=gluA[:], op0=ALU.add, op1=ALU.mult, inc="dv")
                    b.wait(gp, "dv", b.val("dv"))
                    b.wait_tok(gp, gb_tok)
                    b.wait_tok(gp, ns.sg_guard)
                    b.op(gp, "tensor_tensor", out=ns.sg_bf[:, ic * HALF + tau * 512: ic * HALF + (tau + 1) * 512],
                         in0=gluB[:], in1=ns.g_bc[:, tau * 512:(tau + 1) * 512], op=ALU.mult,
                         inc="gp")
                    ns.glu_free[par] = b.val("gp")
            ns.w1_slot_tok[w1slot] = b.tok("pe")
        sg_tok = b.tok("gp")

        b.wait_tok("tensor", sg_tok)
        for hc in range(DC):
            w2slot = hc % 2
            w2tok = ns.w2_pre_tok_l0 if hc < 2 else ns.w2_cur_tok_l0[w2slot]
            b.wait_tok("tensor", w2tok)
            b.wait_tok("tensor", ns.psY_tok[0])
            for tau in range(2):
                psY = ns.psY[:, tau * 512:(tau + 1) * 512]
                for ic in range(8):
                    b.op("tensor", "matmul", psY,
                         ns.w2_sb[w2slot][:, ic * 128:(ic + 1) * 128],
                         ns.sg_bf[:, ic * HALF + tau * 512: ic * HALF + (tau + 1) * 512],
                         start=(ic == 0), stop=(ic == 7 and e != 0),
                         inc=("pe" if (ic == 7 and e != 0 and tau == 1) else None))
                if e == 0:
                    b.op("tensor", "matmul", psY,
                         ns.b2_bf[:, hc * 128:(hc + 1) * 128],
                         ns.gp_Tb[0:16, t0 + tau * 512: t0 + (tau + 1) * 512],
                         start=False, stop=True, inc=("pe" if tau == 1 else None))
            peY = b.tok("pe")
            b.wait_tok("vector", peY)
            b.op("vector", "tensor_tensor",
                 out=Xc(ns, hc, t0, HALF),
                 in0=Xc(ns, hc, t0, HALF), in1=ns.psY[:, 0:HALF], op=ALU.add, inc="dv")
            ns.psY_tok[0] = b.tok("dv")
            ns.w2_slot_tok[w2slot] = b.tok("pe")
            if hc + 2 < DC:
                ns.w2_cur_tok_l0[w2slot] = _l0_w2_load(b, ns, e, hc + 2, w2slot,
                                                       "w2a" if w2slot == 0 else "w2b")
        ns.sg_guard = b.tok("pe")





def carve_l0attn(ns):
    f32, bf, al, pos = _mk_carver(ns)
    ns.a_vh = bf(16 * 1024)
    u0 = al(0, align=64)
    f32p, bfp, alp, posp = _mk_carver(ns)
    posp[0] = u0
    ns.a_bcm = f32p(512)
    ns.a_bcr = f32p(512)
    ns.a_xln = bfp(8 * 512)
    ns.a_pbf = bfp(8 * 512)
    ns.a_cos = f32p(2 * 512)
    ns.a_sin = f32p(2 * 512)
    ns.a_bvi = f32p(1024)
    ns.sq = f32p(1024)
    ns.tmp1 = f32p(512)
    ns.a_w = bfp(8 * 1024)
    ns.a_wi = bfp(8 * 1024)
    ns.a_stage = bfp(8 * 512)
    ns.a_rs1 = f32p(512)
    ns.a_rs2 = f32p(512)
    proj_end = posp[0]
    f32a, bfa, ala, posa = _mk_carver(ns)
    posa[0] = u0
    pj = ala(8192)
    ns.a_junk = ns.AR[:, pj // 4: pj // 4 + 2048]
    ns.a_expT = ns.AR_B[:, pj // 2: pj // 2 + 4096]
    ns.a_outw = bfa(8 * 1024)
    ns.a_m1 = bfa(4 * 512)
    ns.a_m2 = bfa(4 * 512)
    ns.a_qh = [bfa(2048, rows=64), bfa(2048, rows=64)]
    ns.a_kh = [bfa(2048, rows=64), bfa(2048, rows=64)]
    ns.a_opair = bfa(T)
    ns.a_zcol = f32a(16)
    ns.a_zbc = bfa(512)
    ns.a_sc1 = f32a(512)
    ns.a_outb = f32a(1024)
    attn_end = posa[0]
    pos[0] = max(proj_end, attn_end)
    ns.a_bytes = pos[0]


def emit_l0attn(b, ns):
    b.mark("l0attn.proj")
    d = ns.dram
    v = "vector"
    NT4 = 4
    H = 16
    HD = 64

    b.dma("sync", out=ns.a_bvi[0:1, 0:D], in_=d["bvi_row"], inc="xl")
    ns.bvi_tok = b.tok("xl")
    for sh in range(2):
        st0 = sh * HALF
        rows = [ns.row_m[0:1, st0:st0 + HALF], ns.row_a[0:1, st0:st0 + HALF]]
        rt = emit_colstats(b, ns, 0, lambda dc: Xc(ns, dc, st0, HALF),
                           lambda bb: _kinds_ln2(bb, ns, nw=8), rows, "aln", nt=8)

    for P in range(3):
        wsrc = ("wq_t", "wk_t", "wv_t")[P]
        wisrc = ("wqi_t", "wki_t", "wvi_t")[P]
        b.wait_tok("gpsimd", ns.aw_guard)
        b.dma("gpsimd", out=ns.a_w.rearrange("p (dc f) -> p dc f", dc=8),
              in_=d[wsrc].rearrange("(dc p) f -> p dc f", p=128), inc="wl")
        b.dma("gpsimd", out=ns.a_wi.rearrange("p (dc f) -> p dc f", dc=8),
              in_=d[wisrc].rearrange("(dc p) f -> p dc f", p=128), inc="wl")
        wtok = b.tok("wl")
        for tau in range(NT4):
            tw = tau * 512
            b.wait(v, "pe", ns.a_xln_guard)
            emit_bcast(b, ns, ns.row_m[0:1, tw:tw + 512], ns.a_bcm, 512, rt, base=0)
            t_r = emit_bcast(b, ns, ns.row_a[0:1, tw:tw + 512], ns.a_bcr, 512, rt, base=32)
            b.wait_tok(v, t_r)
            for dc in range(DC):
                b.op(v, "tensor_tensor", out=ns.tmp1[:, 0:512], in0=Xc(ns, dc, tw, 512),
                     in1=ns.a_bcm[:, 0:512], op=ALU.subtract)
                b.op(v, "tensor_tensor", out=ns.a_xln[:, dc * 512:(dc + 1) * 512],
                     in0=ns.tmp1[:, 0:512], in1=ns.a_bcr[:, 0:512], op=ALU.mult, inc="dv")
            xln_tok = b.tok("dv")
            def cs_load(i, csl):
                b.wait("sync", "dv", ns.a_cs_guard[csl])
                b.dma("sync", out=ns.a_cos[:, csl * 512:(csl + 1) * 512],
                      in_=d["rope_cos"][i * 128:(i + 1) * 128, tw:tw + 512], inc="m2")
                b.dma("sync", out=ns.a_sin[:, csl * 512:(csl + 1) * 512],
                      in_=d["rope_sin"][i * 128:(i + 1) * 128, tw:tw + 512], inc="m2")
                return b.tok("m2")

            cs_load(0, 0)
            cs_pre = cs_load(1, 1)
            cs_next = [cs_pre, cs_pre]
            b.wait_tok("tensor", wtok)
            b.wait_tok("tensor", xln_tok)
            for i in range(4):
                csl = i % 2
                cs_tok = cs_next[csl]
                for half_ in range(2):
                    fc = i + 4 * half_
                    ps = ns.ps4[:, half_ * 512:(half_ + 1) * 512]
                    b.wait_tok("tensor", ns.a_ps_tok[half_])
                    for dc in range(DC):
                        b.op("tensor", "matmul", ps,
                             ns.a_w[:, dc * 1024 + fc * 128: dc * 1024 + fc * 128 + 128],
                             ns.a_xln[:, dc * 512:(dc + 1) * 512],
                             start=(dc == 0), stop=(dc == DC - 1),
                             inc=("pe" if dc == DC - 1 else None))
                pe_pair = b.tok("pe")
                b.wait_tok(v, pe_pair)
                b.wait_tok(v, cs_tok)
                b.wait(v, "pe", ns.a_pbf_guard)
                ci = ns.a_cos[:, csl * 512:(csl + 1) * 512]
                si = ns.a_sin[:, csl * 512:(csl + 1) * 512]
                xe = ns.ps4[:, 0:512]
                xo = ns.ps4[:, 512:1024]
                b.op(v, "tensor_tensor", out=ns.a_rs1[:, 0:512], in0=xe, in1=ci, op=ALU.mult)
                b.op(v, "tensor_tensor", out=ns.a_rs2[:, 0:512], in0=xo, in1=si, op=ALU.mult)
                b.op(v, "tensor_tensor", out=ns.a_pbf[:, i * 512:(i + 1) * 512],
                     in0=ns.a_rs1[:, 0:512], in1=ns.a_rs2[:, 0:512], op=ALU.add)
                b.op(v, "tensor_tensor", out=ns.a_rs1[:, 0:512], in0=xo, in1=ci, op=ALU.mult)
                b.op(v, "tensor_tensor", out=ns.a_rs2[:, 0:512], in0=xe, in1=si, op=ALU.mult,
                     inc="dv")
                ns.a_ps_tok = [b.tok("dv"), b.tok("dv")]
                b.op(v, "tensor_tensor", out=ns.a_pbf[:, (i + 4) * 512:(i + 5) * 512],
                     in0=ns.a_rs1[:, 0:512], in1=ns.a_rs2[:, 0:512], op=ALU.subtract, inc="dv")
                ns.a_cs_guard[csl] = b.val("dv")
                if i == 1:
                    cs_load(2, 0)
                    tk = cs_load(3, 1)
                    cs_next = [tk, tk]
            pbf_tok = b.tok("dv")
            b.wait_tok("tensor", pbf_tok)
            if P < 2:
                bia = (ns.bqi_p, ns.bki_p)[P]
                for fc in range(DC):
                    ps = ns.psY[:, (fc % 2) * 512:(fc % 2) * 512 + 512]
                    b.wait_tok("tensor", ns.a_psY_tok[fc % 2])
                    for dc in range(DC):
                        b.op("tensor", "matmul", ps,
                             ns.a_wi[:, dc * 1024 + fc * 128: dc * 1024 + fc * 128 + 128],
                             ns.a_pbf[:, dc * 512:(dc + 1) * 512],
                             start=(dc == 0), stop=(dc == DC - 1),
                             inc=("pe" if dc == DC - 1 else None))
                    peq = b.tok("pe")
                    b.wait_tok("scalar", peq)
                    b.wait("scalar", "m1", ns.a_stage_guard)
                    b.op("scalar", "activation", out=ns.a_stage[:, fc * 512:(fc + 1) * 512],
                         in_=ps, func=AF.Identity, bias=bia[:, fc:fc + 1], inc="ac")
                    ns.a_psY_tok[fc % 2] = b.tok("ac")
                st_tok = b.tok("ac")
                b.wait_tok("sync", st_tok)
                dst = d["qh_dram"] if P == 0 else d["kh_dram"]
                b.dma("sync", out=dst.rearrange("(fc p) t -> p fc t", p=128)[:, :, tw:tw + 512],
                      in_=ns.a_stage.rearrange("p (fc t) -> p fc t", fc=8), inc="m1")
                ns.a_stage_guard = b.val("m1")
            else:
                b.wait_tok("tensor", ns.bvi_tok)
                for nw_ in range(4):
                    for fh in range(2):
                        ps = ns.psY[:, fh * 512:(fh + 1) * 512]
                        b.wait_tok("tensor", ns.a_psY_tok[fh])
                        for dc in range(DC):
                            b.op("tensor", "matmul", ps,
                                 ns.a_pbf[:, dc * 512 + nw_ * 128: dc * 512 + nw_ * 128 + 128],
                                 ns.a_wi[:, dc * 1024 + fh * 512: dc * 1024 + (fh + 1) * 512],
                                 start=(dc == 0), stop=False)
                        b.op("tensor", "matmul", ps,
                             ns.ones_col[0:1, 0:128],
                             ns.a_bvi[0:1, fh * 512:(fh + 1) * 512],
                             start=False, stop=True, inc="pe")
                        pev = b.tok("pe")
                        b.wait_tok(v, pev)
                        b.op(v, "tensor_copy",
                             out=ns.a_vh[:, (tau * 4 + nw_) * 1024 + fh * 512:
                                         (tau * 4 + nw_) * 1024 + (fh + 1) * 512],
                             in_=ps, inc="dv")
                        ns.a_psY_tok[fh] = b.tok("dv")
            ns.a_xln_guard = b.val("pe")
            ns.a_pbf_guard = b.val("pe")
        ns.aw_guard = b.tok("pe")

    b.mark("l0attn.attn")
    b.wait("sync", "m1")
    b.barrier()
    b.dma("gpsimd", out=ns.a_m1.rearrange("p (j k) -> p j k", j=4),
          in_=d["mask1"].rearrange("j p k -> p j k"), inc="wl")
    b.dma("gpsimd", out=ns.a_m2.rearrange("p (j k) -> p j k", j=4),
          in_=d["mask2"].rearrange("j p k -> p j k"), inc="wl")
    b.dma("gpsimd", out=ns.a_outw.rearrange("p (pr f) -> p pr f", pr=8),
          in_=d["outw_t"].rearrange("(pr p) f -> p pr f", p=128), inc="wl")
    b.dma("sync", out=ns.a_outb[0:1, 0:D], in_=d["outb_row"], inc="m1")
    amw_tok = b.tok("wl")
    outb_tok = b.tok("m1")
    b.wait_tok("tensor", outb_tok)
    b.wait_tok("tensor", amw_tok)

    def qk_load(h, hs, sem):
        b.wait_tok("gpsimd", ns.a_qk_guard[hs])
        b.dma("gpsimd", out=ns.a_qh[hs][0:64, :], in_=d["qh_dram"][h * 64:(h + 1) * 64, :], inc=sem)
        b.dma("gpsimd", out=ns.a_kh[hs][0:64, :], in_=d["kh_dram"][h * 64:(h + 1) * 64, :], inc=sem)
        return b.tok(sem)

    qk_load(0, 0, "qk")
    qk_pre = qk_load(1, 1, "qk")
    qk_next = [qk_pre, qk_pre]
    for h in range(H):
        hs = h % 2
        qk_tok = qk_next[hs]
        b.wait_tok("tensor", qk_tok)
        for qs in range(16):
            ktn = qs // 4 + 1
            for kt in range(ktn):
                ps = ns.ps4[:, kt * 512:(kt + 1) * 512]
                b.wait_tok("tensor", ns.a_p1_tok[kt])
                b.op("tensor", "matmul", ps,
                     ns.a_qh[hs][0:64, qs * 128:(qs + 1) * 128],
                     ns.a_kh[hs][0:64, kt * 512:(kt + 1) * 512],
                     start=True, stop=True, inc="pe")
            p1_tok = b.tok("pe")
            b.wait_tok(v, p1_tok)
            jdiag = qs % 4
            b.op(v, "tensor_tensor", out=ns.ps4[:, (ktn - 1) * 512:ktn * 512],
                 in0=ns.ps4[:, (ktn - 1) * 512:ktn * 512],
                 in1=ns.a_m1[:, jdiag * 512:(jdiag + 1) * 512], op=ALU.add, inc="dv")
            b.op(v, "reduce_max", out=ns.gm[:, 0:1], in_=ns.ps4[:, 0:ktn * 512], axis=AX.X,
                 negate=True)
            b.op(v, "tensor_scalar", out=ns.gm[:, 1:2], in0=ns.gm[:, 0:1],
                 scalar1=0.125, scalar2=None, op0=ALU.mult, inc="dv")
            b.wait("scalar", "dv", b.val("dv"))
            b.op("scalar", "activation", out=ns.a_junk[:, 0:ktn * 512], in_=ns.ps4[:, 0:ktn * 512],
                 func=AF.Exp, scale=0.125, bias=ns.gm[:, 1:2],
                 accum_out=ns.gm[:, 2:3], inc="ac")
            ps1_free = b.tok("ac")
            ns.a_p1_tok = [ps1_free] * 4
            b.wait("scalar", "ac", b.val("ac"))
            b.op("scalar", "activation", out=ns.gm[:, 3:4], in_=ns.gm[:, 2:3], func=AF.Ln, inc="ac")
            b.wait(v, "ac", b.val("ac"))
            b.op(v, "tensor_tensor", out=ns.a_zcol[:, qs:qs + 1], in0=ns.gm[:, 1:2],
                 in1=ns.gm[:, 3:4], op=ALU.subtract, inc="dv")
        zc_tok = b.tok("dv")
        b.wait_tok("tensor", zc_tok)
        b.wait("tensor", "dv", b.val("dv"))
        b.op("tensor", "transpose", ns.ps_misc2[0:16, 0:128], ns.a_zcol[:, 0:16],
             ns.ident_f[:], inc="pe")
        b.wait(v, "pe", b.val("pe"))
        b.op(v, "tensor_copy", out=ns.rowstage[0:16, 0:128], in_=ns.ps_misc2[0:16, 0:128], inc="dv")
        b.wait("sync", "dv", b.val("dv"))
        b.dma("sync", out=d["row_dram"][0:16, 0:128], in_=ns.rowstage[0:16, 0:128], inc="m1")
        b.wait("sync", "m1", b.val("m1"))
        b.wait("gpsimd", "m1", b.val("m1"))
        b.dma("gpsimd", out=ns.g_row_bf[0:1, 0:T],
              in_=d["row_dram"][0:16, 0:128].rearrange("a c -> (a c)")[None, :], inc="gx")
        zrow_tok = b.tok("gx")

        for tq in range(NT4):
            b.wait_tok("tensor", zrow_tok)
            b.wait("tensor", "dv", b.val("dv"))
            b.op("tensor", "matmul", ns.ps_misc[:, 0:512],
                 ns.ones_bf[0:1, 0:128], ns.g_row_bf[0:1, tq * 512:(tq + 1) * 512],
                 start=True, stop=True, inc="pe")
            b.wait(v, "pe", b.val("pe"))
            b.op(v, "tensor_copy", out=ns.a_zbc[:, 0:512], in_=ns.ps_misc[:, 0:512], inc="dv")
            zbc_tok = b.tok("dv")
            ktn2 = tq * 4 + 4
            for kt in range(ktn2):
                slot = kt % 8
                ps = ns.psY[:, (kt % 2) * 512:(kt % 2) * 512 + 512]
                b.wait_tok("tensor", ns.a_p2_tok[kt % 2])
                b.op("tensor", "matmul", ps,
                     ns.a_kh[hs][0:64, kt * 128:(kt + 1) * 128],
                     ns.a_qh[hs][0:64, tq * 512:(tq + 1) * 512],
                     start=True, stop=True, inc="pe")
                p2_tok = b.tok("pe")
                b.wait_tok(v, p2_tok)
                b.wait_tok(v, zbc_tok)
                b.wait_tok(v, ns.a_expT_guard[slot])
                jd = kt - (tq * 4)
                if jd >= 0:
                    b.op(v, "tensor_tensor", out=ps, in0=ps,
                         in1=ns.a_m2[:, jd * 512:(jd + 1) * 512], op=ALU.add)
                b.op(v, "scalar_tensor_tensor", out=ns.a_sc1[:, 0:512], in0=ps, scalar=0.125,
                     in1=ns.a_zbc[:, 0:512], op0=ALU.mult, op1=ALU.add, inc="dv")
                ns.a_p2_tok[kt % 2] = b.tok("dv")
                b.wait("scalar", "dv", b.val("dv"))
                b.op("scalar", "activation", out=ns.a_expT[:, slot * 512:(slot + 1) * 512],
                     in_=ns.a_sc1[:, 0:512], func=AF.Exp, inc="ac")
                exp_tok = b.tok("ac")
                b.wait_tok("tensor", exp_tok)
                b.op("tensor", "matmul", ns.ps_misc2[0:64, 0:512],
                     ns.a_vh[:, kt * 1024 + h * 64: kt * 1024 + h * 64 + 64],
                     ns.a_expT[:, slot * 512:(slot + 1) * 512],
                     start=(kt == 0), stop=(kt == ktn2 - 1),
                     inc=("pe" if kt == ktn2 - 1 else None))
                ns.a_expT_guard[slot] = b.tok("pe")
            av_tok = b.tok("pe")
            b.wait_tok(v, av_tok)
            b.op(v, "tensor_copy", out=ns.a_opair[64 * hs:64 * hs + 64, tq * 512:(tq + 1) * 512],
                 in_=ns.ps_misc2[0:64, 0:512], inc="dv")

        if hs == 1:
            opair_tok = b.tok("dv")
            pair = h // 2
            b.wait_tok("tensor", opair_tok)
            for dc in range(DC):
                for tq in range(NT4):
                    ps = ns.ps_misc[:, 0:512]
                    b.wait("tensor", "dv", ns.a_op_guard)
                    if pair == 0:
                        b.op("tensor", "matmul", ps,
                             ns.a_outw[:, pair * 1024 + dc * 128: pair * 1024 + dc * 128 + 128],
                             ns.a_opair[:, tq * 512:(tq + 1) * 512],
                             start=True, stop=False)
                        b.op("tensor", "matmul", ps,
                             ns.a_outb[0:1, dc * 128:(dc + 1) * 128],
                             ns.ones_col[0:1, 0:512],
                             start=False, stop=True, inc="pe")
                    else:
                        b.op("tensor", "matmul", ps,
                             ns.a_outw[:, pair * 1024 + dc * 128: pair * 1024 + dc * 128 + 128],
                             ns.a_opair[:, tq * 512:(tq + 1) * 512],
                             start=True, stop=True, inc="pe")
                    peo = b.tok("pe")
                    b.wait_tok(v, peo)
                    b.op(v, "tensor_tensor", out=Xc(ns, dc, tq * 512, 512),
                         in0=Xc(ns, dc, tq * 512, 512), in1=ps, op=ALU.add, inc="dv")
                    ns.a_op_guard = b.val("dv")
        ns.a_qk_guard[hs] = b.tok("pe")
        if h + 2 < H:
            qk_next[hs] = qk_load(h + 2, hs, "qka" if hs == 0 else "qkb")



def carve_l1attn(ns):
    f32, bf, al, pos = _mk_carver(ns)
    ns.n_ztile = bf(8 * 512)
    ns.n_w = bf(3 * 8 * 1024)
    ns.n_ktile = bf(4 * 1024)
    ns.n_vtile = bf(4 * 1024)
    ns.n_stage = bf(8 * 512)
    ns.n_vk = f32(32 * 32)
    ns.n_vkT = bf(64)
    ns.n_qh = [bf(2048, rows=32), bf(2048, rows=32)]
    ns.n_rbc = f32(512)
    u0 = al(0, align=64)
    c1 = _mk_carver(ns); c1[3][0] = u0
    ns.sq = c1[0](1024)
    ns.tmp1 = c1[0](512)
    ns.bc_m = c1[0](512)
    ns.bc_a = c1[0](512)
    end1 = c1[3][0]
    c2 = _mk_carver(ns); c2[3][0] = u0
    ns.n_ohead = c2[0](2048)
    ns.n_ostage = c2[1](2048)
    end2 = c2[3][0]
    pos[0] = max(end1, end2)
    ns.n_ochunk = [ns.n_stage[:, 0:2048], ns.n_stage[:, 2048:4096]]
    ns.n_den = ns.g_row_full
    ns.n_bp = ns.row_b
    ns.l1a_bytes = pos[0]


def emit_l1attn(b, ns):
    b.mark("l1attn.proj")
    d = ns.dram
    v = "vector"
    NH = 32
    vk_sb = ns.n_vk

    b.dma("sync", out=ns.n_bp[0:1, 0:D], in_=d["l1_bp_row"], inc="xl")
    bp_tok = b.tok("xl")
    for sh in range(2):
        st0 = sh * HALF
        rows = [ns.row_m[0:1, st0:st0 + HALF], ns.row_a[0:1, st0:st0 + HALF]]
        emit_colstats(b, ns, 0, lambda dc: Xc(ns, dc, st0, HALF),
                      lambda bb: _kinds_ln2(bb, ns, nw=8), rows, "nln", nt=8)
    rt = b.tok("m1")

    b.wait_tok("gpsimd", ns.aw_guard)
    for i, wn in enumerate(("l1_wqT", "l1_wkT", "l1_wvT")):
        b.dma("gpsimd", out=ns.n_w.rearrange("p (i dc f) -> p (i dc) f", i=3, dc=8)
              [:, i * 8:(i + 1) * 8, :],
              in_=d[wn].rearrange("(dc p) f -> p dc f", p=128), inc="wl")
    wtok = b.tok("wl")
    b.op(v, "memset", vk_sb[:, 0:1024], 0.0, inc="dv")
    vk_zero = b.tok("dv")

    for tau in range(4):
        tw = tau * 512
        b.wait(v, "pe", ns.a_xln_guard)
        emit_bcast(b, ns, ns.row_m[0:1, tw:tw + 512], ns.bc_m, 512, rt, base=0)
        t_r = emit_bcast(b, ns, ns.row_a[0:1, tw:tw + 512], ns.bc_a, 512, rt, base=32)
        b.wait_tok(v, t_r)
        for dc in range(DC):
            b.op(v, "tensor_tensor", out=ns.tmp1[:, 0:512], in0=Xc(ns, dc, tw, 512),
                 in1=ns.bc_m[:, 0:512], op=ALU.subtract)
            b.op(v, "tensor_tensor", out=ns.n_ztile[:, dc * 512:(dc + 1) * 512],
                 in0=ns.tmp1[:, 0:512], in1=ns.bc_a[:, 0:512], op=ALU.mult, inc="dv")
        z_tok = b.tok("dv")
        b.wait_tok("tensor", wtok)
        b.wait_tok("tensor", z_tok)
        for fc in range(DC):
            ps = ns.psY[:, (fc % 2) * 512:(fc % 2) * 512 + 512]
            b.wait_tok("tensor", ns.a_psY_tok[fc % 2])
            for dc in range(DC):
                b.op("tensor", "matmul", ps,
                     ns.n_w[:, dc * 1024 + fc * 128: dc * 1024 + fc * 128 + 128],
                     ns.n_ztile[:, dc * 512:(dc + 1) * 512],
                     start=(dc == 0), stop=(dc == DC - 1),
                     inc=("pe" if dc == DC - 1 else None))
            peq = b.tok("pe")
            b.wait_tok("scalar", peq)
            b.wait("scalar", "m1", ns.a_stage_guard)
            b.op("scalar", "activation", out=ns.n_stage[:, fc * 512:(fc + 1) * 512],
                 in_=ps, func=AF.Relu, inc="ac")
            ns.a_psY_tok[fc % 2] = b.tok("ac")
        st_tok = b.tok("ac")
        b.wait_tok("sync", st_tok)
        b.dma("sync", out=d["q2_dram"].rearrange("(fc p) t -> p fc t", p=128)[:, :, tw:tw + 512],
              in_=ns.n_stage.rearrange("p (fc t) -> p fc t", fc=8), inc="m1")
        ns.a_stage_guard = b.val("m1")
        for P in (1, 2):
            dst = ns.n_ktile if P == 1 else ns.n_vtile
            for nw_ in range(4):
                for fh in range(2):
                    ps = ns.psY[:, fh * 512:(fh + 1) * 512]
                    b.wait_tok("tensor", ns.a_psY_tok[fh])
                    for dc in range(DC):
                        b.op("tensor", "matmul", ps,
                             ns.n_ztile[:, dc * 512 + nw_ * 128: dc * 512 + nw_ * 128 + 128],
                             ns.n_w[:, P * 8192 + dc * 1024 + fh * 512: P * 8192 + dc * 1024 + (fh + 1) * 512],
                             start=(dc == 0), stop=(dc == DC - 1),
                             inc=("pe" if dc == DC - 1 else None))
                    pet = b.tok("pe")
                    b.wait_tok("scalar", pet)
                    b.wait_tok("scalar", ns.n_kv_guard)
                    b.op("scalar", "activation", out=dst[:, nw_ * 1024 + fh * 512: nw_ * 1024 + (fh + 1) * 512],
                         in_=ps, func=(AF.Relu if P == 1 else AF.Identity), inc="ac")
                    ns.a_psY_tok[fh] = b.tok("ac")
        kv_tok = b.tok("ac")
        b.wait_tok("tensor", kv_tok)
        if tau == 0:
            b.wait_tok("tensor", vk_zero)
        for h in range(NH):
            par = h % 2
            vkps = ns.ps4[0:32, par * 512:par * 512 + 32]
            denps = ns.ps4[0:1, 1024 + par * 512:1024 + par * 512 + 32]
            b.wait_tok("tensor", ns.n_vkps_tok[par])
            for ns_ in range(4):
                b.op("tensor", "matmul", vkps,
                     ns.n_vtile[:, ns_ * 1024 + h * 32: ns_ * 1024 + h * 32 + 32],
                     ns.n_ktile[:, ns_ * 1024 + h * 32: ns_ * 1024 + h * 32 + 32],
                     start=(ns_ == 0), stop=(ns_ == 3))
                b.op("tensor", "matmul", denps,
                     ns.ones_bf[:, 0:1],
                     ns.n_ktile[:, ns_ * 1024 + h * 32: ns_ * 1024 + h * 32 + 32],
                     start=(ns_ == 0), stop=(ns_ == 3),
                     inc=("pe" if ns_ == 3 else None))
            pevk = b.tok("pe")
            b.wait_tok(v, pevk)
            b.op(v, "tensor_tensor", out=vk_sb[0:32, h * 32:(h + 1) * 32],
                 in0=vk_sb[0:32, h * 32:(h + 1) * 32], in1=vkps,
                 op=ALU.add)
            b.op(v, "tensor_tensor", out=vk_sb[32:33, h * 32:(h + 1) * 32],
                 in0=vk_sb[32:33, h * 32:(h + 1) * 32], in1=denps,
                 op=ALU.add, inc="dv")
            ns.n_vkps_tok[par] = b.tok("dv")
        ns.n_vk_tok = b.tok("dv")
        ns.a_xln_guard = b.val("pe")
        ns.n_kv_guard = ("pe", b.val("pe"))

    b.mark("l1attn.heads")
    b.wait("sync", "m1")
    b.barrier()

    def qh_load(h, hs, sem):
        b.wait_tok("gpsimd", ns.a_qk_guard[hs])
        b.dma("gpsimd", out=ns.n_qh[hs][0:32, :], in_=d["q2_dram"][h * 32:(h + 1) * 32, :], inc=sem)
        return b.tok(sem)

    qh_load(0, 0, "qk")
    qh_pre = qh_load(1, 1, "qk")
    qh_next = [qh_pre, qh_pre]
    for h in range(NH):
        hs = h % 2
        qh_tok = qh_next[hs]
        b.wait_tok("tensor", ns.n_vk_tok)
        b.wait("tensor", "dv", b.val("dv"))
        b.op("tensor", "transpose", ns.ps_misc2[0:32, 0:33], vk_sb[0:33, h * 32:(h + 1) * 32],
             ns.ident_f[0:33, 0:33], inc="pe")
        b.wait(v, "pe", b.val("pe"))
        b.op(v, "tensor_copy", out=ns.n_vkT[0:32, 0:33], in_=ns.ps_misc2[0:32, 0:33], inc="dv")
        vkt_tok = b.tok("dv")
        b.wait_tok("tensor", qh_tok)
        b.wait_tok("tensor", vkt_tok)
        for tau in range(4):
            ps33 = ns.psY[0:33, (tau % 2) * 512:(tau % 2) * 512 + 512]
            b.wait_tok("tensor", ns.a_psY_tok[tau % 2])
            b.op("tensor", "matmul", ps33,
                 ns.n_vkT[0:32, 0:33],
                 ns.n_qh[hs][0:32, tau * 512:(tau + 1) * 512],
                 start=True, stop=True, inc="pe")
            peo = b.tok("pe")
            b.wait_tok(v, peo)
            b.wait(v, "m1", ns.n_oh_guard)
            b.op(v, "tensor_copy", out=ns.n_ohead[0:33, tau * 512:(tau + 1) * 512],
                 in_=ps33, inc="dv")
            ns.a_psY_tok[tau % 2] = b.tok("dv")
        oh_tok = b.tok("dv")
        b.wait_tok(v, oh_tok)
        b.op(v, "tensor_scalar", out=ns.n_den[0:1, 0:T], in0=ns.n_ohead[32:33, 0:T],
             scalar1=1e-15, scalar2=None, op0=ALU.add, inc="dv")
        b.wait("scalar", "dv", b.val("dv"))
        b.op("scalar", "activation", out=ns.n_den[0:1, 0:T], in_=ns.n_den[0:1, 0:T],
             func=AF.Ln)
        b.op("scalar", "activation", out=ns.n_den[0:1, 0:T], in_=ns.n_den[0:1, 0:T],
             func=AF.Exp, scale=-1.0, inc="ac")
        r_tok = b.tok("ac")
        for tau in range(4):
            b.wait_tok("tensor", r_tok)
            b.wait("tensor", "dv", b.val("dv"))
            b.op("tensor", "matmul", ns.ps_misc[:, 0:512],
                 ns.ones_col[0:1, 0:128], ns.n_den[0:1, tau * 512:(tau + 1) * 512],
                 start=True, stop=True, inc="pe")
            b.wait(v, "pe", b.val("pe"))
            b.op(v, "tensor_copy", out=ns.n_rbc[:, 0:512], in_=ns.ps_misc[:, 0:512], inc="dv")
            b.drain(v)
            b.op(v, "tensor_tensor", out=ns.n_ostage[0:32, tau * 512:(tau + 1) * 512],
                 in0=ns.n_ohead[0:32, tau * 512:(tau + 1) * 512],
                 in1=ns.n_rbc[0:32, 0:512], op=ALU.mult, inc="dv")
        ost_tok = b.tok("dv")
        b.wait_tok("sync", ost_tok)
        b.dma("sync", out=d["o2_dram"][h * 32:(h + 1) * 32, :], in_=ns.n_ostage[0:32, 0:T], inc="m1")
        ns.n_oh_guard = b.val("m1")
        ns.a_qk_guard[hs] = b.tok("pe")
        if h + 2 < NH:
            qh_next[hs] = qh_load(h + 2, hs, "qka" if hs == 0 else "qkb")

    b.mark("l1attn.oproj")
    b.wait("sync", "m1")
    b.barrier()
    b.dma("gpsimd", out=ns.n_w.rearrange("p (i dc f) -> p (i dc) f", i=3, dc=8)[:, 0:8, :],
          in_=d["l1_wpT"].rearrange("(dc p) f -> p dc f", p=128), inc="wl")
    wp_tok = b.tok("wl")
    b.wait_tok("tensor", bp_tok)
    oc_next = [None, None]
    for oc in range(DC):
        slot = oc % 2
        b.wait_tok("gpsimd", ns.n_oc_guard[slot])
        b.dma("gpsimd", out=ns.n_ochunk[slot],
              in_=d["o2_dram"][oc * 128:(oc + 1) * 128, :],
              inc=("qka" if slot == 0 else "qkb"))
        octok = b.tok("qka" if slot == 0 else "qkb")
        b.wait_tok("tensor", wp_tok)
        b.wait_tok("tensor", octok)
        for dc in range(DC):
            for tau in range(4):
                ps = ns.ps_misc[:, 0:512]
                b.wait("tensor", "dv", ns.a_op_guard)
                wsl = ns.n_w[:, oc * 1024 + dc * 128: oc * 1024 + dc * 128 + 128]
                if oc == 0:
                    b.op("tensor", "matmul", ps, wsl,
                         ns.n_ochunk[slot][:, tau * 512:(tau + 1) * 512],
                         start=True, stop=False)
                    b.op("tensor", "matmul", ps,
                         ns.n_bp[0:1, dc * 128:(dc + 1) * 128],
                         ns.ones_col[64:65, 0:512],
                         start=False, stop=True, inc="pe")
                else:
                    b.op("tensor", "matmul", ps, wsl,
                         ns.n_ochunk[slot][:, tau * 512:(tau + 1) * 512],
                         start=True, stop=True, inc="pe")
                peo = b.tok("pe")
                b.wait_tok(v, peo)
                b.op(v, "tensor_tensor", out=Xc(ns, dc, tau * 512, 512),
                     in0=Xc(ns, dc, tau * 512, 512), in1=ps, op=ALU.add, inc="dv")
                ns.a_op_guard = b.val("dv")
        ns.n_oc_guard[slot] = b.tok("pe")



def _kinds_ln2(b, ns, nw=8):
    v = "vector"; s = "scalar"
    S = ns.scol
    P = ns.ps_misc
    b.op(v, "tensor_scalar", out=S[:, 0:nw], in0=P[:, 0:2 * nw].rearrange("p (w k) -> p k w", k=2)[:, 0, :],
         scalar1=1.0 / D, scalar2=None, op0=ALU.mult)
    b.op(v, "tensor_scalar", out=S[:, 3 * nw:4 * nw], in0=P[:, 0:2 * nw].rearrange("p (w k) -> p k w", k=2)[:, 1, :],
         scalar1=1.0 / D, scalar2=None, op0=ALU.mult)
    b.op(v, "tensor_tensor", out=S[:, 4 * nw:5 * nw], in0=S[:, 0:nw], in1=S[:, 0:nw], op=ALU.mult)
    b.op(v, "tensor_tensor", out=S[:, 5 * nw:6 * nw], in0=S[:, 3 * nw:4 * nw], in1=S[:, 4 * nw:5 * nw], op=ALU.subtract)
    b.op(v, "tensor_scalar", out=S[:, 6 * nw:7 * nw], in0=S[:, 5 * nw:6 * nw],
         scalar1=1e-5, scalar2=None, op0=ALU.add, inc="dv")
    b.wait(s, "dv", b.val("dv"))
    b.op(s, "activation", out=S[:, 7 * nw:8 * nw], in_=S[:, 6 * nw:7 * nw], func=AF.Ln)
    b.op(s, "activation", out=S[:, nw:2 * nw], in_=S[:, 7 * nw:8 * nw], func=AF.Exp, scale=-0.5, inc="ac")
    b.wait(v, "ac", b.val("ac"))
    b.op(v, "tensor_copy", out=S[:, nw:2 * nw], in_=S[:, nw:2 * nw], inc="dv")


def emit_l1ffn(b, ns):
    for half in range(2):
        _l1ffn_half(b, ns, half, half * HALF)
        b.wait("scalar", "cw")
        b.barrier()


def _l1_stats_bcast(b, ns, src, tag):
    rows = [ns.row_m, ns.row_a]
    rt = emit_colstats(b, ns, 0, src, lambda bb: _kinds_ln2(bb, ns), rows, tag)
    t_m = emit_bcast(b, ns, ns.row_m, ns.bc_m, HALF, rt, base=0)
    t_a = emit_bcast(b, ns, ns.row_a, ns.bc_a, HALF, rt, base=32)
    return t_a


def _l1_gate(b, ns, half, t0):
    nw = 8
    b.wait_tok("tensor", ns.wg_tok)
    bgt = emit_bcast(b, ns, ns.l1_gb_row, ns.bc_g16, 8)
    ps_free = bgt
    v = "vector"
    for w in range(nw):
        tw = t0 + 128 * w
        b.wait_tok("tensor", ps_free)
        for dc in range(DC):
            b.op("tensor", "matmul", ns.ps_misc[:, 0:8],
                 ns.l1_in_bf[:, dc * HALF + 128 * w: dc * HALF + 128 * w + 128],
                 ns.l1_gw_bf[:, dc * 8:(dc + 1) * 8],
                 start=(dc == 0), stop=(dc == DC - 1), inc=("pe" if dc == DC - 1 else None))
        b.wait(v, "pe", b.val("pe"))
        L = ns.gl[:, 0:8]
        G = ns.gex
        b.op(v, "tensor_tensor", out=L, in0=ns.ps_misc[:, 0:8], in1=ns.bc_g16[:, 0:8], op=ALU.add,
             inc="dv")
        ps_free = b.tok("dv")
        b.op(v, "reduce_max", out=ns.gm[:, 0:1], in_=L, axis=AX.X, negate=True)
        b.wait("scalar", "dv", b.val("dv"))
        b.op(v, "tensor_copy", out=ns.gm[:, 1:2], in_=ns.gm[:, 0:1], inc="dv")
        b.wait("scalar", "dv", b.val("dv"))
        b.op("scalar", "activation", out=ns.gcur[:, 0:8], in_=L, func=AF.Exp, bias=ns.gm[:, 1:2], inc="ac")
        b.wait(v, "ac", b.val("ac"))
        b.op(v, "reduce_sum", out=ns.gm[:, 2:3], in_=ns.gcur[:, 0:8], axis=AX.X)
        b.op(v, "reciprocal", out=ns.gm[:, 3:4], in_=ns.gm[:, 2:3])
        b.op(v, "tensor_scalar", out=ns.gl[:, 8:16], in0=ns.gcur[:, 0:8],
             scalar1=ns.gm[:, 3:4], scalar2=None, op0=ALU.mult)
        W_ = ns.gl[:, 8:16]
        b.op(v, "reduce_max", out=ns.gm[:, 0:1], in_=W_, axis=AX.X, negate=True)
        b.op(v, "tensor_scalar", out=G[:, 0:8], in0=W_, scalar1=ns.gm[:, 0:1],
             scalar2=0.0, op0=ALU.add, op1=ALU.is_ge)
        b.op(v, "tensor_scalar", out=G[:, 8:9], in0=ns.gm[:, 0:1],
             scalar1=-1.0, scalar2=None, op0=ALU.mult)
        b.op(v, "scalar_tensor_tensor", out=ns.gcur[:, 0:8], in0=G[:, 0:8], scalar=-1e30,
             in1=W_, op0=ALU.mult, op1=ALU.add)
        b.op(v, "reduce_max", out=ns.gm[:, 1:2], in_=ns.gcur[:, 0:8], axis=AX.X, negate=True)
        b.op(v, "tensor_scalar", out=ns.gmask[:, 0:8], in0=ns.gcur[:, 0:8], scalar1=ns.gm[:, 1:2],
             scalar2=0.0, op0=ALU.add, op1=ALU.is_ge)
        b.op(v, "tensor_scalar", out=ns.gm[:, 2:3], in0=ns.gm[:, 1:2],
             scalar1=-1.0, scalar2=None, op0=ALU.mult)
        b.op(v, "tensor_scalar", out=G[:, 32:40], in0=ns.gmask[:, 0:8],
             scalar1=ns.gm[:, 2:3], scalar2=None, op0=ALU.mult, inc="dv")
        b.wait("tensor", "dv", b.val("dv"))
        b.op("tensor", "transpose", ns.ps_misc2[0:40, 0:128], G[:, 0:40], ns.ident_f[:], inc="pe")
        b.wait(v, "pe", b.val("pe"))
        lw = tw - t0
        b.op(v, "tensor_copy", out=ns.l1_oh[0:9, lw:lw + 128], in_=ns.ps_misc2[0:9, 0:128])
        b.op(v, "tensor_copy", out=ns.l1_oh[32:40, lw:lw + 128], in_=ns.ps_misc2[32:40, 0:128])
        b.op(v, "tensor_copy", out=ns.l1_ohb[0:8, lw:lw + 128], in_=ns.ps_misc2[0:8, 0:128])
        b.op(v, "tensor_copy", out=ns.l1_ohb[32:40, lw:lw + 128], in_=ns.ps_misc2[32:40, 0:128], inc="dv")
    return b.tok("dv")


def _l1_iter(b, ns, half, t0, it, gtok):
    d = ns.dram
    E = 8
    rbase = 0 if it == 0 else 32
    def w2_load(e, hc, slot, sem="wl", flush=True):
        tok = cached_load(b, ns, ("l1w2", e, hc), ns.l1_w2[slot],
                          d["l1_w2p"][e, hc], d["c_l1w2"][e, hc], sem,
                          ns.w2_slot_tok[slot])
        if flush:
            flush_cw(b, ns, tok)
        return tok

    for e in range(E):
        w1v = ns.l1_w1full.rearrange("p (dc f) -> p dc f", dc=8)
        for q in range(4):
            cached_load(b, ns, ("l1w1", e, q),
                        w1v[:, :, q * 512:(q + 1) * 512],
                        d["l1_w1"][e, :, q * 512:(q + 1) * 512]
                        .rearrange("(dc p) f -> p dc f", p=128),
                        d["c_l1w1"][e, q].rearrange("p (dc f) -> p dc f", dc=8),
                        "wl", ns.w1_slot_tok[0] if q == 0 else None)
        w2_load(e, 0, 0, flush=False)
        w2_load(e, 1, 1, flush=False)
        wtok = [b.tok("wl"), b.tok("wlc")]
        flush_cw(b, ns, wtok)
        ns.w2_pre_tok = wtok

        b.wait_tok("gpsimd", gtok)
        b.wait("gpsimd", "dv", ns.gbc_guard)
        b.dma("gpsimd", out=ns.g_row_bf[0:1, 0:HALF],
              in_=ns.l1_oh[rbase + e:rbase + e + 1, 0:HALF], inc="gx")
        b.wait("tensor", "gx", b.val("gx"))
        mb_tok = emit_bcast(b, ns, ns.g_row_bf, ns.l1_ohbc, HALF, bf=True)
        ns.gbc_guard = b.val("dv")

        b.wait_tok("tensor", wtok)
        if e == 0:
            b.wait_tok("tensor", ns.l1_in_tok)
        hview = ns.l1_h.rearrange("p (fc t) -> p fc t", fc=16)
        for tau in range(2):
            for fc in range(16):
                psH = ns.psA[:, (fc % 2) * 512:(fc % 2) * 512 + 512]
                b.wait_tok("tensor", ns.psA_tok[fc % 2])
                for dc in range(DC):
                    b.op("tensor", "matmul", psH,
                         ns.l1_w1full[:, dc * 2048 + fc * 128: dc * 2048 + fc * 128 + 128],
                         ns.l1_in_bf[:, dc * HALF + tau * 512: dc * HALF + (tau + 1) * 512],
                         start=(dc == 0), stop=(dc == DC - 1),
                         inc=("pe" if dc == DC - 1 else None))
                peH = b.tok("pe")
                v = "vector"
                b.wait_tok(v, peH)
                b.wait_tok(v, ns.l1_h_guard)
                b.op(v, "tensor_scalar", out=ns.l1_hs[:, 0:512],
                     in0=psH, scalar1=ns.l1_b1[:, e * 16 + fc:e * 16 + fc + 1],
                     scalar2=0.0, op0=ALU.add, op1=ALU.max, inc="dv")
                ns.psA_tok[fc % 2] = b.tok("dv")
                b.wait_tok(v, mb_tok)
                b.op(v, "tensor_tensor",
                     out=hview[:, fc, 0:512],
                     in0=ns.l1_hs[:, 0:512], in1=ns.l1_ohbc[:, tau * 512:(tau + 1) * 512],
                     op=ALU.mult, inc="dv")
            h_tok = b.tok("dv")
            if tau == 1:
                ns.l1_w1_free = b.tok("pe")

            for hc in range(DC):
                gidx = tau * DC + hc
                w2slot = gidx % 2
                w2tok = ns.w2_pre_tok if gidx < 2 else ns.w2_cur_tok[w2slot]
                psY = ns.psY[:, (hc % 2) * 512:(hc % 2) * 512 + 512]
                b.wait_tok("tensor", ns.psY_tok[hc % 2])
                b.wait_tok("tensor", h_tok)
                b.wait_tok("tensor", w2tok)
                for ic in range(16):
                    last = (ic == 15 and e != 0)
                    b.op("tensor", "matmul", psY,
                         ns.l1_w2[w2slot][:, ic * 128: ic * 128 + 128],
                         hview[:, ic, 0:512],
                         start=(ic == 0), stop=last, inc=("pe" if last else None))
                if e == 0:
                    b.op("tensor", "matmul", psY,
                         ns.l1_b2t[rbase:rbase + 8, hc * 128:(hc + 1) * 128],
                         ns.l1_ohb[rbase:rbase + 8, tau * 512: (tau + 1) * 512],
                         start=False, stop=True, inc="pe")
                peY = b.tok("pe")
                ns.w2_slot_tok[w2slot] = peY
                if gidx + 2 < 16:
                    g2i = gidx + 2
                    ns.w2_cur_tok[w2slot] = w2_load(e, g2i % DC, w2slot,
                                                    sem=("w2a" if w2slot == 0 else "w2b"))
                b.wait_tok("vector", peY)
                if it == 0 and e == 0:
                    b.op("vector", "tensor_copy",
                         out=ns.l1_acc[:, hc * HALF + tau * 512: hc * HALF + (tau + 1) * 512],
                         in_=psY, inc="dv")
                else:
                    b.op("vector", "tensor_tensor",
                         out=ns.l1_acc[:, hc * HALF + tau * 512: hc * HALF + (tau + 1) * 512],
                         in0=ns.l1_acc[:, hc * HALF + tau * 512: hc * HALF + (tau + 1) * 512],
                         in1=psY, op=ALU.add, inc="dv")
                ns.psY_tok[hc % 2] = b.tok("dv")
            ns.l1_h_guard = b.tok("pe")
            if tau == 1:
                ns.w1_slot_tok = [ns.l1_w1_free, ns.l1_w1_free]


def _l1ffn_half(b, ns, half, t0):
    b.mark("l1ffn.stats")
    v = "vector"
    ns.l1_h_guard = None
    t_a = _l1_stats_bcast(b, ns, lambda dc: Xc(ns, dc, t0, HALF), "l1ln")
    b.wait_tok(v, t_a)
    for dc in range(DC):
        X = Xc(ns, dc, t0, HALF)
        b.op(v, "tensor_tensor", out=ns.tmp1[:, 0:HALF], in0=X, in1=ns.bc_m[:, 0:HALF], op=ALU.subtract)
        b.op(v, "tensor_tensor", out=ns.l1_in_bf[:, dc * HALF:(dc + 1) * HALF],
             in0=ns.tmp1[:, 0:HALF], in1=ns.bc_a[:, 0:HALF], op=ALU.mult, inc="dv")
    ns.l1_in_tok = b.tok("dv")

    b.wait_tok("tensor", ns.l1_in_tok)
    b.wait("tensor", "dv", b.val("dv"))
    b.mark("l1ffn.gate")
    gtok = _l1_gate(b, ns, half, t0)
    b.mark("l1ffn.iter1")

    _l1_iter(b, ns, half, t0, 0, gtok)

    b.mark("l1ffn.boundary")
    b.wait(v, "pe", b.val("pe"))
    t_m2 = emit_bcast(b, ns, ns.row_m, ns.bc_m, HALF, None, base=0)
    t_a2 = emit_bcast(b, ns, ns.row_a, ns.bc_a, HALF, None, base=32)
    b.wait("gpsimd", "dv", ns.gbc_guard)
    b.dma("gpsimd", out=ns.g_row_bf[0:1, 0:HALF], in_=ns.l1_oh[8:9, 0:HALF], inc="gx")
    b.wait("tensor", "gx", b.val("gx"))
    v1_tok = emit_bcast(b, ns, ns.g_row_bf, ns.l1_v1bc, HALF, bf=True)
    ns.gbc_guard = b.val("dv")
    b.wait_tok(v, t_a2)
    b.wait_tok(v, v1_tok)
    for dc in range(DC):
        X = Xc(ns, dc, t0, HALF)
        ACC = ns.l1_acc[:, dc * HALF:(dc + 1) * HALF]
        b.op(v, "tensor_tensor", out=ns.tmp1[:, 0:HALF], in0=X, in1=ns.bc_m[:, 0:HALF], op=ALU.subtract)
        b.op(v, "tensor_tensor", out=ns.tmp2[:, 0:HALF], in0=ns.tmp1[:, 0:HALF], in1=ns.bc_a[:, 0:HALF],
             op=ALU.mult)
        b.op(v, "scalar_tensor_tensor", out=ns.l1_in_bf[:, dc * HALF:(dc + 1) * HALF],
             in0=ACC, scalar=0.5, in1=ns.tmp2[:, 0:HALF], op0=ALU.mult, op1=ALU.add)
        b.op(v, "tensor_tensor", out=ACC, in0=ACC, in1=ns.l1_v1bc[:, 0:HALF], op=ALU.mult)
        b.op(v, "tensor_tensor", out=ACC, in0=ACC, in1=ns.tmp2[:, 0:HALF], op=ALU.add, inc="dv")
    ns.l1_in_tok = b.tok("dv")
    ns.l1_h_guard = None

    b.mark("l1ffn.iter2")
    _l1_iter(b, ns, half, t0, 1, gtok)
    b.mark("l1ffn.final")

    b.wait(v, "pe", b.val("pe"))
    b.wait("scalar", "pe", b.val("pe"))
    b.wait("scalar", "dv", b.val("dv"))
    b.wait("tensor", "dv", b.val("dv"))
    t_a3 = _l1_stats_bcast(b, ns, lambda dc: ns.l1_acc[:, dc * HALF:(dc + 1) * HALF], "l1ln2")
    b.wait_tok(v, t_a3)
    for dc in range(DC):
        X = Xc(ns, dc, t0, HALF)
        ACC = ns.l1_acc[:, dc * HALF:(dc + 1) * HALF]
        b.op(v, "tensor_tensor", out=ns.tmp1[:, 0:HALF], in0=ACC, in1=ns.bc_m[:, 0:HALF], op=ALU.subtract)
        b.op(v, "tensor_tensor", out=ns.tmp2[:, 0:HALF], in0=ns.tmp1[:, 0:HALF], in1=ns.bc_a[:, 0:HALF],
             op=ALU.mult, inc="dv")
        b.wait("scalar", "dv", b.val("dv"))
        b.op("scalar", "activation", out=ns.tmp1[:, 0:HALF], in_=ns.tmp2[:, 0:HALF], func=AF.Identity,
             scale=ns.l1_gam05[:, dc:dc + 1], bias=ns.l1_bet05[:, dc:dc + 1], inc="ac")
        b.wait(v, "ac", b.val("ac"))
        b.op(v, "tensor_tensor", out=X, in0=X, in1=ns.tmp1[:, 0:HALF], op=ALU.add, inc="dv")


ARENA_WORDS = 29952


def _mk_carver(ns):
    pos = [0]

    def al(nbytes, align=64):
        p = (pos[0] + align - 1) // align * align
        pos[0] = p + nbytes
        assert pos[0] <= ARENA_WORDS * 4, f"arena overflow {pos[0]}"
        return p

    def f32(n, rows=128):
        p = al(n * 4)
        return ns.AR[0:rows, p // 4: p // 4 + n]

    def bf(n, rows=128):
        p = al(n * 2)
        return ns.AR_B[0:rows, p // 2: p // 2 + n]

    return f32, bf, al, pos


def carve_l0ffn(ns):
    f32, bf, al, pos = _mk_carver(ns)
    ns.xn_bf = bf(DC * HALF)
    ns.sg_bf = bf(DC * HALF)
    ns.w1a_sb = [bf(8 * 512), bf(8 * 512)]
    ns.w1b_sb = [bf(8 * 512), bf(8 * 512)]
    ns.w2_sb = [bf(8 * 128), bf(8 * 128)]
    ns.gluA2 = [f32(512), f32(512)]
    ns.gluS2 = [f32(512), f32(512)]
    ns.gluB2 = [f32(512), f32(512)]
    ns.g_bc = bf(HALF)
    ns.bc_b = f32(HALF)
    ns.gp_T = f32(T)
    ns.gp_Tb = bf(T)
    ns.bc_m = f32(HALF)
    ns.bc_a = f32(HALF)
    ns.tmp1 = f32(HALF)
    ns.tmp2 = f32(HALF)
    ns.sq = ns.tmp2
    ns.l0_bytes = pos[0]


def carve_l1ffn(ns):
    f32, bf, al, pos = _mk_carver(ns)
    ns.l1_in_bf = bf(DC * HALF)
    ns.l1_acc = f32(DC * HALF)
    ns.l1_w1full = bf(8 * 2048)
    ns.l1_w2 = [bf(16 * 128), bf(16 * 128)]
    ns.l1_hs = f32(512)
    ns.l1_oh = f32(HALF)
    ns.l1_ohb = bf(HALF)
    ns.l1_ohbc = bf(HALF)
    ns.l1_v1bc = bf(HALF)
    p = al(16384)
    ns.bc_m = ns.AR[:, p // 4: p // 4 + HALF]
    ns.bc_a = ns.AR[:, p // 4 + HALF: p // 4 + 2 * HALF]
    ns.tmp1 = ns.AR[:, p // 4 + 2 * HALF: p // 4 + 3 * HALF]
    ns.tmp2 = ns.AR[:, p // 4 + 3 * HALF: p // 4 + 4 * HALF]
    ns.sq = ns.tmp2
    ns.l1_h = ns.AR_B[:, p // 2: p // 2 + 16 * 512]
    ns.l1_bytes = pos[0]


def build_program(g, sublayers):
    nc = bass.Bass("TRN2", target_bir_lowering=False, debug=False)
    ns = make_ns(nc, g, sublayers)
    d = ns.dram

    with ExitStack() as ctx:
        sb = lambda name, shape, dt: ctx.enter_context(nc.sbuf_tensor("s_" + name, shape, dt))
        psum = lambda name, shape: ctx.enter_context(nc.psum_tensor(name, shape, F32))

        ns.X = sb("X", [128, DC * T], F32)
        ns.ident_f = sb("ident_f", [128, 128], F32)
        ns.ones_col = sb("ones_col", [128, 512], F32)
        ns.ones_bf = sb("ones_bf", [128, 128], BF16)
        arena = sb("arena", [128, ARENA_WORDS], F32)
        ns.AR = arena[:, :]
        ns.AR_B = arena.bitcast(BF16)[:, :]

        rows1 = sb("rows1", [65, T], F32)
        rows2 = sb("rows2", [1, T], F32)
        ns.row_m = rows1[0:1, :]
        ns.row_a = rows1[32:33, :]
        ns.row_b = rows1[64:65, :]
        ns.g_row = rows2[0:1, :]
        ns.g_row_full = rows2[0:1, :]
        ns.g_row_bf = rows2.bitcast(BF16)[0:1, :]
        ns.bc_g16 = sb("bc_g16", [128, 16], F32)
        ns.scol = sb("scol", [128, 128], F32)
        ns.rowstage = sb("rowstage", [128, 128], F32)

        ns.gl = sb("gl", [128, 16], F32)
        ns.gcur = sb("gcur", [128, 16], F32)
        ns.gmask = sb("gmask", [128, 16], F32)
        ns.gm = sb("gm", [128, 4], F32)
        ns.gex = sb("gex", [128, 48], F32)

        ns.wg_bf = sb("wg_bf", [128, DC * 16], BF16)
        ns.bg_row = sb("bg_row", [1, 16], F32)[0:1, :]
        ns.l0_ns = sb("l0_ns", [128, DC], F32)
        ns.l0_b1a = sb("l0_b1a", [128, 16 * 8], F32)
        ns.l0_b1b = sb("l0_b1b", [128, 16 * 8], F32)
        ns.b2_bf = sb("b2_bf", [16, 1024], BF16)

        ns.bqi_p = sb("bqi_p", [128, DC], F32)
        ns.bki_p = sb("bki_p", [128, DC], F32)

        ns.l1_gw_bf = sb("l1_gw_bf", [128, DC * 8], BF16)
        ns.l1_gb_row = sb("l1_gb_row", [1, 8], F32)[0:1, :]
        ns.l1_b1 = sb("l1_b1", [128, 8 * 16], F32)
        ns.l1_b2t = sb("l1_b2t", [40, 1024], BF16)
        ns.l1_gam = sb("l1_gam", [128, DC], F32)
        ns.l1_bet = sb("l1_bet", [128, DC], F32)
        ns.l1_gam05 = sb("l1_gam05", [128, DC], F32)
        ns.l1_bet05 = sb("l1_bet05", [128, DC], F32)

        ns.ps4 = psum("ps4", [128, 2048])
        ns.psA = ns.ps4[:, 0:1024]
        ns.psB = ns.ps4[:, 1024:2048]
        ns.psY = psum("psY", [128, 1024])
        ns.ps_misc = psum("ps_misc", [128, 512])
        ns.ps_misc2 = psum("ps_misc2", [128, 512])

        ns.sems = {}
        for s in ("xl", "wl", "w2a", "w2b", "qk", "qka", "qkb", "gx", "cw", "wlc", "w2ac", "w2bc", "m1", "m2", "pe", "dv", "ac", "gp", "bar", "st"):
            ns.sems[s] = ctx.enter_context(nc.semaphore(s))

        block = ctx.enter_context(nc.Block())
        b = Builder(ns)

        def program(bb):
            ns.w1_slot_idx = 0
            ns.w2_slot_idx = 0
            ns.w1_slot_tok = [None, None]
            ns.w2_slot_tok = [None, None]
            ns.w1_load_tok = {}
            ns.w2_load_tok = {}
            ns.psA_tok = [None, None]
            ns.psB_tok = [None, None]
            ns.psY_tok = [None, None]
            ns.gbc_guard = 0
            ns.sg_guard = None
            ns.glu_par = 0
            ns.glu_free = [0, 0]
            ns.w2_cur_tok = [None, None]
            ns.w2_pre_tok = None
            ns.aw_guard = None
            ns.a_xln_guard = 0
            ns.a_pbf_guard = 0
            ns.a_cs_guard = [0, 0]
            ns.a_ps_tok = [None, None]
            ns.a_psY_tok = [None, None]
            ns.a_stage_guard = 0
            ns.a_p1_tok = [None] * 4
            ns.a_p2_tok = [None, None]
            ns.a_expT_guard = [None] * 8
            ns.a_qk_guard = [None, None]
            ns.a_op_guard = 0
            ns.bvi_tok = None
            ns.n_kv_guard = None
            ns.n_vk_tok = None
            ns.n_oh_guard = 0
            ns.n_oc_guard = [None, None]
            ns.n_vkps_tok = [None, None]
            ns.l1_w1_free = None
            ns.cache_done = {}
            ns.pending_cw = []
            for e in Builder.AUTODRAIN:
                setattr(bb, "_dirty_" + e, False)

            emit_load_consts(bb, ns)
            emit_load_x(bb, ns)
            need_l0 = "l0ffn" in sublayers
            need_l1 = "l1ffn" in sublayers
            if need_l0:
                bb.dma("gpsimd", out=ns.wg_bf.rearrange("p (dc e) -> p dc e", dc=DC),
                       in_=d["l0_wg"].rearrange("(dc p) e -> p dc e", p=128), inc="wl")
                bb.dma("sync", out=ns.bg_row, in_=d["l0_bg_row"], inc="xl")
                bb.dma("sync", out=ns.l0_ns[:], in_=d["l0_ns_p"], inc="xl")
                bb.dma("sync", out=ns.l0_b1a[:], in_=d["l0_b1a_p"], inc="xl")
                bb.dma("sync", out=ns.l0_b1b[:], in_=d["l0_b1b_p"], inc="xl")
                bb.dma("gpsimd", out=ns.b2_bf[:], in_=d["l0_b2"], inc="wl")
            if "l0attn" in sublayers:
                bb.dma("sync", out=ns.bqi_p[:], in_=d["bqi_p"], inc="xl")
                bb.dma("sync", out=ns.bki_p[:], in_=d["bki_p"], inc="xl")
            if need_l1:
                bb.dma("gpsimd", out=ns.l1_gw_bf.rearrange("p (dc e) -> p dc e", dc=DC),
                       in_=d["l1_gw"].rearrange("(dc p) e -> p dc e", p=128), inc="wl")
                bb.dma("sync", out=ns.l1_gb_row, in_=d["l1_gb_row"], inc="xl")
                bb.dma("sync", out=ns.l1_b1[:], in_=d["l1_b1_p"], inc="xl")
                bb.dma("gpsimd", out=ns.l1_b2t[0:8, :], in_=d["l1_b2"], inc="wl")
                bb.dma("gpsimd", out=ns.l1_b2t[32:40, :], in_=d["l1_b2"], inc="wl")
                bb.dma("sync", out=ns.l1_gam[:], in_=d["l1_gam_p"], inc="xl")
                bb.dma("sync", out=ns.l1_bet[:], in_=d["l1_bet_p"], inc="xl")
            ns.wg_tok = ("wl", bb.val("wl"))
            bb.wait("vector", "xl")
            bb.wait("vector", "wl")
            bb.wait("tensor", "xl")
            bb.wait_tok("tensor", ns.ones_tok)
            bb.wait("tensor", "wl")
            bb.wait("scalar", "xl")
            if need_l1:
                bb.op("vector", "tensor_scalar", out=ns.l1_gam05[:], in0=ns.l1_gam[:],
                      scalar1=0.5, scalar2=None, op0=ALU.mult)
                bb.op("vector", "tensor_scalar", out=ns.l1_bet05[:], in0=ns.l1_bet[:],
                      scalar1=0.5, scalar2=None, op0=ALU.mult, inc="dv")
            bb.barrier()

            for sl in sublayers:
                if sl == "l0ffn":
                    carve_l0ffn(ns)
                    emit_l0ffn(bb, ns)
                elif sl == "l1ffn":
                    carve_l1ffn(ns)
                    emit_l1ffn(bb, ns)
                elif sl == "l0attn":
                    carve_l0attn(ns)
                    emit_l0attn(bb, ns)
                elif sl == "l1attn":
                    carve_l1attn(ns)
                    emit_l1attn(bb, ns)
                else:
                    raise NotImplementedError(sl)
                bb.barrier()

            bb.mark("store")
            emit_store_x(bb, ns)

        for name in ENGINES:
            def body(engine, _name=name):
                b.begin(_name, engine)
                program(b)
            getattr(block, name)(body)

    return nc


def _run(nc, in_maps, **kw):
    return run_bass_kernel_spmd(nc, in_maps, core_ids=list(range(NCORES)), **kw)



def kernel(**inputs):
    g, xs = prep_host(inputs)
    seq = ["l0ffn", "l0attn", "l0ffn", "l1ffn", "l1attn", "l1ffn"]
    nc = build_program(g, seq)
    in_maps = [dict(g, xT=xs[b]) for b in range(NCORES)]
    res = run_bass_kernel_spmd(nc, in_maps, core_ids=list(range(NCORES)))
    out = np.stack([np.asarray(r["yT"]).T for r in res.results])
    return np.ascontiguousarray(out, dtype=np.float32)
